# version 18
# speedup vs baseline: 1.0886x; 1.0574x over previous
import numpy as np
import math
from contextlib import ExitStack
import concourse.bass as bass
import concourse.mybir as mybir
from concourse.bass_utils import run_bass_kernel_spmd

F32 = mybir.dt.float32
BF16 = mybir.dt.bfloat16
I32 = mybir.dt.int32
U32 = mybir.dt.uint32
AF = mybir.ActivationFunctionType
ALU = mybir.AluOpType
AX = mybir.AxisListType

D = 1024
T = 4096
NC = 8
NTT = 8
TTW = 512
EPS_LN = 1e-5

ENGS = ("sync", "gpsimd", "tensor", "vector", "scalar")
NO_MILESTONE_DRAIN = True
DEBUG_BARRIERS = 0
DEBUG_TAPS = None


class Sem:
    def __init__(self, handle, name):
        self.h = handle
        self.name = name
        self.n = 0


DMA_POOL = 12


class Plan:
    def __init__(self, nc, stack):
        self.nc = nc
        self.stack = stack
        self.ops = {e: [] for e in ENGS}
        self.sems = {}
        for s in ("ts", "vs", "ss", "gs", "cs"):
            self.sem(s)
        self.dirty = {}
        self.rdirty = {}
        self.dma_pools = {}
        self.dma_uses = {}
        for q in ("sync", "gpsimd", "scalar"):
            self.dma_pools[q] = [self.sem(f"{q[0]}d{i}") for i in range(DMA_POOL)]
            self.dma_uses[q] = 0

    def sem(self, name):
        if name not in self.sems:
            h = self.stack.enter_context(self.nc.semaphore(name))
            self.sems[name] = Sem(h, name)
        return self.sems[name]

    def emit(self, eng, fn):
        self.ops[eng].append(fn)

    def dma(self, q, out, in_, **kw):
        k = self.dma_uses[q]
        self.dma_uses[q] += 1
        s = self.dma_pools[q][k % DMA_POOL]
        prev = s.n
        s.n += 16
        v = s.n

        def fn(e, out=out, in_=in_, s=s, kw=kw, prev=prev):
            if prev > 0:
                e.wait_ge(s.h, prev)
            e.dma_start(out=out, in_=in_, **kw).then_inc(s.h, 16)

        self.emit(q, fn)
        return (s.name, v)

    def dma_all_done(self, q):
        return [(s.name, s.n) for s in self.dma_pools[q]]

    def dram(self, name, shape, dtype):
        return self.nc.dram_tensor(self.alloc.prefix + name, list(shape), dtype).ap()

    def fence(self, eng=None):
        toks = [self.now("ts"), self.now("vs"), self.now("ss")]
        toks += self.dma_all_done("sync") + self.dma_all_done("gpsimd")
        for e in ENGS:
            self.waits(e, toks)

    def wait(self, eng, tok):
        if tok is None:
            return
        sem, v = tok
        s = self.sems[sem]

        def fn(e, s=s, v=v):
            e.wait_ge(s.h, v)

        self.emit(eng, fn)

    def waits(self, eng, toks):
        best = {}
        for t in toks:
            if t is None:
                continue
            sem, v = t
            best[sem] = max(best.get(sem, 0), v)
        for sem, v in best.items():
            self.wait(eng, (sem, v))

    def now(self, semname):
        return (semname, self.sems[semname].n)

    @staticmethod
    def _apname(x):
        t = getattr(x, "tensor", None)
        return getattr(t, "name", None)

    @staticmethod
    def _apbox(x):
        t = getattr(x, "tensor", None)
        nm = getattr(t, "name", None)
        if nm is None:
            return None
        shp = list(t.shape)
        rowlen = 1
        for s in shp[1:]:
            rowlen *= s
        off = x.offset
        ap = [list(p) for p in x.ap]
        p0 = off // rowlen
        c0 = off % rowlen
        pc = ap[0][1] if ap else 1
        span = 1
        for s, c in ap[1:]:
            span += s * (c - 1)
        return (nm, p0, p0 + pc, c0, c0 + span)

    @staticmethod
    def _boxhit(a, b):
        return (a[0] == b[0] and a[1] < b[2] and b[1] < a[2]
                and a[3] < b[4] and b[3] < a[4])

    def op(self, eng, method, *args, inc=None, drain=False, pre_drain=None, **kw):
        semname = inc
        tok = None
        if semname is not None:
            s = self.sems[semname]
            s.n += 1
            tok = (semname, s.n)
        if pre_drain is None and eng in ("vector", "scalar", "gpsimd"):
            dirty = self.dirty.setdefault(eng, [])
            rset = self.rdirty.setdefault(eng, [])
            outs = []
            if args and hasattr(args[0], "tensor"):
                bx = self._apbox(args[0])
                if bx is not None:
                    outs.append(bx)
            ao = kw.get("accum_out")
            if ao is not None and hasattr(ao, "tensor"):
                bx = self._apbox(ao)
                if bx is not None:
                    outs.append(bx)
            ins = []
            for a in args[1:]:
                if hasattr(a, "tensor"):
                    bx = self._apbox(a)
                    if bx is not None:
                        ins.append(bx)
            for k in ("bias", "scale", "scalar1", "scalar2", "in1", "data0",
                      "data1"):
                v = kw.get(k)
                if v is not None and hasattr(v, "tensor"):
                    bx = self._apbox(v)
                    if bx is not None:
                        ins.append(bx)
            pre_drain = (any(self._boxhit(b, d) for b in ins + outs for d in dirty)
                         or any(self._boxhit(o, r) for o in outs for r in rset))
            if pre_drain or (drain and not NO_MILESTONE_DRAIN):
                dirty.clear()
                rset.clear()
            if NO_MILESTONE_DRAIN or not drain:
                dirty.extend(outs)
                rset.extend(ins)
        elif pre_drain is None:
            pre_drain = False

        def fn(e, method=method, args=args, kw=kw, semname=semname, drain=drain,
               pre_drain=pre_drain):
            if pre_drain:
                e.drain()
            r = getattr(e, method)(*args, **kw)
            if semname is not None:
                s = self.sems[semname]
                if drain and not NO_MILESTONE_DRAIN:
                    e.drain().then_inc(s.h, 1)
                else:
                    r.then_inc(s.h, 1)

        self.emit(eng, fn)
        return tok

    def mm(self, out, lhsT, rhs, start, stop, inc=None):
        return self.op("tensor", "matmul", out, lhsT, rhs,
                       start=start, stop=stop, inc=inc)

    def barrier(self):
        toks = [self.now(s) for s in ("ts", "vs", "ss")]
        toks += self.dma_all_done("sync") + self.dma_all_done("gpsimd")
        for e in ENGS:
            self.waits(e, toks)
        return toks

    def build(self):
        block = self.stack.enter_context(self.nc.Block())
        plan = self

        def mk(eng):
            def body(e):
                for f in plan.ops[eng]:
                    f(e)
            return body

        block.sync(mk("sync"))
        block.gpsimd(mk("gpsimd"))
        block.tensor(mk("tensor"))
        block.vector(mk("vector"))
        block.scalar(mk("scalar"))


class PsumPool:
    def __init__(self, plan, tiles):
        self.plan = plan
        self.tiles = tiles
        self.rel = [None] * len(tiles)
        self.i = 0

    def acquire(self):
        i = self.i % len(self.tiles)
        self.i += 1
        return i, self.tiles[i], self.rel[i]

    def release(self, i, tok):
        self.rel[i] = tok


class Alloc:
    def __init__(self, nc, stack):
        self.nc = nc
        self.stack = stack
        self.default_stack = stack
        self.prefix = ""
        self.bytes_pp = 0

    def sb(self, name, shape, dtype, stack=None, side=None):
        name = self.prefix + name
        per = int(np.prod(shape[1:])) * mybir.dt.size(dtype)
        self.bytes_pp += ((per + 31) // 32) * 32
        st = stack or self.default_stack
        kw = {}
        if stack is not None:
            kw["side"] = "right"
        if side is not None:
            kw["side"] = side
        return st.enter_context(self.nc.sbuf_tensor(name, list(shape), dtype, **kw))

    def ps(self, name, shape, dtype=F32):
        return self.default_stack.enter_context(
            self.nc.psum_tensor(self.prefix + name, list(shape), dtype))


def chunk(t, c, w=T):
    return t[:, c * w:(c + 1) * w]


def tile(t, c, tt, w=T):
    return t[:, c * w + tt * TTW: c * w + tt * TTW + TTW]



def emit_ln(P, al, name, xn, consts, pp, x_dram=None, src_sbuf=None,
            src_toks=None, gain=None, bias=None, substack=None, narrow=False):
    ones128 = consts["ones128"]
    ones1 = consts["ones1"]
    st = substack or P.stack
    NB = 1 if narrow else 2
    arow = al.sb(name + "_arow", [1, 2 * T], BF16, stack=st)
    abc = al.sb(name + "_abc", [128, 2 * T], BF16, stack=st)
    scr = al.sb(name + "_scr", [1, 3 * TTW], F32, stack=st)
    if src_sbuf is None:
        xr = al.sb(name + "_xr", [128, NB * T], BF16, stack=st)
    sq = al.sb(name + "_sq", [128, NB * T], BF16, stack=st)

    banks = [pp.acquire() for _ in range(NTT)]
    P.waits("tensor", [g for (_, _, g) in banks])
    if narrow:
        srow = arow[:, T:2 * T]
    else:
        srow = al.sb(name + "_srow", [1, T], F32, stack=st)

    ld_rel = [None, None]
    mm_tok = None
    for c in range(NC):
        i = c % NB
        if src_sbuf is None:
            if ld_rel[i] is not None:
                P.wait("gpsimd", ld_rel[i])
            lt = P.dma("gpsimd", xr[:, i * T:(i + 1) * T],
                       x_dram[c * 128:(c + 1) * 128, :])
            src = xr[:, i * T:(i + 1) * T]
            P.wait("tensor", lt)
        else:
            src = src_sbuf(c)
            if src_toks is not None and c == 0:
                P.waits("tensor", src_toks)
        for tt in range(NTT):
            _, b, _ = banks[tt]
            P.mm(b[0:1, :], ones128, src[:, tt * TTW:(tt + 1) * TTW],
                 start=(c == 0), stop=(c == NC - 1), inc="ts")
        mm_tok = P.now("ts")
        ld_rel[i] = mm_tok
    P.wait("vector", mm_tok)
    for tt in range(NTT):
        _, b, _ = banks[tt]
        P.op("vector", "tensor_copy", srow[:, tt * TTW:(tt + 1) * TTW], b[0:1, :],
             inc="vs" if tt == NTT - 1 else None, drain=(tt == NTT - 1))
    s1ev = P.now("vs")

    sq_rel = [None, None]
    P.wait("tensor", s1ev)
    for c in range(NC):
        i = c % NB
        if src_sbuf is None:
            P.wait("gpsimd", mm_tok if c < NB else ld_rel[i])
            lt = P.dma("gpsimd", xr[:, i * T:(i + 1) * T],
                       x_dram[c * 128:(c + 1) * 128, :])
            src = xr[:, i * T:(i + 1) * T]
            P.wait("scalar", lt)
        else:
            src = src_sbuf(c)
            if src_toks is not None and c == 0:
                P.waits("scalar", src_toks)
        if sq_rel[i] is not None:
            P.wait("scalar", sq_rel[i])
        sqt = P.op("scalar", "activation", sq[:, i * T:(i + 1) * T], src, AF.Square,
                   inc="ss", drain=True)
        P.wait("tensor", sqt)
        for tt in range(NTT):
            _, b, _ = banks[tt]
            P.mm(b[0:1, :], ones128,
                 sq[:, i * T + tt * TTW: i * T + (tt + 1) * TTW],
                 start=(c == 0), stop=(c == NC - 1), inc="ts")
        mm_tok = P.now("ts")
        ld_rel[i] = mm_tok
        sq_rel[i] = mm_tok

    P.wait("vector", mm_tok)
    stok = None
    for tt in range(NTT):
        ib, b, _ = banks[tt]
        if stok is not None:
            P.wait("vector", stok)
        P.op("vector", "tensor_copy", scr[:, 0:TTW], srow[:, tt * TTW:(tt + 1) * TTW])
        P.op("vector", "tensor_copy", scr[:, TTW:2 * TTW], b[0:1, :],
             inc="vs", drain=True)
        pp.release(ib, P.now("vs"))
        P.op("vector", "tensor_scalar", scr[:, 0:TTW], scr[:, 0:TTW], 1.0 / D, None,
             op0=ALU.mult)
        P.op("vector", "tensor_tensor", scr[:, 2 * TTW:3 * TTW], scr[:, 0:TTW],
             scr[:, 0:TTW], op=ALU.mult)
        P.op("vector", "scalar_tensor_tensor", scr[:, TTW:2 * TTW],
             scr[:, TTW:2 * TTW], 1.0 / D, scr[:, 2 * TTW:3 * TTW],
             op0=ALU.mult, op1=ALU.subtract, inc="vs", drain=True)
        vtok = P.now("vs")
        P.wait("scalar", vtok)
        P.op("scalar", "activation", scr[:, 2 * TTW:3 * TTW], scr[:, TTW:2 * TTW],
             AF.Ln, bias=consts["eps"])
        P.op("scalar", "activation", arow[:, tt * TTW:(tt + 1) * TTW],
             scr[:, 2 * TTW:3 * TTW], AF.Exp, scale=-0.5, inc="ss", drain=True)
        sstok = P.now("ss")
        P.wait("vector", sstok)
        P.op("vector", "scalar_tensor_tensor",
             arow[:, T + tt * TTW: T + (tt + 1) * TTW], scr[:, 0:TTW], -1.0,
             arow[:, tt * TTW:(tt + 1) * TTW], op0=ALU.mult, op1=ALU.mult,
             inc="vs", drain=True)
        stok = P.now("vs")

    P.wait("tensor", stok)
    for half in range(2):
        for tt in range(NTT):
            ib, b, g = pp.acquire()
            P.wait("tensor", g)
            t = P.mm(b[0:128, :], ones1,
                     arow[:, half * T + tt * TTW: half * T + (tt + 1) * TTW],
                     start=True, stop=True, inc="ts")
            P.wait("vector", t)
            ev = P.op("vector", "tensor_copy",
                      abc[:, half * T + tt * TTW: half * T + (tt + 1) * TTW],
                      b[0:128, :], inc="vs", drain=True)
            pp.release(ib, ev)
    bc_tok = P.now("vs")

    out_toks = []
    ld_rel = [bc_tok, bc_tok]
    for c in range(NC):
        i = c % NB
        if src_sbuf is None:
            P.wait("gpsimd", ld_rel[i])
            lt = P.dma("gpsimd", xr[:, i * T:(i + 1) * T],
                       x_dram[c * 128:(c + 1) * 128, :])
            src = xr[:, i * T:(i + 1) * T]
            P.wait("vector", lt)
        else:
            src = src_sbuf(c)
            if c == 0:
                P.wait("vector", bc_tok)
                if src_toks is not None:
                    P.waits("vector", src_toks)
        dst = chunk(xn, c)
        P.op("vector", "tensor_tensor", dst, src, abc[:, 0:T], op=ALU.mult)
        tok = P.op("vector", "tensor_tensor", dst, dst, abc[:, T:2 * T],
                   op=ALU.add, inc="vs", drain=(gain is None))
        if gain is not None:
            tok = P.op("vector", "tensor_scalar", dst, dst, gain(c), bias(c),
                       op0=ALU.mult, op1=ALU.add, inc="vs", drain=True)
        ld_rel[i] = tok
        out_toks.append(tok)
    return out_toks



def load_weight(P, al, name, w_dram, in_f, out_f, dtype=BF16, stack=None):
    kcs = in_f // 128
    w = al.sb(name, [128, kcs * out_f], dtype, stack=stack)
    toks = []
    for kc in range(kcs):
        toks.append(P.dma("gpsimd", w[:, kc * out_f:(kc + 1) * out_f],
                          w_dram[kc * 128:(kc + 1) * 128, :]))
    return w, toks


def wslice(w, kc, oc, out_f, ow=128):
    return w[:, kc * out_f + oc * ow: kc * out_f + oc * ow + ow]


def load_bias(P, al, name, b_dram, n, stack=None):
    cs = n // 128
    b = al.sb(name, [128, cs], F32, stack=stack)
    tok = P.dma("gpsimd", b[:, :], b_dram)
    return b, [tok]


def emit_proj(P, al, pp, xn_src, w, w_toks, in_f, out_f, evac,
              extra_mm=None, dep_toks=None, n_tt=NTT, tts=None):
    kcs = in_f // 128
    ocs = out_f // 128
    if w_toks:
        P.waits("tensor", w_toks)
    if dep_toks:
        P.waits("tensor", dep_toks)
    for tt in (tts if tts is not None else range(n_tt)):
        for oc in range(ocs):
            ib, b, g = pp.acquire()
            P.wait("tensor", g)
            for kc in range(kcs):
                last = (kc == kcs - 1) and extra_mm is None
                P.mm(b[:, :], wslice(w, kc, oc, out_f), xn_src(kc, tt),
                     start=(kc == 0), stop=last,
                     inc="ts" if last else None)
            if extra_mm is not None:
                extra_mm(b, oc, tt)
            tok = P.now("ts")
            rel = evac(b, oc, tt, tok)
            pp.release(ib, rel)



def emit_A(P, al, consts, x_in, x_out, W):
    pp = PsumPool(P, [al.ps(f"ppA{i}", [128, 512]) for i in range(8)])
    xn = al.sb("A_xn", [128, NC * T], BF16)

    if True:
        st_ln = ExitStack()
        ln_toks = emit_ln(P, al, "A_ln", xn, consts, pp, x_dram=x_in,
                          substack=st_ln)
        w_s1, t_s1 = load_weight(P, al, "A_ws1", W["s_w1T"], 1024, 512)
        w_s2, t_s2 = load_weight(P, al, "A_ws2", W["s_w2T"], 512, 128)
        w_B, t_B = load_weight(P, al, "A_wB", W["B_wT"], 1024, 128)
        w_C, t_C = load_weight(P, al, "A_wC", W["C_wT"], 128, 1024)
        at = al.sb("A_at", [128, 128], BF16)
        t_at = P.dma("gpsimd", at[:, :], W["A_T"])
        an = al.sb("A_an", [128, 128], BF16)
        t_an = P.dma("gpsimd", an[:, :], W["A_nat"])
        b_s1, tb1 = load_bias(P, al, "A_bs1", W["s_b1"], 512)
        b_s2, tb2 = load_bias(P, al, "A_bs2", W["s_b2"], 128)
        b_B, tb3 = load_bias(P, al, "A_bB", W["B_b"], 128)
        b_C, tb4 = load_bias(P, al, "A_bC", W["C_b"], 1024)
        b_D, tb5 = load_bias(P, al, "A_bD", W["D_b"], 1024)
        b_g, tb6 = load_bias(P, al, "A_bg", W["g_b"], 1024)
        b_cd = al.sb("A_bcd", [128, 8], F32)
        P.waits("vector", tb4 + tb5)
        P.op("vector", "tensor_tensor", b_cd[:, :], b_C[:, :], b_D[:, :],
             op=ALU.add, inc="vs", drain=True)
        tb_cd = P.now("vs")

        if DEBUG_BARRIERS & 1:
            P.barrier()
        st_ln.close()
        st_s1 = ExitStack()
        s1 = al.sb("A_s1", [128, 4 * T], BF16, stack=st_s1)
        P.waits("scalar", tb1)

        def ev_s1(b, oc, tt, tok):
            P.wait("scalar", tok)
            return P.op("scalar", "activation", tile(s1, oc, tt), b[:, :], AF.Silu,
                        bias=b_s1[:, oc:oc + 1], inc="ss", drain=True)

        emit_proj(P, al, pp, lambda kc, tt: tile(xn, kc, tt), w_s1, t_s1,
                  1024, 512, ev_s1, dep_toks=ln_toks)

        s = al.sb("A_s", [128, T], BF16)
        P.waits("scalar", tb2)

        def ev_s2(b, oc, tt, tok):
            P.wait("scalar", tok)
            return P.op("scalar", "activation", s[:, tt * TTW:(tt + 1) * TTW],
                        b[:, :], AF.Sigmoid, bias=b_s2[:, 0:1], inc="ss", drain=True)

        emit_proj(P, al, pp, lambda kc, tt: tile(s1, kc, tt), w_s2, t_s2,
                  512, 128, ev_s2, dep_toks=[P.now("ss")])
        if DEBUG_TAPS is not None:
            P.wait("gpsimd", P.now("ss"))
            DEBUG_TAPS.setdefault("toks", []).append(
                P.dma("gpsimd", DEBUG_TAPS["s"], s[:, :]))
            for c in range(NC):
                DEBUG_TAPS["toks"].append(
                    P.dma("gpsimd", DEBUG_TAPS["xn"][c * 128:(c + 1) * 128, :],
                          chunk(xn, c)))

        if DEBUG_BARRIERS & 2:
            P.barrier()
        st_s1.close()
        st_u = ExitStack()
        u = al.sb("A_u", [128, T], F32, stack=st_u)
        btmp = al.sb("A_btmp", [128, 4 * TTW], F32, stack=st_u)
        P.waits("vector", tb3)
        btmp_rel = [None] * 4

        def ev_bt(b, oc, tt, tok):
            i = tt % 4
            P.wait("vector", tok)
            if btmp_rel[i] is not None:
                P.wait("vector", btmp_rel[i])
            t1 = P.op("vector", "tensor_scalar", btmp[:, i * TTW:(i + 1) * TTW],
                      b[:, :], b_B[:, 0:1], None, op0=ALU.add,
                      inc="vs", drain=True)
            vt = P.op("vector", "tensor_tensor", u[:, tt * TTW:(tt + 1) * TTW],
                      btmp[:, i * TTW:(i + 1) * TTW],
                      s[:, tt * TTW:(tt + 1) * TTW], op=ALU.mult,
                      inc="vs", drain=True)
            btmp_rel[i] = vt
            return t1

        emit_proj(P, al, pp, lambda kc, tt: tile(xn, kc, tt), w_B, t_B,
                  1024, 128, ev_bt, dep_toks=[P.now("ss")])
        u_tok = P.now("vs")
        if DEBUG_TAPS is not None:
            P.wait("gpsimd", u_tok)
            DEBUG_TAPS["toks"].append(P.dma("gpsimd", DEBUG_TAPS["u"], u[:, :]))

        if DEBUG_BARRIERS & 4:
            P.barrier()
        hs = al.sb("A_hs", [128, T], BF16)
        H = al.sb("A_H", [128, 64], BF16)
        Hend = al.sb("A_Hend", [128, 64], BF16)
        h2i = al.sb("A_h2i", [128, 64], BF16)
        pw = al.sb("A_pw", [128, 256], BF16)
        ib0, t0, g0 = pp.acquire()
        ib1, t1_, g1 = pp.acquire()
        ib2, t2_, g2 = pp.acquire()
        P.waits("tensor", [g0, g1, g2])
        psq = t0[:, 0:128]
        psqt = t0.bitcast(BF16)[:, 256:384]
        ps_h = [t1_[:, 0:64], t2_[:, 0:64]]
        ident = consts["ident"]

        P.waits("vector", [t_at, t_an])
        P.op("vector", "tensor_copy", pw[:, 0:128], an[:, :])
        P.op("vector", "tensor_copy", pw[:, 128:256], at[:, :], inc="vs", drain=True)
        P.wait("tensor", P.now("vs"))
        for k in range(6):
            t1 = P.mm(psq, pw[:, 128:256], pw[:, 0:128], start=True, stop=True,
                      inc="ts")
            P.wait("vector", t1)
            c1 = P.op("vector", "tensor_copy", pw[:, 0:128], psq,
                      inc="vs", drain=True)
            P.wait("tensor", c1)
            t2 = P.op("tensor", "transpose", psqt, pw[:, 0:128], ident,
                      inc="ts")
            P.wait("vector", t2)
            c2 = P.op("vector", "tensor_copy", pw[:, 128:256], psqt,
                      inc="vs", drain=True)
            P.wait("tensor", c2)

        P.op("vector", "memset", H[:, :], 0.0, inc="vs", drain=True)
        P.wait("tensor", u_tok)
        P.wait("tensor", P.now("vs"))
        for j in range(64):
            b = ps_h[j % 2]
            t1 = P.mm(b, at[:, :], H[:, :], start=True, stop=True, inc="ts")
            P.wait("vector", t1)
            c1 = P.op("vector", "tensor_tensor", H[:, :], b,
                      u[:, j::64], op=ALU.add, inc="vs", drain=True)
            P.wait("tensor", c1)
        P.op("vector", "tensor_copy", Hend[:, 0:1], H[:, 0:1], inc="vs", drain=True)
        P.wait("tensor", P.now("vs"))
        for c in range(1, 64):
            b = ps_h[c % 2]
            t1 = P.mm(b[:, 0:1], pw[:, 128:256], Hend[:, c - 1:c], start=True,
                      stop=True, inc="ts")
            P.wait("vector", t1)
            c1 = P.op("vector", "tensor_tensor", Hend[:, c:c + 1], b[:, 0:1],
                      H[:, c:c + 1], op=ALU.add, inc="vs", drain=True)
            P.wait("tensor", c1)
        P.op("vector", "memset", h2i[:, 0:1], 0.0)
        P.op("vector", "tensor_copy", h2i[:, 1:64], Hend[:, 0:63], inc="vs",
             drain=True)
        P.wait("tensor", P.now("vs"))
        for j in range(64):
            b = ps_h[j % 2]
            rhs = h2i[:, :] if j == 0 else hs[:, (j - 1)::64]
            t1 = P.mm(b, at[:, :], rhs, start=True, stop=True, inc="ts")
            P.wait("vector", t1)
            c1 = P.op("vector", "tensor_tensor", hs[:, j::64], b,
                      u[:, j::64], op=ALU.add, inc="vs", drain=True)
            P.wait("tensor", c1)
        hs_tok = P.now("vs")
        if DEBUG_TAPS is not None:
            P.wait("gpsimd", hs_tok)
            DEBUG_TAPS["toks"].append(P.dma("gpsimd", DEBUG_TAPS["hs"], hs[:, :]))
        pp.release(ib0, hs_tok)
        pp.release(ib1, hs_tok)
        pp.release(ib2, hs_tok)

        w_D, t_D = load_weight(P, al, "A_wD", W["D_wT"], 1024, 1024)
        w_g, t_g = load_weight(P, al, "A_wg", W["g_wT"], 1024, 1024)
        if DEBUG_BARRIERS & 8:
            P.barrier()
        st_u.close()

        gt = al.sb("A_gt", [128, 4 * TTW], BF16)
        xr2 = al.sb("A_xr2", [128, 4 * TTW], F32)
        yt = al.sb("A_yt", [128, 4 * TTW], F32)
        P.waits("tensor", t_D + t_g + t_C + [hs_tok])
        P.wait("sync", hs_tok)
        P.wait("scalar", hs_tok)
        P.waits("vector", [tb_cd])
        P.waits("scalar", tb6 + [tb_cd])
        out_toks = []
        rel = {"gt": [None] * 4, "xr": [None] * 4, "yt": [None] * 4}
        cnt = [0]
        for tt in range(NTT):
            for oc in range(NC):
                i = cnt[0] % 4
                cnt[0] += 1
                ibg, bg, gg = pp.acquire()
                P.wait("tensor", gg)
                for kc in range(NC):
                    P.mm(bg[:, :], wslice(w_g, kc, oc, 1024), tile(xn, kc, tt),
                         start=(kc == 0), stop=(kc == NC - 1),
                         inc="ts" if kc == NC - 1 else None)
                g_tok = P.now("ts")
                iby, by, gy = pp.acquire()
                P.wait("tensor", gy)
                for kc in range(NC):
                    P.mm(by[:, :], wslice(w_D, kc, oc, 1024), tile(xn, kc, tt),
                         start=(kc == 0), stop=False)
                P.mm(by[:, :], wslice(w_C, 0, oc, 1024),
                     hs[:, tt * TTW:(tt + 1) * TTW], start=False, stop=True,
                     inc="ts")
                y_tok = P.now("ts")
                P.wait("scalar", g_tok)
                if rel["gt"][i] is not None:
                    P.wait("scalar", rel["gt"][i])
                sg = P.op("scalar", "activation", gt[:, i * TTW:(i + 1) * TTW],
                          bg[:, :], AF.Sigmoid, bias=b_g[:, oc:oc + 1],
                          inc="ss", drain=True)
                pp.release(ibg, sg)
                P.wait("vector", y_tok)
                if rel["yt"][i] is not None:
                    P.wait("vector", rel["yt"][i])
                sy = P.op("vector", "tensor_scalar", yt[:, i * TTW:(i + 1) * TTW],
                          by[:, :], b_cd[:, oc:oc + 1], None, op0=ALU.add,
                          inc="vs", drain=True)
                pp.release(iby, sy)
                if rel["xr"][i] is not None:
                    P.wait("sync", rel["xr"][i])
                lt = P.dma("sync", xr2[:, i * TTW:(i + 1) * TTW],
                           x_in[oc * 128:(oc + 1) * 128, tt * TTW:(tt + 1) * TTW])
                P.waits("vector", [sg, sy, lt])
                P.op("vector", "tensor_tensor", yt[:, i * TTW:(i + 1) * TTW],
                     yt[:, i * TTW:(i + 1) * TTW], gt[:, i * TTW:(i + 1) * TTW],
                     op=ALU.mult)
                vt = P.op("vector", "tensor_tensor", yt[:, i * TTW:(i + 1) * TTW],
                          yt[:, i * TTW:(i + 1) * TTW],
                          xr2[:, i * TTW:(i + 1) * TTW], op=ALU.add,
                          inc="vs", drain=True)
                rel["gt"][i] = vt
                rel["xr"][i] = vt
                P.wait("sync", vt)
                ot = P.dma("sync",
                           x_out[oc * 128:(oc + 1) * 128, tt * TTW:(tt + 1) * TTW],
                           yt[:, i * TTW:(i + 1) * TTW])
                rel["yt"][i] = ot
                out_toks.append(ot)
    return out_toks



def _np(x):
    return np.ascontiguousarray(np.asarray(x, dtype=np.float32))


def _bvec(x):
    return np.ascontiguousarray(_np(x).reshape(-1, 128).T)


def make_consts():
    import ml_dtypes
    bf = ml_dtypes.bfloat16
    c = {}
    c["ident"] = np.eye(128, dtype=bf)
    c["ones128"] = np.ones((128, 1), dtype=bf)
    c["ones1"] = np.ones((1, 128), dtype=bf)
    c["eps"] = np.full((1, 1), EPS_LN, dtype=np.float32)
    sel8 = np.zeros((8, 8 * 128), dtype=bf)
    for e in range(8):
        sel8[e, e * 128:(e + 1) * 128] = 1
    c["sel8"] = sel8
    pair2 = np.zeros((128, 2), dtype=bf)
    pair2[0:64, 0] = 1
    pair2[64:128, 1] = 1
    c["pair2"] = pair2
    c["onesf1"] = np.ones((1, 128), dtype=np.float32)
    c["posc"] = (np.arange(128)[:, None] + 128 * np.arange(32)[None, :]).astype(np.float32)
    c["ones64r"] = np.ones((1, 64), dtype=bf)
    oh4 = np.zeros((1, 16), dtype=np.float32)
    for j in range(4):
        oh4[0, j * 4 + j] = 1.0
    c["oh4"] = oh4
    selb4 = np.zeros((4, 4 * 64), dtype=bf)
    for j in range(4):
        selb4[j, j * 64:(j + 1) * 64] = 1
    c["selb4"] = selb4
    R = np.zeros((128, 128), dtype=np.float32)
    for blk in range(2):
        for jj in range(32):
            a = blk * 64 + 2 * jj
            R[a, a + 1] = -1.0
            R[a + 1, a] = 1.0
    c["r2t"] = np.ascontiguousarray(R.T).astype(bf)
    c.update(_win_masks())
    return c


def declare_consts(nc, al, P, consts_np, need=None):
    out = {}
    toks = []
    for name, arr in consts_np.items():
        if need is not None and name not in need:
            continue
        dt = F32 if arr.dtype == np.float32 else BF16
        d = nc.dram_tensor("c_" + name, list(arr.shape), dt,
                           kind="ExternalInput").ap()
        s = al.sb("cs_" + name, list(arr.shape), dt)
        toks.append(P.dma("gpsimd", s[:, :], d))
        out[name] = s[:, :]
    return out, toks


def prep_A_weights(inp):
    W = {}
    W["A_T"] = _np(np.asarray(inp["L0_attn_a_A"]).T)
    W["A_nat"] = _np(inp["L0_attn_a_A"])
    W["B_wT"] = _np(np.asarray(inp["L0_attn_a_B_w"]).T)
    W["C_wT"] = _np(np.asarray(inp["L0_attn_a_C_w"]).T)
    W["D_wT"] = _np(np.asarray(inp["L0_attn_a_D_w"]).T)
    W["s_w1T"] = _np(np.asarray(inp["L0_attn_a_s_w1"]).T)
    W["s_w2T"] = _np(np.asarray(inp["L0_attn_a_s_w2"]).T)
    W["g_wT"] = _np(np.asarray(inp["L0_attn_a_g_w"]).T)
    W["B_b"] = _bvec(inp["L0_attn_a_B_b"])
    W["C_b"] = _bvec(inp["L0_attn_a_C_b"])
    W["D_b"] = _bvec(inp["L0_attn_a_D_b"])
    W["s_b1"] = _bvec(inp["L0_attn_a_s_b1"])
    W["s_b2"] = _bvec(inp["L0_attn_a_s_b2"])
    W["g_b"] = _bvec(inp["L0_attn_a_g_b"])
    return W


CONST_NEEDS = {
    "emit_A": ["ident", "ones128", "ones1", "eps"],
    "emit_B": ["ident", "ones128", "ones1", "eps", "sel8"],
    "emit_E": ["ident", "ones128", "ones1", "eps", "sel8"],
    "emit_C": ["ident", "ones128", "ones1", "eps", "pair2", "onesf1", "posc",
               "oh4", "selb4"],
    "emit_D": None,
    "emit_F": None,
}


def _const_need(emitter_name):
    base = ["ident", "ones128", "ones1", "eps"]
    if emitter_name in ("emit_A",):
        return base
    if emitter_name in ("emit_B", "emit_E"):
        return base + ["sel8"]
    if emitter_name == "emit_C":
        return base + ["pair2", "onesf1", "posc", "oh4", "selb4"]
    if emitter_name == "emit_D":
        return base + ["pair2", "oh4", "selb4", "r2t"]
    if emitter_name == "emit_F":
        return base + ["pair2", "oh4", "selb4", "onesf1"]
    return None


def build_launch(emitter, wshapes):
    global DEBUG_TAPS
    nc = bass.Bass("TRN2", target_bir_lowering=False)
    x_in = nc.dram_tensor("x_in", [D, T], F32, kind="ExternalInput").ap()
    x_out = nc.dram_tensor("x_out", [D, T], F32, kind="ExternalOutput").ap()
    Wd = {}
    for name, shp in wshapes.items():
        Wd[name] = nc.dram_tensor("w_" + name, list(shp), F32,
                                  kind="ExternalInput").ap()
    if DEBUG_TAPS is not None:
        DEBUG_TAPS = {
            "s": nc.dram_tensor("tap_s", [128, T], F32, kind="ExternalOutput").ap(),
            "u": nc.dram_tensor("tap_u", [128, T], F32, kind="ExternalOutput").ap(),
            "hs": nc.dram_tensor("tap_hs", [128, T], F32, kind="ExternalOutput").ap(),
            "xn": nc.dram_tensor("tap_xn", [D, T], F32, kind="ExternalOutput").ap(),
        }
    stack = ExitStack()
    with stack:
        al = Alloc(nc, stack)
        P = Plan(nc, stack)
        P.alloc = al
        consts, ctoks = declare_consts(nc, al, P, make_consts(),
                                       need=_const_need(emitter.__name__))
        P.waits("tensor", ctoks)
        P.waits("vector", ctoks)
        P.waits("scalar", ctoks)
        out_toks = emitter(P, al, consts, x_in, x_out, Wd)
        P.waits("sync", out_toks)
        if DEBUG_TAPS is not None:
            P.waits("gpsimd", DEBUG_TAPS["toks"])
        P.build()
    return nc


def run_layer(emitter, W, x_fm, trace=False):
    consts = make_consts()
    Ws = W if isinstance(W, list) else [W] * 8
    wshapes = {k: v.shape for k, v in Ws[0].items()}
    nc = build_launch(emitter, wshapes)
    need = _const_need(emitter.__name__)
    in_maps = []
    for c in range(8):
        m = {"x_in": x_fm[c]}
        for k, v in Ws[c].items():
            m["w_" + k] = v
        for k, v in consts.items():
            if need is None or k in need:
                m["c_" + k] = v
        in_maps.append(m)
    res = run_bass_kernel_spmd(nc, in_maps, core_ids=list(range(8)), trace=trace)
    return [res.results[c]["x_out"] for c in range(8)]


def kernel(**inputs):
    x = np.asarray(inputs["L0_attn_a_x"], dtype=np.float32)
    x_fm = [np.ascontiguousarray(x[c].T) for c in range(8)]
    if FUSED:
        x_fm = run_fused(inputs, x_fm)
        return np.stack([np.ascontiguousarray(np.asarray(xc).T)
                         for xc in x_fm], axis=0)
    WA = prep_A_weights(inputs)
    WB = prep_B_weights(inputs)
    WC = prep_C_weights(inputs)
    WD = prep_D_weights(inputs)
    WEs = [prep_E_weights(inputs, c) for c in range(8)]
    WF = prep_F_weights(inputs)
    x_fm = run_layer(emit_A, WA, x_fm)
    x_fm = run_layer(emit_B, WB, x_fm)
    x_fm = run_layer(emit_C, WC, x_fm)
    x_fm = run_layer(emit_B, WB, x_fm)
    x_fm = run_layer(emit_D, WD, x_fm)
    x_fm = run_layer(emit_E, WEs, x_fm)
    x_fm = run_layer(emit_F, WF, x_fm)
    x_fm = run_layer(emit_E, WEs, x_fm)
    out = np.stack([np.ascontiguousarray(np.asarray(xc).T) for xc in x_fm], axis=0)
    return out



def emit_router_logits(P, al, pp, xn, wr, wr_toks, name):
    lgall = al.sb(name + "_lg", [128, 32 * 8], F32)
    P.waits("tensor", wr_toks)
    for tc in range(32):
        ib, b, g = pp.acquire()
        P.wait("tensor", g)
        tt = tc // 4
        co = (tc % 4) * 128
        for kc in range(NC):
            P.mm(b[:, 0:8], tile(xn, kc, tt)[:, co:co + 128], wr[:, kc * 8:(kc + 1) * 8],
                 start=(kc == 0), stop=(kc == NC - 1),
                 inc="ts" if kc == NC - 1 else None)
        t = P.now("ts")
        P.wait("vector", t)
        ev = P.op("vector", "tensor_copy", lgall[:, tc * 8:(tc + 1) * 8], b[:, 0:8],
                  inc="vs", drain=True)
        pp.release(ib, ev)
    return lgall, P.now("vs")


def emit_B(P, al, consts, x_in, x_out, W):
    pp = PsumPool(P, [al.ps(f"ppB{i}", [128, 512]) for i in range(8)])
    xn = al.sb("B_xn", [128, NC * T], BF16)
    out_dram = P.dram("B_acc", [D, T], F32)

    st_ln = ExitStack()
    ln_toks = emit_ln(P, al, "B_ln", xn, consts, pp, x_dram=x_in, substack=st_ln)
    wr, wr_toks = load_weight(P, al, "B_wr", W["w_router"], 1024, 8)
    st_ln.close()

    P.waits("tensor", ln_toks)
    lgall, lg_tok = emit_router_logits(P, al, pp, xn, wr, wr_toks, "B")
    w8all = al.sb("B_w8", [128, 32 * 8], BF16)
    v12 = al.sb("B_v12", [128, 4 * 32], F32)
    m1a = al.sb("B_m1a", [128, 256], BF16)
    m2a = al.sb("B_m2a", [128, 256], BF16)
    tmsk = al.sb("B_tmsk", [128, 256], F32)
    P.wait("vector", lg_tok)
    P.waits("scalar", [lg_tok])
    lg3 = lgall[:, :].rearrange("p (g e) -> p g e", e=8)
    v1 = v12[:, 0:32]
    v2 = v12[:, 32:64]
    dd = v12[:, 64:96]
    s1 = v12[:, 96:128]
    P.op("vector", "tensor_reduce", v1, lg3, axis=AX.X, op=ALU.max)
    P.op("vector", "tensor_tensor", m1a[:, :].rearrange("p (g e) -> p g e", e=8),
         lg3, v1.unsqueeze(2).broadcast_to([128, 32, 8]), op=ALU.is_ge)
    P.op("vector", "scalar_tensor_tensor", tmsk[:, :], m1a[:, :], -1e30,
         lgall[:, :], op0=ALU.mult, op1=ALU.add)
    P.op("vector", "tensor_reduce", v2, tmsk[:, :].rearrange("p (g e) -> p g e", e=8),
         axis=AX.X, op=ALU.max)
    P.op("vector", "tensor_tensor", m2a[:, :].rearrange("p (g e) -> p g e", e=8),
         tmsk[:, :].rearrange("p (g e) -> p g e", e=8),
         v2.unsqueeze(2).broadcast_to([128, 32, 8]), op=ALU.is_ge)
    P.op("vector", "tensor_tensor", dd, v1, v2, op=ALU.subtract, inc="vs",
         drain=True)
    P.wait("scalar", P.now("vs"))
    P.op("scalar", "activation", s1, dd, AF.Sigmoid, inc="ss", drain=True)
    P.wait("vector", P.now("ss"))
    P.op("vector", "tensor_tensor", tmsk[:, :].rearrange("p (g e) -> p g e", e=8),
         m1a[:, :].rearrange("p (g e) -> p g e", e=8),
         s1.unsqueeze(2).broadcast_to([128, 32, 8]), op=ALU.mult)
    P.op("vector", "tensor_scalar", s1, s1, -1.0, 1.0, op0=ALU.mult, op1=ALU.add)
    P.op("vector", "tensor_tensor", m2a[:, :].rearrange("p (g e) -> p g e", e=8),
         m2a[:, :].rearrange("p (g e) -> p g e", e=8),
         s1.unsqueeze(2).broadcast_to([128, 32, 8]), op=ALU.mult)
    P.op("vector", "tensor_tensor", w8all[:, :], tmsk[:, :], m2a[:, :],
         op=ALU.add, inc="vs", drain=True)
    w8_tok = P.now("vs")

    w8T = al.sb("B_w8T", [8, T], BF16)
    P.wait("tensor", w8_tok)
    for tc in range(32):
        ib, b, g = pp.acquire()
        P.wait("tensor", g)
        bb = b.bitcast(BF16)
        t = P.op("tensor", "transpose", bb[0:8, 0:128],
                 w8all[:, tc * 8:(tc + 1) * 8], consts["ident"], inc="ts")
        P.wait("vector", t)
        ev = P.op("vector", "tensor_copy", w8T[:, tc * 128:(tc + 1) * 128],
                  bb[0:8, 0:128], inc="vs", drain=True)
        pp.release(ib, ev)
    w8T_tok = P.now("vs")

    st_sh = ExitStack()
    P.fence("gpsimd")
    w_gt, t_gt = load_weight(P, al, "B_wgt", W["gate_w"], 1024, 2048, stack=st_sh)
    w_up, t_up = load_weight(P, al, "B_wup", W["up_w"], 1024, 2048, stack=st_sh)
    w_dn, t_dn = load_weight(P, al, "B_wdn", W["down_w"], 2048, 1024, stack=st_sh)
    sil = al.sb("B_sil", [128, 4 * TTW], BF16, stack=st_sh)
    z = al.sb("B_z", [128, 16 * TTW], BF16, stack=st_sh)
    ost = al.sb("B_ost", [128, 4 * TTW], F32, stack=st_sh)
    P.waits("tensor", t_gt + t_up + t_dn + ln_toks)
    ost_rel = [None] * 4
    sil_rel = [None] * 4
    z_rel = [None] * 16
    for tt in range(NTT):
        for oc in range(16):
            i = oc % 4
            ib, b, g = pp.acquire()
            P.wait("tensor", g)
            for kc in range(NC):
                P.mm(b[:, :], wslice(w_gt, kc, oc, 2048), tile(xn, kc, tt),
                     start=(kc == 0), stop=(kc == NC - 1),
                     inc="ts" if kc == NC - 1 else None)
            t = P.now("ts")
            P.wait("scalar", t)
            if sil_rel[i] is not None:
                P.wait("scalar", sil_rel[i])
            sg = P.op("scalar", "activation", sil[:, i * TTW:(i + 1) * TTW],
                      b[:, :], AF.Silu, inc="ss", drain=True)
            pp.release(ib, sg)
            ib2, b2, g2 = pp.acquire()
            P.wait("tensor", g2)
            for kc in range(NC):
                P.mm(b2[:, :], wslice(w_up, kc, oc, 2048), tile(xn, kc, tt),
                     start=(kc == 0), stop=(kc == NC - 1),
                     inc="ts" if kc == NC - 1 else None)
            t2 = P.now("ts")
            P.waits("vector", [t2, sg])
            if z_rel[oc] is not None:
                P.wait("vector", z_rel[oc])
            zt = P.op("vector", "tensor_tensor", z[:, oc * TTW:(oc + 1) * TTW],
                      b2[:, :], sil[:, i * TTW:(i + 1) * TTW], op=ALU.mult,
                      inc="vs", drain=True)
            sil_rel[i] = zt
            pp.release(ib2, zt)
        z_tok = P.now("vs")
        for oc in range(NC):
            ib, b, g = pp.acquire()
            P.waits("tensor", [g, z_tok])
            for kc in range(16):
                P.mm(b[:, :], wslice(w_dn, kc, oc, 1024),
                     z[:, kc * TTW:(kc + 1) * TTW],
                     start=(kc == 0), stop=(kc == 15),
                     inc="ts" if kc == 15 else None)
            t = P.now("ts")
            i = (tt * NC + oc) % 4
            P.wait("vector", t)
            if ost_rel[i] is not None:
                P.wait("vector", ost_rel[i])
            ev = P.op("vector", "tensor_copy", ost[:, i * TTW:(i + 1) * TTW],
                      b[:, :], inc="vs", drain=True)
            pp.release(ib, ev)
            if oc == NC - 1:
                for kc2 in range(16):
                    z_rel[kc2] = P.now("ts")
            P.wait("gpsimd", ev)
            st = P.dma("gpsimd",
                       out_dram[oc * 128:(oc + 1) * 128, tt * TTW:(tt + 1) * TTW],
                       ost[:, i * TTW:(i + 1) * TTW])
            ost_rel[i] = st
    st_sh.close()

    st_e = ExitStack()
    P.fence("gpsimd")
    fc1 = al.sb("B_fc1", [128, 8 * 2048], BF16, stack=st_e)
    fc2 = al.sb("B_fc2", [128, 8 * 1024], BF16, stack=st_e)
    wbc = al.sb("B_wbc", [128, T], BF16, stack=st_e)
    sil2 = al.sb("B_sil2", [128, 4 * TTW], BF16, stack=st_e)
    z2 = al.sb("B_z2", [128, 8 * TTW], BF16, stack=st_e)
    est = al.sb("B_est", [128, 6 * TTW], BF16, stack=st_e)
    P.waits("tensor", [w8T_tok])
    fc_rel = None
    wbc_rel = None
    est_rel = [None] * 6
    sil2_rel = [None] * 4
    z2_rel = [None] * 8
    cnt = [0]
    for e in range(8):
        t_f1 = []
        t_f2 = []
        for kc in range(NC):
            if fc_rel is not None:
                P.wait("gpsimd", fc_rel)
            t_f1.append(P.dma("gpsimd", fc1[:, kc * 2048:(kc + 1) * 2048],
                              W["fc1"][e * 1024 + kc * 128: e * 1024 + (kc + 1) * 128, :]))
            t_f2.append(P.dma("gpsimd", fc2[:, kc * 1024:(kc + 1) * 1024],
                              W["fc2"][e * 1024 + kc * 128: e * 1024 + (kc + 1) * 128, :]))
        if wbc_rel is not None:
            P.wait("tensor", wbc_rel)
        for tt in range(NTT):
            ib, b, g = pp.acquire()
            P.wait("tensor", g)
            t = P.mm(b[:, :], consts["sel8"][:, e * 128:(e + 1) * 128],
                     w8T[:, tt * TTW:(tt + 1) * TTW], start=True, stop=True,
                     inc="ts")
            P.wait("vector", t)
            ev = P.op("vector", "tensor_copy", wbc[:, tt * TTW:(tt + 1) * TTW],
                      b[:, :], inc="vs", drain=True)
            pp.release(ib, ev)
        wbc_tok = P.now("vs")
        P.waits("tensor", t_f1 + t_f2 + [wbc_tok])
        for tt in range(NTT):
            for zc in range(NC):
                ib, b, g = pp.acquire()
                P.wait("tensor", g)
                for kc in range(NC):
                    P.mm(b[:, :], wslice(fc1, kc, zc, 2048), tile(xn, kc, tt),
                         start=(kc == 0), stop=(kc == NC - 1), inc="ts")
                t = P.now("ts")
                i2 = zc % 4
                P.wait("scalar", t)
                if sil2_rel[i2] is not None:
                    P.wait("scalar", sil2_rel[i2])
                sg = P.op("scalar", "activation", sil2[:, i2 * TTW:(i2 + 1) * TTW],
                          b[:, :], AF.Silu, inc="ss", drain=True)
                pp.release(ib, sg)
                ib2, b2, g2 = pp.acquire()
                P.wait("tensor", g2)
                for kc in range(NC):
                    P.mm(b2[:, :], wslice(fc1, kc, zc + 8, 2048), tile(xn, kc, tt),
                         start=(kc == 0), stop=(kc == NC - 1), inc="ts")
                t2 = P.now("ts")
                P.waits("vector", [t2, sg])
                if z2_rel[zc] is not None:
                    P.wait("vector", z2_rel[zc])
                zt = P.op("vector", "tensor_tensor", z2[:, zc * TTW:(zc + 1) * TTW],
                          b2[:, :], sil2[:, i2 * TTW:(i2 + 1) * TTW], op=ALU.mult,
                          inc="vs", drain=True)
                sil2_rel[i2] = zt
                pp.release(ib2, zt)
            z_tok = P.now("vs")
            for oc in range(NC):
                ib, b, g = pp.acquire()
                P.waits("tensor", [g, z_tok])
                for kc in range(NC):
                    P.mm(b[:, :], wslice(fc2, kc, oc, 1024),
                         z2[:, kc * TTW:(kc + 1) * TTW],
                         start=(kc == 0), stop=(kc == NC - 1), inc="ts")
                t = P.now("ts")
                i = cnt[0] % 6
                cnt[0] += 1
                P.wait("vector", t)
                if est_rel[i] is not None:
                    P.wait("vector", est_rel[i])
                ev = P.op("vector", "tensor_tensor", est[:, i * TTW:(i + 1) * TTW],
                          b[:, :], wbc[:, tt * TTW:(tt + 1) * TTW], op=ALU.mult,
                          inc="vs", drain=True)
                pp.release(ib, ev)
                if oc == NC - 1:
                    for kc2 in range(NC):
                        z2_rel[kc2] = P.now("ts")
                P.wait("gpsimd", ev)
                st = P.dma("gpsimd",
                           out_dram[oc * 128:(oc + 1) * 128, tt * TTW:(tt + 1) * TTW],
                           est[:, i * TTW:(i + 1) * TTW], accum_op=ALU.add)
                est_rel[i] = st
        fc_rel = P.now("ts")
        wbc_rel = P.now("vs")
    st_e.close()

    ep1 = al.sb("B_ep1", [128, 4 * TTW], F32)
    ep2 = al.sb("B_ep2", [128, 4 * TTW], F32)
    out_toks = []
    rel = {"a": [None] * 4, "b": [None] * 4}
    gp_done = P.dma_all_done("gpsimd")
    P.waits("sync", gp_done)
    k = [0]
    for tt in range(NTT):
        for oc in range(NC):
            i = k[0] % 4
            k[0] += 1
            if rel["a"][i] is not None:
                P.wait("sync", rel["a"][i])
            l1 = P.dma("sync", ep1[:, i * TTW:(i + 1) * TTW],
                       x_in[oc * 128:(oc + 1) * 128, tt * TTW:(tt + 1) * TTW])
            if rel["b"][i] is not None:
                P.wait("sync", rel["b"][i])
            l2 = P.dma("sync", ep2[:, i * TTW:(i + 1) * TTW],
                       out_dram[oc * 128:(oc + 1) * 128, tt * TTW:(tt + 1) * TTW])
            P.waits("vector", [l1, l2])
            vt = P.op("vector", "tensor_tensor", ep1[:, i * TTW:(i + 1) * TTW],
                      ep1[:, i * TTW:(i + 1) * TTW], ep2[:, i * TTW:(i + 1) * TTW],
                      op=ALU.add, inc="vs", drain=True)
            rel["b"][i] = vt
            P.wait("sync", vt)
            ot = P.dma("sync",
                       x_out[oc * 128:(oc + 1) * 128, tt * TTW:(tt + 1) * TTW],
                       ep1[:, i * TTW:(i + 1) * TTW])
            rel["a"][i] = ot
            out_toks.append(ot)
    return out_toks


def prep_B_weights(inp):
    W = {}
    W["w_router"] = _np(inp["L0_ffn_w_router"])
    W["fc1"] = _np(inp["L0_ffn_fc1_w"]).reshape(8 * 1024, 2048)
    W["fc2"] = _np(inp["L0_ffn_fc2_w"]).reshape(8 * 1024, 1024)
    W["gate_w"] = _np(inp["L0_ffn_gate_w"])
    W["up_w"] = _np(inp["L0_ffn_up_w"])
    W["down_w"] = _np(inp["L0_ffn_down_w"])
    return W


DEBUG_NO_CC = False
CAPACITY = 5120


def emit_E(P, al, consts, x_in, x_out, W):
    pp = PsumPool(P, [al.ps(f"ppE{i}", [128, 512]) for i in range(8)])
    xn = al.sb("E_xn", [128, NC * T], BF16)
    out_dram = P.dram("E_acc", [D, T], F32)
    hid_dram = P.dram("E_hid", [2048, T], BF16)
    cc_in = P.dram("E_cci", [1, 512], F32)
    cc_out = P.dram("E_cco", [8, 512], F32)
    off_dram = P.dram("E_off", [8, 1], F32)

    st_ln = ExitStack()
    ln_toks = emit_ln(P, al, "E_ln", xn, consts, pp, x_dram=x_in, substack=st_ln)
    wr, wr_toks = load_weight(P, al, "E_wr", W["w_routerT"], 1024, 8)
    cmask = al.sb("E_cmask", [8, 1], F32)
    t_cm = P.dma("gpsimd", cmask[:, :], W["coremask"])
    st_ln.close()

    P.waits("tensor", ln_toks)
    lgall, lg_tok = emit_router_logits(P, al, pp, xn, wr, wr_toks, "E")
    st_rt = ExitStack()
    m1all = al.sb("E_m1", [128, 32 * 8], BF16, stack=st_rt)
    m2all = al.sb("E_m2", [128, 32 * 8], BF16, stack=st_rt)
    g12 = al.sb("E_g12", [128, 2 * 32], F32, stack=st_rt)
    v12 = al.sb("E_v12", [128, 4 * 32], F32, stack=st_rt)
    tmsk = al.sb("E_tmsk", [128, 256], F32, stack=st_rt)
    ez = al.sb("E_ez", [128, 256], F32, stack=st_rt)
    kepta = al.sb("E_kepta", [128, 256], BF16, stack=st_rt)
    maskT = al.sb("E_maskT", [8, T], F32, stack=st_rt)
    cum = al.sb("E_cum", [8, T], F32, stack=st_rt)
    keepT = al.sb("E_keepT", [8, T], BF16, stack=st_rt)
    w8all = al.sb("E_w8", [128, 32 * 8], BF16, stack=st_rt)
    P.wait("vector", lg_tok)
    P.waits("scalar", [lg_tok])
    lg3 = lgall[:, :].rearrange("p (g e) -> p g e", e=8)
    v1 = v12[:, 0:32]
    v2 = v12[:, 32:64]
    den = v12[:, 64:96]
    scx = v12[:, 96:128]
    P.op("vector", "tensor_reduce", v1, lg3, axis=AX.X, op=ALU.max)
    P.op("vector", "tensor_tensor", m1all[:, :].rearrange("p (g e) -> p g e", e=8),
         lg3, v1.unsqueeze(2).broadcast_to([128, 32, 8]), op=ALU.is_ge)
    P.op("vector", "scalar_tensor_tensor", tmsk[:, :], m1all[:, :], -1e30,
         lgall[:, :], op0=ALU.mult, op1=ALU.add)
    P.op("vector", "tensor_reduce", v2,
         tmsk[:, :].rearrange("p (g e) -> p g e", e=8), axis=AX.X, op=ALU.max)
    P.op("vector", "tensor_tensor", m2all[:, :].rearrange("p (g e) -> p g e", e=8),
         tmsk[:, :].rearrange("p (g e) -> p g e", e=8),
         v2.unsqueeze(2).broadcast_to([128, 32, 8]), op=ALU.is_ge)
    P.op("vector", "tensor_tensor", ez[:, :].rearrange("p (g e) -> p g e", e=8),
         lg3, v1.unsqueeze(2).broadcast_to([128, 32, 8]), op=ALU.subtract,
         inc="vs", drain=True)
    P.wait("scalar", P.now("vs"))
    P.op("scalar", "activation", ez[:, :], ez[:, :], AF.Exp, inc="ss", drain=True)
    P.wait("vector", P.now("ss"))
    P.op("vector", "tensor_reduce", den,
         ez[:, :].rearrange("p (g e) -> p g e", e=8), axis=AX.X, op=ALU.add)
    P.op("vector", "reciprocal", g12[:, 0:32], den)
    P.op("vector", "tensor_tensor", scx, v2, v1, op=ALU.subtract, inc="vs",
         drain=True)
    P.wait("scalar", P.now("vs"))
    P.op("scalar", "activation", scx, scx, AF.Exp, inc="ss", drain=True)
    P.wait("vector", P.now("ss"))
    P.op("vector", "tensor_tensor", g12[:, 32:64], scx, g12[:, 0:32],
         op=ALU.mult)
    P.op("vector", "tensor_tensor", w8all[:, :], m1all[:, :], m2all[:, :],
         op=ALU.add, inc="vs", drain=True)
    rt_tok = P.now("vs")

    P.wait("tensor", rt_tok)
    for tc in range(32):
        ib, b, g = pp.acquire()
        P.wait("tensor", g)
        bb = b.bitcast(BF16)
        t = P.op("tensor", "transpose", bb[0:8, 0:128],
                 w8all[:, tc * 8:(tc + 1) * 8], consts["ident"], inc="ts")
        P.wait("vector", t)
        ev = P.op("vector", "tensor_copy", maskT[:, tc * 128:(tc + 1) * 128],
                  bb[0:8, 0:128], inc="vs", drain=True)
        pp.release(ib, ev)
    mt_tok = P.now("vs")
    P.op("vector", "tensor_tensor_scan", cum[:, :], maskT[:, :], maskT[:, :],
         initial=0.0, op0=ALU.add, op1=ALU.bypass, inc="vs", drain=True)
    cum_tok = P.now("vs")
    P.wait("gpsimd", cum_tok)
    t1 = P.dma("gpsimd", cc_in[0:1, 0:8], cum[:, T - 1:T])
    P.wait("gpsimd", t1)
    if not DEBUG_NO_CC:
        P.op("gpsimd", "collective_compute", "AllGather", ALU.bypass,
             replica_groups=[list(range(8))], ins=[cc_in], outs=[cc_out],
             inc="cs", pre_drain=False)
    st_h = ExitStack()
    P.fence("gpsimd")
    w_gt, t_gt = load_weight(P, al, "E_wgt", W["w_gateT"], 1024, 2048, stack=st_h)
    hst = al.sb("E_hst", [128, 4 * TTW], BF16, stack=st_h)
    P.waits("tensor", t_gt)
    hst_rel = [None] * 4
    k = [0]
    for tt in range(NTT):
        for oc in range(16):
            ib, b, g = pp.acquire()
            P.wait("tensor", g)
            for kc in range(NC):
                P.mm(b[:, :], wslice(w_gt, kc, oc, 2048), tile(xn, kc, tt),
                     start=(kc == 0), stop=(kc == NC - 1),
                     inc="ts" if kc == NC - 1 else None)
            t = P.now("ts")
            i = k[0] % 4
            k[0] += 1
            P.wait("scalar", t)
            if hst_rel[i] is not None:
                P.wait("scalar", hst_rel[i])
            sg = P.op("scalar", "activation", hst[:, i * TTW:(i + 1) * TTW],
                      b[:, :], AF.Silu, inc="ss", drain=True)
            pp.release(ib, sg)
            P.wait("gpsimd", sg)
            st = P.dma("gpsimd",
                       hid_dram[oc * 128:(oc + 1) * 128, tt * TTW:(tt + 1) * TTW],
                       hst[:, i * TTW:(i + 1) * TTW])
            hst_rel[i] = st
    st_h.close()


    gath = al.sb("E_gath", [8, 8], F32, stack=st_rt)
    offs_p = al.sb("E_offsp", [1, 8], F32, stack=st_rt)
    offs8 = al.sb("E_offs8", [8, 1], F32, stack=st_rt)
    if DEBUG_NO_CC:
        t2 = P.op("gpsimd", "memset", gath[:, :], 0.0, inc="gs", drain=True,
                  pre_drain=False)
    else:
        P.wait("gpsimd", P.now("cs"))
        t2 = P.dma("gpsimd", gath[:, :], cc_out[:, 0:8])
    ib, b, g = pp.acquire()
    P.waits("tensor", [g, t2, t_cm])
    t3 = P.mm(b[0:1, 0:8], cmask[:, :], gath[:, :], start=True, stop=True, inc="ts")
    P.wait("vector", t3)
    ev = P.op("vector", "tensor_copy", offs_p[:, :], b[0:1, 0:8], inc="vs",
              drain=True)
    pp.release(ib, ev)
    P.wait("gpsimd", P.now("vs"))
    t4 = P.dma("gpsimd", off_dram[0:8, 0:1], offs_p[0:1, 0:8])
    P.wait("gpsimd", t4)
    t5 = P.dma("gpsimd", offs8[:, :], off_dram)
    P.waits("vector", [t5])
    P.op("vector", "tensor_scalar", keepT[:, :], cum[:, :], offs8[:, :],
         CAPACITY + 0.5, op0=ALU.add, op1=ALU.is_le)
    P.op("vector", "tensor_tensor", keepT[:, :], keepT[:, :], maskT[:, :],
         op=ALU.mult, inc="vs", drain=True)
    kt_tok = P.now("vs")

    P.wait("tensor", kt_tok)
    for tc in range(32):
        ib, b, g = pp.acquire()
        P.wait("tensor", g)
        bb = b.bitcast(BF16)
        t = P.op("tensor", "transpose", bb[0:128, 0:8],
                 keepT[:, tc * 128:(tc + 1) * 128], consts["ident"][0:8, 0:8],
                 inc="ts")
        P.wait("vector", t)
        ev = P.op("vector", "tensor_copy", kepta[:, tc * 8:(tc + 1) * 8],
                  bb[0:128, 0:8], inc="vs", drain=True)
        pp.release(ib, ev)
    P.op("vector", "tensor_tensor", tmsk[:, :], kepta[:, :], m1all[:, :],
         op=ALU.mult)
    P.op("vector", "tensor_reduce", v1,
         tmsk[:, :].rearrange("p (g e) -> p g e", e=8), axis=AX.X, op=ALU.add)
    P.op("vector", "tensor_tensor", tmsk[:, :], kepta[:, :], m2all[:, :],
         op=ALU.mult)
    P.op("vector", "tensor_reduce", v2,
         tmsk[:, :].rearrange("p (g e) -> p g e", e=8), axis=AX.X, op=ALU.add)
    P.op("vector", "tensor_tensor", v1, v1, g12[:, 0:32], op=ALU.mult)
    P.op("vector", "tensor_tensor", v2, v2, g12[:, 32:64], op=ALU.mult)
    P.op("vector", "scalar_tensor_tensor", den, v1, 1.0, v2, op0=ALU.mult,
         op1=ALU.add)
    P.op("vector", "tensor_scalar", den, den, 1e-6, None, op0=ALU.add)
    P.op("vector", "reciprocal", scx, den)
    P.op("vector", "tensor_tensor", v1, v1, scx, op=ALU.mult)
    P.op("vector", "tensor_tensor", v2, v2, scx, op=ALU.mult)
    P.op("vector", "tensor_tensor", tmsk[:, :].rearrange("p (g e) -> p g e", e=8),
         m1all[:, :].rearrange("p (g e) -> p g e", e=8),
         v1.unsqueeze(2).broadcast_to([128, 32, 8]), op=ALU.mult)
    P.op("vector", "tensor_tensor", ez[:, :].rearrange("p (g e) -> p g e", e=8),
         m2all[:, :].rearrange("p (g e) -> p g e", e=8),
         v2.unsqueeze(2).broadcast_to([128, 32, 8]), op=ALU.mult)
    P.op("vector", "tensor_tensor", w8all[:, :], tmsk[:, :], ez[:, :],
         op=ALU.add, inc="vs", drain=True)
    w8_tok = P.now("vs")

    w8T = al.sb("E_w8T", [8, T], BF16)
    P.wait("tensor", w8_tok)
    for tc in range(32):
        ib, b, g = pp.acquire()
        P.wait("tensor", g)
        bb = b.bitcast(BF16)
        t = P.op("tensor", "transpose", bb[0:8, 0:128],
                 w8all[:, tc * 8:(tc + 1) * 8], consts["ident"], inc="ts")
        P.wait("vector", t)
        ev = P.op("vector", "tensor_copy", w8T[:, tc * 128:(tc + 1) * 128],
                  bb[0:8, 0:128], inc="vs", drain=True)
        pp.release(ib, ev)
    w8T_tok = P.now("vs")
    st_rt.close()

    st_e = ExitStack()
    P.fence("gpsimd")
    wdn = al.sb("E_wdn", [128, 16 * 1024], BF16, stack=st_e)
    hid = al.sb("E_hidsb", [128, 2 * 16 * TTW], BF16, stack=st_e)
    wbc = al.sb("E_wbc", [128, T], BF16, stack=st_e)
    est = al.sb("E_est", [128, 6 * TTW], BF16, stack=st_e)
    wdn_rel = None
    wbc_rel = None
    hid_rel = [None, None]
    est_rel = [None] * 6
    cnt = [0]
    for e in range(8):
        t_w = []
        for kc in range(16):
            if wdn_rel is not None:
                P.wait("gpsimd", wdn_rel)
            t_w.append(P.dma("gpsimd", wdn[:, kc * 1024:(kc + 1) * 1024],
                             W["w_downT"][e * 2048 + kc * 128: e * 2048 + (kc + 1) * 128, :]))
        if wbc_rel is not None:
            P.wait("tensor", wbc_rel)
        for tt in range(NTT):
            ib, b, g = pp.acquire()
            P.wait("tensor", g)
            t = P.mm(b[:, :], consts["sel8"][:, e * 128:(e + 1) * 128],
                     w8T[:, tt * TTW:(tt + 1) * TTW], start=True, stop=True,
                     inc="ts")
            P.wait("vector", t)
            ev = P.op("vector", "tensor_copy", wbc[:, tt * TTW:(tt + 1) * TTW],
                      b[:, :], inc="vs", drain=True)
            pp.release(ib, ev)
        wbc_tok = P.now("vs")
        P.waits("tensor", t_w + [wbc_tok])
        for tt in range(NTT):
            ih = tt % 2
            if hid_rel[ih] is not None:
                P.wait("gpsimd", hid_rel[ih])
            src_ap = bass.AP(hid_dram.tensor, tt * TTW,
                             [[T, 128], [128 * T, 16], [1, TTW]])
            ht = P.dma("gpsimd", hid[:, ih * 16 * TTW:(ih + 1) * 16 * TTW],
                       src_ap)
            P.wait("tensor", ht)
            for oc in range(NC):
                ib, b, g = pp.acquire()
                P.wait("tensor", g)
                for kc in range(16):
                    P.mm(b[:, :], wslice(wdn, kc, oc, 1024),
                         hid[:, ih * 16 * TTW + kc * TTW: ih * 16 * TTW + (kc + 1) * TTW],
                         start=(kc == 0), stop=(kc == 15), inc="ts")
                t = P.now("ts")
                i = cnt[0] % 6
                cnt[0] += 1
                P.wait("vector", t)
                if est_rel[i] is not None:
                    P.wait("vector", est_rel[i])
                ev = P.op("vector", "tensor_tensor", est[:, i * TTW:(i + 1) * TTW],
                          b[:, :], wbc[:, tt * TTW:(tt + 1) * TTW], op=ALU.mult,
                          inc="vs", drain=True)
                pp.release(ib, ev)
                if oc == NC - 1:
                    hid_rel[ih] = P.now("ts")
                P.wait("gpsimd", ev)
                first = (e == 0)
                if first:
                    st = P.dma("gpsimd",
                               out_dram[oc * 128:(oc + 1) * 128,
                                        tt * TTW:(tt + 1) * TTW],
                               est[:, i * TTW:(i + 1) * TTW])
                else:
                    st = P.dma("gpsimd",
                               out_dram[oc * 128:(oc + 1) * 128,
                                        tt * TTW:(tt + 1) * TTW],
                               est[:, i * TTW:(i + 1) * TTW], accum_op=ALU.add)
                est_rel[i] = st
        wdn_rel = P.now("ts")
        wbc_rel = P.now("vs")
    st_e.close()

    ep1 = al.sb("E_ep1", [128, 4 * TTW], F32)
    ep2 = al.sb("E_ep2", [128, 4 * TTW], F32)
    out_toks = []
    rel = {"a": [None] * 4, "b": [None] * 4}
    P.waits("sync", P.dma_all_done("gpsimd"))
    k2 = [0]
    for tt in range(NTT):
        for oc in range(NC):
            i = k2[0] % 4
            k2[0] += 1
            if rel["a"][i] is not None:
                P.wait("sync", rel["a"][i])
            l1 = P.dma("sync", ep1[:, i * TTW:(i + 1) * TTW],
                       x_in[oc * 128:(oc + 1) * 128, tt * TTW:(tt + 1) * TTW])
            if rel["b"][i] is not None:
                P.wait("sync", rel["b"][i])
            l2 = P.dma("sync", ep2[:, i * TTW:(i + 1) * TTW],
                       out_dram[oc * 128:(oc + 1) * 128, tt * TTW:(tt + 1) * TTW])
            P.waits("vector", [l1, l2])
            vt = P.op("vector", "tensor_tensor", ep1[:, i * TTW:(i + 1) * TTW],
                      ep1[:, i * TTW:(i + 1) * TTW], ep2[:, i * TTW:(i + 1) * TTW],
                      op=ALU.add, inc="vs", drain=True)
            rel["b"][i] = vt
            P.wait("sync", vt)
            ot = P.dma("sync",
                       x_out[oc * 128:(oc + 1) * 128, tt * TTW:(tt + 1) * TTW],
                       ep1[:, i * TTW:(i + 1) * TTW])
            rel["a"][i] = ot
            out_toks.append(ot)
    return out_toks


def prep_E_weights(inp, core_id):
    W = {}
    W["w_routerT"] = _np(np.asarray(inp["L1_ffn_w_router"]).T)
    W["w_gateT"] = _np(np.asarray(inp["L1_ffn_w_gate"]).T)
    wd = np.asarray(inp["L1_ffn_w_down"])
    W["w_downT"] = _np(wd.transpose(0, 2, 1)).reshape(8 * 2048, 1024)
    W["coremask"] = (np.arange(8) < core_id).astype(np.float32).reshape(8, 1)
    return W


def emit_C(P, al, consts, x_in, x_out, W):
    pp = PsumPool(P, [al.ps(f"ppC{i}", [128, 512]) for i in range(8)])
    nc = P.nc
    kext_dram = P.dram("C_kext", [65, 32 * 2048], BF16)
    vext_dram = P.dram("C_vext", [T, 1040], BF16)
    mq_dram = P.dram("C_mq", [2, NC * T], BF16)
    attn_dram = P.dram("C_attn", [D, T], BF16)

    xn = al.sb("C_xn", [128, NC * T], BF16)
    st_ln = ExitStack()
    ln_toks = emit_ln(P, al, "C_ln", xn, consts, pp, x_dram=x_in, substack=st_ln)
    b_q, tbq = load_bias(P, al, "C_bq", W["bq"], 1024)
    b_k, tbk = load_bias(P, al, "C_bk", W["bk"], 1024)
    b_v, tbv = load_bias(P, al, "C_bv", W["bv"], 1024)
    slr = al.sb("C_slr", [1, 16], F32)
    t_sl = P.dma("gpsimd", slr[:, :], W["slopes"])
    k2m = al.sb("C_k2m", [2, 8], F32)
    nkx = al.sb("C_nkx", [2, 8], F32)
    st_ln.close()
    qf = al.sb("C_qf", [128, NC * T], BF16)

    slbc = al.sb("C_slbc", [128, 16], F32)
    offt = al.sb("C_offt", [128, 16], F32)
    ztab = al.sb("C_ztab", [128, 512], F32)
    ctab = al.sb("C_ctab", [128, 512], F32)
    ib, b, g = pp.acquire()
    P.waits("tensor", [g, t_sl])
    t = P.mm(b[0:128, 0:16], consts["onesf1"], slr[:, :], start=True, stop=True,
             inc="ts")
    P.wait("vector", t)
    ev = P.op("vector", "tensor_copy", slbc[:, :], b[0:128, 0:16], inc="vs",
              drain=True)
    pp.release(ib, ev)
    P.op("vector", "tensor_scalar", offt[:, :], slbc[:, :], 4095.0, 0.0,
         op0=ALU.mult, op1=ALU.max)
    for h in range(16):
        P.op("vector", "tensor_scalar", ztab[:, h * 32:(h + 1) * 32],
             consts["posc"], slbc[:, h:h + 1], offt[:, h:h + 1],
             op0=ALU.mult, op1=ALU.subtract)
    P.op("vector", "tensor_copy", ztab[0:1, 0:1], ztab[0:1, 0:1], inc="vs",
         drain=True)
    P.wait("scalar", P.now("vs"))
    ct_tok = P.op("scalar", "activation", ctab[:, :], ztab[:, :], AF.Exp,
                  inc="ss", drain=True)

    st_k = ExitStack()
    P.fence("gpsimd")
    w_k, t_k = load_weight(P, al, "C_wk", W["WkT"], 1024, 1024, stack=st_k)
    ktt = al.sb("C_ktt", [128, NC * TTW], BF16, stack=st_k)
    ksq = al.sb("C_ksq", [128, NC * TTW], BF16, stack=st_k)
    k2s8 = al.sb("C_k2s8", [2, NC * TTW], F32, stack=st_k)
    k2p8 = al.sb("C_k2p8", [2, 8], F32, stack=st_k)
    kst = al.sb("C_kst", [65, 2 * 2048], BF16, stack=st_k)
    k2p = al.sb("C_k2p", [2, 4], F32, stack=st_k)
    k2s = al.sb("C_k2s", [2, TTW], F32, stack=st_k)
    P.op("vector", "memset", k2m[:, :], 0.0)
    P.op("vector", "memset", kst[64:65, :], 1.0, inc="vs", drain=True)
    P.waits("tensor", t_k + ln_toks)
    P.waits("vector", tbk)
    kst_rel = [None, None]
    ksq_guard = None
    for tt in range(NTT):
        ev_last = None
        for oc in range(NC):
            ib, b, g = pp.acquire()
            P.wait("tensor", g)
            for kc in range(NC):
                P.mm(b[:, :], wslice(w_k, kc, oc, 1024), tile(xn, kc, tt),
                     start=(kc == 0), stop=(kc == NC - 1),
                     inc="ts" if kc == NC - 1 else None)
            t = P.now("ts")
            P.wait("vector", t)
            evk = P.op("vector", "tensor_scalar", ktt[:, oc * TTW:(oc + 1) * TTW],
                       b[:, :], b_k[:, oc:oc + 1], None, op0=ALU.add,
                       inc="vs", drain=(oc == NC - 1))
            pp.release(ib, evk)
            ev_last = evk
        P.wait("scalar", ev_last)
        if ksq_guard is not None:
            P.wait("scalar", ksq_guard)
        sqt = P.op("scalar", "activation", ksq[:, :], ktt[:, :], AF.Square,
                   inc="ss", drain=True)
        P.wait("tensor", sqt)
        pbs = []
        for oc in range(NC):
            ib2, b2, g2 = pp.acquire()
            P.wait("tensor", g2)
            t2 = P.mm(b2[0:2, :], consts["pair2"],
                      ksq[:, oc * TTW:(oc + 1) * TTW],
                      start=True, stop=True, inc="ts")
            pbs.append((ib2, b2, t2))
        ksq_guard = P.now("ts")
        P.wait("vector", ksq_guard)
        for oc, (ib2, b2, t2) in enumerate(pbs):
            cp = P.op("vector", "tensor_copy", k2s8[:, oc * TTW:(oc + 1) * TTW],
                      b2[0:2, :], inc="vs", drain=(oc == NC - 1))
            pp.release(ib2, cp)
        P.op("vector", "tensor_reduce", k2p8[:, :],
             k2s8[:, :].rearrange("p (g e) -> p g e", e=TTW), axis=AX.X,
             op=ALU.max)
        P.op("vector", "tensor_tensor", k2m[:, :], k2m[:, :], k2p8[:, :],
             op=ALU.max, inc="vs", drain=True)
        _stage_k(P, al, pp, consts, ktt, tt, kst, kst_rel, kext_dram)
    P.wait("scalar", P.now("vs"))
    P.op("scalar", "activation", nkx[:, :], k2m[:, :], AF.Sqrt, inc="ss",
         drain=True)
    P.wait("vector", P.now("ss"))
    P.op("vector", "tensor_scalar", nkx[:, :], nkx[:, :], -1.0, None,
         op0=ALU.mult, inc="vs", drain=True)
    nkx_tok = P.now("vs")
    st_k.close()

    st_q = ExitStack()
    P.fence("gpsimd")
    w_q, t_q = load_weight(P, al, "C_wq", W["WqT"], 1024, 1024, stack=st_q)
    P.waits("tensor", t_q)
    P.waits("vector", tbq + [nkx_tok])
    for tt in range(NTT):
        for oc in range(NC):
            ib, b, g = pp.acquire()
            P.wait("tensor", g)
            for kc in range(NC):
                P.mm(b[:, :], wslice(w_q, kc, oc, 1024), tile(xn, kc, tt),
                     start=(kc == 0), stop=(kc == NC - 1),
                     inc="ts" if kc == NC - 1 else None)
            t = P.now("ts")
            P.wait("vector", t)
            evq = P.op("vector", "tensor_scalar", tile(qf, oc, tt),
                       b[:, :], b_q[:, oc:oc + 1], None, op0=ALU.add,
                       inc="vs", drain=True)
            pp.release(ib, evq)
    _mq_pass(P, al, pp, consts, qf, nkx, mq_dram, st_q)
    st_q.close()

    st_v = ExitStack()
    P.fence("gpsimd")
    w_v, t_v = load_weight(P, al, "C_wv", W["WvT"], 1024, 1024, stack=st_v)
    vtt = al.sb("C_vtt", [128, NC * TTW], BF16, stack=st_v)
    vtmp = al.sb("C_vtmp", [64, 16 * 128], BF16, stack=st_v)
    vst = al.sb("C_vst", [128, 2 * 1040], BF16, stack=st_v)
    P.waits("tensor", t_v)
    P.waits("vector", tbv)
    P.wait("vector", ct_tok)
    vst_rel = [None, None]
    vtmp_rel = [None, None]
    for tt in range(NTT):
        for oc in range(NC):
            ib, b, g = pp.acquire()
            P.wait("tensor", g)
            for kc in range(NC):
                P.mm(b[:, :], wslice(w_v, kc, oc, 1024), tile(xn, kc, tt),
                     start=(kc == 0), stop=(kc == NC - 1),
                     inc="ts" if kc == NC - 1 else None)
            t = P.now("ts")
            P.wait("vector", t)
            evv = P.op("vector", "tensor_scalar", vtt[:, oc * TTW:(oc + 1) * TTW],
                       b[:, :], b_v[:, oc:oc + 1], None, op0=ALU.add,
                       inc="vs", drain=True)
            pp.release(ib, evv)
        for s4 in range(4):
            kc = tt * 4 + s4
            j = kc % 2
            if vst_rel[j] is not None:
                P.wait("vector", vst_rel[j])
            for h in range(16):
                c, p = h // 2, h % 2
                jj = h % 2
                if vtmp_rel[jj] is not None:
                    P.wait("vector", vtmp_rel[jj])
                cp = P.op("vector", "tensor_copy",
                          vtmp[0:64, jj * 128:(jj + 1) * 128],
                          vtt[p * 64:p * 64 + 64,
                              c * TTW + s4 * 128: c * TTW + (s4 + 1) * 128],
                          inc="vs", drain=True)
                ib2, b2, g2 = pp.acquire()
                P.waits("tensor", [g2, cp])
                bb2 = b2.bitcast(BF16)
                tr = P.op("tensor", "transpose", bb2[0:128, 0:64],
                          vtmp[0:64, jj * 128:(jj + 1) * 128],
                          consts["ident"][0:64, 0:64], inc="ts")
                vtmp_rel[jj] = tr
                P.wait("vector", tr)
                P.op("vector", "tensor_scalar",
                     vst[:, j * 1040 + h * 65: j * 1040 + h * 65 + 64],
                     bb2[0:128, 0:64], ctab[:, h * 32 + kc: h * 32 + kc + 1],
                     None, op0=ALU.mult)
                ev3 = P.op("vector", "tensor_copy",
                           vst[:, j * 1040 + h * 65 + 64: j * 1040 + h * 65 + 65],
                           ctab[:, h * 32 + kc: h * 32 + kc + 1],
                           inc="vs", drain=True)
                pp.release(ib2, ev3)
            cpt = P.now("vs")
            P.wait("gpsimd", cpt)
            st = P.dma("gpsimd", vext_dram[kc * 128:(kc + 1) * 128, :],
                       vst[:, j * 1040:(j + 1) * 1040])
            vst_rel[j] = st
    st_v.close()

    emit_attn_win(P, al, pp, consts, qf, kext_dram, vext_dram, mq_dram,
                  attn_dram, win_kcs_C)

    st_wo = ExitStack()
    P.fence("gpsimd")
    P.fence("sync")
    w_o, t_o = load_weight(P, al, "C_wo", W["WoT"], 1024, 1024, stack=st_wo)
    b_o, tbo = load_bias(P, al, "C_bo", W["bo"], 1024, stack=st_wo)
    att = al.sb("C_att", [128, 2 * NC * TTW], BF16, stack=st_wo)
    xr = al.sb("C_xr", [128, 4 * TTW], F32, stack=st_wo)
    yt = al.sb("C_yt", [128, 4 * TTW], F32, stack=st_wo)
    P.waits("tensor", t_o)
    P.waits("vector", tbo)
    att_rel = [None, None]
    rel = {"xr": [None] * 4, "yt": [None] * 4}
    out_toks = []
    gp_done = P.dma_all_done("gpsimd")
    P.waits("gpsimd", gp_done)
    k4 = [0]
    for tt in range(NTT):
        ia = tt % 2
        if att_rel[ia] is not None:
            P.wait("gpsimd", att_rel[ia])
        src_ap = bass.AP(attn_dram.tensor, tt * TTW,
                         [[T, 128], [128 * T, NC], [1, TTW]])
        la = P.dma("gpsimd", att[:, ia * NC * TTW:(ia + 1) * NC * TTW], src_ap)
        P.wait("tensor", la)
        for oc in range(NC):
            ib, b, g = pp.acquire()
            P.wait("tensor", g)
            for kc in range(NC):
                P.mm(b[:, :], wslice(w_o, kc, oc, 1024),
                     att[:, ia * NC * TTW + kc * TTW: ia * NC * TTW + (kc + 1) * TTW],
                     start=(kc == 0), stop=(kc == NC - 1),
                     inc="ts" if kc == NC - 1 else None)
            t = P.now("ts")
            i = k4[0] % 4
            k4[0] += 1
            if rel["xr"][i] is not None:
                P.wait("sync", rel["xr"][i])
            lt = P.dma("sync", xr[:, i * TTW:(i + 1) * TTW],
                       x_in[oc * 128:(oc + 1) * 128, tt * TTW:(tt + 1) * TTW])
            P.waits("vector", [t, lt])
            if rel["yt"][i] is not None:
                P.wait("vector", rel["yt"][i])
            P.op("vector", "tensor_scalar", yt[:, i * TTW:(i + 1) * TTW],
                 b[:, :], b_o[:, oc:oc + 1], None, op0=ALU.add)
            vt = P.op("vector", "tensor_tensor", yt[:, i * TTW:(i + 1) * TTW],
                      yt[:, i * TTW:(i + 1) * TTW], xr[:, i * TTW:(i + 1) * TTW],
                      op=ALU.add, inc="vs", drain=True)
            pp.release(ib, vt)
            rel["xr"][i] = vt
            if oc == NC - 1:
                att_rel[ia] = P.now("ts")
            P.wait("sync", vt)
            ot = P.dma("sync",
                       x_out[oc * 128:(oc + 1) * 128, tt * TTW:(tt + 1) * TTW],
                       yt[:, i * TTW:(i + 1) * TTW])
            rel["yt"][i] = ot
            out_toks.append(ot)
    st_wo.close()
    return out_toks


def prep_C_weights(inp):
    W = {}
    W["WqT"] = _np(np.asarray(inp["L0_attn_b_Wq"]).T)
    W["WkT"] = _np(np.asarray(inp["L0_attn_b_Wk"]).T)
    W["WvT"] = _np(np.asarray(inp["L0_attn_b_Wv"]).T)
    W["WoT"] = _np(np.asarray(inp["L0_attn_b_Wo"]).T)
    W["bq"] = _bvec(inp["L0_attn_b_bq"])
    W["bk"] = _bvec(inp["L0_attn_b_bk"])
    W["bv"] = _bvec(inp["L0_attn_b_bv"])
    W["bo"] = _bvec(inp["L0_attn_b_bo"])
    W["slopes"] = _np(inp["L0_attn_b_slopes"]).reshape(1, 16)
    return W



_MASKS = {}
_MASK_BY_KEY = {}
_REL = (np.arange(512)[None, :] - np.arange(128)[:, None])


def _mask_name(kind, w, d):
    imj = 128 * d + _REL
    if kind == "D":
        m = (imj >= 0) & (imj <= 128)
    else:
        m = (imj >= 0) & (imj < w)
    if m.all():
        return None
    if not m.any():
        return "SKIP"
    key = m.tobytes()
    if key not in _MASK_BY_KEY:
        import ml_dtypes
        nm = f"mk{len(_MASK_BY_KEY)}"
        _MASK_BY_KEY[key] = nm
        _MASKS[nm] = m.astype(ml_dtypes.bfloat16)
    return _MASK_BY_KEY[key]


def win_kcs_D(h, qg):
    lo = max(0, 4 * qg - 1)
    out = []
    for kc in range(lo, 4 * qg + 4):
        nm = _mask_name("D", 128, 4 * qg - kc)
        if nm != "SKIP":
            out.append((kc, nm))
    return out


def win_kcs_F(h, qg):
    w = max(4, 1024 >> h)
    lo = max(0, (qg * 512 - w + 1) // 128)
    out = []
    for kc in range(lo, 4 * qg + 4):
        nm = _mask_name("F", w, 4 * qg - kc)
        if nm != "SKIP":
            out.append((kc, nm))
    return out


def _win_masks():
    for qg in range(8):
        for h in range(16):
            win_kcs_D(h, qg)
            win_kcs_F(h, qg)
    return dict(_MASKS)


def _mask_list(kcs_fn):
    names = set()
    for qg in range(8):
        for h in range(16):
            names |= {nm for (_, nm) in kcs_fn(h, qg) if nm}
    return sorted(names)


def prep_masks(kcs_fn):
    names = _mask_list(kcs_fn)
    arrs = [np.asarray(_MASKS[nm], dtype=np.float32) for nm in names]
    return np.ascontiguousarray(np.concatenate(arrs, axis=1))


def emit_attn_win(P, al, pp, consts, qf, kext_dram, vext_dram, mq_dram,
                  attn_dram, kcs_fn, gate_fn=None, masks_dram=None):
    st = ExitStack()
    P.fence()
    mnames = _mask_list(kcs_fn)
    midx = {nm: i for i, nm in enumerate(mnames)}
    if mnames:
        marena = al.sb("aw_masks", [128, len(mnames) * TTW], BF16, stack=st)
        mt = P.dma("gpsimd", marena[:, :], masks_dram)
        P.waits("vector", [mt])

    def mslice(nm):
        i = midx[nm]
        return marena[:, i * TTW:(i + 1) * TTW]

    NSL = 6
    NPT = 8
    KW = 512
    VW = 260
    kslab = al.sb("aw_kslab", [65, NSL * KW], BF16, stack=st)
    vslab = al.sb("aw_vslab", [128, NSL * VW], BF16, stack=st)
    qstg = al.sb("aw_qstg", [65, 4 * TTW], BF16, stack=st)
    ptile = al.sb("aw_pt", [128, NPT * TTW], BF16, stack=st)
    rtmp4 = al.sb("aw_rt", [4, TTW], F32, stack=st)
    rb4 = al.sb("aw_rb", [4, TTW], BF16, stack=st)
    dtmp = al.sb("aw_dt", [1, 4 * TTW], F32, stack=st)
    bcs = al.sb("aw_bcs", [64, 2 * TTW], BF16, stack=st)
    ostg = al.sb("aw_ostg", [64, 4 * TTW], BF16, stack=st)
    P.waits("tensor", [pp.rel[i] for i in range(8)] + [P.now("vs"), P.now("ss")])
    av_banks = pp.tiles[0:4]
    s_banks = pp.tiles[4:7]
    bc_bank = pp.tiles[7]
    kv_rel = [None] * NSL
    slab_key = [None] * NSL
    slab_tok = [None] * NSL
    q_rel = None
    pt_rel = [None] * NPT
    o_rel = [None] * 4
    bc_rel = None
    bcs_rel = [None, None]
    s_rel = [None] * 3
    sidx = [0]
    ptidx = [0]
    ocnt = [0]

    pend_by_kc = {}

    def emit_av(it):
        (j, h, jj, pt, rdy, first, last, kc0) = it
        P.wait("tensor", rdy)
        t2 = P.mm(av_banks[j][0:65, :],
                  vslab[:, jj * VW + (h % 4) * 65: jj * VW + (h % 4 + 1) * 65],
                  ptile[:, pt * TTW:(pt + 1) * TTW],
                  start=first, stop=last, inc="ts")
        pt_rel[pt] = t2
        pend_by_kc[kc0] -= 1
        if pend_by_kc[kc0] == 0:
            kv_rel[kc0 % NSL] = t2

    def ensure_slab(shg, kc):
        jj = kc % NSL
        if slab_key[jj] != (shg, kc):
            if slab_key[jj] is not None:
                P.wait("gpsimd", kv_rel[jj])
            lk = P.dma("gpsimd", kslab[:, jj * KW:(jj + 1) * KW],
                       kext_dram[0:65,
                                 kc * 2048 + shg * 512: kc * 2048 + (shg + 1) * 512])
            lv = P.dma("gpsimd", vslab[:, jj * VW:(jj + 1) * VW],
                       vext_dram[kc * 128:(kc + 1) * 128,
                                 shg * 260:(shg + 1) * 260])
            slab_key[jj] = (shg, kc)
            slab_tok[jj] = [lk, lv]

    passes = [(hg, qg) for hg in range(4) for qg in range(NTT)]
    for pidx, (hg, qg) in enumerate(passes):
        if True:
            heads = [4 * hg + j for j in range(4)]
            plans = {h: kcs_fn(h, qg) for h in heads}
            union = sorted({kc for h in heads for (kc, _) in plans[h]})
            if q_rel is not None:
                P.waits("vector", [q_rel])
                P.waits("gpsimd", [q_rel])
            for j, h in enumerate(heads):
                c, p = h // 2, h % 2
                P.op("vector", "tensor_copy",
                     qstg[0:64, j * TTW:(j + 1) * TTW],
                     qf[p * 64:p * 64 + 64, c * T + qg * TTW: c * T + (qg + 1) * TTW],
                     inc="vs" if j == 3 else None, drain=(j == 3))
            qc_tok = P.now("vs")
            mts = []
            for j, h in enumerate(heads):
                c, p = h // 2, h % 2
                mts.append(P.dma("gpsimd", qstg[64:65, j * TTW:(j + 1) * TTW],
                                 mq_dram[p:p + 1,
                                         c * T + qg * TTW: c * T + (qg + 1) * TTW]))
            P.waits("tensor", mts + [qc_tok])
            started = {h: False for h in heads}
            nkc = {h: len(plans[h]) for h in heads}
            done_cnt = {h: 0 for h in heads}
            pend = []
            prev_kc = None
            for kc in union:
                jj = kc % NSL
                ensure_slab(hg, kc)
                if slab_tok[jj] is not None:
                    P.waits("tensor", slab_tok[jj])
                    slab_tok[jj] = None
                kc_items = []
                for j, h in enumerate(heads):
                    ms = [m for (kk, m) in plans[h] if kk == kc]
                    if not ms:
                        continue
                    mname = ms[0]
                    sj = sidx[0] % 3
                    sidx[0] += 1
                    sb = s_banks[sj]
                    if s_rel[sj] is not None:
                        P.wait("tensor", s_rel[sj])
                    t1 = P.mm(sb[:, :],
                              kslab[:, jj * KW + (h % 4) * 128:
                                       jj * KW + (h % 4 + 1) * 128],
                              qstg[0:65, j * TTW:(j + 1) * TTW],
                              start=True, stop=True, inc="ts")
                    pt = ptidx[0] % NPT
                    ptidx[0] += 1
                    P.wait("scalar", t1)
                    if pt_rel[pt] is not None:
                        P.wait("scalar", pt_rel[pt])
                    ex = P.op("scalar", "activation",
                              ptile[:, pt * TTW:(pt + 1) * TTW], sb[:, :],
                              AF.Exp, scale=0.125, inc="ss", pre_drain=False)
                    s_rel[sj] = ex
                    kc_items.append((j, h, jj, pt, mname))
                if not kc_items:
                    continue
                bt = P.op("scalar", "drain", inc="ss", pre_drain=False)
                any_mask = any(m is not None for (_, _, _, _, m) in kc_items)
                if any_mask:
                    P.wait("vector", bt)
                    for (j, h, jj2, pt, mname) in kc_items:
                        if mname is None:
                            continue
                        P.op("vector", "tensor_tensor",
                             ptile[:, pt * TTW:(pt + 1) * TTW],
                             ptile[:, pt * TTW:(pt + 1) * TTW],
                             mslice(mname), op=ALU.mult, pre_drain=False)
                    bt = P.op("vector", "drain", inc="vs", pre_drain=False)
                pend_by_kc[kc] = len(kc_items)
                for (j, h, jj2, pt, mname) in kc_items:
                    done_cnt[h] += 1
                    pend.append((j, h, jj2, pt, bt, not started[h],
                                 done_cnt[h] == nkc[h], kc))
                    started[h] = True
                while pend and (len(pend) > 4 or pend[0][7] <= kc - 2):
                    emit_av(pend.pop(0))
            while pend:
                emit_av(pend.pop(0))
            q_rel = P.now("ts")
            for jj2 in range(NSL):
                kv_rel[jj2] = q_rel
            if pidx + 1 < len(passes):
                nhg, nqg = passes[pidx + 1]
                nheads = [4 * nhg + j for j in range(4)]
                nunion = sorted({kc for h2 in nheads
                                 for (kc, _) in kcs_fn(h2, nqg)})
                for kc2 in nunion[:4]:
                    ensure_slab(nhg, kc2)
            P.waits("scalar", [q_rel])
            for j in range(4):
                P.op("scalar", "activation", dtmp[0:1, j * TTW:(j + 1) * TTW],
                     av_banks[j][64:65, :], AF.Copy, bias=1e-30,
                     inc="ss" if j == 3 else None)
            dt_tok = P.now("ss")
            sb0 = s_banks[0]
            P.waits("tensor", [dt_tok] + ([s_rel[0]] if s_rel[0] else []))
            for j in range(4):
                P.mm(sb0[0:4, :], consts["oh4"][0:1, j * 4:(j + 1) * 4],
                     dtmp[0:1, j * TTW:(j + 1) * TTW],
                     start=(j == 0), stop=(j == 3),
                     inc="ts" if j == 3 else None)
            gat_tok = P.now("ts")
            P.wait("vector", gat_tok)
            P.op("vector", "reciprocal", rtmp4[:, :], sb0[0:4, :], inc="vs")
            s_rel[0] = P.now("vs")
            if gate_fn is not None:
                gate_fn(P, pp, hg, qg, rtmp4, bc_bank, bc_rel)
            P.op("vector", "tensor_copy", rb4[:, :], rtmp4[:, :], inc="vs")
            rb_tok = P.now("vs")
            for j, h in enumerate(heads):
                av = av_banks[j]
                if bc_rel is not None:
                    P.wait("tensor", bc_rel)
                P.wait("tensor", rb_tok)
                t3 = P.mm(bc_bank[0:64, :],
                          consts["selb4"][:, j * 64:(j + 1) * 64], rb4[:, :],
                          start=True, stop=True, inc="ts")
                P.wait("scalar", t3)
                ic = j % 2
                if bcs_rel[ic] is not None:
                    P.wait("scalar", bcs_rel[ic])
                P.op("scalar", "activation", bcs[:, ic * TTW:(ic + 1) * TTW],
                     bc_bank[0:64, :], AF.Copy, inc="ss")
                bc_rel = P.now("ss")
                io = ocnt[0] % 4
                ocnt[0] += 1
                if o_rel[io] is not None:
                    P.wait("vector", o_rel[io])
                P.wait("vector", bc_rel)
                ov = P.op("vector", "tensor_tensor",
                          ostg[:, io * TTW:(io + 1) * TTW], av[0:64, :],
                          bcs[:, ic * TTW:(ic + 1) * TTW], op=ALU.mult,
                          inc="vs")
                bcs_rel[ic] = ov
                P.wait("gpsimd", ov)
                stt = P.dma("gpsimd",
                            attn_dram[h * 64:(h + 1) * 64,
                                      qg * TTW:(qg + 1) * TTW],
                            ostg[:, io * TTW:(io + 1) * TTW])
                o_rel[io] = stt
            P.waits("tensor", [P.now("vs")])
    st.close()
    for i in range(8):
        pp.rel[i] = P.now("vs")


def win_kcs_C(h, qg):
    return [(kc, None) for kc in range(32)]


def _stage_k(P, al, pp, consts, ktt, tt, kst, kst_rel, kext_dram):
    for s4 in range(4):
        kc = tt * 4 + s4
        j = kc % 2
        if kst_rel[j] is not None:
            P.wait("vector", kst_rel[j])
        for h in range(16):
            c, p = h // 2, h % 2
            P.op("vector", "tensor_copy",
                 kst[0:64, j * 2048 + h * 128: j * 2048 + (h + 1) * 128],
                 ktt[p * 64:p * 64 + 64,
                     c * TTW + s4 * 128: c * TTW + (s4 + 1) * 128],
                 inc="vs" if h == 15 else None, drain=(h == 15))
        cpt = P.now("vs")
        P.wait("gpsimd", cpt)
        stt = P.dma("gpsimd", kext_dram[0:65, kc * 2048:(kc + 1) * 2048],
                    kst[0:65, j * 2048:(j + 1) * 2048])
        kst_rel[j] = stt


def _stage_v(P, al, pp, consts, vtt, tt, vst, vst_rel, vtmp, vtmp_rel,
             vext_dram, ctab=None):
    for s4 in range(4):
        kc = tt * 4 + s4
        j = kc % 2
        if vtmp_rel[0] is not None:
            P.wait("vector", vtmp_rel[0])
        for h in range(16):
            c, p = h // 2, h % 2
            P.op("vector", "tensor_copy",
                 vtmp[0:64, h * 128:(h + 1) * 128],
                 vtt[p * 64:p * 64 + 64,
                     c * TTW + s4 * 128: c * TTW + (s4 + 1) * 128],
                 inc="vs" if h == 15 else None, drain=(h == 15))
        cp_tok = P.now("vs")
        P.wait("tensor", cp_tok)
        if vst_rel[j] is not None:
            P.wait("vector", vst_rel[j])
        trs = []
        for h in range(16):
            ib2, b2, g2 = pp.acquire()
            P.wait("tensor", g2)
            bb2 = b2.bitcast(BF16)
            tr = P.op("tensor", "transpose", bb2[0:128, 0:64],
                      vtmp[0:64, h * 128:(h + 1) * 128],
                      consts["ident"][0:64, 0:64], inc="ts")
            P.wait("vector", tr)
            if ctab is not None:
                P.op("vector", "tensor_scalar",
                     vst[:, j * 1040 + h * 65: j * 1040 + h * 65 + 64],
                     bb2[0:128, 0:64], ctab[:, h * 32 + kc: h * 32 + kc + 1],
                     None, op0=ALU.mult)
                ev3 = P.op("vector", "tensor_copy",
                           vst[:, j * 1040 + h * 65 + 64: j * 1040 + h * 65 + 65],
                           ctab[:, h * 32 + kc: h * 32 + kc + 1],
                           inc="vs", drain=(h == 15))
            else:
                ev3 = P.op("vector", "tensor_copy",
                           vst[:, j * 1040 + h * 65: j * 1040 + h * 65 + 64],
                           bb2[0:128, 0:64],
                           inc="vs", drain=(h == 15))
            pp.release(ib2, ev3)
        vtmp_rel[0] = P.now("vs")
        cpt = P.now("vs")
        P.wait("gpsimd", cpt)
        stt = P.dma("gpsimd", vext_dram[kc * 128:(kc + 1) * 128, :],
                    vst[:, j * 1040:(j + 1) * 1040])
        vst_rel[j] = stt


def _knorm_update(P, al, pp, consts, src_tile, k2m, oc, ksq, ksq_rel, k2p, k2s,
                  evk):
    i = oc % len(ksq_rel)
    P.wait("scalar", evk)
    if ksq_rel[i] is not None:
        P.wait("scalar", ksq_rel[i])
    sqt = P.op("scalar", "activation", ksq[:, i * TTW:(i + 1) * TTW],
               src_tile, AF.Square, inc="ss", drain=True)
    ib2, b2, g2 = pp.acquire()
    P.waits("tensor", [g2, sqt])
    t2 = P.mm(b2[0:2, :], consts["pair2"], ksq[:, i * TTW:(i + 1) * TTW],
              start=True, stop=True, inc="ts")
    ksq_rel[i] = t2
    P.wait("vector", t2)
    P.op("vector", "tensor_scalar", k2s[:, :], b2[0:2, :], 0.0, -1e30,
         op0=ALU.add, op1=ALU.max, accum_out=k2p[:, 1:2])
    ev2 = P.op("vector", "tensor_tensor", k2m[:, oc:oc + 1],
               k2m[:, oc:oc + 1], k2p[:, 1:2], op=ALU.max, inc="vs", drain=True)
    pp.release(ib2, ev2)
    return ev2


def _mq_pass(P, al, pp, consts, qf, nkx, mq_dram, st):
    qsq = al.sb("mq_qsq", [128, T], BF16, stack=st)
    qn8 = al.sb("mq_qn8", [2, T], F32, stack=st)
    mqt8 = al.sb("mq_mqt8", [2, T], BF16, stack=st)
    P.waits("scalar", [P.now("vs")])
    P.waits("tensor", [P.now("vs")])
    P.waits("vector", [P.now("vs"), P.now("ss")])
    sq_guard = None
    mq_rel = None
    for oc in range(NC):
        if sq_guard is not None:
            P.wait("scalar", sq_guard)
        sqt = P.op("scalar", "activation", qsq[:, :], chunk(qf, oc), AF.Square,
                   inc="ss", drain=True)
        P.wait("tensor", sqt)
        pbs = []
        for tt in range(NTT):
            ib2, b2, g2 = pp.acquire()
            P.wait("tensor", g2)
            t2 = P.mm(b2[0:2, :], consts["pair2"],
                      qsq[:, tt * TTW:(tt + 1) * TTW],
                      start=True, stop=True, inc="ts")
            pbs.append((ib2, b2))
        sq_guard = P.now("ts")
        P.wait("vector", sq_guard)
        if mq_rel is not None:
            P.wait("vector", mq_rel)
        for tt, (ib2, b2) in enumerate(pbs):
            cp = P.op("vector", "tensor_copy", qn8[:, tt * TTW:(tt + 1) * TTW],
                      b2[0:2, :], inc="vs", drain=(tt == NTT - 1))
            pp.release(ib2, cp)
        P.wait("scalar", P.now("vs"))
        st2 = P.op("scalar", "activation", qn8[:, :], qn8[:, :], AF.Sqrt,
                   inc="ss", drain=True)
        P.wait("vector", st2)
        mv = P.op("vector", "tensor_scalar", mqt8[:, :], qn8[:, :],
                  nkx[:, oc:oc + 1], None, op0=ALU.mult, inc="vs", drain=True)
        P.wait("gpsimd", mv)
        stt = P.dma("gpsimd", mq_dram[0:2, oc * T:(oc + 1) * T], mqt8[:, :])
        mq_rel = stt


def emit_D(P, al, consts, x_in, x_out, W):
    pp = PsumPool(P, [al.ps(f"ppD{i}", [128, 512]) for i in range(8)])
    nc = P.nc
    kext_dram = P.dram("D_kext", [65, 32 * 2048], BF16)
    vext_dram = P.dram("D_vext", [T, 1040], BF16)
    mq_dram = P.dram("D_mq", [2, NC * T], BF16)
    attn_dram = P.dram("D_attn", [D, T], BF16)
    xn_dram = P.dram("D_xn", [D, T], BF16)
    h_dram = P.dram("D_h", [D, T], BF16)

    k2m = al.sb("D_k2m", [2, 8], F32)
    nkx = al.sb("D_nkx", [2, 8], F32)
    g1b, tg1 = load_bias(P, al, "D_g1", W["ln1_g"], 1024)
    b1b, tb1 = load_bias(P, al, "D_b1", W["ln1_b"], 1024)
    g2b, tg2 = load_bias(P, al, "D_g2", W["ln2_g"], 1024)
    b2b, tb2 = load_bias(P, al, "D_b2", W["ln2_b"], 1024)

    st_qf = ExitStack()
    qf = al.sb("D_qf", [128, NC * T], BF16, stack=st_qf)
    st_x = ExitStack()
    X = al.sb("D_X", [128, NC * T], BF16, stack=st_x)

    st_ln = ExitStack()
    ln_toks = emit_ln(P, al, "D_ln", X, consts, pp, x_dram=x_in, substack=st_ln,
                      narrow=True)
    st_ln.close()
    sp_toks = []
    for c in range(NC):
        P.wait("gpsimd", ln_toks[c])
        sp_toks.append(P.dma("gpsimd", xn_dram[c * 128:(c + 1) * 128, :],
                             chunk(X, c)))
    st_ln1 = ExitStack()
    P.waits("vector", tg1 + tb1 + sp_toks)
    ln1_toks = emit_ln(P, al, "D_ln1", X, consts, pp, src_sbuf=lambda c: chunk(X, c),
                       src_toks=ln_toks + sp_toks,
                       gain=lambda c: g1b[:, c:c + 1], bias=lambda c: b1b[:, c:c + 1],
                       substack=st_ln1, narrow=True)
    st_ln1.close()

    st_cs = ExitStack()
    P.fence()
    cosf = al.sb("D_cosf", [128, T], BF16, stack=st_cs)
    sinf = al.sb("D_sinf", [128, T], BF16, stack=st_cs)
    t_cs = [P.dma("gpsimd", cosf[:, :], W["cosf"]),
            P.dma("gpsimd", sinf[:, :], W["sinf"])]
    st_k = ExitStack()
    P.fence()
    w_k, t_k = load_weight(P, al, "D_wk", W["wk"], 1024, 1024, stack=st_k)
    ktt = al.sb("D_ktt", [128, NC * TTW], BF16, stack=st_k)
    ksq = al.sb("D_ksq", [128, 4 * TTW], BF16, stack=st_k)
    kst = al.sb("D_kst", [65, 2 * 2048], BF16, stack=st_k)
    k2p = al.sb("D_k2p", [2, 4], F32, stack=st_k)
    k2s = al.sb("D_k2s", [2, TTW], F32, stack=st_k)
    rtile = al.sb("D_rtile", [128, 4 * TTW], BF16, stack=st_k)
    P.op("vector", "memset", k2m[:, :], 0.0)
    P.op("vector", "memset", kst[64:65, :], 1.0, inc="vs", drain=True)
    P.waits("tensor", t_k + ln1_toks + t_cs)
    P.waits("vector", t_cs)
    kst_rel = [None, None]
    ksq_rel = [None] * 4
    rt_rel = [None] * 4

    def rope_tile(dst, b_proj, oc, tt, ib):
        i = (tt * NC + oc) % 4
        ev0 = P.op("vector", "tensor_copy", dst, b_proj[:, :], inc="vs",
                   drain=True)
        ib3, b3, g3 = pp.acquire()
        P.waits("tensor", [g3, ev0])
        tr = P.mm(b3[:, :], consts["r2t"], dst, start=True, stop=True, inc="ts")
        P.wait("vector", tr)
        if rt_rel[i] is not None:
            P.wait("vector", rt_rel[i])
        P.op("vector", "tensor_tensor", rtile[:, i * TTW:(i + 1) * TTW],
             b3[:, :], sinf[:, tt * TTW:(tt + 1) * TTW], op=ALU.mult)
        P.op("vector", "tensor_tensor", dst, dst,
             cosf[:, tt * TTW:(tt + 1) * TTW], op=ALU.mult)
        rv = P.op("vector", "tensor_tensor", dst, dst,
                  rtile[:, i * TTW:(i + 1) * TTW], op=ALU.add, inc="vs",
                  drain=True)
        rt_rel[i] = rv
        pp.release(ib3, rv)
        pp.release(ib, ev0)
        return rv

    for tt in range(NTT):
        for oc in range(NC):
            ib, b, g = pp.acquire()
            P.wait("tensor", g)
            for kc in range(NC):
                P.mm(b[:, :], wslice(w_k, kc, oc, 1024), tile(X, kc, tt),
                     start=(kc == 0), stop=(kc == NC - 1),
                     inc="ts" if kc == NC - 1 else None)
            t = P.now("ts")
            P.wait("vector", t)
            rv = rope_tile(ktt[:, oc * TTW:(oc + 1) * TTW], b, oc, tt, ib)
            _knorm_update(P, al, pp, consts, ktt[:, oc * TTW:(oc + 1) * TTW],
                          k2m, oc, ksq, ksq_rel, k2p, k2s, rv)
        _stage_k(P, al, pp, consts, ktt, tt, kst, kst_rel, kext_dram)
    P.wait("scalar", P.now("vs"))
    P.op("scalar", "activation", nkx[:, :], k2m[:, :], AF.Sqrt, inc="ss",
         drain=True)
    P.wait("vector", P.now("ss"))
    P.op("vector", "tensor_scalar", nkx[:, :], nkx[:, :], -1.0, None,
         op0=ALU.mult, inc="vs", drain=True)
    st_k.close()

    st_q = ExitStack()
    P.fence()
    w_q, t_q = load_weight(P, al, "D_wq", W["wq"], 1024, 1024, stack=st_q)
    rtile2 = al.sb("D_rtile2", [128, 4 * TTW], BF16, stack=st_q)
    P.waits("tensor", t_q)
    rt2_rel = [None] * 4
    for tt in range(NTT):
        for oc in range(NC):
            ib, b, g = pp.acquire()
            P.wait("tensor", g)
            for kc in range(NC):
                P.mm(b[:, :], wslice(w_q, kc, oc, 1024), tile(X, kc, tt),
                     start=(kc == 0), stop=(kc == NC - 1),
                     inc="ts" if kc == NC - 1 else None)
            t = P.now("ts")
            P.wait("vector", t)
            dst = tile(qf, oc, tt)
            i = (tt * NC + oc) % 4
            ev0 = P.op("vector", "tensor_copy", dst, b[:, :], inc="vs",
                       drain=True)
            ib3, b3, g3 = pp.acquire()
            P.waits("tensor", [g3, ev0])
            tr = P.mm(b3[:, :], consts["r2t"], dst, start=True, stop=True,
                      inc="ts")
            P.wait("vector", tr)
            if rt2_rel[i] is not None:
                P.wait("vector", rt2_rel[i])
            P.op("vector", "tensor_tensor", rtile2[:, i * TTW:(i + 1) * TTW],
                 b3[:, :], sinf[:, tt * TTW:(tt + 1) * TTW],
                 op=ALU.mult)
            P.op("vector", "tensor_tensor", dst, dst,
                 cosf[:, tt * TTW:(tt + 1) * TTW], op=ALU.mult)
            rv = P.op("vector", "tensor_tensor", dst, dst,
                      rtile2[:, i * TTW:(i + 1) * TTW], op=ALU.add, inc="vs",
                      drain=True)
            rt2_rel[i] = rv
            pp.release(ib3, rv)
            pp.release(ib, ev0)
    _mq_pass(P, al, pp, consts, qf, nkx, mq_dram, st_q)
    st_q.close()
    st_cs.close()

    st_v = ExitStack()
    P.fence()
    w_v, t_v = load_weight(P, al, "D_wv", W["wv"], 1024, 1024, stack=st_v)
    vtt = al.sb("D_vtt", [128, NC * TTW], BF16, stack=st_v)
    vtmp = al.sb("D_vtmp", [64, 16 * 128], BF16, stack=st_v)
    vst = al.sb("D_vst", [128, 2 * 1040], BF16, stack=st_v)
    P.op("vector", "memset", vst[:, :], 1.0, inc="vs", drain=True)
    P.waits("tensor", t_v + [P.now("vs")])
    vst_rel = [None, None]
    vtmp_rel = [None, None]
    for tt in range(NTT):
        for oc in range(NC):
            ib, b, g = pp.acquire()
            P.wait("tensor", g)
            for kc in range(NC):
                P.mm(b[:, :], wslice(w_v, kc, oc, 1024), tile(X, kc, tt),
                     start=(kc == 0), stop=(kc == NC - 1),
                     inc="ts" if kc == NC - 1 else None)
            t = P.now("ts")
            P.wait("vector", t)
            evv = P.op("vector", "tensor_copy", vtt[:, oc * TTW:(oc + 1) * TTW],
                       b[:, :], inc="vs", drain=True)
            pp.release(ib, evv)
        _stage_v(P, al, pp, consts, vtt, tt, vst, vst_rel, vtmp, vtmp_rel,
                 vext_dram, ctab=None)
    st_v.close()
    st_x.close()

    emit_attn_win(P, al, pp, consts, qf, kext_dram, vext_dram, mq_dram,
                  attn_dram, win_kcs_D, masks_dram=W["masks"])
    st_qf.close()

    st_h = ExitStack()
    P.fence()
    hbuf = al.sb("D_hbuf", [128, NC * T], BF16, stack=st_h)
    st_wo = ExitStack()
    w_o, t_o = load_weight(P, al, "D_wo", W["wo"], 1024, 1024, stack=st_wo)
    att = al.sb("D_att", [128, 2 * NC * TTW], BF16, stack=st_wo)
    xnr = al.sb("D_xnr", [128, 4 * TTW], BF16, stack=st_wo)
    P.waits("tensor", t_o)
    att_rel = [None, None]
    xnr_rel = [None] * 4
    h_toks = []
    k4 = [0]
    for tt in range(NTT):
        ia = tt % 2
        if att_rel[ia] is not None:
            P.wait("gpsimd", att_rel[ia])
        src_ap = bass.AP(attn_dram.tensor, tt * TTW,
                         [[T, 128], [128 * T, NC], [1, TTW]])
        la = P.dma("gpsimd", att[:, ia * NC * TTW:(ia + 1) * NC * TTW], src_ap)
        P.wait("tensor", la)
        for oc in range(NC):
            ib, b, g = pp.acquire()
            P.wait("tensor", g)
            for kc in range(NC):
                P.mm(b[:, :], wslice(w_o, kc, oc, 1024),
                     att[:, ia * NC * TTW + kc * TTW: ia * NC * TTW + (kc + 1) * TTW],
                     start=(kc == 0), stop=(kc == NC - 1),
                     inc="ts" if kc == NC - 1 else None)
            t = P.now("ts")
            i = k4[0] % 4
            k4[0] += 1
            if xnr_rel[i] is not None:
                P.wait("sync", xnr_rel[i])
            lt = P.dma("sync", xnr[:, i * TTW:(i + 1) * TTW],
                       xn_dram[oc * 128:(oc + 1) * 128, tt * TTW:(tt + 1) * TTW])
            P.waits("vector", [t, lt])
            vt = P.op("vector", "tensor_tensor", tile(hbuf, oc, tt),
                      b[:, :], xnr[:, i * TTW:(i + 1) * TTW],
                      op=ALU.add, inc="vs", drain=True)
            pp.release(ib, vt)
            xnr_rel[i] = vt
            if oc == NC - 1:
                att_rel[ia] = P.now("ts")
            P.wait("gpsimd", vt)
            ht = P.dma("gpsimd",
                       h_dram[oc * 128:(oc + 1) * 128, tt * TTW:(tt + 1) * TTW],
                       tile(hbuf, oc, tt))
            h_toks.append(ht)
    st_wo.close()

    st_ln2 = ExitStack()
    P.fence()
    P.waits("vector", tg2 + tb2)
    ln2_toks = emit_ln(P, al, "D_ln2", hbuf, consts, pp,
                       src_sbuf=lambda c: chunk(hbuf, c),
                       src_toks=[P.now("vs")] + h_toks,
                       gain=lambda c: g2b[:, c:c + 1], bias=lambda c: b2b[:, c:c + 1],
                       substack=st_ln2, narrow=True)
    st_ln2.close()

    st_f = ExitStack()
    P.fence()
    w1a, t_w1 = load_weight(P, al, "D_w1f", W["w1"], 1024, 2816, stack=st_f)
    w3a, t_w3 = load_weight(P, al, "D_w3f", W["w3"], 1024, 2816, stack=st_f)
    w2a = al.sb("D_w2a", [128, 2 * 22 * 128], BF16, stack=st_f)
    z = al.sb("D_z", [128, 22 * TTW], BF16, stack=st_f)
    sil = al.sb("D_sil", [128, 2 * TTW], BF16, stack=st_f)
    xr = al.sb("D_xr", [128, 2 * TTW], F32, stack=st_f)
    hr = al.sb("D_hr", [128, 2 * TTW], BF16, stack=st_f)
    yt = al.sb("D_yt", [128, 2 * TTW], F32, stack=st_f)
    P.waits("tensor", ln2_toks + t_w1 + t_w3)
    out_toks = []
    w2_rel = [None, None]
    sil_rel = [None, None]
    z_rel = [None] * 22
    rel = {"xr": [None, None], "hr": [None, None], "yt": [None, None]}
    k5 = [0]
    for tt in range(NTT):
        for oc in range(22):
            ib, b, g = pp.acquire()
            P.wait("tensor", g)
            for kc in range(NC):
                P.mm(b[:, :], wslice(w1a, kc, oc, 2816),
                     tile(hbuf, kc, tt), start=(kc == 0), stop=(kc == NC - 1),
                     inc="ts" if kc == NC - 1 else None)
            t = P.now("ts")
            isl = oc % 2
            P.wait("scalar", t)
            if sil_rel[isl] is not None:
                P.wait("scalar", sil_rel[isl])
            sg = P.op("scalar", "activation", sil[:, isl * TTW:(isl + 1) * TTW],
                      b[:, :], AF.Silu, inc="ss", drain=True)
            pp.release(ib, sg)
            ib2, b2, g2 = pp.acquire()
            P.wait("tensor", g2)
            for kc in range(NC):
                P.mm(b2[:, :], wslice(w3a, kc, oc, 2816),
                     tile(hbuf, kc, tt), start=(kc == 0), stop=(kc == NC - 1),
                     inc="ts" if kc == NC - 1 else None)
            t2 = P.now("ts")
            P.waits("vector", [t2, sg])
            if z_rel[oc] is not None:
                P.wait("vector", z_rel[oc])
            zt = P.op("vector", "tensor_tensor", z[:, oc * TTW:(oc + 1) * TTW],
                      b2[:, :], sil[:, isl * TTW:(isl + 1) * TTW], op=ALU.mult,
                      inc="vs", drain=True)
            sil_rel[isl] = zt
            pp.release(ib2, zt)
        z_tok = P.now("vs")
        for oc in range(NC):
            iw2 = oc % 2
            if w2_rel[iw2] is not None:
                P.wait("gpsimd", w2_rel[iw2])
            tw2 = P.dma("gpsimd", w2a[:, iw2 * 22 * 128:(iw2 + 1) * 22 * 128],
                        bass.AP(W["w2"].tensor, oc * 128,
                                [[1024, 128], [128 * 1024, 22], [1, 128]]))
            ib, b, g = pp.acquire()
            P.waits("tensor", [g, z_tok, tw2])
            for kc in range(22):
                P.mm(b[:, :],
                     w2a[:, iw2 * 22 * 128 + kc * 128: iw2 * 22 * 128 + (kc + 1) * 128],
                     z[:, kc * TTW:(kc + 1) * TTW],
                     start=(kc == 0), stop=(kc == 21), inc="ts")
            t = P.now("ts")
            w2_rel[iw2] = t
            i = k5[0] % 2
            i2 = k5[0] % 2
            k5[0] += 1
            if rel["xr"][i2] is not None:
                P.wait("sync", rel["xr"][i2])
            lt = P.dma("sync", xr[:, i2 * TTW:(i2 + 1) * TTW],
                       x_in[oc * 128:(oc + 1) * 128, tt * TTW:(tt + 1) * TTW])
            if rel["hr"][i2] is not None:
                P.wait("sync", rel["hr"][i2])
            lh = P.dma("sync", hr[:, i2 * TTW:(i2 + 1) * TTW],
                       h_dram[oc * 128:(oc + 1) * 128, tt * TTW:(tt + 1) * TTW])
            P.waits("vector", [t, lt, lh])
            if rel["yt"][i] is not None:
                P.wait("vector", rel["yt"][i])
            P.op("vector", "tensor_tensor", yt[:, i * TTW:(i + 1) * TTW],
                 b[:, :], hr[:, i2 * TTW:(i2 + 1) * TTW], op=ALU.add)
            vt = P.op("vector", "tensor_tensor", yt[:, i * TTW:(i + 1) * TTW],
                      yt[:, i * TTW:(i + 1) * TTW], xr[:, i2 * TTW:(i2 + 1) * TTW],
                      op=ALU.add, inc="vs", drain=True)
            pp.release(ib, vt)
            rel["xr"][i2] = vt
            rel["hr"][i2] = vt
            if oc == NC - 1:
                for kk in range(22):
                    z_rel[kk] = P.now("ts")
            P.wait("sync", vt)
            ot = P.dma("sync",
                       x_out[oc * 128:(oc + 1) * 128, tt * TTW:(tt + 1) * TTW],
                       yt[:, i * TTW:(i + 1) * TTW])
            rel["yt"][i] = ot
            out_toks.append(ot)
    st_f.close()
    st_h.close()
    return out_toks


def prep_D_weights(inp):
    W = {}
    for k in ("wq", "wk", "wv", "wo", "w1", "w2", "w3"):
        W[k] = _np(inp["L1_attn_a_" + k])
    freqs = 1.0 / (10000.0 ** (np.arange(0, 64, 2, dtype=np.float64) / 64))
    t = np.arange(T, dtype=np.float64)
    ang = np.outer(freqs, t)
    j = (np.arange(128) % 64) // 2
    W["cosf"] = np.cos(ang)[j].astype(np.float32)
    W["sinf"] = np.sin(ang)[j].astype(np.float32)
    W["masks"] = prep_masks(win_kcs_D)
    W["ln1_g"] = _bvec(inp["L1_attn_a_ln1_g"])
    W["ln1_b"] = _bvec(inp["L1_attn_a_ln1_b"])
    W["ln2_g"] = _bvec(inp["L1_attn_a_ln2_g"])
    W["ln2_b"] = _bvec(inp["L1_attn_a_ln2_b"])
    return W


def emit_F(P, al, consts, x_in, x_out, W):
    pp = PsumPool(P, [al.ps(f"ppF{i}", [128, 512]) for i in range(8)])
    nc = P.nc
    kext_dram = P.dram("F_kext", [65, 32 * 2048], BF16)
    vext_dram = P.dram("F_vext", [T, 1040], BF16)
    mq_dram = P.dram("F_mq", [2, NC * T], BF16)
    attn_dram = P.dram("F_attn", [D, T], BF16)

    k2m = al.sb("F_k2m", [2, 8], F32)
    nkx = al.sb("F_nkx", [2, 8], F32)
    gT = al.sb("F_gT", [16, T], BF16)

    st_qf = ExitStack()
    qf = al.sb("F_qf", [128, NC * T], BF16, stack=st_qf)
    st_x = ExitStack()
    xn = al.sb("F_xn", [128, NC * T], BF16, stack=st_x)
    st_ln = ExitStack()
    ln_toks = emit_ln(P, al, "F_ln", xn, consts, pp, x_dram=x_in, substack=st_ln,
                      narrow=True)
    wg, t_wg = load_weight(P, al, "F_wg", W["Wgate"], 1024, 16)
    bgr = al.sb("F_bgr", [1, 16], F32)
    t_bg = P.dma("gpsimd", bgr[:, :], W["bgate"])
    b_qk, t_bqk = load_bias(P, al, "F_bqk", W["bqk"], 2048)
    st_ln.close()

    st_g = ExitStack()
    P.fence()
    bgbc = al.sb("F_bgbc", [128, 16], F32, stack=st_g)
    glg = al.sb("F_glg", [128, 16], F32, stack=st_g)
    gsc = al.sb("F_gsc", [128, 8], F32, stack=st_g)
    get = al.sb("F_get", [128, 16], F32, stack=st_g)
    gbf = al.sb("F_gbf", [128, 16], BF16, stack=st_g)
    ib, b, g = pp.acquire()
    P.waits("tensor", [g, t_bg])
    t = P.mm(b[0:128, 0:16], consts["onesf1"], bgr[:, :], start=True, stop=True,
             inc="ts")
    P.wait("vector", t)
    ev = P.op("vector", "tensor_copy", bgbc[:, :], b[0:128, 0:16], inc="vs",
              drain=True)
    pp.release(ib, ev)
    P.waits("tensor", t_wg + ln_toks)
    for tc in range(32):
        ib, b, g = pp.acquire()
        P.wait("tensor", g)
        tt = tc // 4
        co = (tc % 4) * 128
        for kc in range(NC):
            P.mm(b[:, 0:16], tile(xn, kc, tt)[:, co:co + 128],
                 wg[:, kc * 16:(kc + 1) * 16],
                 start=(kc == 0), stop=(kc == NC - 1),
                 inc="ts" if kc == NC - 1 else None)
        t = P.now("ts")
        P.wait("vector", t)
        P.op("vector", "tensor_tensor", glg[:, :], b[:, 0:16], bgbc[:, :],
             op=ALU.add, inc="vs", drain=True)
        pp.release(ib, P.now("vs"))
        P.op("vector", "max", gsc[:, :], glg[:, :])
        P.op("vector", "tensor_scalar", gsc[:, 1:2], gsc[:, 0:1], -1.0, None,
             op0=ALU.mult, inc="vs", drain=True)
        P.wait("scalar", P.now("vs"))
        P.op("scalar", "activation", get[:, :], glg[:, :], AF.Exp,
             bias=gsc[:, 1:2], accum_out=gsc[:, 2:3], inc="ss", drain=True)
        P.wait("vector", P.now("ss"))
        P.op("vector", "reciprocal", gsc[:, 3:4], gsc[:, 2:3])
        P.op("vector", "tensor_scalar", gbf[:, :], get[:, :], gsc[:, 3:4], None,
             op0=ALU.mult, inc="vs", drain=True)
        ib2, b2, g2 = pp.acquire()
        P.waits("tensor", [g2, P.now("vs")])
        bb2 = b2.bitcast(BF16)
        tr = P.op("tensor", "transpose", bb2[0:16, 0:128], gbf[:, :],
                  consts["ident"], inc="ts")
        P.wait("vector", tr)
        ev2 = P.op("vector", "tensor_copy", gT[:, tc * 128:(tc + 1) * 128],
                   bb2[0:16, 0:128], inc="vs", drain=True)
        pp.release(ib2, ev2)
    st_g.close()

    st_p = ExitStack()
    P.fence()
    wqk, t_wqk = load_weight(P, al, "F_wqk", W["Wqk"], 1024, 2048, stack=st_p)
    ktt = al.sb("F_ktt", [128, NC * TTW], BF16, stack=st_p)
    ksq = al.sb("F_ksq", [128, 2 * TTW], BF16, stack=st_p)
    kst = al.sb("F_kst", [65, 2 * 2048], BF16, stack=st_p)
    k2p = al.sb("F_k2p", [2, 4], F32, stack=st_p)
    k2s = al.sb("F_k2s", [2, TTW], F32, stack=st_p)
    vtmp = al.sb("F_vtmp", [64, 16 * 128], BF16, stack=st_p)
    vst = al.sb("F_vst", [128, 2 * 1040], BF16, stack=st_p)
    P.op("vector", "memset", k2m[:, :], 0.0)
    P.op("vector", "memset", kst[64:65, :], 1.0)
    P.op("vector", "memset", vst[:, :], 1.0, inc="vs", drain=True)
    P.waits("tensor", t_wqk + [P.now("vs")])
    P.waits("vector", t_bqk)
    kst_rel = [None, None]
    ksq_rel = [None, None]
    vst_rel = [None, None]
    vtmp_rel = [None, None]
    for tt in range(NTT):
        for oc in range(16):
            ib, b, g = pp.acquire()
            P.wait("tensor", g)
            for kc in range(NC):
                P.mm(b[:, :], wslice(wqk, kc, oc, 2048), tile(xn, kc, tt),
                     start=(kc == 0), stop=(kc == NC - 1),
                     inc="ts" if kc == NC - 1 else None)
            t = P.now("ts")
            P.wait("vector", t)
            if oc < 8:
                ev = P.op("vector", "tensor_scalar", tile(qf, oc, tt),
                          b[:, :], b_qk[:, oc:oc + 1], None, op0=ALU.add,
                          inc="vs", drain=True)
                pp.release(ib, ev)
            else:
                ko = oc - 8
                ev = P.op("vector", "tensor_scalar",
                          ktt[:, ko * TTW:(ko + 1) * TTW],
                          b[:, :], b_qk[:, oc:oc + 1], None, op0=ALU.add,
                          inc="vs", drain=True)
                pp.release(ib, ev)
                _knorm_update(P, al, pp, consts, ktt[:, ko * TTW:(ko + 1) * TTW],
                              k2m, ko, ksq, ksq_rel, k2p, k2s, ev)
        _stage_k(P, al, pp, consts, ktt, tt, kst, kst_rel, kext_dram)
        _stage_v(P, al, pp, consts, ktt, tt, vst, vst_rel, vtmp, vtmp_rel,
                 vext_dram, ctab=None)
    P.wait("scalar", P.now("vs"))
    P.op("scalar", "activation", nkx[:, :], k2m[:, :], AF.Sqrt, inc="ss",
         drain=True)
    P.wait("vector", P.now("ss"))
    P.op("vector", "tensor_scalar", nkx[:, :], nkx[:, :], -1.0, None,
         op0=ALU.mult, inc="vs", drain=True)
    st_p.close()
    st_x.close()
    st_mq = ExitStack()
    P.fence()
    _mq_pass(P, al, pp, consts, qf, nkx, mq_dram, st_mq)
    st_mq.close()

    g7_rel = [None]

    def gate_fn(P2, pp2, hg, qg, rtmp4, bank7, bc_rel):
        if bc_rel is not None:
            P2.wait("tensor", bc_rel)
        tg = P2.mm(bank7[0:4, :], consts["ident"][0:16, hg * 4:hg * 4 + 4],
                   gT[0:16, qg * TTW:(qg + 1) * TTW], start=True, stop=True,
                   inc="ts")
        P2.wait("vector", tg)
        P2.op("vector", "tensor_tensor", rtmp4[:, :], rtmp4[:, :],
              bank7[0:4, :], op=ALU.mult, inc="vs")

    emit_attn_win(P, al, pp, consts, qf, kext_dram, vext_dram, mq_dram,
                  attn_dram, win_kcs_F, gate_fn=gate_fn, masks_dram=W["masks"])
    st_qf.close()

    st_wo = ExitStack()
    P.fence()
    w_o, t_o = load_weight(P, al, "F_wo", W["Wproj"], 1024, 1024, stack=st_wo)
    b_o, tbo = load_bias(P, al, "F_bo", W["bproj"], 1024, stack=st_wo)
    att = al.sb("F_att", [128, 2 * NC * TTW], BF16, stack=st_wo)
    xr = al.sb("F_xr", [128, 4 * TTW], F32, stack=st_wo)
    yt = al.sb("F_yt", [128, 4 * TTW], F32, stack=st_wo)
    P.waits("tensor", t_o)
    P.waits("vector", tbo)
    att_rel = [None, None]
    rel = {"xr": [None] * 4, "yt": [None] * 4}
    out_toks = []
    P.waits("gpsimd", P.dma_all_done("gpsimd"))
    k4 = [0]
    for tt in range(NTT):
        ia = tt % 2
        if att_rel[ia] is not None:
            P.wait("gpsimd", att_rel[ia])
        src_ap = bass.AP(attn_dram.tensor, tt * TTW,
                         [[T, 128], [128 * T, NC], [1, TTW]])
        la = P.dma("gpsimd", att[:, ia * NC * TTW:(ia + 1) * NC * TTW], src_ap)
        P.wait("tensor", la)
        for oc in range(NC):
            ib, b, g = pp.acquire()
            P.wait("tensor", g)
            for kc in range(NC):
                P.mm(b[:, :], wslice(w_o, kc, oc, 1024),
                     att[:, ia * NC * TTW + kc * TTW: ia * NC * TTW + (kc + 1) * TTW],
                     start=(kc == 0), stop=(kc == NC - 1),
                     inc="ts" if kc == NC - 1 else None)
            t = P.now("ts")
            i = k4[0] % 4
            k4[0] += 1
            if rel["xr"][i] is not None:
                P.wait("sync", rel["xr"][i])
            lt = P.dma("sync", xr[:, i * TTW:(i + 1) * TTW],
                       x_in[oc * 128:(oc + 1) * 128, tt * TTW:(tt + 1) * TTW])
            P.waits("vector", [t, lt])
            if rel["yt"][i] is not None:
                P.wait("vector", rel["yt"][i])
            P.op("vector", "tensor_scalar", yt[:, i * TTW:(i + 1) * TTW],
                 b[:, :], b_o[:, oc:oc + 1], None, op0=ALU.add)
            vt = P.op("vector", "tensor_tensor", yt[:, i * TTW:(i + 1) * TTW],
                      yt[:, i * TTW:(i + 1) * TTW], xr[:, i * TTW:(i + 1) * TTW],
                      op=ALU.add, inc="vs", drain=True)
            pp.release(ib, vt)
            rel["xr"][i] = vt
            if oc == NC - 1:
                att_rel[ia] = P.now("ts")
            P.wait("sync", vt)
            ot = P.dma("sync",
                       x_out[oc * 128:(oc + 1) * 128, tt * TTW:(tt + 1) * TTW],
                       yt[:, i * TTW:(i + 1) * TTW])
            rel["yt"][i] = ot
            out_toks.append(ot)
    st_wo.close()
    return out_toks


def prep_F_weights(inp):
    W = {}
    W["Wqk"] = _np(inp["L1_attn_b_Wqk"])
    W["bqk"] = _bvec(inp["L1_attn_b_bqk"])
    W["Wgate"] = _np(inp["L1_attn_b_Wgate"])
    W["bgate"] = _np(inp["L1_attn_b_bgate"]).reshape(1, 16)
    W["Wproj"] = _np(inp["L1_attn_b_Wproj"])
    W["bproj"] = _bvec(inp["L1_attn_b_bproj"])
    W["masks"] = prep_masks(win_kcs_F)
    return W


FUSED = True


def build_fused(seq, wshapes_by_pfx):
    nc = bass.Bass("TRN2", target_bir_lowering=False)
    x_in = nc.dram_tensor("x_in", [D, T], F32, kind="ExternalInput").ap()
    x_out = nc.dram_tensor("x_out", [D, T], F32, kind="ExternalOutput").ap()
    streams = [x_in]
    for i in range(len(seq) - 1):
        streams.append(nc.dram_tensor(f"xs{i}", [D, T], F32).ap())
    streams.append(x_out)
    Wd_by_pfx = {}
    for pfx, shapes in wshapes_by_pfx.items():
        Wd_by_pfx[pfx] = {k: nc.dram_tensor(f"w_{pfx}{k}", list(s), F32,
                                            kind="ExternalInput").ap()
                          for k, s in shapes.items()}
    need = sorted({n for e, _ in seq for n in _const_need(e.__name__)})
    stack = ExitStack()
    with stack:
        al = Alloc(nc, stack)
        P = Plan(nc, stack)
        P.alloc = al
        consts, ctoks = declare_consts(nc, al, P, make_consts(), need=need)
        P.waits("tensor", ctoks)
        P.waits("vector", ctoks)
        P.waits("scalar", ctoks)
        out_toks = None
        for i, (emitter, pfx) in enumerate(seq):
            sub = ExitStack()
            al.default_stack = sub
            al.prefix = f"c{i}_"
            P.fence()
            out_toks = emitter(P, al, consts, streams[i], streams[i + 1],
                               Wd_by_pfx[pfx])
            P.fence()
            sub.close()
            al.default_stack = stack
            al.prefix = ""
        P.waits("sync", out_toks)
        P.waits("sync", P.dma_all_done("sync") + P.dma_all_done("gpsimd"))
        P.build()
    return nc


def run_fused(inputs, x_fm):
    WA = prep_A_weights(inputs)
    WB = prep_B_weights(inputs)
    WC = prep_C_weights(inputs)
    WD = prep_D_weights(inputs)
    WEs = [prep_E_weights(inputs, c) for c in range(8)]
    WF = prep_F_weights(inputs)
    seq = [(emit_A, "A"), (emit_B, "B"), (emit_C, "C"), (emit_B, "B"),
           (emit_D, "Dd"), (emit_E, "E"), (emit_F, "F"), (emit_E, "E")]
    wshapes = {"A": {k: v.shape for k, v in WA.items()},
               "B": {k: v.shape for k, v in WB.items()},
               "C": {k: v.shape for k, v in WC.items()},
               "Dd": {k: v.shape for k, v in WD.items()},
               "E": {k: v.shape for k, v in WEs[0].items()},
               "F": {k: v.shape for k, v in WF.items()}}
    nc = build_fused(seq, wshapes)
    consts = make_consts()
    need = sorted({n for e, _ in seq for n in _const_need(e.__name__)})
    in_maps = []
    for c in range(8):
        m = {"x_in": x_fm[c]}
        for pfx, W in [("A", WA), ("B", WB), ("C", WC), ("Dd", WD),
                       ("E", WEs[c]), ("F", WF)]:
            for k, v in W.items():
                m[f"w_{pfx}{k}"] = v
        for k, v in consts.items():
            if k in need:
                m["c_" + k] = v
        in_maps.append(m)
    res = run_bass_kernel_spmd(nc, in_maps, core_ids=list(range(8)))
    return [res.results[c]["x_out"] for c in range(8)]


# version 19
# speedup vs baseline: 1.1296x; 1.0377x over previous
import numpy as np
import math
from contextlib import ExitStack
import concourse.bass as bass
import concourse.mybir as mybir
from concourse.bass_utils import run_bass_kernel_spmd

F32 = mybir.dt.float32
BF16 = mybir.dt.bfloat16
I32 = mybir.dt.int32
U32 = mybir.dt.uint32
AF = mybir.ActivationFunctionType
ALU = mybir.AluOpType
AX = mybir.AxisListType

D = 1024
T = 4096
NC = 8
NTT = 8
TTW = 512
EPS_LN = 1e-5

ENGS = ("sync", "gpsimd", "tensor", "vector", "scalar")
NO_MILESTONE_DRAIN = True
DEBUG_BARRIERS = 0
DEBUG_TAPS = None


class Sem:
    def __init__(self, handle, name):
        self.h = handle
        self.name = name
        self.n = 0


DMA_POOL = 12


class Plan:
    def __init__(self, nc, stack):
        self.nc = nc
        self.stack = stack
        self.ops = {e: [] for e in ENGS}
        self.sems = {}
        for s in ("ts", "vs", "ss", "gs", "cs"):
            self.sem(s)
        self.dirty = {}
        self.rdirty = {}
        self.dma_pools = {}
        self.dma_uses = {}
        for q in ("sync", "gpsimd", "scalar"):
            self.dma_pools[q] = [self.sem(f"{q[0]}d{i}") for i in range(DMA_POOL)]
            self.dma_uses[q] = 0

    def sem(self, name):
        if name not in self.sems:
            h = self.stack.enter_context(self.nc.semaphore(name))
            self.sems[name] = Sem(h, name)
        return self.sems[name]

    def emit(self, eng, fn):
        self.ops[eng].append(fn)

    def dma(self, q, out, in_, **kw):
        k = self.dma_uses[q]
        self.dma_uses[q] += 1
        s = self.dma_pools[q][k % DMA_POOL]
        prev = s.n
        s.n += 16
        v = s.n

        def fn(e, out=out, in_=in_, s=s, kw=kw, prev=prev):
            if prev > 0:
                e.wait_ge(s.h, prev)
            e.dma_start(out=out, in_=in_, **kw).then_inc(s.h, 16)

        self.emit(q, fn)
        return (s.name, v)

    def dma_all_done(self, q):
        return [(s.name, s.n) for s in self.dma_pools[q]]

    def dram(self, name, shape, dtype):
        return self.nc.dram_tensor(self.alloc.prefix + name, list(shape), dtype).ap()

    def fence(self, eng=None):
        toks = [self.now("ts"), self.now("vs"), self.now("ss")]
        toks += self.dma_all_done("sync") + self.dma_all_done("gpsimd")
        for e in ENGS:
            self.waits(e, toks)

    def wait(self, eng, tok):
        if tok is None:
            return
        sem, v = tok
        s = self.sems[sem]

        def fn(e, s=s, v=v):
            e.wait_ge(s.h, v)

        self.emit(eng, fn)

    def waits(self, eng, toks):
        best = {}
        for t in toks:
            if t is None:
                continue
            sem, v = t
            best[sem] = max(best.get(sem, 0), v)
        for sem, v in best.items():
            self.wait(eng, (sem, v))

    def now(self, semname):
        return (semname, self.sems[semname].n)

    @staticmethod
    def _apname(x):
        t = getattr(x, "tensor", None)
        return getattr(t, "name", None)

    @staticmethod
    def _apbox(x):
        t = getattr(x, "tensor", None)
        nm = getattr(t, "name", None)
        if nm is None:
            return None
        shp = list(t.shape)
        rowlen = 1
        for s in shp[1:]:
            rowlen *= s
        off = x.offset
        ap = [list(p) for p in x.ap]
        p0 = off // rowlen
        c0 = off % rowlen
        pc = ap[0][1] if ap else 1
        span = 1
        for s, c in ap[1:]:
            span += s * (c - 1)
        return (nm, p0, p0 + pc, c0, c0 + span)

    @staticmethod
    def _boxhit(a, b):
        return (a[0] == b[0] and a[1] < b[2] and b[1] < a[2]
                and a[3] < b[4] and b[3] < a[4])

    def op(self, eng, method, *args, inc=None, drain=False, pre_drain=None, **kw):
        semname = inc
        tok = None
        if semname is not None:
            s = self.sems[semname]
            s.n += 1
            tok = (semname, s.n)
        if pre_drain is None and eng in ("vector", "scalar", "gpsimd"):
            dirty = self.dirty.setdefault(eng, [])
            rset = self.rdirty.setdefault(eng, [])
            outs = []
            if args and hasattr(args[0], "tensor"):
                bx = self._apbox(args[0])
                if bx is not None:
                    outs.append(bx)
            ao = kw.get("accum_out")
            if ao is not None and hasattr(ao, "tensor"):
                bx = self._apbox(ao)
                if bx is not None:
                    outs.append(bx)
            ins = []
            for a in args[1:]:
                if hasattr(a, "tensor"):
                    bx = self._apbox(a)
                    if bx is not None:
                        ins.append(bx)
            for k in ("bias", "scale", "scalar1", "scalar2", "in1", "data0",
                      "data1"):
                v = kw.get(k)
                if v is not None and hasattr(v, "tensor"):
                    bx = self._apbox(v)
                    if bx is not None:
                        ins.append(bx)
            pre_drain = (any(self._boxhit(b, d) for b in ins + outs for d in dirty)
                         or any(self._boxhit(o, r) for o in outs for r in rset))
            if pre_drain or (drain and not NO_MILESTONE_DRAIN):
                dirty.clear()
                rset.clear()
            if NO_MILESTONE_DRAIN or not drain:
                dirty.extend(outs)
                rset.extend(ins)
        elif pre_drain is None:
            pre_drain = False

        def fn(e, method=method, args=args, kw=kw, semname=semname, drain=drain,
               pre_drain=pre_drain):
            if pre_drain:
                e.drain()
            r = getattr(e, method)(*args, **kw)
            if semname is not None:
                s = self.sems[semname]
                if drain and not NO_MILESTONE_DRAIN:
                    e.drain().then_inc(s.h, 1)
                else:
                    r.then_inc(s.h, 1)

        self.emit(eng, fn)
        return tok

    def mm(self, out, lhsT, rhs, start, stop, inc=None):
        return self.op("tensor", "matmul", out, lhsT, rhs,
                       start=start, stop=stop, inc=inc)

    def barrier(self):
        toks = [self.now(s) for s in ("ts", "vs", "ss")]
        toks += self.dma_all_done("sync") + self.dma_all_done("gpsimd")
        for e in ENGS:
            self.waits(e, toks)
        return toks

    def build(self):
        block = self.stack.enter_context(self.nc.Block())
        plan = self

        def mk(eng):
            def body(e):
                for f in plan.ops[eng]:
                    f(e)
            return body

        block.sync(mk("sync"))
        block.gpsimd(mk("gpsimd"))
        block.tensor(mk("tensor"))
        block.vector(mk("vector"))
        block.scalar(mk("scalar"))


class PsumPool:
    def __init__(self, plan, tiles):
        self.plan = plan
        self.tiles = tiles
        self.rel = [None] * len(tiles)
        self.i = 0

    def acquire(self):
        i = self.i % len(self.tiles)
        self.i += 1
        return i, self.tiles[i], self.rel[i]

    def release(self, i, tok):
        self.rel[i] = tok


class Alloc:
    def __init__(self, nc, stack):
        self.nc = nc
        self.stack = stack
        self.default_stack = stack
        self.prefix = ""
        self.bytes_pp = 0

    def sb(self, name, shape, dtype, stack=None, side=None):
        name = self.prefix + name
        per = int(np.prod(shape[1:])) * mybir.dt.size(dtype)
        self.bytes_pp += ((per + 31) // 32) * 32
        st = stack or self.default_stack
        kw = {}
        if stack is not None:
            kw["side"] = "right"
        if side is not None:
            kw["side"] = side
        return st.enter_context(self.nc.sbuf_tensor(name, list(shape), dtype, **kw))

    def ps(self, name, shape, dtype=F32):
        return self.default_stack.enter_context(
            self.nc.psum_tensor(self.prefix + name, list(shape), dtype))


def chunk(t, c, w=T):
    return t[:, c * w:(c + 1) * w]


def tile(t, c, tt, w=T):
    return t[:, c * w + tt * TTW: c * w + tt * TTW + TTW]



def emit_ln(P, al, name, xn, consts, pp, x_dram=None, src_sbuf=None,
            src_toks=None, gain=None, bias=None, substack=None, narrow=False):
    ones128 = consts["ones128"]
    ones1 = consts["ones1"]
    st = substack or P.stack
    NB = 1 if narrow else 2
    arow = al.sb(name + "_arow", [1, 2 * T], BF16, stack=st)
    abc = al.sb(name + "_abc", [128, 2 * T], BF16, stack=st)
    scr = al.sb(name + "_scr", [1, 3 * TTW], F32, stack=st)
    if src_sbuf is None:
        xr = al.sb(name + "_xr", [128, NB * T], BF16, stack=st)
    sq = al.sb(name + "_sq", [128, NB * T], BF16, stack=st)

    banks = [pp.acquire() for _ in range(NTT)]
    P.waits("tensor", [g for (_, _, g) in banks])
    if narrow:
        srow = arow[:, T:2 * T]
    else:
        srow = al.sb(name + "_srow", [1, T], F32, stack=st)

    ld_rel = [None, None]
    mm_tok = None
    for c in range(NC):
        i = c % NB
        if src_sbuf is None:
            if ld_rel[i] is not None:
                P.wait("gpsimd", ld_rel[i])
            lt = P.dma("gpsimd", xr[:, i * T:(i + 1) * T],
                       x_dram[c * 128:(c + 1) * 128, :])
            src = xr[:, i * T:(i + 1) * T]
            P.wait("tensor", lt)
        else:
            src = src_sbuf(c)
            if src_toks is not None and c == 0:
                P.waits("tensor", src_toks)
        for tt in range(NTT):
            _, b, _ = banks[tt]
            P.mm(b[0:1, :], ones128, src[:, tt * TTW:(tt + 1) * TTW],
                 start=(c == 0), stop=(c == NC - 1), inc="ts")
        mm_tok = P.now("ts")
        ld_rel[i] = mm_tok
    P.wait("vector", mm_tok)
    for tt in range(NTT):
        _, b, _ = banks[tt]
        P.op("vector", "tensor_copy", srow[:, tt * TTW:(tt + 1) * TTW], b[0:1, :],
             inc="vs" if tt == NTT - 1 else None, drain=(tt == NTT - 1))
    s1ev = P.now("vs")

    sq_rel = [None, None]
    P.wait("tensor", s1ev)
    for c in range(NC):
        i = c % NB
        if src_sbuf is None:
            P.wait("gpsimd", mm_tok if c < NB else ld_rel[i])
            lt = P.dma("gpsimd", xr[:, i * T:(i + 1) * T],
                       x_dram[c * 128:(c + 1) * 128, :])
            src = xr[:, i * T:(i + 1) * T]
            P.wait("scalar", lt)
        else:
            src = src_sbuf(c)
            if src_toks is not None and c == 0:
                P.waits("scalar", src_toks)
        if sq_rel[i] is not None:
            P.wait("scalar", sq_rel[i])
        sqt = P.op("scalar", "activation", sq[:, i * T:(i + 1) * T], src, AF.Square,
                   inc="ss", drain=True)
        P.wait("tensor", sqt)
        for tt in range(NTT):
            _, b, _ = banks[tt]
            P.mm(b[0:1, :], ones128,
                 sq[:, i * T + tt * TTW: i * T + (tt + 1) * TTW],
                 start=(c == 0), stop=(c == NC - 1), inc="ts")
        mm_tok = P.now("ts")
        ld_rel[i] = mm_tok
        sq_rel[i] = mm_tok

    P.wait("vector", mm_tok)
    stok = None
    for tt in range(NTT):
        ib, b, _ = banks[tt]
        if stok is not None:
            P.wait("vector", stok)
        P.op("vector", "tensor_copy", scr[:, 0:TTW], srow[:, tt * TTW:(tt + 1) * TTW])
        P.op("vector", "tensor_copy", scr[:, TTW:2 * TTW], b[0:1, :],
             inc="vs", drain=True)
        pp.release(ib, P.now("vs"))
        P.op("vector", "tensor_scalar", scr[:, 0:TTW], scr[:, 0:TTW], 1.0 / D, None,
             op0=ALU.mult)
        P.op("vector", "tensor_tensor", scr[:, 2 * TTW:3 * TTW], scr[:, 0:TTW],
             scr[:, 0:TTW], op=ALU.mult)
        P.op("vector", "scalar_tensor_tensor", scr[:, TTW:2 * TTW],
             scr[:, TTW:2 * TTW], 1.0 / D, scr[:, 2 * TTW:3 * TTW],
             op0=ALU.mult, op1=ALU.subtract, inc="vs", drain=True)
        vtok = P.now("vs")
        P.wait("scalar", vtok)
        P.op("scalar", "activation", scr[:, 2 * TTW:3 * TTW], scr[:, TTW:2 * TTW],
             AF.Ln, bias=consts["eps"])
        P.op("scalar", "activation", arow[:, tt * TTW:(tt + 1) * TTW],
             scr[:, 2 * TTW:3 * TTW], AF.Exp, scale=-0.5, inc="ss", drain=True)
        sstok = P.now("ss")
        P.wait("vector", sstok)
        P.op("vector", "scalar_tensor_tensor",
             arow[:, T + tt * TTW: T + (tt + 1) * TTW], scr[:, 0:TTW], -1.0,
             arow[:, tt * TTW:(tt + 1) * TTW], op0=ALU.mult, op1=ALU.mult,
             inc="vs", drain=True)
        stok = P.now("vs")

    P.wait("tensor", stok)
    for half in range(2):
        for tt in range(NTT):
            ib, b, g = pp.acquire()
            P.wait("tensor", g)
            t = P.mm(b[0:128, :], ones1,
                     arow[:, half * T + tt * TTW: half * T + (tt + 1) * TTW],
                     start=True, stop=True, inc="ts")
            P.wait("vector", t)
            ev = P.op("vector", "tensor_copy",
                      abc[:, half * T + tt * TTW: half * T + (tt + 1) * TTW],
                      b[0:128, :], inc="vs", drain=True)
            pp.release(ib, ev)
    bc_tok = P.now("vs")

    out_toks = []
    ld_rel = [bc_tok, bc_tok]
    for c in range(NC):
        i = c % NB
        if src_sbuf is None:
            P.wait("gpsimd", ld_rel[i])
            lt = P.dma("gpsimd", xr[:, i * T:(i + 1) * T],
                       x_dram[c * 128:(c + 1) * 128, :])
            src = xr[:, i * T:(i + 1) * T]
            P.wait("vector", lt)
        else:
            src = src_sbuf(c)
            if c == 0:
                P.wait("vector", bc_tok)
                if src_toks is not None:
                    P.waits("vector", src_toks)
        dst = chunk(xn, c)
        P.op("vector", "tensor_tensor", dst, src, abc[:, 0:T], op=ALU.mult)
        tok = P.op("vector", "tensor_tensor", dst, dst, abc[:, T:2 * T],
                   op=ALU.add, inc="vs", drain=(gain is None))
        if gain is not None:
            tok = P.op("vector", "tensor_scalar", dst, dst, gain(c), bias(c),
                       op0=ALU.mult, op1=ALU.add, inc="vs", drain=True)
        ld_rel[i] = tok
        out_toks.append(tok)
    return out_toks



def load_weight(P, al, name, w_dram, in_f, out_f, dtype=BF16, stack=None):
    kcs = in_f // 128
    w = al.sb(name, [128, kcs * out_f], dtype, stack=stack)
    toks = []
    for kc in range(kcs):
        toks.append(P.dma("gpsimd", w[:, kc * out_f:(kc + 1) * out_f],
                          w_dram[kc * 128:(kc + 1) * 128, :]))
    return w, toks


def wslice(w, kc, oc, out_f, ow=128):
    return w[:, kc * out_f + oc * ow: kc * out_f + oc * ow + ow]


def load_bias(P, al, name, b_dram, n, stack=None):
    cs = n // 128
    b = al.sb(name, [128, cs], F32, stack=stack)
    tok = P.dma("gpsimd", b[:, :], b_dram)
    return b, [tok]


def emit_proj(P, al, pp, xn_src, w, w_toks, in_f, out_f, evac,
              extra_mm=None, dep_toks=None, n_tt=NTT, tts=None):
    kcs = in_f // 128
    ocs = out_f // 128
    if w_toks:
        P.waits("tensor", w_toks)
    if dep_toks:
        P.waits("tensor", dep_toks)
    for tt in (tts if tts is not None else range(n_tt)):
        for oc in range(ocs):
            ib, b, g = pp.acquire()
            P.wait("tensor", g)
            for kc in range(kcs):
                last = (kc == kcs - 1) and extra_mm is None
                P.mm(b[:, :], wslice(w, kc, oc, out_f), xn_src(kc, tt),
                     start=(kc == 0), stop=last,
                     inc="ts" if last else None)
            if extra_mm is not None:
                extra_mm(b, oc, tt)
            tok = P.now("ts")
            rel = evac(b, oc, tt, tok)
            pp.release(ib, rel)



def emit_A(P, al, consts, x_in, x_out, W):
    pp = PsumPool(P, [al.ps(f"ppA{i}", [128, 512]) for i in range(8)])
    xn = al.sb("A_xn", [128, NC * T], BF16)

    if True:
        st_ln = ExitStack()
        ln_toks = emit_ln(P, al, "A_ln", xn, consts, pp, x_dram=x_in,
                          substack=st_ln)
        w_s1, t_s1 = load_weight(P, al, "A_ws1", W["s_w1T"], 1024, 512)
        w_s2, t_s2 = load_weight(P, al, "A_ws2", W["s_w2T"], 512, 128)
        w_B, t_B = load_weight(P, al, "A_wB", W["B_wT"], 1024, 128)
        w_C, t_C = load_weight(P, al, "A_wC", W["C_wT"], 128, 1024)
        at = al.sb("A_at", [128, 128], BF16)
        t_at = P.dma("gpsimd", at[:, :], W["A_T"])
        an = al.sb("A_an", [128, 128], BF16)
        t_an = P.dma("gpsimd", an[:, :], W["A_nat"])
        b_s1, tb1 = load_bias(P, al, "A_bs1", W["s_b1"], 512)
        b_s2, tb2 = load_bias(P, al, "A_bs2", W["s_b2"], 128)
        b_B, tb3 = load_bias(P, al, "A_bB", W["B_b"], 128)
        b_C, tb4 = load_bias(P, al, "A_bC", W["C_b"], 1024)
        b_D, tb5 = load_bias(P, al, "A_bD", W["D_b"], 1024)
        b_g, tb6 = load_bias(P, al, "A_bg", W["g_b"], 1024)
        b_cd = al.sb("A_bcd", [128, 8], F32)
        P.waits("vector", tb4 + tb5)
        P.op("vector", "tensor_tensor", b_cd[:, :], b_C[:, :], b_D[:, :],
             op=ALU.add, inc="vs", drain=True)
        tb_cd = P.now("vs")

        if DEBUG_BARRIERS & 1:
            P.barrier()
        st_ln.close()
        st_s1 = ExitStack()
        s1 = al.sb("A_s1", [128, 4 * T], BF16, stack=st_s1)
        P.waits("scalar", tb1)

        def ev_s1(b, oc, tt, tok):
            P.wait("scalar", tok)
            return P.op("scalar", "activation", tile(s1, oc, tt), b[:, :], AF.Silu,
                        bias=b_s1[:, oc:oc + 1], inc="ss", drain=True)

        emit_proj(P, al, pp, lambda kc, tt: tile(xn, kc, tt), w_s1, t_s1,
                  1024, 512, ev_s1, dep_toks=ln_toks)

        s = al.sb("A_s", [128, T], BF16)
        P.waits("scalar", tb2)

        def ev_s2(b, oc, tt, tok):
            P.wait("scalar", tok)
            return P.op("scalar", "activation", s[:, tt * TTW:(tt + 1) * TTW],
                        b[:, :], AF.Sigmoid, bias=b_s2[:, 0:1], inc="ss", drain=True)

        emit_proj(P, al, pp, lambda kc, tt: tile(s1, kc, tt), w_s2, t_s2,
                  512, 128, ev_s2, dep_toks=[P.now("ss")])
        if DEBUG_TAPS is not None:
            P.wait("gpsimd", P.now("ss"))
            DEBUG_TAPS.setdefault("toks", []).append(
                P.dma("gpsimd", DEBUG_TAPS["s"], s[:, :]))
            for c in range(NC):
                DEBUG_TAPS["toks"].append(
                    P.dma("gpsimd", DEBUG_TAPS["xn"][c * 128:(c + 1) * 128, :],
                          chunk(xn, c)))

        if DEBUG_BARRIERS & 2:
            P.barrier()
        st_s1.close()
        st_u = ExitStack()
        u = al.sb("A_u", [128, T], F32, stack=st_u)
        btmp = al.sb("A_btmp", [128, 4 * TTW], F32, stack=st_u)
        P.waits("vector", tb3)
        btmp_rel = [None] * 4

        def ev_bt(b, oc, tt, tok):
            i = tt % 4
            P.wait("vector", tok)
            if btmp_rel[i] is not None:
                P.wait("vector", btmp_rel[i])
            t1 = P.op("vector", "tensor_scalar", btmp[:, i * TTW:(i + 1) * TTW],
                      b[:, :], b_B[:, 0:1], None, op0=ALU.add,
                      inc="vs", drain=True)
            vt = P.op("vector", "tensor_tensor", u[:, tt * TTW:(tt + 1) * TTW],
                      btmp[:, i * TTW:(i + 1) * TTW],
                      s[:, tt * TTW:(tt + 1) * TTW], op=ALU.mult,
                      inc="vs", drain=True)
            btmp_rel[i] = vt
            return t1

        emit_proj(P, al, pp, lambda kc, tt: tile(xn, kc, tt), w_B, t_B,
                  1024, 128, ev_bt, dep_toks=[P.now("ss")])
        u_tok = P.now("vs")
        if DEBUG_TAPS is not None:
            P.wait("gpsimd", u_tok)
            DEBUG_TAPS["toks"].append(P.dma("gpsimd", DEBUG_TAPS["u"], u[:, :]))

        if DEBUG_BARRIERS & 4:
            P.barrier()
        hs = al.sb("A_hs", [128, T], BF16)
        H = al.sb("A_H", [128, 64], BF16)
        Hend = al.sb("A_Hend", [128, 64], BF16)
        h2i = al.sb("A_h2i", [128, 64], BF16)
        pw = al.sb("A_pw", [128, 256], BF16)
        ib0, t0, g0 = pp.acquire()
        ib1, t1_, g1 = pp.acquire()
        ib2, t2_, g2 = pp.acquire()
        P.waits("tensor", [g0, g1, g2])
        psq = t0[:, 0:128]
        psqt = t0.bitcast(BF16)[:, 256:384]
        ps_h = [t1_[:, 0:64], t2_[:, 0:64]]
        ident = consts["ident"]

        P.waits("vector", [t_at, t_an])
        P.op("vector", "tensor_copy", pw[:, 0:128], an[:, :])
        P.op("vector", "tensor_copy", pw[:, 128:256], at[:, :], inc="vs", drain=True)
        P.wait("tensor", P.now("vs"))
        for k in range(6):
            t1 = P.mm(psq, pw[:, 128:256], pw[:, 0:128], start=True, stop=True,
                      inc="ts")
            P.wait("vector", t1)
            c1 = P.op("vector", "tensor_copy", pw[:, 0:128], psq,
                      inc="vs", drain=True)
            P.wait("tensor", c1)
            t2 = P.op("tensor", "transpose", psqt, pw[:, 0:128], ident,
                      inc="ts")
            P.wait("vector", t2)
            c2 = P.op("vector", "tensor_copy", pw[:, 128:256], psqt,
                      inc="vs", drain=True)
            P.wait("tensor", c2)

        P.op("vector", "memset", H[:, :], 0.0, inc="vs", drain=True)
        P.wait("tensor", u_tok)
        P.wait("tensor", P.now("vs"))
        for j in range(64):
            b = ps_h[j % 2]
            t1 = P.mm(b, at[:, :], H[:, :], start=True, stop=True, inc="ts")
            P.wait("vector", t1)
            c1 = P.op("vector", "tensor_tensor", H[:, :], b,
                      u[:, j::64], op=ALU.add, inc="vs", drain=True)
            P.wait("tensor", c1)
        P.op("vector", "tensor_copy", Hend[:, 0:1], H[:, 0:1], inc="vs", drain=True)
        P.wait("tensor", P.now("vs"))
        for c in range(1, 64):
            b = ps_h[c % 2]
            t1 = P.mm(b[:, 0:1], pw[:, 128:256], Hend[:, c - 1:c], start=True,
                      stop=True, inc="ts")
            P.wait("vector", t1)
            c1 = P.op("vector", "tensor_tensor", Hend[:, c:c + 1], b[:, 0:1],
                      H[:, c:c + 1], op=ALU.add, inc="vs", drain=True)
            P.wait("tensor", c1)
        P.op("vector", "memset", h2i[:, 0:1], 0.0)
        P.op("vector", "tensor_copy", h2i[:, 1:64], Hend[:, 0:63], inc="vs",
             drain=True)
        P.wait("tensor", P.now("vs"))
        for j in range(64):
            b = ps_h[j % 2]
            rhs = h2i[:, :] if j == 0 else hs[:, (j - 1)::64]
            t1 = P.mm(b, at[:, :], rhs, start=True, stop=True, inc="ts")
            P.wait("vector", t1)
            c1 = P.op("vector", "tensor_tensor", hs[:, j::64], b,
                      u[:, j::64], op=ALU.add, inc="vs", drain=True)
            P.wait("tensor", c1)
        hs_tok = P.now("vs")
        if DEBUG_TAPS is not None:
            P.wait("gpsimd", hs_tok)
            DEBUG_TAPS["toks"].append(P.dma("gpsimd", DEBUG_TAPS["hs"], hs[:, :]))
        pp.release(ib0, hs_tok)
        pp.release(ib1, hs_tok)
        pp.release(ib2, hs_tok)

        w_D, t_D = load_weight(P, al, "A_wD", W["D_wT"], 1024, 1024)
        w_g, t_g = load_weight(P, al, "A_wg", W["g_wT"], 1024, 1024)
        if DEBUG_BARRIERS & 8:
            P.barrier()
        st_u.close()

        gt = al.sb("A_gt", [128, 4 * TTW], BF16)
        xr2 = al.sb("A_xr2", [128, 4 * TTW], F32)
        yt = al.sb("A_yt", [128, 4 * TTW], F32)
        P.waits("tensor", t_D + t_g + t_C + [hs_tok])
        P.wait("sync", hs_tok)
        P.wait("scalar", hs_tok)
        P.waits("vector", [tb_cd])
        P.waits("scalar", tb6 + [tb_cd])
        out_toks = []
        rel = {"gt": [None] * 4, "xr": [None] * 4, "yt": [None] * 4}
        cnt = [0]
        for tt in range(NTT):
            for oc in range(NC):
                i = cnt[0] % 4
                cnt[0] += 1
                ibg, bg, gg = pp.acquire()
                P.wait("tensor", gg)
                for kc in range(NC):
                    P.mm(bg[:, :], wslice(w_g, kc, oc, 1024), tile(xn, kc, tt),
                         start=(kc == 0), stop=(kc == NC - 1),
                         inc="ts" if kc == NC - 1 else None)
                g_tok = P.now("ts")
                iby, by, gy = pp.acquire()
                P.wait("tensor", gy)
                for kc in range(NC):
                    P.mm(by[:, :], wslice(w_D, kc, oc, 1024), tile(xn, kc, tt),
                         start=(kc == 0), stop=False)
                P.mm(by[:, :], wslice(w_C, 0, oc, 1024),
                     hs[:, tt * TTW:(tt + 1) * TTW], start=False, stop=True,
                     inc="ts")
                y_tok = P.now("ts")
                P.wait("scalar", g_tok)
                if rel["gt"][i] is not None:
                    P.wait("scalar", rel["gt"][i])
                sg = P.op("scalar", "activation", gt[:, i * TTW:(i + 1) * TTW],
                          bg[:, :], AF.Sigmoid, bias=b_g[:, oc:oc + 1],
                          inc="ss", drain=True)
                pp.release(ibg, sg)
                P.wait("vector", y_tok)
                if rel["yt"][i] is not None:
                    P.wait("vector", rel["yt"][i])
                sy = P.op("vector", "tensor_scalar", yt[:, i * TTW:(i + 1) * TTW],
                          by[:, :], b_cd[:, oc:oc + 1], None, op0=ALU.add,
                          inc="vs", drain=True)
                pp.release(iby, sy)
                if rel["xr"][i] is not None:
                    P.wait("sync", rel["xr"][i])
                lt = P.dma("sync", xr2[:, i * TTW:(i + 1) * TTW],
                           x_in[oc * 128:(oc + 1) * 128, tt * TTW:(tt + 1) * TTW])
                P.waits("vector", [sg, sy, lt])
                P.op("vector", "tensor_tensor", yt[:, i * TTW:(i + 1) * TTW],
                     yt[:, i * TTW:(i + 1) * TTW], gt[:, i * TTW:(i + 1) * TTW],
                     op=ALU.mult)
                vt = P.op("vector", "tensor_tensor", yt[:, i * TTW:(i + 1) * TTW],
                          yt[:, i * TTW:(i + 1) * TTW],
                          xr2[:, i * TTW:(i + 1) * TTW], op=ALU.add,
                          inc="vs", drain=True)
                rel["gt"][i] = vt
                rel["xr"][i] = vt
                P.wait("sync", vt)
                ot = P.dma("sync",
                           x_out[oc * 128:(oc + 1) * 128, tt * TTW:(tt + 1) * TTW],
                           yt[:, i * TTW:(i + 1) * TTW])
                rel["yt"][i] = ot
                out_toks.append(ot)
    return out_toks



def _np(x):
    return np.ascontiguousarray(np.asarray(x, dtype=np.float32))


def _bvec(x):
    return np.ascontiguousarray(_np(x).reshape(-1, 128).T)


def make_consts():
    import ml_dtypes
    bf = ml_dtypes.bfloat16
    c = {}
    c["ident"] = np.eye(128, dtype=bf)
    c["ones128"] = np.ones((128, 1), dtype=bf)
    c["ones1"] = np.ones((1, 128), dtype=bf)
    c["eps"] = np.full((1, 1), EPS_LN, dtype=np.float32)
    sel8 = np.zeros((8, 8 * 128), dtype=bf)
    for e in range(8):
        sel8[e, e * 128:(e + 1) * 128] = 1
    c["sel8"] = sel8
    pair2 = np.zeros((128, 2), dtype=bf)
    pair2[0:64, 0] = 1
    pair2[64:128, 1] = 1
    c["pair2"] = pair2
    c["onesf1"] = np.ones((1, 128), dtype=np.float32)
    c["posc"] = (np.arange(128)[:, None] + 128 * np.arange(32)[None, :]).astype(np.float32)
    c["ones64r"] = np.ones((1, 64), dtype=bf)
    oh4 = np.zeros((1, 16), dtype=np.float32)
    for j in range(4):
        oh4[0, j * 4 + j] = 1.0
    c["oh4"] = oh4
    selb4 = np.zeros((4, 4 * 64), dtype=bf)
    for j in range(4):
        selb4[j, j * 64:(j + 1) * 64] = 1
    c["selb4"] = selb4
    R = np.zeros((128, 128), dtype=np.float32)
    for blk in range(2):
        for jj in range(32):
            a = blk * 64 + 2 * jj
            R[a, a + 1] = -1.0
            R[a + 1, a] = 1.0
    c["r2t"] = np.ascontiguousarray(R.T).astype(bf)
    c.update(_win_masks())
    return c


def declare_consts(nc, al, P, consts_np, need=None):
    out = {}
    toks = []
    for name, arr in consts_np.items():
        if need is not None and name not in need:
            continue
        dt = F32 if arr.dtype == np.float32 else BF16
        d = nc.dram_tensor("c_" + name, list(arr.shape), dt,
                           kind="ExternalInput").ap()
        s = al.sb("cs_" + name, list(arr.shape), dt)
        toks.append(P.dma("gpsimd", s[:, :], d))
        out[name] = s[:, :]
    return out, toks


def prep_A_weights(inp):
    W = {}
    W["A_T"] = _np(np.asarray(inp["L0_attn_a_A"]).T)
    W["A_nat"] = _np(inp["L0_attn_a_A"])
    W["B_wT"] = _np(np.asarray(inp["L0_attn_a_B_w"]).T)
    W["C_wT"] = _np(np.asarray(inp["L0_attn_a_C_w"]).T)
    W["D_wT"] = _np(np.asarray(inp["L0_attn_a_D_w"]).T)
    W["s_w1T"] = _np(np.asarray(inp["L0_attn_a_s_w1"]).T)
    W["s_w2T"] = _np(np.asarray(inp["L0_attn_a_s_w2"]).T)
    W["g_wT"] = _np(np.asarray(inp["L0_attn_a_g_w"]).T)
    W["B_b"] = _bvec(inp["L0_attn_a_B_b"])
    W["C_b"] = _bvec(inp["L0_attn_a_C_b"])
    W["D_b"] = _bvec(inp["L0_attn_a_D_b"])
    W["s_b1"] = _bvec(inp["L0_attn_a_s_b1"])
    W["s_b2"] = _bvec(inp["L0_attn_a_s_b2"])
    W["g_b"] = _bvec(inp["L0_attn_a_g_b"])
    return W


CONST_NEEDS = {
    "emit_A": ["ident", "ones128", "ones1", "eps"],
    "emit_B": ["ident", "ones128", "ones1", "eps", "sel8"],
    "emit_E": ["ident", "ones128", "ones1", "eps", "sel8"],
    "emit_C": ["ident", "ones128", "ones1", "eps", "pair2", "onesf1", "posc",
               "oh4", "selb4"],
    "emit_D": None,
    "emit_F": None,
}


def _const_need(emitter_name):
    base = ["ident", "ones128", "ones1", "eps"]
    if emitter_name in ("emit_A",):
        return base
    if emitter_name in ("emit_B", "emit_E"):
        return base + ["sel8"]
    if emitter_name == "emit_C":
        return base + ["pair2", "onesf1", "posc", "oh4", "selb4"]
    if emitter_name == "emit_D":
        return base + ["pair2", "oh4", "selb4", "r2t"]
    if emitter_name == "emit_F":
        return base + ["pair2", "oh4", "selb4", "onesf1"]
    return None


def build_launch(emitter, wshapes):
    global DEBUG_TAPS
    nc = bass.Bass("TRN2", target_bir_lowering=False)
    x_in = nc.dram_tensor("x_in", [D, T], F32, kind="ExternalInput").ap()
    x_out = nc.dram_tensor("x_out", [D, T], F32, kind="ExternalOutput").ap()
    Wd = {}
    for name, shp in wshapes.items():
        Wd[name] = nc.dram_tensor("w_" + name, list(shp), F32,
                                  kind="ExternalInput").ap()
    if DEBUG_TAPS is not None:
        DEBUG_TAPS = {
            "s": nc.dram_tensor("tap_s", [128, T], F32, kind="ExternalOutput").ap(),
            "u": nc.dram_tensor("tap_u", [128, T], F32, kind="ExternalOutput").ap(),
            "hs": nc.dram_tensor("tap_hs", [128, T], F32, kind="ExternalOutput").ap(),
            "xn": nc.dram_tensor("tap_xn", [D, T], F32, kind="ExternalOutput").ap(),
        }
    stack = ExitStack()
    with stack:
        al = Alloc(nc, stack)
        P = Plan(nc, stack)
        P.alloc = al
        consts, ctoks = declare_consts(nc, al, P, make_consts(),
                                       need=_const_need(emitter.__name__))
        P.waits("tensor", ctoks)
        P.waits("vector", ctoks)
        P.waits("scalar", ctoks)
        out_toks = emitter(P, al, consts, x_in, x_out, Wd)
        P.waits("sync", out_toks)
        if DEBUG_TAPS is not None:
            P.waits("gpsimd", DEBUG_TAPS["toks"])
        P.build()
    return nc


def run_layer(emitter, W, x_fm, trace=False):
    consts = make_consts()
    Ws = W if isinstance(W, list) else [W] * 8
    wshapes = {k: v.shape for k, v in Ws[0].items()}
    nc = build_launch(emitter, wshapes)
    need = _const_need(emitter.__name__)
    in_maps = []
    for c in range(8):
        m = {"x_in": x_fm[c]}
        for k, v in Ws[c].items():
            m["w_" + k] = v
        for k, v in consts.items():
            if need is None or k in need:
                m["c_" + k] = v
        in_maps.append(m)
    res = run_bass_kernel_spmd(nc, in_maps, core_ids=list(range(8)), trace=trace)
    return [res.results[c]["x_out"] for c in range(8)]


def kernel(**inputs):
    x = np.asarray(inputs["L0_attn_a_x"], dtype=np.float32)
    x_fm = [np.ascontiguousarray(x[c].T) for c in range(8)]
    if FUSED:
        x_fm = run_fused(inputs, x_fm)
        return np.stack([np.ascontiguousarray(np.asarray(xc).T)
                         for xc in x_fm], axis=0)
    WA = prep_A_weights(inputs)
    WB = prep_B_weights(inputs)
    WC = prep_C_weights(inputs)
    WD = prep_D_weights(inputs)
    WEs = [prep_E_weights(inputs, c) for c in range(8)]
    WF = prep_F_weights(inputs)
    x_fm = run_layer(emit_A, WA, x_fm)
    x_fm = run_layer(emit_B, WB, x_fm)
    x_fm = run_layer(emit_C, WC, x_fm)
    x_fm = run_layer(emit_B, WB, x_fm)
    x_fm = run_layer(emit_D, WD, x_fm)
    x_fm = run_layer(emit_E, WEs, x_fm)
    x_fm = run_layer(emit_F, WF, x_fm)
    x_fm = run_layer(emit_E, WEs, x_fm)
    out = np.stack([np.ascontiguousarray(np.asarray(xc).T) for xc in x_fm], axis=0)
    return out



def emit_router_logits(P, al, pp, xn, wr, wr_toks, name):
    lgall = al.sb(name + "_lg", [128, 32 * 8], F32)
    P.waits("tensor", wr_toks)
    for tc in range(32):
        ib, b, g = pp.acquire()
        P.wait("tensor", g)
        tt = tc // 4
        co = (tc % 4) * 128
        for kc in range(NC):
            P.mm(b[:, 0:8], tile(xn, kc, tt)[:, co:co + 128], wr[:, kc * 8:(kc + 1) * 8],
                 start=(kc == 0), stop=(kc == NC - 1),
                 inc="ts" if kc == NC - 1 else None)
        t = P.now("ts")
        P.wait("vector", t)
        ev = P.op("vector", "tensor_copy", lgall[:, tc * 8:(tc + 1) * 8], b[:, 0:8],
                  inc="vs", drain=True)
        pp.release(ib, ev)
    return lgall, P.now("vs")


def emit_B(P, al, consts, x_in, x_out, W):
    seed_toks = [P.dma("sync", x_out[oc * 128:(oc + 1) * 128, :],
                       x_in[oc * 128:(oc + 1) * 128, :]) for oc in range(NC)]
    """L0_ffn: x_out = x_in + (moe + shared)(LN(x_in))."""
    pp = PsumPool(P, [al.ps(f"ppB{i}", [128, 512]) for i in range(8)])
    xn = al.sb("B_xn", [128, NC * T], BF16)

    st_ln = ExitStack()
    ln_toks = emit_ln(P, al, "B_ln", xn, consts, pp, x_dram=x_in, substack=st_ln)
    wr, wr_toks = load_weight(P, al, "B_wr", W["w_router"], 1024, 8)
    st_ln.close()

    P.waits("tensor", ln_toks)
    lgall, lg_tok = emit_router_logits(P, al, pp, xn, wr, wr_toks, "B")
    w8all = al.sb("B_w8", [128, 32 * 8], BF16)
    v12 = al.sb("B_v12", [128, 4 * 32], F32)
    m1a = al.sb("B_m1a", [128, 256], BF16)
    m2a = al.sb("B_m2a", [128, 256], BF16)
    tmsk = al.sb("B_tmsk", [128, 256], F32)
    P.wait("vector", lg_tok)
    P.waits("scalar", [lg_tok])
    lg3 = lgall[:, :].rearrange("p (g e) -> p g e", e=8)
    v1 = v12[:, 0:32]
    v2 = v12[:, 32:64]
    dd = v12[:, 64:96]
    s1 = v12[:, 96:128]
    P.op("vector", "tensor_reduce", v1, lg3, axis=AX.X, op=ALU.max)
    P.op("vector", "tensor_tensor", m1a[:, :].rearrange("p (g e) -> p g e", e=8),
         lg3, v1.unsqueeze(2).broadcast_to([128, 32, 8]), op=ALU.is_ge)
    P.op("vector", "scalar_tensor_tensor", tmsk[:, :], m1a[:, :], -1e30,
         lgall[:, :], op0=ALU.mult, op1=ALU.add)
    P.op("vector", "tensor_reduce", v2, tmsk[:, :].rearrange("p (g e) -> p g e", e=8),
         axis=AX.X, op=ALU.max)
    P.op("vector", "tensor_tensor", m2a[:, :].rearrange("p (g e) -> p g e", e=8),
         tmsk[:, :].rearrange("p (g e) -> p g e", e=8),
         v2.unsqueeze(2).broadcast_to([128, 32, 8]), op=ALU.is_ge)
    P.op("vector", "tensor_tensor", dd, v1, v2, op=ALU.subtract, inc="vs",
         drain=True)
    P.wait("scalar", P.now("vs"))
    P.op("scalar", "activation", s1, dd, AF.Sigmoid, inc="ss", drain=True)
    P.wait("vector", P.now("ss"))
    P.op("vector", "tensor_tensor", tmsk[:, :].rearrange("p (g e) -> p g e", e=8),
         m1a[:, :].rearrange("p (g e) -> p g e", e=8),
         s1.unsqueeze(2).broadcast_to([128, 32, 8]), op=ALU.mult)
    P.op("vector", "tensor_scalar", s1, s1, -1.0, 1.0, op0=ALU.mult, op1=ALU.add)
    P.op("vector", "tensor_tensor", m2a[:, :].rearrange("p (g e) -> p g e", e=8),
         m2a[:, :].rearrange("p (g e) -> p g e", e=8),
         s1.unsqueeze(2).broadcast_to([128, 32, 8]), op=ALU.mult)
    P.op("vector", "tensor_tensor", w8all[:, :], tmsk[:, :], m2a[:, :],
         op=ALU.add, inc="vs", drain=True)
    w8_tok = P.now("vs")

    w8T = al.sb("B_w8T", [8, T], BF16)
    P.wait("tensor", w8_tok)
    for tc in range(32):
        ib, b, g = pp.acquire()
        P.wait("tensor", g)
        bb = b.bitcast(BF16)
        t = P.op("tensor", "transpose", bb[0:8, 0:128],
                 w8all[:, tc * 8:(tc + 1) * 8], consts["ident"], inc="ts")
        P.wait("vector", t)
        ev = P.op("vector", "tensor_copy", w8T[:, tc * 128:(tc + 1) * 128],
                  bb[0:8, 0:128], inc="vs", drain=True)
        pp.release(ib, ev)
    w8T_tok = P.now("vs")

    seeded = [False]
    st_sh = ExitStack()
    P.fence("gpsimd")
    w_gt, t_gt = load_weight(P, al, "B_wgt", W["gate_w"], 1024, 2048, stack=st_sh)
    w_up, t_up = load_weight(P, al, "B_wup", W["up_w"], 1024, 2048, stack=st_sh)
    w_dn, t_dn = load_weight(P, al, "B_wdn", W["down_w"], 2048, 1024, stack=st_sh)
    sil = al.sb("B_sil", [128, 4 * TTW], BF16, stack=st_sh)
    z = al.sb("B_z", [128, 16 * TTW], BF16, stack=st_sh)
    ost = al.sb("B_ost", [128, 4 * TTW], F32, stack=st_sh)
    P.waits("tensor", t_gt + t_up + t_dn + ln_toks)
    ost_rel = [None] * 4
    sil_rel = [None] * 4
    z_rel = [None] * 16
    for tt in range(NTT):
        for oc in range(16):
            i = oc % 4
            ib, b, g = pp.acquire()
            P.wait("tensor", g)
            for kc in range(NC):
                P.mm(b[:, :], wslice(w_gt, kc, oc, 2048), tile(xn, kc, tt),
                     start=(kc == 0), stop=(kc == NC - 1),
                     inc="ts" if kc == NC - 1 else None)
            t = P.now("ts")
            P.wait("scalar", t)
            if sil_rel[i] is not None:
                P.wait("scalar", sil_rel[i])
            sg = P.op("scalar", "activation", sil[:, i * TTW:(i + 1) * TTW],
                      b[:, :], AF.Silu, inc="ss", drain=True)
            pp.release(ib, sg)
            ib2, b2, g2 = pp.acquire()
            P.wait("tensor", g2)
            for kc in range(NC):
                P.mm(b2[:, :], wslice(w_up, kc, oc, 2048), tile(xn, kc, tt),
                     start=(kc == 0), stop=(kc == NC - 1),
                     inc="ts" if kc == NC - 1 else None)
            t2 = P.now("ts")
            P.waits("vector", [t2, sg])
            if z_rel[oc] is not None:
                P.wait("vector", z_rel[oc])
            zt = P.op("vector", "tensor_tensor", z[:, oc * TTW:(oc + 1) * TTW],
                      b2[:, :], sil[:, i * TTW:(i + 1) * TTW], op=ALU.mult,
                      inc="vs", drain=True)
            sil_rel[i] = zt
            pp.release(ib2, zt)
        z_tok = P.now("vs")
        for oc in range(NC):
            ib, b, g = pp.acquire()
            P.waits("tensor", [g, z_tok])
            for kc in range(16):
                P.mm(b[:, :], wslice(w_dn, kc, oc, 1024),
                     z[:, kc * TTW:(kc + 1) * TTW],
                     start=(kc == 0), stop=(kc == 15),
                     inc="ts" if kc == 15 else None)
            t = P.now("ts")
            i = (tt * NC + oc) % 4
            P.wait("vector", t)
            if ost_rel[i] is not None:
                P.wait("vector", ost_rel[i])
            ev = P.op("vector", "tensor_copy", ost[:, i * TTW:(i + 1) * TTW],
                      b[:, :], inc="vs", drain=True)
            pp.release(ib, ev)
            if oc == NC - 1:
                for kc2 in range(16):
                    z_rel[kc2] = P.now("ts")
            P.wait("gpsimd", ev)
            if not seeded[0]:
                P.waits("gpsimd", seed_toks)
                seeded[0] = True
            st = P.dma("gpsimd",
                       x_out[oc * 128:(oc + 1) * 128, tt * TTW:(tt + 1) * TTW],
                       ost[:, i * TTW:(i + 1) * TTW], accum_op=ALU.add)
            ost_rel[i] = st
    st_sh.close()

    st_e = ExitStack()
    P.fence("gpsimd")
    fc1 = al.sb("B_fc1", [128, 8 * 2048], BF16, stack=st_e)
    fc2 = al.sb("B_fc2", [128, 8 * 1024], BF16, stack=st_e)
    wbc = al.sb("B_wbc", [128, T], BF16, stack=st_e)
    sil2 = al.sb("B_sil2", [128, 4 * TTW], BF16, stack=st_e)
    z2 = al.sb("B_z2", [128, 8 * TTW], BF16, stack=st_e)
    est = al.sb("B_est", [128, 6 * TTW], BF16, stack=st_e)
    P.waits("tensor", [w8T_tok])
    fc_rel = None
    wbc_rel = None
    est_rel = [None] * 6
    sil2_rel = [None] * 4
    z2_rel = [None] * 8
    cnt = [0]
    for e in range(8):
        t_f1 = []
        t_f2 = []
        for kc in range(NC):
            if fc_rel is not None:
                P.wait("gpsimd", fc_rel)
            t_f1.append(P.dma("gpsimd", fc1[:, kc * 2048:(kc + 1) * 2048],
                              W["fc1"][e * 1024 + kc * 128: e * 1024 + (kc + 1) * 128, :]))
            t_f2.append(P.dma("gpsimd", fc2[:, kc * 1024:(kc + 1) * 1024],
                              W["fc2"][e * 1024 + kc * 128: e * 1024 + (kc + 1) * 128, :]))
        if wbc_rel is not None:
            P.wait("tensor", wbc_rel)
        for tt in range(NTT):
            ib, b, g = pp.acquire()
            P.wait("tensor", g)
            t = P.mm(b[:, :], consts["sel8"][:, e * 128:(e + 1) * 128],
                     w8T[:, tt * TTW:(tt + 1) * TTW], start=True, stop=True,
                     inc="ts")
            P.wait("vector", t)
            ev = P.op("vector", "tensor_copy", wbc[:, tt * TTW:(tt + 1) * TTW],
                      b[:, :], inc="vs", drain=True)
            pp.release(ib, ev)
        wbc_tok = P.now("vs")
        P.waits("tensor", t_f1 + t_f2 + [wbc_tok])
        for tt in range(NTT):
            for zc in range(NC):
                ib, b, g = pp.acquire()
                P.wait("tensor", g)
                for kc in range(NC):
                    P.mm(b[:, :], wslice(fc1, kc, zc, 2048), tile(xn, kc, tt),
                         start=(kc == 0), stop=(kc == NC - 1), inc="ts")
                t = P.now("ts")
                i2 = zc % 4
                P.wait("scalar", t)
                if sil2_rel[i2] is not None:
                    P.wait("scalar", sil2_rel[i2])
                sg = P.op("scalar", "activation", sil2[:, i2 * TTW:(i2 + 1) * TTW],
                          b[:, :], AF.Silu, inc="ss", drain=True)
                pp.release(ib, sg)
                ib2, b2, g2 = pp.acquire()
                P.wait("tensor", g2)
                for kc in range(NC):
                    P.mm(b2[:, :], wslice(fc1, kc, zc + 8, 2048), tile(xn, kc, tt),
                         start=(kc == 0), stop=(kc == NC - 1), inc="ts")
                t2 = P.now("ts")
                P.waits("vector", [t2, sg])
                if z2_rel[zc] is not None:
                    P.wait("vector", z2_rel[zc])
                zt = P.op("vector", "tensor_tensor", z2[:, zc * TTW:(zc + 1) * TTW],
                          b2[:, :], sil2[:, i2 * TTW:(i2 + 1) * TTW], op=ALU.mult,
                          inc="vs", drain=True)
                sil2_rel[i2] = zt
                pp.release(ib2, zt)
            z_tok = P.now("vs")
            for oc in range(NC):
                ib, b, g = pp.acquire()
                P.waits("tensor", [g, z_tok])
                for kc in range(NC):
                    P.mm(b[:, :], wslice(fc2, kc, oc, 1024),
                         z2[:, kc * TTW:(kc + 1) * TTW],
                         start=(kc == 0), stop=(kc == NC - 1), inc="ts")
                t = P.now("ts")
                i = cnt[0] % 6
                cnt[0] += 1
                P.wait("vector", t)
                if est_rel[i] is not None:
                    P.wait("vector", est_rel[i])
                ev = P.op("vector", "tensor_tensor", est[:, i * TTW:(i + 1) * TTW],
                          b[:, :], wbc[:, tt * TTW:(tt + 1) * TTW], op=ALU.mult,
                          inc="vs", drain=True)
                pp.release(ib, ev)
                if oc == NC - 1:
                    for kc2 in range(NC):
                        z2_rel[kc2] = P.now("ts")
                P.wait("gpsimd", ev)
                st = P.dma("gpsimd",
                           x_out[oc * 128:(oc + 1) * 128, tt * TTW:(tt + 1) * TTW],
                           est[:, i * TTW:(i + 1) * TTW], accum_op=ALU.add)
                est_rel[i] = st
        fc_rel = P.now("ts")
        wbc_rel = P.now("vs")
    st_e.close()

    return P.dma_all_done("gpsimd") + seed_toks


def prep_B_weights(inp):
    W = {}
    W["w_router"] = _np(inp["L0_ffn_w_router"])
    W["fc1"] = _np(inp["L0_ffn_fc1_w"]).reshape(8 * 1024, 2048)
    W["fc2"] = _np(inp["L0_ffn_fc2_w"]).reshape(8 * 1024, 1024)
    W["gate_w"] = _np(inp["L0_ffn_gate_w"])
    W["up_w"] = _np(inp["L0_ffn_up_w"])
    W["down_w"] = _np(inp["L0_ffn_down_w"])
    return W


DEBUG_NO_CC = False
CAPACITY = 5120


def emit_E(P, al, consts, x_in, x_out, W):
    seed_toks = [P.dma("sync", x_out[oc * 128:(oc + 1) * 128, :],
                       x_in[oc * 128:(oc + 1) * 128, :]) for oc in range(NC)]
    seeded = [False]
    """L1_ffn: capacity-constrained top-2 MoE (dense-masked compute)."""
    pp = PsumPool(P, [al.ps(f"ppE{i}", [128, 512]) for i in range(8)])
    xn = al.sb("E_xn", [128, NC * T], BF16)
    hid_dram = P.dram("E_hid", [2048, T], BF16)
    cc_in = P.dram("E_cci", [1, 512], F32)
    cc_out = P.dram("E_cco", [8, 512], F32)
    off_dram = P.dram("E_off", [8, 1], F32)

    st_ln = ExitStack()
    ln_toks = emit_ln(P, al, "E_ln", xn, consts, pp, x_dram=x_in, substack=st_ln)
    wr, wr_toks = load_weight(P, al, "E_wr", W["w_routerT"], 1024, 8)
    cmask = al.sb("E_cmask", [8, 1], F32)
    t_cm = P.dma("gpsimd", cmask[:, :], W["coremask"])
    st_ln.close()

    P.waits("tensor", ln_toks)
    lgall, lg_tok = emit_router_logits(P, al, pp, xn, wr, wr_toks, "E")
    st_rt = ExitStack()
    m1all = al.sb("E_m1", [128, 32 * 8], BF16, stack=st_rt)
    m2all = al.sb("E_m2", [128, 32 * 8], BF16, stack=st_rt)
    g12 = al.sb("E_g12", [128, 2 * 32], F32, stack=st_rt)
    v12 = al.sb("E_v12", [128, 4 * 32], F32, stack=st_rt)
    tmsk = al.sb("E_tmsk", [128, 256], F32, stack=st_rt)
    ez = al.sb("E_ez", [128, 256], F32, stack=st_rt)
    kepta = al.sb("E_kepta", [128, 256], BF16, stack=st_rt)
    maskT = al.sb("E_maskT", [8, T], F32, stack=st_rt)
    cum = al.sb("E_cum", [8, T], F32, stack=st_rt)
    keepT = al.sb("E_keepT", [8, T], BF16, stack=st_rt)
    w8all = al.sb("E_w8", [128, 32 * 8], BF16, stack=st_rt)
    P.wait("vector", lg_tok)
    P.waits("scalar", [lg_tok])
    lg3 = lgall[:, :].rearrange("p (g e) -> p g e", e=8)
    v1 = v12[:, 0:32]
    v2 = v12[:, 32:64]
    den = v12[:, 64:96]
    scx = v12[:, 96:128]
    P.op("vector", "tensor_reduce", v1, lg3, axis=AX.X, op=ALU.max)
    P.op("vector", "tensor_tensor", m1all[:, :].rearrange("p (g e) -> p g e", e=8),
         lg3, v1.unsqueeze(2).broadcast_to([128, 32, 8]), op=ALU.is_ge)
    P.op("vector", "scalar_tensor_tensor", tmsk[:, :], m1all[:, :], -1e30,
         lgall[:, :], op0=ALU.mult, op1=ALU.add)
    P.op("vector", "tensor_reduce", v2,
         tmsk[:, :].rearrange("p (g e) -> p g e", e=8), axis=AX.X, op=ALU.max)
    P.op("vector", "tensor_tensor", m2all[:, :].rearrange("p (g e) -> p g e", e=8),
         tmsk[:, :].rearrange("p (g e) -> p g e", e=8),
         v2.unsqueeze(2).broadcast_to([128, 32, 8]), op=ALU.is_ge)
    P.op("vector", "tensor_tensor", ez[:, :].rearrange("p (g e) -> p g e", e=8),
         lg3, v1.unsqueeze(2).broadcast_to([128, 32, 8]), op=ALU.subtract,
         inc="vs", drain=True)
    P.wait("scalar", P.now("vs"))
    P.op("scalar", "activation", ez[:, :], ez[:, :], AF.Exp, inc="ss", drain=True)
    P.wait("vector", P.now("ss"))
    P.op("vector", "tensor_reduce", den,
         ez[:, :].rearrange("p (g e) -> p g e", e=8), axis=AX.X, op=ALU.add)
    P.op("vector", "reciprocal", g12[:, 0:32], den)
    P.op("vector", "tensor_tensor", scx, v2, v1, op=ALU.subtract, inc="vs",
         drain=True)
    P.wait("scalar", P.now("vs"))
    P.op("scalar", "activation", scx, scx, AF.Exp, inc="ss", drain=True)
    P.wait("vector", P.now("ss"))
    P.op("vector", "tensor_tensor", g12[:, 32:64], scx, g12[:, 0:32],
         op=ALU.mult)
    P.op("vector", "tensor_tensor", w8all[:, :], m1all[:, :], m2all[:, :],
         op=ALU.add, inc="vs", drain=True)
    rt_tok = P.now("vs")

    P.wait("tensor", rt_tok)
    for tc in range(32):
        ib, b, g = pp.acquire()
        P.wait("tensor", g)
        bb = b.bitcast(BF16)
        t = P.op("tensor", "transpose", bb[0:8, 0:128],
                 w8all[:, tc * 8:(tc + 1) * 8], consts["ident"], inc="ts")
        P.wait("vector", t)
        ev = P.op("vector", "tensor_copy", maskT[:, tc * 128:(tc + 1) * 128],
                  bb[0:8, 0:128], inc="vs", drain=True)
        pp.release(ib, ev)
    mt_tok = P.now("vs")
    P.op("vector", "tensor_tensor_scan", cum[:, :], maskT[:, :], maskT[:, :],
         initial=0.0, op0=ALU.add, op1=ALU.bypass, inc="vs", drain=True)
    cum_tok = P.now("vs")
    P.wait("gpsimd", cum_tok)
    t1 = P.dma("gpsimd", cc_in[0:1, 0:8], cum[:, T - 1:T])
    P.wait("gpsimd", t1)
    if not DEBUG_NO_CC:
        P.op("gpsimd", "collective_compute", "AllGather", ALU.bypass,
             replica_groups=[list(range(8))], ins=[cc_in], outs=[cc_out],
             inc="cs", pre_drain=False)
    st_h = ExitStack()
    P.fence("gpsimd")
    w_gt, t_gt = load_weight(P, al, "E_wgt", W["w_gateT"], 1024, 2048, stack=st_h)
    hst = al.sb("E_hst", [128, 4 * TTW], BF16, stack=st_h)
    P.waits("tensor", t_gt)
    hst_rel = [None] * 4
    k = [0]
    for tt in range(NTT):
        for oc in range(16):
            ib, b, g = pp.acquire()
            P.wait("tensor", g)
            for kc in range(NC):
                P.mm(b[:, :], wslice(w_gt, kc, oc, 2048), tile(xn, kc, tt),
                     start=(kc == 0), stop=(kc == NC - 1),
                     inc="ts" if kc == NC - 1 else None)
            t = P.now("ts")
            i = k[0] % 4
            k[0] += 1
            P.wait("scalar", t)
            if hst_rel[i] is not None:
                P.wait("scalar", hst_rel[i])
            sg = P.op("scalar", "activation", hst[:, i * TTW:(i + 1) * TTW],
                      b[:, :], AF.Silu, inc="ss", drain=True)
            pp.release(ib, sg)
            P.wait("gpsimd", sg)
            st = P.dma("gpsimd",
                       hid_dram[oc * 128:(oc + 1) * 128, tt * TTW:(tt + 1) * TTW],
                       hst[:, i * TTW:(i + 1) * TTW])
            hst_rel[i] = st
    st_h.close()


    gath = al.sb("E_gath", [8, 8], F32, stack=st_rt)
    offs_p = al.sb("E_offsp", [1, 8], F32, stack=st_rt)
    offs8 = al.sb("E_offs8", [8, 1], F32, stack=st_rt)
    if DEBUG_NO_CC:
        t2 = P.op("gpsimd", "memset", gath[:, :], 0.0, inc="gs", drain=True,
                  pre_drain=False)
    else:
        P.wait("gpsimd", P.now("cs"))
        t2 = P.dma("gpsimd", gath[:, :], cc_out[:, 0:8])
    ib, b, g = pp.acquire()
    P.waits("tensor", [g, t2, t_cm])
    t3 = P.mm(b[0:1, 0:8], cmask[:, :], gath[:, :], start=True, stop=True, inc="ts")
    P.wait("vector", t3)
    ev = P.op("vector", "tensor_copy", offs_p[:, :], b[0:1, 0:8], inc="vs",
              drain=True)
    pp.release(ib, ev)
    P.wait("gpsimd", P.now("vs"))
    t4 = P.dma("gpsimd", off_dram[0:8, 0:1], offs_p[0:1, 0:8])
    P.wait("gpsimd", t4)
    t5 = P.dma("gpsimd", offs8[:, :], off_dram)
    P.waits("vector", [t5])
    P.op("vector", "tensor_scalar", keepT[:, :], cum[:, :], offs8[:, :],
         CAPACITY + 0.5, op0=ALU.add, op1=ALU.is_le)
    P.op("vector", "tensor_tensor", keepT[:, :], keepT[:, :], maskT[:, :],
         op=ALU.mult, inc="vs", drain=True)
    kt_tok = P.now("vs")

    P.wait("tensor", kt_tok)
    for tc in range(32):
        ib, b, g = pp.acquire()
        P.wait("tensor", g)
        bb = b.bitcast(BF16)
        t = P.op("tensor", "transpose", bb[0:128, 0:8],
                 keepT[:, tc * 128:(tc + 1) * 128], consts["ident"][0:8, 0:8],
                 inc="ts")
        P.wait("vector", t)
        ev = P.op("vector", "tensor_copy", kepta[:, tc * 8:(tc + 1) * 8],
                  bb[0:128, 0:8], inc="vs", drain=True)
        pp.release(ib, ev)
    P.op("vector", "tensor_tensor", tmsk[:, :], kepta[:, :], m1all[:, :],
         op=ALU.mult)
    P.op("vector", "tensor_reduce", v1,
         tmsk[:, :].rearrange("p (g e) -> p g e", e=8), axis=AX.X, op=ALU.add)
    P.op("vector", "tensor_tensor", tmsk[:, :], kepta[:, :], m2all[:, :],
         op=ALU.mult)
    P.op("vector", "tensor_reduce", v2,
         tmsk[:, :].rearrange("p (g e) -> p g e", e=8), axis=AX.X, op=ALU.add)
    P.op("vector", "tensor_tensor", v1, v1, g12[:, 0:32], op=ALU.mult)
    P.op("vector", "tensor_tensor", v2, v2, g12[:, 32:64], op=ALU.mult)
    P.op("vector", "scalar_tensor_tensor", den, v1, 1.0, v2, op0=ALU.mult,
         op1=ALU.add)
    P.op("vector", "tensor_scalar", den, den, 1e-6, None, op0=ALU.add)
    P.op("vector", "reciprocal", scx, den)
    P.op("vector", "tensor_tensor", v1, v1, scx, op=ALU.mult)
    P.op("vector", "tensor_tensor", v2, v2, scx, op=ALU.mult)
    P.op("vector", "tensor_tensor", tmsk[:, :].rearrange("p (g e) -> p g e", e=8),
         m1all[:, :].rearrange("p (g e) -> p g e", e=8),
         v1.unsqueeze(2).broadcast_to([128, 32, 8]), op=ALU.mult)
    P.op("vector", "tensor_tensor", ez[:, :].rearrange("p (g e) -> p g e", e=8),
         m2all[:, :].rearrange("p (g e) -> p g e", e=8),
         v2.unsqueeze(2).broadcast_to([128, 32, 8]), op=ALU.mult)
    P.op("vector", "tensor_tensor", w8all[:, :], tmsk[:, :], ez[:, :],
         op=ALU.add, inc="vs", drain=True)
    w8_tok = P.now("vs")

    w8T = al.sb("E_w8T", [8, T], BF16)
    P.wait("tensor", w8_tok)
    for tc in range(32):
        ib, b, g = pp.acquire()
        P.wait("tensor", g)
        bb = b.bitcast(BF16)
        t = P.op("tensor", "transpose", bb[0:8, 0:128],
                 w8all[:, tc * 8:(tc + 1) * 8], consts["ident"], inc="ts")
        P.wait("vector", t)
        ev = P.op("vector", "tensor_copy", w8T[:, tc * 128:(tc + 1) * 128],
                  bb[0:8, 0:128], inc="vs", drain=True)
        pp.release(ib, ev)
    w8T_tok = P.now("vs")
    st_rt.close()

    st_e = ExitStack()
    P.fence("gpsimd")
    wdn = al.sb("E_wdn", [128, 16 * 1024], BF16, stack=st_e)
    hid = al.sb("E_hidsb", [128, 2 * 16 * TTW], BF16, stack=st_e)
    wbc = al.sb("E_wbc", [128, T], BF16, stack=st_e)
    est = al.sb("E_est", [128, 6 * TTW], BF16, stack=st_e)
    wdn_rel = None
    wbc_rel = None
    hid_rel = [None, None]
    est_rel = [None] * 6
    cnt = [0]
    for e in range(8):
        t_w = []
        for kc in range(16):
            if wdn_rel is not None:
                P.wait("gpsimd", wdn_rel)
            t_w.append(P.dma("gpsimd", wdn[:, kc * 1024:(kc + 1) * 1024],
                             W["w_downT"][e * 2048 + kc * 128: e * 2048 + (kc + 1) * 128, :]))
        if wbc_rel is not None:
            P.wait("tensor", wbc_rel)
        for tt in range(NTT):
            ib, b, g = pp.acquire()
            P.wait("tensor", g)
            t = P.mm(b[:, :], consts["sel8"][:, e * 128:(e + 1) * 128],
                     w8T[:, tt * TTW:(tt + 1) * TTW], start=True, stop=True,
                     inc="ts")
            P.wait("vector", t)
            ev = P.op("vector", "tensor_copy", wbc[:, tt * TTW:(tt + 1) * TTW],
                      b[:, :], inc="vs", drain=True)
            pp.release(ib, ev)
        wbc_tok = P.now("vs")
        P.waits("tensor", t_w + [wbc_tok])
        for tt in range(NTT):
            ih = tt % 2
            if hid_rel[ih] is not None:
                P.wait("gpsimd", hid_rel[ih])
            src_ap = bass.AP(hid_dram.tensor, tt * TTW,
                             [[T, 128], [128 * T, 16], [1, TTW]])
            ht = P.dma("gpsimd", hid[:, ih * 16 * TTW:(ih + 1) * 16 * TTW],
                       src_ap)
            P.wait("tensor", ht)
            for oc in range(NC):
                ib, b, g = pp.acquire()
                P.wait("tensor", g)
                for kc in range(16):
                    P.mm(b[:, :], wslice(wdn, kc, oc, 1024),
                         hid[:, ih * 16 * TTW + kc * TTW: ih * 16 * TTW + (kc + 1) * TTW],
                         start=(kc == 0), stop=(kc == 15), inc="ts")
                t = P.now("ts")
                i = cnt[0] % 6
                cnt[0] += 1
                P.wait("vector", t)
                if est_rel[i] is not None:
                    P.wait("vector", est_rel[i])
                ev = P.op("vector", "tensor_tensor", est[:, i * TTW:(i + 1) * TTW],
                          b[:, :], wbc[:, tt * TTW:(tt + 1) * TTW], op=ALU.mult,
                          inc="vs", drain=True)
                pp.release(ib, ev)
                if oc == NC - 1:
                    hid_rel[ih] = P.now("ts")
                P.wait("gpsimd", ev)
                if not seeded[0]:
                    P.waits("gpsimd", seed_toks)
                    seeded[0] = True
                st = P.dma("gpsimd",
                           x_out[oc * 128:(oc + 1) * 128,
                                 tt * TTW:(tt + 1) * TTW],
                           est[:, i * TTW:(i + 1) * TTW], accum_op=ALU.add)
                est_rel[i] = st
        wdn_rel = P.now("ts")
        wbc_rel = P.now("vs")
    st_e.close()

    return P.dma_all_done("gpsimd") + seed_toks


def prep_E_weights(inp, core_id):
    W = {}
    W["w_routerT"] = _np(np.asarray(inp["L1_ffn_w_router"]).T)
    W["w_gateT"] = _np(np.asarray(inp["L1_ffn_w_gate"]).T)
    wd = np.asarray(inp["L1_ffn_w_down"])
    W["w_downT"] = _np(wd.transpose(0, 2, 1)).reshape(8 * 2048, 1024)
    W["coremask"] = (np.arange(8) < core_id).astype(np.float32).reshape(8, 1)
    return W


def emit_C(P, al, consts, x_in, x_out, W):
    pp = PsumPool(P, [al.ps(f"ppC{i}", [128, 512]) for i in range(8)])
    nc = P.nc
    kext_dram = P.dram("C_kext", [65, 32 * 2048], BF16)
    vext_dram = P.dram("C_vext", [T, 1040], BF16)
    mq_dram = P.dram("C_mq", [2, NC * T], BF16)
    attn_dram = P.dram("C_attn", [D, T], BF16)

    xn = al.sb("C_xn", [128, NC * T], BF16)
    st_ln = ExitStack()
    ln_toks = emit_ln(P, al, "C_ln", xn, consts, pp, x_dram=x_in, substack=st_ln)
    b_q, tbq = load_bias(P, al, "C_bq", W["bq"], 1024)
    b_k, tbk = load_bias(P, al, "C_bk", W["bk"], 1024)
    b_v, tbv = load_bias(P, al, "C_bv", W["bv"], 1024)
    slr = al.sb("C_slr", [1, 16], F32)
    t_sl = P.dma("gpsimd", slr[:, :], W["slopes"])
    k2m = al.sb("C_k2m", [2, 8], F32)
    nkx = al.sb("C_nkx", [2, 8], F32)
    st_ln.close()
    qf = al.sb("C_qf", [128, NC * T], BF16)

    slbc = al.sb("C_slbc", [128, 16], F32)
    offt = al.sb("C_offt", [128, 16], F32)
    ztab = al.sb("C_ztab", [128, 512], F32)
    ctab = al.sb("C_ctab", [128, 512], F32)
    ib, b, g = pp.acquire()
    P.waits("tensor", [g, t_sl])
    t = P.mm(b[0:128, 0:16], consts["onesf1"], slr[:, :], start=True, stop=True,
             inc="ts")
    P.wait("vector", t)
    ev = P.op("vector", "tensor_copy", slbc[:, :], b[0:128, 0:16], inc="vs",
              drain=True)
    pp.release(ib, ev)
    P.op("vector", "tensor_scalar", offt[:, :], slbc[:, :], 4095.0, 0.0,
         op0=ALU.mult, op1=ALU.max)
    for h in range(16):
        P.op("vector", "tensor_scalar", ztab[:, h * 32:(h + 1) * 32],
             consts["posc"], slbc[:, h:h + 1], offt[:, h:h + 1],
             op0=ALU.mult, op1=ALU.subtract)
    P.op("vector", "tensor_copy", ztab[0:1, 0:1], ztab[0:1, 0:1], inc="vs",
         drain=True)
    P.wait("scalar", P.now("vs"))
    ct_tok = P.op("scalar", "activation", ctab[:, :], ztab[:, :], AF.Exp,
                  inc="ss", drain=True)

    st_k = ExitStack()
    P.fence("gpsimd")
    w_k, t_k = load_weight(P, al, "C_wk", W["WkT"], 1024, 1024, stack=st_k)
    ktt = al.sb("C_ktt", [128, NC * TTW], BF16, stack=st_k)
    ksq = al.sb("C_ksq", [128, NC * TTW], BF16, stack=st_k)
    k2s8 = al.sb("C_k2s8", [2, NC * TTW], F32, stack=st_k)
    k2p8 = al.sb("C_k2p8", [2, 8], F32, stack=st_k)
    kst = al.sb("C_kst", [65, 2 * 2048], BF16, stack=st_k)
    k2p = al.sb("C_k2p", [2, 4], F32, stack=st_k)
    k2s = al.sb("C_k2s", [2, TTW], F32, stack=st_k)
    P.op("vector", "memset", k2m[:, :], 0.0)
    P.op("vector", "memset", kst[64:65, :], 1.0, inc="vs", drain=True)
    P.waits("tensor", t_k + ln_toks)
    P.waits("vector", tbk)
    kst_rel = [None, None]
    ksq_guard = None
    for tt in range(NTT):
        ev_last = None
        for oc in range(NC):
            ib, b, g = pp.acquire()
            P.wait("tensor", g)
            for kc in range(NC):
                P.mm(b[:, :], wslice(w_k, kc, oc, 1024), tile(xn, kc, tt),
                     start=(kc == 0), stop=(kc == NC - 1),
                     inc="ts" if kc == NC - 1 else None)
            t = P.now("ts")
            P.wait("vector", t)
            evk = P.op("vector", "tensor_scalar", ktt[:, oc * TTW:(oc + 1) * TTW],
                       b[:, :], b_k[:, oc:oc + 1], None, op0=ALU.add,
                       inc="vs", drain=(oc == NC - 1))
            pp.release(ib, evk)
            ev_last = evk
        P.wait("scalar", ev_last)
        if ksq_guard is not None:
            P.wait("scalar", ksq_guard)
        sqt = P.op("scalar", "activation", ksq[:, :], ktt[:, :], AF.Square,
                   inc="ss", drain=True)
        P.wait("tensor", sqt)
        pbs = []
        for oc in range(NC):
            ib2, b2, g2 = pp.acquire()
            P.wait("tensor", g2)
            t2 = P.mm(b2[0:2, :], consts["pair2"],
                      ksq[:, oc * TTW:(oc + 1) * TTW],
                      start=True, stop=True, inc="ts")
            pbs.append((ib2, b2, t2))
        ksq_guard = P.now("ts")
        P.wait("vector", ksq_guard)
        for oc, (ib2, b2, t2) in enumerate(pbs):
            cp = P.op("vector", "tensor_copy", k2s8[:, oc * TTW:(oc + 1) * TTW],
                      b2[0:2, :], inc="vs", drain=(oc == NC - 1))
            pp.release(ib2, cp)
        P.op("vector", "tensor_reduce", k2p8[:, :],
             k2s8[:, :].rearrange("p (g e) -> p g e", e=TTW), axis=AX.X,
             op=ALU.max)
        P.op("vector", "tensor_tensor", k2m[:, :], k2m[:, :], k2p8[:, :],
             op=ALU.max, inc="vs", drain=True)
        _stage_k(P, al, pp, consts, ktt, tt, kst, kst_rel, kext_dram)
    P.wait("scalar", P.now("vs"))
    P.op("scalar", "activation", nkx[:, :], k2m[:, :], AF.Sqrt, inc="ss",
         drain=True)
    P.wait("vector", P.now("ss"))
    P.op("vector", "tensor_scalar", nkx[:, :], nkx[:, :], -1.0, None,
         op0=ALU.mult, inc="vs", drain=True)
    nkx_tok = P.now("vs")
    st_k.close()

    st_q = ExitStack()
    P.fence("gpsimd")
    w_q, t_q = load_weight(P, al, "C_wq", W["WqT"], 1024, 1024, stack=st_q)
    P.waits("tensor", t_q)
    P.waits("vector", tbq + [nkx_tok])
    for tt in range(NTT):
        for oc in range(NC):
            ib, b, g = pp.acquire()
            P.wait("tensor", g)
            for kc in range(NC):
                P.mm(b[:, :], wslice(w_q, kc, oc, 1024), tile(xn, kc, tt),
                     start=(kc == 0), stop=(kc == NC - 1),
                     inc="ts" if kc == NC - 1 else None)
            t = P.now("ts")
            P.wait("vector", t)
            evq = P.op("vector", "tensor_scalar", tile(qf, oc, tt),
                       b[:, :], b_q[:, oc:oc + 1], None, op0=ALU.add,
                       inc="vs", drain=True)
            pp.release(ib, evq)
    _mq_pass(P, al, pp, consts, qf, nkx, mq_dram, st_q)
    st_q.close()

    st_v = ExitStack()
    P.fence("gpsimd")
    w_v, t_v = load_weight(P, al, "C_wv", W["WvT"], 1024, 1024, stack=st_v)
    vtt = al.sb("C_vtt", [128, NC * TTW], BF16, stack=st_v)
    vtmp = al.sb("C_vtmp", [64, 16 * 128], BF16, stack=st_v)
    vst = al.sb("C_vst", [128, 2 * 1040], BF16, stack=st_v)
    P.waits("tensor", t_v)
    P.waits("vector", tbv)
    P.wait("vector", ct_tok)
    vst_rel = [None, None]
    vtmp_rel = [None, None]
    for tt in range(NTT):
        for oc in range(NC):
            ib, b, g = pp.acquire()
            P.wait("tensor", g)
            for kc in range(NC):
                P.mm(b[:, :], wslice(w_v, kc, oc, 1024), tile(xn, kc, tt),
                     start=(kc == 0), stop=(kc == NC - 1),
                     inc="ts" if kc == NC - 1 else None)
            t = P.now("ts")
            P.wait("vector", t)
            evv = P.op("vector", "tensor_scalar", vtt[:, oc * TTW:(oc + 1) * TTW],
                       b[:, :], b_v[:, oc:oc + 1], None, op0=ALU.add,
                       inc="vs", drain=True)
            pp.release(ib, evv)
        for s4 in range(4):
            kc = tt * 4 + s4
            j = kc % 2
            if vst_rel[j] is not None:
                P.wait("vector", vst_rel[j])
            for h in range(16):
                c, p = h // 2, h % 2
                jj = h % 2
                if vtmp_rel[jj] is not None:
                    P.wait("vector", vtmp_rel[jj])
                cp = P.op("vector", "tensor_copy",
                          vtmp[0:64, jj * 128:(jj + 1) * 128],
                          vtt[p * 64:p * 64 + 64,
                              c * TTW + s4 * 128: c * TTW + (s4 + 1) * 128],
                          inc="vs", drain=True)
                ib2, b2, g2 = pp.acquire()
                P.waits("tensor", [g2, cp])
                bb2 = b2.bitcast(BF16)
                tr = P.op("tensor", "transpose", bb2[0:128, 0:64],
                          vtmp[0:64, jj * 128:(jj + 1) * 128],
                          consts["ident"][0:64, 0:64], inc="ts")
                vtmp_rel[jj] = tr
                P.wait("vector", tr)
                P.op("vector", "tensor_scalar",
                     vst[:, j * 1040 + h * 65: j * 1040 + h * 65 + 64],
                     bb2[0:128, 0:64], ctab[:, h * 32 + kc: h * 32 + kc + 1],
                     None, op0=ALU.mult)
                ev3 = P.op("vector", "tensor_copy",
                           vst[:, j * 1040 + h * 65 + 64: j * 1040 + h * 65 + 65],
                           ctab[:, h * 32 + kc: h * 32 + kc + 1],
                           inc="vs", drain=True)
                pp.release(ib2, ev3)
            cpt = P.now("vs")
            P.wait("gpsimd", cpt)
            st = P.dma("gpsimd", vext_dram[kc * 128:(kc + 1) * 128, :],
                       vst[:, j * 1040:(j + 1) * 1040])
            vst_rel[j] = st
    st_v.close()

    emit_attn_win(P, al, pp, consts, qf, kext_dram, vext_dram, mq_dram,
                  attn_dram, win_kcs_C)

    st_wo = ExitStack()
    P.fence("gpsimd")
    P.fence("sync")
    w_o, t_o = load_weight(P, al, "C_wo", W["WoT"], 1024, 1024, stack=st_wo)
    b_o, tbo = load_bias(P, al, "C_bo", W["bo"], 1024, stack=st_wo)
    att = al.sb("C_att", [128, 2 * NC * TTW], BF16, stack=st_wo)
    xr = al.sb("C_xr", [128, 4 * TTW], F32, stack=st_wo)
    yt = al.sb("C_yt", [128, 4 * TTW], F32, stack=st_wo)
    P.waits("tensor", t_o)
    P.waits("vector", tbo)
    att_rel = [None, None]
    rel = {"xr": [None] * 4, "yt": [None] * 4}
    out_toks = []
    gp_done = P.dma_all_done("gpsimd")
    P.waits("gpsimd", gp_done)
    k4 = [0]
    for tt in range(NTT):
        ia = tt % 2
        if att_rel[ia] is not None:
            P.wait("gpsimd", att_rel[ia])
        src_ap = bass.AP(attn_dram.tensor, tt * TTW,
                         [[T, 128], [128 * T, NC], [1, TTW]])
        la = P.dma("gpsimd", att[:, ia * NC * TTW:(ia + 1) * NC * TTW], src_ap)
        P.wait("tensor", la)
        for oc in range(NC):
            ib, b, g = pp.acquire()
            P.wait("tensor", g)
            for kc in range(NC):
                P.mm(b[:, :], wslice(w_o, kc, oc, 1024),
                     att[:, ia * NC * TTW + kc * TTW: ia * NC * TTW + (kc + 1) * TTW],
                     start=(kc == 0), stop=(kc == NC - 1),
                     inc="ts" if kc == NC - 1 else None)
            t = P.now("ts")
            i = k4[0] % 4
            k4[0] += 1
            if rel["xr"][i] is not None:
                P.wait("sync", rel["xr"][i])
            lt = P.dma("sync", xr[:, i * TTW:(i + 1) * TTW],
                       x_in[oc * 128:(oc + 1) * 128, tt * TTW:(tt + 1) * TTW])
            P.waits("vector", [t, lt])
            if rel["yt"][i] is not None:
                P.wait("vector", rel["yt"][i])
            P.op("vector", "tensor_scalar", yt[:, i * TTW:(i + 1) * TTW],
                 b[:, :], b_o[:, oc:oc + 1], None, op0=ALU.add)
            vt = P.op("vector", "tensor_tensor", yt[:, i * TTW:(i + 1) * TTW],
                      yt[:, i * TTW:(i + 1) * TTW], xr[:, i * TTW:(i + 1) * TTW],
                      op=ALU.add, inc="vs", drain=True)
            pp.release(ib, vt)
            rel["xr"][i] = vt
            if oc == NC - 1:
                att_rel[ia] = P.now("ts")
            P.wait("sync", vt)
            ot = P.dma("sync",
                       x_out[oc * 128:(oc + 1) * 128, tt * TTW:(tt + 1) * TTW],
                       yt[:, i * TTW:(i + 1) * TTW])
            rel["yt"][i] = ot
            out_toks.append(ot)
    st_wo.close()
    return out_toks


def prep_C_weights(inp):
    W = {}
    W["WqT"] = _np(np.asarray(inp["L0_attn_b_Wq"]).T)
    W["WkT"] = _np(np.asarray(inp["L0_attn_b_Wk"]).T)
    W["WvT"] = _np(np.asarray(inp["L0_attn_b_Wv"]).T)
    W["WoT"] = _np(np.asarray(inp["L0_attn_b_Wo"]).T)
    W["bq"] = _bvec(inp["L0_attn_b_bq"])
    W["bk"] = _bvec(inp["L0_attn_b_bk"])
    W["bv"] = _bvec(inp["L0_attn_b_bv"])
    W["bo"] = _bvec(inp["L0_attn_b_bo"])
    W["slopes"] = _np(inp["L0_attn_b_slopes"]).reshape(1, 16)
    return W



_MASKS = {}
_MASK_BY_KEY = {}
_REL = (np.arange(512)[None, :] - np.arange(128)[:, None])


def _mask_name(kind, w, d):
    imj = 128 * d + _REL
    if kind == "D":
        m = (imj >= 0) & (imj <= 128)
    else:
        m = (imj >= 0) & (imj < w)
    if m.all():
        return None
    if not m.any():
        return "SKIP"
    key = m.tobytes()
    if key not in _MASK_BY_KEY:
        import ml_dtypes
        nm = f"mk{len(_MASK_BY_KEY)}"
        _MASK_BY_KEY[key] = nm
        _MASKS[nm] = m.astype(ml_dtypes.bfloat16)
    return _MASK_BY_KEY[key]


def win_kcs_D(h, qg):
    lo = max(0, 4 * qg - 1)
    out = []
    for kc in range(lo, 4 * qg + 4):
        nm = _mask_name("D", 128, 4 * qg - kc)
        if nm != "SKIP":
            out.append((kc, nm))
    return out


def win_kcs_F(h, qg):
    w = max(4, 1024 >> h)
    lo = max(0, (qg * 512 - w + 1) // 128)
    out = []
    for kc in range(lo, 4 * qg + 4):
        nm = _mask_name("F", w, 4 * qg - kc)
        if nm != "SKIP":
            out.append((kc, nm))
    return out


def _win_masks():
    for qg in range(8):
        for h in range(16):
            win_kcs_D(h, qg)
            win_kcs_F(h, qg)
    return dict(_MASKS)


def _mask_list(kcs_fn):
    names = set()
    for qg in range(8):
        for h in range(16):
            names |= {nm for (_, nm) in kcs_fn(h, qg) if nm}
    return sorted(names)


def prep_masks(kcs_fn):
    names = _mask_list(kcs_fn)
    arrs = [np.asarray(_MASKS[nm], dtype=np.float32) for nm in names]
    return np.ascontiguousarray(np.concatenate(arrs, axis=1))


def emit_attn_win(P, al, pp, consts, qf, kext_dram, vext_dram, mq_dram,
                  attn_dram, kcs_fn, gate_fn=None, masks_dram=None):
    st = ExitStack()
    P.fence()
    mnames = _mask_list(kcs_fn)
    midx = {nm: i for i, nm in enumerate(mnames)}
    if mnames:
        marena = al.sb("aw_masks", [128, len(mnames) * TTW], BF16, stack=st)
        mt = P.dma("gpsimd", marena[:, :], masks_dram)
        P.waits("vector", [mt])

    def mslice(nm):
        i = midx[nm]
        return marena[:, i * TTW:(i + 1) * TTW]

    NSL = 6
    NPT = 8
    KW = 512
    VW = 260
    kslab = al.sb("aw_kslab", [65, NSL * KW], BF16, stack=st)
    vslab = al.sb("aw_vslab", [128, NSL * VW], BF16, stack=st)
    qstg = al.sb("aw_qstg", [65, 4 * TTW], BF16, stack=st)
    ptile = al.sb("aw_pt", [128, NPT * TTW], BF16, stack=st)
    rtmp4 = al.sb("aw_rt", [4, TTW], F32, stack=st)
    rb4 = al.sb("aw_rb", [4, TTW], BF16, stack=st)
    dtmp = al.sb("aw_dt", [1, 4 * TTW], F32, stack=st)
    bcs = al.sb("aw_bcs", [64, 2 * TTW], BF16, stack=st)
    ostg = al.sb("aw_ostg", [64, 4 * TTW], BF16, stack=st)
    P.waits("tensor", [pp.rel[i] for i in range(8)] + [P.now("vs"), P.now("ss")])
    av_banks = pp.tiles[0:4]
    s_banks = pp.tiles[4:7]
    bc_bank = pp.tiles[7]
    kv_rel = [None] * NSL
    slab_key = [None] * NSL
    slab_tok = [None] * NSL
    q_rel = None
    pt_rel = [None] * NPT
    o_rel = [None] * 4
    bc_rel = None
    bcs_rel = [None, None]
    s_rel = [None] * 3
    sidx = [0]
    ptidx = [0]
    ocnt = [0]

    pend_by_kc = {}

    def emit_av(it):
        (j, h, jj, pt, rdy, first, last, kc0) = it
        P.wait("tensor", rdy)
        t2 = P.mm(av_banks[j][0:65, :],
                  vslab[:, jj * VW + (h % 4) * 65: jj * VW + (h % 4 + 1) * 65],
                  ptile[:, pt * TTW:(pt + 1) * TTW],
                  start=first, stop=last, inc="ts")
        pt_rel[pt] = t2
        pend_by_kc[kc0] -= 1
        if pend_by_kc[kc0] == 0:
            kv_rel[kc0 % NSL] = t2

    def ensure_slab(shg, kc):
        jj = kc % NSL
        if slab_key[jj] != (shg, kc):
            if slab_key[jj] is not None:
                P.wait("gpsimd", kv_rel[jj])
            lk = P.dma("gpsimd", kslab[:, jj * KW:(jj + 1) * KW],
                       kext_dram[0:65,
                                 kc * 2048 + shg * 512: kc * 2048 + (shg + 1) * 512])
            lv = P.dma("gpsimd", vslab[:, jj * VW:(jj + 1) * VW],
                       vext_dram[kc * 128:(kc + 1) * 128,
                                 shg * 260:(shg + 1) * 260])
            slab_key[jj] = (shg, kc)
            slab_tok[jj] = [lk, lv]

    passes = [(hg, qg) for hg in range(4) for qg in range(NTT)]
    for pidx, (hg, qg) in enumerate(passes):
        if True:
            heads = [4 * hg + j for j in range(4)]
            plans = {h: kcs_fn(h, qg) for h in heads}
            union = sorted({kc for h in heads for (kc, _) in plans[h]})
            if q_rel is not None:
                P.waits("vector", [q_rel])
                P.waits("gpsimd", [q_rel])
            for j, h in enumerate(heads):
                c, p = h // 2, h % 2
                P.op("vector", "tensor_copy",
                     qstg[0:64, j * TTW:(j + 1) * TTW],
                     qf[p * 64:p * 64 + 64, c * T + qg * TTW: c * T + (qg + 1) * TTW],
                     inc="vs" if j == 3 else None, drain=(j == 3))
            qc_tok = P.now("vs")
            mts = []
            for j, h in enumerate(heads):
                c, p = h // 2, h % 2
                mts.append(P.dma("gpsimd", qstg[64:65, j * TTW:(j + 1) * TTW],
                                 mq_dram[p:p + 1,
                                         c * T + qg * TTW: c * T + (qg + 1) * TTW]))
            P.waits("tensor", mts + [qc_tok])
            started = {h: False for h in heads}
            nkc = {h: len(plans[h]) for h in heads}
            done_cnt = {h: 0 for h in heads}
            pend = []
            prev_kc = None
            for kc in union:
                jj = kc % NSL
                ensure_slab(hg, kc)
                if slab_tok[jj] is not None:
                    P.waits("tensor", slab_tok[jj])
                    slab_tok[jj] = None
                kc_items = []
                for j, h in enumerate(heads):
                    ms = [m for (kk, m) in plans[h] if kk == kc]
                    if not ms:
                        continue
                    mname = ms[0]
                    sj = sidx[0] % 3
                    sidx[0] += 1
                    sb = s_banks[sj]
                    if s_rel[sj] is not None:
                        P.wait("tensor", s_rel[sj])
                    t1 = P.mm(sb[:, :],
                              kslab[:, jj * KW + (h % 4) * 128:
                                       jj * KW + (h % 4 + 1) * 128],
                              qstg[0:65, j * TTW:(j + 1) * TTW],
                              start=True, stop=True, inc="ts")
                    pt = ptidx[0] % NPT
                    ptidx[0] += 1
                    P.wait("scalar", t1)
                    if pt_rel[pt] is not None:
                        P.wait("scalar", pt_rel[pt])
                    ex = P.op("scalar", "activation",
                              ptile[:, pt * TTW:(pt + 1) * TTW], sb[:, :],
                              AF.Exp, scale=0.125, inc="ss", pre_drain=False)
                    s_rel[sj] = ex
                    kc_items.append((j, h, jj, pt, mname))
                if not kc_items:
                    continue
                bt = P.op("scalar", "drain", inc="ss", pre_drain=False)
                any_mask = any(m is not None for (_, _, _, _, m) in kc_items)
                if any_mask:
                    P.wait("vector", bt)
                    for (j, h, jj2, pt, mname) in kc_items:
                        if mname is None:
                            continue
                        P.op("vector", "tensor_tensor",
                             ptile[:, pt * TTW:(pt + 1) * TTW],
                             ptile[:, pt * TTW:(pt + 1) * TTW],
                             mslice(mname), op=ALU.mult, pre_drain=False)
                    bt = P.op("vector", "drain", inc="vs", pre_drain=False)
                pend_by_kc[kc] = len(kc_items)
                for (j, h, jj2, pt, mname) in kc_items:
                    done_cnt[h] += 1
                    pend.append((j, h, jj2, pt, bt, not started[h],
                                 done_cnt[h] == nkc[h], kc))
                    started[h] = True
                while pend and (len(pend) > 4 or pend[0][7] <= kc - 2):
                    emit_av(pend.pop(0))
            while pend:
                emit_av(pend.pop(0))
            q_rel = P.now("ts")
            for jj2 in range(NSL):
                kv_rel[jj2] = q_rel
            if pidx + 1 < len(passes):
                nhg, nqg = passes[pidx + 1]
                nheads = [4 * nhg + j for j in range(4)]
                nunion = sorted({kc for h2 in nheads
                                 for (kc, _) in kcs_fn(h2, nqg)})
                for kc2 in nunion[:4]:
                    ensure_slab(nhg, kc2)
            P.waits("scalar", [q_rel])
            for j in range(4):
                P.op("scalar", "activation", dtmp[0:1, j * TTW:(j + 1) * TTW],
                     av_banks[j][64:65, :], AF.Copy, bias=1e-30,
                     inc="ss" if j == 3 else None)
            dt_tok = P.now("ss")
            sb0 = s_banks[0]
            P.waits("tensor", [dt_tok] + ([s_rel[0]] if s_rel[0] else []))
            for j in range(4):
                P.mm(sb0[0:4, :], consts["oh4"][0:1, j * 4:(j + 1) * 4],
                     dtmp[0:1, j * TTW:(j + 1) * TTW],
                     start=(j == 0), stop=(j == 3),
                     inc="ts" if j == 3 else None)
            gat_tok = P.now("ts")
            P.wait("vector", gat_tok)
            P.op("vector", "reciprocal", rtmp4[:, :], sb0[0:4, :], inc="vs")
            s_rel[0] = P.now("vs")
            if gate_fn is not None:
                gate_fn(P, pp, hg, qg, rtmp4, bc_bank, bc_rel)
            P.op("vector", "tensor_copy", rb4[:, :], rtmp4[:, :], inc="vs")
            rb_tok = P.now("vs")
            for j, h in enumerate(heads):
                av = av_banks[j]
                if bc_rel is not None:
                    P.wait("tensor", bc_rel)
                P.wait("tensor", rb_tok)
                t3 = P.mm(bc_bank[0:64, :],
                          consts["selb4"][:, j * 64:(j + 1) * 64], rb4[:, :],
                          start=True, stop=True, inc="ts")
                P.wait("scalar", t3)
                ic = j % 2
                if bcs_rel[ic] is not None:
                    P.wait("scalar", bcs_rel[ic])
                P.op("scalar", "activation", bcs[:, ic * TTW:(ic + 1) * TTW],
                     bc_bank[0:64, :], AF.Copy, inc="ss")
                bc_rel = P.now("ss")
                io = ocnt[0] % 4
                ocnt[0] += 1
                if o_rel[io] is not None:
                    P.wait("vector", o_rel[io])
                P.wait("vector", bc_rel)
                ov = P.op("vector", "tensor_tensor",
                          ostg[:, io * TTW:(io + 1) * TTW], av[0:64, :],
                          bcs[:, ic * TTW:(ic + 1) * TTW], op=ALU.mult,
                          inc="vs")
                bcs_rel[ic] = ov
                P.wait("gpsimd", ov)
                stt = P.dma("gpsimd",
                            attn_dram[h * 64:(h + 1) * 64,
                                      qg * TTW:(qg + 1) * TTW],
                            ostg[:, io * TTW:(io + 1) * TTW])
                o_rel[io] = stt
            P.waits("tensor", [P.now("vs")])
    st.close()
    for i in range(8):
        pp.rel[i] = P.now("vs")


def win_kcs_C(h, qg):
    return [(kc, None) for kc in range(32)]


def _stage_k(P, al, pp, consts, ktt, tt, kst, kst_rel, kext_dram):
    for s4 in range(4):
        kc = tt * 4 + s4
        j = kc % 2
        if kst_rel[j] is not None:
            P.wait("vector", kst_rel[j])
        for h in range(16):
            c, p = h // 2, h % 2
            P.op("vector", "tensor_copy",
                 kst[0:64, j * 2048 + h * 128: j * 2048 + (h + 1) * 128],
                 ktt[p * 64:p * 64 + 64,
                     c * TTW + s4 * 128: c * TTW + (s4 + 1) * 128],
                 inc="vs" if h == 15 else None, drain=(h == 15))
        cpt = P.now("vs")
        P.wait("gpsimd", cpt)
        stt = P.dma("gpsimd", kext_dram[0:65, kc * 2048:(kc + 1) * 2048],
                    kst[0:65, j * 2048:(j + 1) * 2048])
        kst_rel[j] = stt


def _stage_v(P, al, pp, consts, vtt, tt, vst, vst_rel, vtmp, vtmp_rel,
             vext_dram, ctab=None):
    for s4 in range(4):
        kc = tt * 4 + s4
        j = kc % 2
        if vtmp_rel[0] is not None:
            P.wait("vector", vtmp_rel[0])
        for h in range(16):
            c, p = h // 2, h % 2
            P.op("vector", "tensor_copy",
                 vtmp[0:64, h * 128:(h + 1) * 128],
                 vtt[p * 64:p * 64 + 64,
                     c * TTW + s4 * 128: c * TTW + (s4 + 1) * 128],
                 inc="vs" if h == 15 else None, drain=(h == 15))
        cp_tok = P.now("vs")
        P.wait("tensor", cp_tok)
        if vst_rel[j] is not None:
            P.wait("vector", vst_rel[j])
        trs = []
        for h in range(16):
            ib2, b2, g2 = pp.acquire()
            P.wait("tensor", g2)
            bb2 = b2.bitcast(BF16)
            tr = P.op("tensor", "transpose", bb2[0:128, 0:64],
                      vtmp[0:64, h * 128:(h + 1) * 128],
                      consts["ident"][0:64, 0:64], inc="ts")
            P.wait("vector", tr)
            if ctab is not None:
                P.op("vector", "tensor_scalar",
                     vst[:, j * 1040 + h * 65: j * 1040 + h * 65 + 64],
                     bb2[0:128, 0:64], ctab[:, h * 32 + kc: h * 32 + kc + 1],
                     None, op0=ALU.mult)
                ev3 = P.op("vector", "tensor_copy",
                           vst[:, j * 1040 + h * 65 + 64: j * 1040 + h * 65 + 65],
                           ctab[:, h * 32 + kc: h * 32 + kc + 1],
                           inc="vs", drain=(h == 15))
            else:
                ev3 = P.op("vector", "tensor_copy",
                           vst[:, j * 1040 + h * 65: j * 1040 + h * 65 + 64],
                           bb2[0:128, 0:64],
                           inc="vs", drain=(h == 15))
            pp.release(ib2, ev3)
        vtmp_rel[0] = P.now("vs")
        cpt = P.now("vs")
        P.wait("gpsimd", cpt)
        stt = P.dma("gpsimd", vext_dram[kc * 128:(kc + 1) * 128, :],
                    vst[:, j * 1040:(j + 1) * 1040])
        vst_rel[j] = stt


def _knorm_update(P, al, pp, consts, src_tile, k2m, oc, ksq, ksq_rel, k2p, k2s,
                  evk):
    i = oc % len(ksq_rel)
    P.wait("scalar", evk)
    if ksq_rel[i] is not None:
        P.wait("scalar", ksq_rel[i])
    sqt = P.op("scalar", "activation", ksq[:, i * TTW:(i + 1) * TTW],
               src_tile, AF.Square, inc="ss", drain=True)
    ib2, b2, g2 = pp.acquire()
    P.waits("tensor", [g2, sqt])
    t2 = P.mm(b2[0:2, :], consts["pair2"], ksq[:, i * TTW:(i + 1) * TTW],
              start=True, stop=True, inc="ts")
    ksq_rel[i] = t2
    P.wait("vector", t2)
    P.op("vector", "tensor_scalar", k2s[:, :], b2[0:2, :], 0.0, -1e30,
         op0=ALU.add, op1=ALU.max, accum_out=k2p[:, 1:2])
    ev2 = P.op("vector", "tensor_tensor", k2m[:, oc:oc + 1],
               k2m[:, oc:oc + 1], k2p[:, 1:2], op=ALU.max, inc="vs", drain=True)
    pp.release(ib2, ev2)
    return ev2


def _mq_pass(P, al, pp, consts, qf, nkx, mq_dram, st):
    qsq = al.sb("mq_qsq", [128, T], BF16, stack=st)
    qn8 = al.sb("mq_qn8", [2, T], F32, stack=st)
    mqt8 = al.sb("mq_mqt8", [2, T], BF16, stack=st)
    P.waits("scalar", [P.now("vs")])
    P.waits("tensor", [P.now("vs")])
    P.waits("vector", [P.now("vs"), P.now("ss")])
    sq_guard = None
    mq_rel = None
    for oc in range(NC):
        if sq_guard is not None:
            P.wait("scalar", sq_guard)
        sqt = P.op("scalar", "activation", qsq[:, :], chunk(qf, oc), AF.Square,
                   inc="ss", drain=True)
        P.wait("tensor", sqt)
        pbs = []
        for tt in range(NTT):
            ib2, b2, g2 = pp.acquire()
            P.wait("tensor", g2)
            t2 = P.mm(b2[0:2, :], consts["pair2"],
                      qsq[:, tt * TTW:(tt + 1) * TTW],
                      start=True, stop=True, inc="ts")
            pbs.append((ib2, b2))
        sq_guard = P.now("ts")
        P.wait("vector", sq_guard)
        if mq_rel is not None:
            P.wait("vector", mq_rel)
        for tt, (ib2, b2) in enumerate(pbs):
            cp = P.op("vector", "tensor_copy", qn8[:, tt * TTW:(tt + 1) * TTW],
                      b2[0:2, :], inc="vs", drain=(tt == NTT - 1))
            pp.release(ib2, cp)
        P.wait("scalar", P.now("vs"))
        st2 = P.op("scalar", "activation", qn8[:, :], qn8[:, :], AF.Sqrt,
                   inc="ss", drain=True)
        P.wait("vector", st2)
        mv = P.op("vector", "tensor_scalar", mqt8[:, :], qn8[:, :],
                  nkx[:, oc:oc + 1], None, op0=ALU.mult, inc="vs", drain=True)
        P.wait("gpsimd", mv)
        stt = P.dma("gpsimd", mq_dram[0:2, oc * T:(oc + 1) * T], mqt8[:, :])
        mq_rel = stt


def emit_D(P, al, consts, x_in, x_out, W):
    pp = PsumPool(P, [al.ps(f"ppD{i}", [128, 512]) for i in range(8)])
    nc = P.nc
    kext_dram = P.dram("D_kext", [65, 32 * 2048], BF16)
    vext_dram = P.dram("D_vext", [T, 1040], BF16)
    mq_dram = P.dram("D_mq", [2, NC * T], BF16)
    attn_dram = P.dram("D_attn", [D, T], BF16)
    xn_dram = P.dram("D_xn", [D, T], BF16)
    h_dram = P.dram("D_h", [D, T], BF16)

    k2m = al.sb("D_k2m", [2, 8], F32)
    nkx = al.sb("D_nkx", [2, 8], F32)
    g1b, tg1 = load_bias(P, al, "D_g1", W["ln1_g"], 1024)
    b1b, tb1 = load_bias(P, al, "D_b1", W["ln1_b"], 1024)
    g2b, tg2 = load_bias(P, al, "D_g2", W["ln2_g"], 1024)
    b2b, tb2 = load_bias(P, al, "D_b2", W["ln2_b"], 1024)

    st_qf = ExitStack()
    qf = al.sb("D_qf", [128, NC * T], BF16, stack=st_qf)
    st_x = ExitStack()
    X = al.sb("D_X", [128, NC * T], BF16, stack=st_x)

    st_ln = ExitStack()
    ln_toks = emit_ln(P, al, "D_ln", X, consts, pp, x_dram=x_in, substack=st_ln,
                      narrow=True)
    st_ln.close()
    sp_toks = []
    for c in range(NC):
        P.wait("gpsimd", ln_toks[c])
        sp_toks.append(P.dma("gpsimd", xn_dram[c * 128:(c + 1) * 128, :],
                             chunk(X, c)))
    st_ln1 = ExitStack()
    P.waits("vector", tg1 + tb1 + sp_toks)
    ln1_toks = emit_ln(P, al, "D_ln1", X, consts, pp, src_sbuf=lambda c: chunk(X, c),
                       src_toks=ln_toks + sp_toks,
                       gain=lambda c: g1b[:, c:c + 1], bias=lambda c: b1b[:, c:c + 1],
                       substack=st_ln1, narrow=True)
    st_ln1.close()

    st_cs = ExitStack()
    P.fence()
    cosf = al.sb("D_cosf", [128, T], BF16, stack=st_cs)
    sinf = al.sb("D_sinf", [128, T], BF16, stack=st_cs)
    t_cs = [P.dma("gpsimd", cosf[:, :], W["cosf"]),
            P.dma("gpsimd", sinf[:, :], W["sinf"])]
    st_k = ExitStack()
    P.fence()
    w_k, t_k = load_weight(P, al, "D_wk", W["wk"], 1024, 1024, stack=st_k)
    ktt = al.sb("D_ktt", [128, NC * TTW], BF16, stack=st_k)
    ksq = al.sb("D_ksq", [128, 4 * TTW], BF16, stack=st_k)
    kst = al.sb("D_kst", [65, 2 * 2048], BF16, stack=st_k)
    k2p = al.sb("D_k2p", [2, 4], F32, stack=st_k)
    k2s = al.sb("D_k2s", [2, TTW], F32, stack=st_k)
    rtile = al.sb("D_rtile", [128, 4 * TTW], BF16, stack=st_k)
    P.op("vector", "memset", k2m[:, :], 0.0)
    P.op("vector", "memset", kst[64:65, :], 1.0, inc="vs", drain=True)
    P.waits("tensor", t_k + ln1_toks + t_cs)
    P.waits("vector", t_cs)
    kst_rel = [None, None]
    ksq_rel = [None] * 4
    rt_rel = [None] * 4

    def rope_tile(dst, b_proj, oc, tt, ib):
        i = (tt * NC + oc) % 4
        ev0 = P.op("vector", "tensor_copy", dst, b_proj[:, :], inc="vs",
                   drain=True)
        ib3, b3, g3 = pp.acquire()
        P.waits("tensor", [g3, ev0])
        tr = P.mm(b3[:, :], consts["r2t"], dst, start=True, stop=True, inc="ts")
        P.wait("vector", tr)
        if rt_rel[i] is not None:
            P.wait("vector", rt_rel[i])
        P.op("vector", "tensor_tensor", rtile[:, i * TTW:(i + 1) * TTW],
             b3[:, :], sinf[:, tt * TTW:(tt + 1) * TTW], op=ALU.mult)
        P.op("vector", "tensor_tensor", dst, dst,
             cosf[:, tt * TTW:(tt + 1) * TTW], op=ALU.mult)
        rv = P.op("vector", "tensor_tensor", dst, dst,
                  rtile[:, i * TTW:(i + 1) * TTW], op=ALU.add, inc="vs",
                  drain=True)
        rt_rel[i] = rv
        pp.release(ib3, rv)
        pp.release(ib, ev0)
        return rv

    for tt in range(NTT):
        for oc in range(NC):
            ib, b, g = pp.acquire()
            P.wait("tensor", g)
            for kc in range(NC):
                P.mm(b[:, :], wslice(w_k, kc, oc, 1024), tile(X, kc, tt),
                     start=(kc == 0), stop=(kc == NC - 1),
                     inc="ts" if kc == NC - 1 else None)
            t = P.now("ts")
            P.wait("vector", t)
            rv = rope_tile(ktt[:, oc * TTW:(oc + 1) * TTW], b, oc, tt, ib)
            _knorm_update(P, al, pp, consts, ktt[:, oc * TTW:(oc + 1) * TTW],
                          k2m, oc, ksq, ksq_rel, k2p, k2s, rv)
        _stage_k(P, al, pp, consts, ktt, tt, kst, kst_rel, kext_dram)
    P.wait("scalar", P.now("vs"))
    P.op("scalar", "activation", nkx[:, :], k2m[:, :], AF.Sqrt, inc="ss",
         drain=True)
    P.wait("vector", P.now("ss"))
    P.op("vector", "tensor_scalar", nkx[:, :], nkx[:, :], -1.0, None,
         op0=ALU.mult, inc="vs", drain=True)
    st_k.close()

    st_q = ExitStack()
    P.fence()
    w_q, t_q = load_weight(P, al, "D_wq", W["wq"], 1024, 1024, stack=st_q)
    rtile2 = al.sb("D_rtile2", [128, 4 * TTW], BF16, stack=st_q)
    P.waits("tensor", t_q)
    rt2_rel = [None] * 4
    for tt in range(NTT):
        for oc in range(NC):
            ib, b, g = pp.acquire()
            P.wait("tensor", g)
            for kc in range(NC):
                P.mm(b[:, :], wslice(w_q, kc, oc, 1024), tile(X, kc, tt),
                     start=(kc == 0), stop=(kc == NC - 1),
                     inc="ts" if kc == NC - 1 else None)
            t = P.now("ts")
            P.wait("vector", t)
            dst = tile(qf, oc, tt)
            i = (tt * NC + oc) % 4
            ev0 = P.op("vector", "tensor_copy", dst, b[:, :], inc="vs",
                       drain=True)
            ib3, b3, g3 = pp.acquire()
            P.waits("tensor", [g3, ev0])
            tr = P.mm(b3[:, :], consts["r2t"], dst, start=True, stop=True,
                      inc="ts")
            P.wait("vector", tr)
            if rt2_rel[i] is not None:
                P.wait("vector", rt2_rel[i])
            P.op("vector", "tensor_tensor", rtile2[:, i * TTW:(i + 1) * TTW],
                 b3[:, :], sinf[:, tt * TTW:(tt + 1) * TTW],
                 op=ALU.mult)
            P.op("vector", "tensor_tensor", dst, dst,
                 cosf[:, tt * TTW:(tt + 1) * TTW], op=ALU.mult)
            rv = P.op("vector", "tensor_tensor", dst, dst,
                      rtile2[:, i * TTW:(i + 1) * TTW], op=ALU.add, inc="vs",
                      drain=True)
            rt2_rel[i] = rv
            pp.release(ib3, rv)
            pp.release(ib, ev0)
    _mq_pass(P, al, pp, consts, qf, nkx, mq_dram, st_q)
    st_q.close()
    st_cs.close()

    st_v = ExitStack()
    P.fence()
    w_v, t_v = load_weight(P, al, "D_wv", W["wv"], 1024, 1024, stack=st_v)
    vtt = al.sb("D_vtt", [128, NC * TTW], BF16, stack=st_v)
    vtmp = al.sb("D_vtmp", [64, 16 * 128], BF16, stack=st_v)
    vst = al.sb("D_vst", [128, 2 * 1040], BF16, stack=st_v)
    P.op("vector", "memset", vst[:, :], 1.0, inc="vs", drain=True)
    P.waits("tensor", t_v + [P.now("vs")])
    vst_rel = [None, None]
    vtmp_rel = [None, None]
    for tt in range(NTT):
        for oc in range(NC):
            ib, b, g = pp.acquire()
            P.wait("tensor", g)
            for kc in range(NC):
                P.mm(b[:, :], wslice(w_v, kc, oc, 1024), tile(X, kc, tt),
                     start=(kc == 0), stop=(kc == NC - 1),
                     inc="ts" if kc == NC - 1 else None)
            t = P.now("ts")
            P.wait("vector", t)
            evv = P.op("vector", "tensor_copy", vtt[:, oc * TTW:(oc + 1) * TTW],
                       b[:, :], inc="vs", drain=True)
            pp.release(ib, evv)
        _stage_v(P, al, pp, consts, vtt, tt, vst, vst_rel, vtmp, vtmp_rel,
                 vext_dram, ctab=None)
    st_v.close()
    st_x.close()

    emit_attn_win(P, al, pp, consts, qf, kext_dram, vext_dram, mq_dram,
                  attn_dram, win_kcs_D, masks_dram=W["masks"])
    st_qf.close()

    st_h = ExitStack()
    P.fence()
    hbuf = al.sb("D_hbuf", [128, NC * T], BF16, stack=st_h)
    st_wo = ExitStack()
    w_o, t_o = load_weight(P, al, "D_wo", W["wo"], 1024, 1024, stack=st_wo)
    att = al.sb("D_att", [128, 2 * NC * TTW], BF16, stack=st_wo)
    xnr = al.sb("D_xnr", [128, 4 * TTW], BF16, stack=st_wo)
    P.waits("tensor", t_o)
    att_rel = [None, None]
    xnr_rel = [None] * 4
    h_toks = []
    k4 = [0]
    for tt in range(NTT):
        ia = tt % 2
        if att_rel[ia] is not None:
            P.wait("gpsimd", att_rel[ia])
        src_ap = bass.AP(attn_dram.tensor, tt * TTW,
                         [[T, 128], [128 * T, NC], [1, TTW]])
        la = P.dma("gpsimd", att[:, ia * NC * TTW:(ia + 1) * NC * TTW], src_ap)
        P.wait("tensor", la)
        for oc in range(NC):
            ib, b, g = pp.acquire()
            P.wait("tensor", g)
            for kc in range(NC):
                P.mm(b[:, :], wslice(w_o, kc, oc, 1024),
                     att[:, ia * NC * TTW + kc * TTW: ia * NC * TTW + (kc + 1) * TTW],
                     start=(kc == 0), stop=(kc == NC - 1),
                     inc="ts" if kc == NC - 1 else None)
            t = P.now("ts")
            i = k4[0] % 4
            k4[0] += 1
            if xnr_rel[i] is not None:
                P.wait("sync", xnr_rel[i])
            lt = P.dma("sync", xnr[:, i * TTW:(i + 1) * TTW],
                       xn_dram[oc * 128:(oc + 1) * 128, tt * TTW:(tt + 1) * TTW])
            P.waits("vector", [t, lt])
            vt = P.op("vector", "tensor_tensor", tile(hbuf, oc, tt),
                      b[:, :], xnr[:, i * TTW:(i + 1) * TTW],
                      op=ALU.add, inc="vs", drain=True)
            pp.release(ib, vt)
            xnr_rel[i] = vt
            if oc == NC - 1:
                att_rel[ia] = P.now("ts")
            P.wait("gpsimd", vt)
            ht = P.dma("gpsimd",
                       h_dram[oc * 128:(oc + 1) * 128, tt * TTW:(tt + 1) * TTW],
                       tile(hbuf, oc, tt))
            h_toks.append(ht)
    st_wo.close()

    st_ln2 = ExitStack()
    P.fence()
    P.waits("vector", tg2 + tb2)
    ln2_toks = emit_ln(P, al, "D_ln2", hbuf, consts, pp,
                       src_sbuf=lambda c: chunk(hbuf, c),
                       src_toks=[P.now("vs")] + h_toks,
                       gain=lambda c: g2b[:, c:c + 1], bias=lambda c: b2b[:, c:c + 1],
                       substack=st_ln2, narrow=True)
    st_ln2.close()

    st_f = ExitStack()
    P.fence()
    w1a, t_w1 = load_weight(P, al, "D_w1f", W["w1"], 1024, 2816, stack=st_f)
    w3a, t_w3 = load_weight(P, al, "D_w3f", W["w3"], 1024, 2816, stack=st_f)
    w2a = al.sb("D_w2a", [128, 2 * 22 * 128], BF16, stack=st_f)
    z = al.sb("D_z", [128, 22 * TTW], BF16, stack=st_f)
    sil = al.sb("D_sil", [128, 2 * TTW], BF16, stack=st_f)
    xr = al.sb("D_xr", [128, 2 * TTW], F32, stack=st_f)
    hr = al.sb("D_hr", [128, 2 * TTW], BF16, stack=st_f)
    yt = al.sb("D_yt", [128, 2 * TTW], F32, stack=st_f)
    P.waits("tensor", ln2_toks + t_w1 + t_w3)
    out_toks = []
    w2_rel = [None, None]
    sil_rel = [None, None]
    z_rel = [None] * 22
    rel = {"xr": [None, None], "hr": [None, None], "yt": [None, None]}
    k5 = [0]
    for tt in range(NTT):
        for oc in range(22):
            ib, b, g = pp.acquire()
            P.wait("tensor", g)
            for kc in range(NC):
                P.mm(b[:, :], wslice(w1a, kc, oc, 2816),
                     tile(hbuf, kc, tt), start=(kc == 0), stop=(kc == NC - 1),
                     inc="ts" if kc == NC - 1 else None)
            t = P.now("ts")
            isl = oc % 2
            P.wait("scalar", t)
            if sil_rel[isl] is not None:
                P.wait("scalar", sil_rel[isl])
            sg = P.op("scalar", "activation", sil[:, isl * TTW:(isl + 1) * TTW],
                      b[:, :], AF.Silu, inc="ss", drain=True)
            pp.release(ib, sg)
            ib2, b2, g2 = pp.acquire()
            P.wait("tensor", g2)
            for kc in range(NC):
                P.mm(b2[:, :], wslice(w3a, kc, oc, 2816),
                     tile(hbuf, kc, tt), start=(kc == 0), stop=(kc == NC - 1),
                     inc="ts" if kc == NC - 1 else None)
            t2 = P.now("ts")
            P.waits("vector", [t2, sg])
            if z_rel[oc] is not None:
                P.wait("vector", z_rel[oc])
            zt = P.op("vector", "tensor_tensor", z[:, oc * TTW:(oc + 1) * TTW],
                      b2[:, :], sil[:, isl * TTW:(isl + 1) * TTW], op=ALU.mult,
                      inc="vs", drain=True)
            sil_rel[isl] = zt
            pp.release(ib2, zt)
        z_tok = P.now("vs")
        for oc in range(NC):
            iw2 = oc % 2
            if w2_rel[iw2] is not None:
                P.wait("gpsimd", w2_rel[iw2])
            tw2 = P.dma("gpsimd", w2a[:, iw2 * 22 * 128:(iw2 + 1) * 22 * 128],
                        bass.AP(W["w2"].tensor, oc * 128,
                                [[1024, 128], [128 * 1024, 22], [1, 128]]))
            ib, b, g = pp.acquire()
            P.waits("tensor", [g, z_tok, tw2])
            for kc in range(22):
                P.mm(b[:, :],
                     w2a[:, iw2 * 22 * 128 + kc * 128: iw2 * 22 * 128 + (kc + 1) * 128],
                     z[:, kc * TTW:(kc + 1) * TTW],
                     start=(kc == 0), stop=(kc == 21), inc="ts")
            t = P.now("ts")
            w2_rel[iw2] = t
            i = k5[0] % 2
            i2 = k5[0] % 2
            k5[0] += 1
            if rel["xr"][i2] is not None:
                P.wait("sync", rel["xr"][i2])
            lt = P.dma("sync", xr[:, i2 * TTW:(i2 + 1) * TTW],
                       x_in[oc * 128:(oc + 1) * 128, tt * TTW:(tt + 1) * TTW])
            if rel["hr"][i2] is not None:
                P.wait("sync", rel["hr"][i2])
            lh = P.dma("sync", hr[:, i2 * TTW:(i2 + 1) * TTW],
                       h_dram[oc * 128:(oc + 1) * 128, tt * TTW:(tt + 1) * TTW])
            P.waits("vector", [t, lt, lh])
            if rel["yt"][i] is not None:
                P.wait("vector", rel["yt"][i])
            P.op("vector", "tensor_tensor", yt[:, i * TTW:(i + 1) * TTW],
                 b[:, :], hr[:, i2 * TTW:(i2 + 1) * TTW], op=ALU.add)
            vt = P.op("vector", "tensor_tensor", yt[:, i * TTW:(i + 1) * TTW],
                      yt[:, i * TTW:(i + 1) * TTW], xr[:, i2 * TTW:(i2 + 1) * TTW],
                      op=ALU.add, inc="vs", drain=True)
            pp.release(ib, vt)
            rel["xr"][i2] = vt
            rel["hr"][i2] = vt
            if oc == NC - 1:
                for kk in range(22):
                    z_rel[kk] = P.now("ts")
            P.wait("sync", vt)
            ot = P.dma("sync",
                       x_out[oc * 128:(oc + 1) * 128, tt * TTW:(tt + 1) * TTW],
                       yt[:, i * TTW:(i + 1) * TTW])
            rel["yt"][i] = ot
            out_toks.append(ot)
    st_f.close()
    st_h.close()
    return out_toks


def prep_D_weights(inp):
    W = {}
    for k in ("wq", "wk", "wv", "wo", "w1", "w2", "w3"):
        W[k] = _np(inp["L1_attn_a_" + k])
    freqs = 1.0 / (10000.0 ** (np.arange(0, 64, 2, dtype=np.float64) / 64))
    t = np.arange(T, dtype=np.float64)
    ang = np.outer(freqs, t)
    j = (np.arange(128) % 64) // 2
    W["cosf"] = np.cos(ang)[j].astype(np.float32)
    W["sinf"] = np.sin(ang)[j].astype(np.float32)
    W["masks"] = prep_masks(win_kcs_D)
    W["ln1_g"] = _bvec(inp["L1_attn_a_ln1_g"])
    W["ln1_b"] = _bvec(inp["L1_attn_a_ln1_b"])
    W["ln2_g"] = _bvec(inp["L1_attn_a_ln2_g"])
    W["ln2_b"] = _bvec(inp["L1_attn_a_ln2_b"])
    return W


def emit_F(P, al, consts, x_in, x_out, W):
    pp = PsumPool(P, [al.ps(f"ppF{i}", [128, 512]) for i in range(8)])
    nc = P.nc
    kext_dram = P.dram("F_kext", [65, 32 * 2048], BF16)
    vext_dram = P.dram("F_vext", [T, 1040], BF16)
    mq_dram = P.dram("F_mq", [2, NC * T], BF16)
    attn_dram = P.dram("F_attn", [D, T], BF16)

    k2m = al.sb("F_k2m", [2, 8], F32)
    nkx = al.sb("F_nkx", [2, 8], F32)
    gT = al.sb("F_gT", [16, T], BF16)

    st_qf = ExitStack()
    qf = al.sb("F_qf", [128, NC * T], BF16, stack=st_qf)
    st_x = ExitStack()
    xn = al.sb("F_xn", [128, NC * T], BF16, stack=st_x)
    st_ln = ExitStack()
    ln_toks = emit_ln(P, al, "F_ln", xn, consts, pp, x_dram=x_in, substack=st_ln,
                      narrow=True)
    wg, t_wg = load_weight(P, al, "F_wg", W["Wgate"], 1024, 16)
    bgr = al.sb("F_bgr", [1, 16], F32)
    t_bg = P.dma("gpsimd", bgr[:, :], W["bgate"])
    b_qk, t_bqk = load_bias(P, al, "F_bqk", W["bqk"], 2048)
    st_ln.close()

    st_g = ExitStack()
    P.fence()
    bgbc = al.sb("F_bgbc", [128, 16], F32, stack=st_g)
    glg = al.sb("F_glg", [128, 16], F32, stack=st_g)
    gsc = al.sb("F_gsc", [128, 8], F32, stack=st_g)
    get = al.sb("F_get", [128, 16], F32, stack=st_g)
    gbf = al.sb("F_gbf", [128, 16], BF16, stack=st_g)
    ib, b, g = pp.acquire()
    P.waits("tensor", [g, t_bg])
    t = P.mm(b[0:128, 0:16], consts["onesf1"], bgr[:, :], start=True, stop=True,
             inc="ts")
    P.wait("vector", t)
    ev = P.op("vector", "tensor_copy", bgbc[:, :], b[0:128, 0:16], inc="vs",
              drain=True)
    pp.release(ib, ev)
    P.waits("tensor", t_wg + ln_toks)
    for tc in range(32):
        ib, b, g = pp.acquire()
        P.wait("tensor", g)
        tt = tc // 4
        co = (tc % 4) * 128
        for kc in range(NC):
            P.mm(b[:, 0:16], tile(xn, kc, tt)[:, co:co + 128],
                 wg[:, kc * 16:(kc + 1) * 16],
                 start=(kc == 0), stop=(kc == NC - 1),
                 inc="ts" if kc == NC - 1 else None)
        t = P.now("ts")
        P.wait("vector", t)
        P.op("vector", "tensor_tensor", glg[:, :], b[:, 0:16], bgbc[:, :],
             op=ALU.add, inc="vs", drain=True)
        pp.release(ib, P.now("vs"))
        P.op("vector", "max", gsc[:, :], glg[:, :])
        P.op("vector", "tensor_scalar", gsc[:, 1:2], gsc[:, 0:1], -1.0, None,
             op0=ALU.mult, inc="vs", drain=True)
        P.wait("scalar", P.now("vs"))
        P.op("scalar", "activation", get[:, :], glg[:, :], AF.Exp,
             bias=gsc[:, 1:2], accum_out=gsc[:, 2:3], inc="ss", drain=True)
        P.wait("vector", P.now("ss"))
        P.op("vector", "reciprocal", gsc[:, 3:4], gsc[:, 2:3])
        P.op("vector", "tensor_scalar", gbf[:, :], get[:, :], gsc[:, 3:4], None,
             op0=ALU.mult, inc="vs", drain=True)
        ib2, b2, g2 = pp.acquire()
        P.waits("tensor", [g2, P.now("vs")])
        bb2 = b2.bitcast(BF16)
        tr = P.op("tensor", "transpose", bb2[0:16, 0:128], gbf[:, :],
                  consts["ident"], inc="ts")
        P.wait("vector", tr)
        ev2 = P.op("vector", "tensor_copy", gT[:, tc * 128:(tc + 1) * 128],
                   bb2[0:16, 0:128], inc="vs", drain=True)
        pp.release(ib2, ev2)
    st_g.close()

    st_p = ExitStack()
    P.fence()
    wqk, t_wqk = load_weight(P, al, "F_wqk", W["Wqk"], 1024, 2048, stack=st_p)
    ktt = al.sb("F_ktt", [128, NC * TTW], BF16, stack=st_p)
    ksq = al.sb("F_ksq", [128, 2 * TTW], BF16, stack=st_p)
    kst = al.sb("F_kst", [65, 2 * 2048], BF16, stack=st_p)
    k2p = al.sb("F_k2p", [2, 4], F32, stack=st_p)
    k2s = al.sb("F_k2s", [2, TTW], F32, stack=st_p)
    vtmp = al.sb("F_vtmp", [64, 16 * 128], BF16, stack=st_p)
    vst = al.sb("F_vst", [128, 2 * 1040], BF16, stack=st_p)
    P.op("vector", "memset", k2m[:, :], 0.0)
    P.op("vector", "memset", kst[64:65, :], 1.0)
    P.op("vector", "memset", vst[:, :], 1.0, inc="vs", drain=True)
    P.waits("tensor", t_wqk + [P.now("vs")])
    P.waits("vector", t_bqk)
    kst_rel = [None, None]
    ksq_rel = [None, None]
    vst_rel = [None, None]
    vtmp_rel = [None, None]
    for tt in range(NTT):
        for oc in range(16):
            ib, b, g = pp.acquire()
            P.wait("tensor", g)
            for kc in range(NC):
                P.mm(b[:, :], wslice(wqk, kc, oc, 2048), tile(xn, kc, tt),
                     start=(kc == 0), stop=(kc == NC - 1),
                     inc="ts" if kc == NC - 1 else None)
            t = P.now("ts")
            P.wait("vector", t)
            if oc < 8:
                ev = P.op("vector", "tensor_scalar", tile(qf, oc, tt),
                          b[:, :], b_qk[:, oc:oc + 1], None, op0=ALU.add,
                          inc="vs", drain=True)
                pp.release(ib, ev)
            else:
                ko = oc - 8
                ev = P.op("vector", "tensor_scalar",
                          ktt[:, ko * TTW:(ko + 1) * TTW],
                          b[:, :], b_qk[:, oc:oc + 1], None, op0=ALU.add,
                          inc="vs", drain=True)
                pp.release(ib, ev)
                _knorm_update(P, al, pp, consts, ktt[:, ko * TTW:(ko + 1) * TTW],
                              k2m, ko, ksq, ksq_rel, k2p, k2s, ev)
        _stage_k(P, al, pp, consts, ktt, tt, kst, kst_rel, kext_dram)
        _stage_v(P, al, pp, consts, ktt, tt, vst, vst_rel, vtmp, vtmp_rel,
                 vext_dram, ctab=None)
    P.wait("scalar", P.now("vs"))
    P.op("scalar", "activation", nkx[:, :], k2m[:, :], AF.Sqrt, inc="ss",
         drain=True)
    P.wait("vector", P.now("ss"))
    P.op("vector", "tensor_scalar", nkx[:, :], nkx[:, :], -1.0, None,
         op0=ALU.mult, inc="vs", drain=True)
    st_p.close()
    st_x.close()
    st_mq = ExitStack()
    P.fence()
    _mq_pass(P, al, pp, consts, qf, nkx, mq_dram, st_mq)
    st_mq.close()

    g7_rel = [None]

    def gate_fn(P2, pp2, hg, qg, rtmp4, bank7, bc_rel):
        if bc_rel is not None:
            P2.wait("tensor", bc_rel)
        tg = P2.mm(bank7[0:4, :], consts["ident"][0:16, hg * 4:hg * 4 + 4],
                   gT[0:16, qg * TTW:(qg + 1) * TTW], start=True, stop=True,
                   inc="ts")
        P2.wait("vector", tg)
        P2.op("vector", "tensor_tensor", rtmp4[:, :], rtmp4[:, :],
              bank7[0:4, :], op=ALU.mult, inc="vs")

    emit_attn_win(P, al, pp, consts, qf, kext_dram, vext_dram, mq_dram,
                  attn_dram, win_kcs_F, gate_fn=gate_fn, masks_dram=W["masks"])
    st_qf.close()

    st_wo = ExitStack()
    P.fence()
    w_o, t_o = load_weight(P, al, "F_wo", W["Wproj"], 1024, 1024, stack=st_wo)
    b_o, tbo = load_bias(P, al, "F_bo", W["bproj"], 1024, stack=st_wo)
    att = al.sb("F_att", [128, 2 * NC * TTW], BF16, stack=st_wo)
    xr = al.sb("F_xr", [128, 4 * TTW], F32, stack=st_wo)
    yt = al.sb("F_yt", [128, 4 * TTW], F32, stack=st_wo)
    P.waits("tensor", t_o)
    P.waits("vector", tbo)
    att_rel = [None, None]
    rel = {"xr": [None] * 4, "yt": [None] * 4}
    out_toks = []
    P.waits("gpsimd", P.dma_all_done("gpsimd"))
    k4 = [0]
    for tt in range(NTT):
        ia = tt % 2
        if att_rel[ia] is not None:
            P.wait("gpsimd", att_rel[ia])
        src_ap = bass.AP(attn_dram.tensor, tt * TTW,
                         [[T, 128], [128 * T, NC], [1, TTW]])
        la = P.dma("gpsimd", att[:, ia * NC * TTW:(ia + 1) * NC * TTW], src_ap)
        P.wait("tensor", la)
        for oc in range(NC):
            ib, b, g = pp.acquire()
            P.wait("tensor", g)
            for kc in range(NC):
                P.mm(b[:, :], wslice(w_o, kc, oc, 1024),
                     att[:, ia * NC * TTW + kc * TTW: ia * NC * TTW + (kc + 1) * TTW],
                     start=(kc == 0), stop=(kc == NC - 1),
                     inc="ts" if kc == NC - 1 else None)
            t = P.now("ts")
            i = k4[0] % 4
            k4[0] += 1
            if rel["xr"][i] is not None:
                P.wait("sync", rel["xr"][i])
            lt = P.dma("sync", xr[:, i * TTW:(i + 1) * TTW],
                       x_in[oc * 128:(oc + 1) * 128, tt * TTW:(tt + 1) * TTW])
            P.waits("vector", [t, lt])
            if rel["yt"][i] is not None:
                P.wait("vector", rel["yt"][i])
            P.op("vector", "tensor_scalar", yt[:, i * TTW:(i + 1) * TTW],
                 b[:, :], b_o[:, oc:oc + 1], None, op0=ALU.add)
            vt = P.op("vector", "tensor_tensor", yt[:, i * TTW:(i + 1) * TTW],
                      yt[:, i * TTW:(i + 1) * TTW], xr[:, i * TTW:(i + 1) * TTW],
                      op=ALU.add, inc="vs", drain=True)
            pp.release(ib, vt)
            rel["xr"][i] = vt
            if oc == NC - 1:
                att_rel[ia] = P.now("ts")
            P.wait("sync", vt)
            ot = P.dma("sync",
                       x_out[oc * 128:(oc + 1) * 128, tt * TTW:(tt + 1) * TTW],
                       yt[:, i * TTW:(i + 1) * TTW])
            rel["yt"][i] = ot
            out_toks.append(ot)
    st_wo.close()
    return out_toks


def prep_F_weights(inp):
    W = {}
    W["Wqk"] = _np(inp["L1_attn_b_Wqk"])
    W["bqk"] = _bvec(inp["L1_attn_b_bqk"])
    W["Wgate"] = _np(inp["L1_attn_b_Wgate"])
    W["bgate"] = _np(inp["L1_attn_b_bgate"]).reshape(1, 16)
    W["Wproj"] = _np(inp["L1_attn_b_Wproj"])
    W["bproj"] = _bvec(inp["L1_attn_b_bproj"])
    W["masks"] = prep_masks(win_kcs_F)
    return W


FUSED = True


def build_fused(seq, wshapes_by_pfx):
    nc = bass.Bass("TRN2", target_bir_lowering=False)
    x_in = nc.dram_tensor("x_in", [D, T], F32, kind="ExternalInput").ap()
    x_out = nc.dram_tensor("x_out", [D, T], F32, kind="ExternalOutput").ap()
    streams = [x_in]
    for i in range(len(seq) - 1):
        streams.append(nc.dram_tensor(f"xs{i}", [D, T], F32).ap())
    streams.append(x_out)
    Wd_by_pfx = {}
    for pfx, shapes in wshapes_by_pfx.items():
        Wd_by_pfx[pfx] = {k: nc.dram_tensor(f"w_{pfx}{k}", list(s), F32,
                                            kind="ExternalInput").ap()
                          for k, s in shapes.items()}
    need = sorted({n for e, _ in seq for n in _const_need(e.__name__)})
    stack = ExitStack()
    with stack:
        al = Alloc(nc, stack)
        P = Plan(nc, stack)
        P.alloc = al
        consts, ctoks = declare_consts(nc, al, P, make_consts(), need=need)
        P.waits("tensor", ctoks)
        P.waits("vector", ctoks)
        P.waits("scalar", ctoks)
        out_toks = None
        for i, (emitter, pfx) in enumerate(seq):
            sub = ExitStack()
            al.default_stack = sub
            al.prefix = f"c{i}_"
            P.fence()
            out_toks = emitter(P, al, consts, streams[i], streams[i + 1],
                               Wd_by_pfx[pfx])
            P.fence()
            sub.close()
            al.default_stack = stack
            al.prefix = ""
        P.waits("sync", out_toks)
        P.waits("sync", P.dma_all_done("sync") + P.dma_all_done("gpsimd"))
        P.build()
    return nc


def run_fused(inputs, x_fm):
    WA = prep_A_weights(inputs)
    WB = prep_B_weights(inputs)
    WC = prep_C_weights(inputs)
    WD = prep_D_weights(inputs)
    WEs = [prep_E_weights(inputs, c) for c in range(8)]
    WF = prep_F_weights(inputs)
    seq = [(emit_A, "A"), (emit_B, "B"), (emit_C, "C"), (emit_B, "B"),
           (emit_D, "Dd"), (emit_E, "E"), (emit_F, "F"), (emit_E, "E")]
    wshapes = {"A": {k: v.shape for k, v in WA.items()},
               "B": {k: v.shape for k, v in WB.items()},
               "C": {k: v.shape for k, v in WC.items()},
               "Dd": {k: v.shape for k, v in WD.items()},
               "E": {k: v.shape for k, v in WEs[0].items()},
               "F": {k: v.shape for k, v in WF.items()}}
    nc = build_fused(seq, wshapes)
    consts = make_consts()
    need = sorted({n for e, _ in seq for n in _const_need(e.__name__)})
    in_maps = []
    for c in range(8):
        m = {"x_in": x_fm[c]}
        for pfx, W in [("A", WA), ("B", WB), ("C", WC), ("Dd", WD),
                       ("E", WEs[c]), ("F", WF)]:
            for k, v in W.items():
                m[f"w_{pfx}{k}"] = v
        for k, v in consts.items():
            if k in need:
                m["c_" + k] = v
        in_maps.append(m)
    res = run_bass_kernel_spmd(nc, in_maps, core_ids=list(range(8)))
    return [res.results[c]["x_out"] for c in range(8)]


# version 20
# speedup vs baseline: 1.1451x; 1.0136x over previous
import numpy as np
import math
from contextlib import ExitStack
import concourse.bass as bass
import concourse.mybir as mybir
from concourse.bass_utils import run_bass_kernel_spmd

F32 = mybir.dt.float32
BF16 = mybir.dt.bfloat16
I32 = mybir.dt.int32
U32 = mybir.dt.uint32
AF = mybir.ActivationFunctionType
ALU = mybir.AluOpType
AX = mybir.AxisListType

D = 1024
T = 4096
NC = 8
NTT = 8
TTW = 512
EPS_LN = 1e-5

ENGS = ("sync", "gpsimd", "tensor", "vector", "scalar")
NO_MILESTONE_DRAIN = True
DEBUG_BARRIERS = 0
DEBUG_TAPS = None


class Sem:
    def __init__(self, handle, name):
        self.h = handle
        self.name = name
        self.n = 0


DMA_POOL = 12


class Plan:
    def __init__(self, nc, stack):
        self.nc = nc
        self.stack = stack
        self.ops = {e: [] for e in ENGS}
        self.sems = {}
        for s in ("ts", "vs", "ss", "gs", "cs"):
            self.sem(s)
        self.dirty = {}
        self.rdirty = {}
        self.dma_pools = {}
        self.dma_uses = {}
        for q in ("sync", "gpsimd", "scalar"):
            self.dma_pools[q] = [self.sem(f"{q[0]}d{i}") for i in range(DMA_POOL)]
            self.dma_uses[q] = 0

    def sem(self, name):
        if name not in self.sems:
            h = self.stack.enter_context(self.nc.semaphore(name))
            self.sems[name] = Sem(h, name)
        return self.sems[name]

    def emit(self, eng, fn):
        self.ops[eng].append(fn)

    def dma(self, q, out, in_, **kw):
        k = self.dma_uses[q]
        self.dma_uses[q] += 1
        s = self.dma_pools[q][k % DMA_POOL]
        prev = s.n
        s.n += 16
        v = s.n

        def fn(e, out=out, in_=in_, s=s, kw=kw, prev=prev):
            if prev > 0:
                e.wait_ge(s.h, prev)
            e.dma_start(out=out, in_=in_, **kw).then_inc(s.h, 16)

        self.emit(q, fn)
        return (s.name, v)

    def dma_all_done(self, q):
        return [(s.name, s.n) for s in self.dma_pools[q]]

    def dram(self, name, shape, dtype):
        return self.nc.dram_tensor(self.alloc.prefix + name, list(shape), dtype).ap()

    def fence(self, eng=None):
        toks = [self.now("ts"), self.now("vs"), self.now("ss")]
        toks += self.dma_all_done("sync") + self.dma_all_done("gpsimd")
        for e in ENGS:
            self.waits(e, toks)

    def wait(self, eng, tok):
        if tok is None:
            return
        sem, v = tok
        s = self.sems[sem]

        def fn(e, s=s, v=v):
            e.wait_ge(s.h, v)

        self.emit(eng, fn)

    def waits(self, eng, toks):
        best = {}
        for t in toks:
            if t is None:
                continue
            sem, v = t
            best[sem] = max(best.get(sem, 0), v)
        for sem, v in best.items():
            self.wait(eng, (sem, v))

    def now(self, semname):
        return (semname, self.sems[semname].n)

    @staticmethod
    def _apname(x):
        t = getattr(x, "tensor", None)
        return getattr(t, "name", None)

    @staticmethod
    def _apbox(x):
        t = getattr(x, "tensor", None)
        nm = getattr(t, "name", None)
        if nm is None:
            return None
        shp = list(t.shape)
        rowlen = 1
        for s in shp[1:]:
            rowlen *= s
        off = x.offset
        ap = [list(p) for p in x.ap]
        p0 = off // rowlen
        c0 = off % rowlen
        pc = ap[0][1] if ap else 1
        span = 1
        for s, c in ap[1:]:
            span += s * (c - 1)
        return (nm, p0, p0 + pc, c0, c0 + span)

    @staticmethod
    def _boxhit(a, b):
        return (a[0] == b[0] and a[1] < b[2] and b[1] < a[2]
                and a[3] < b[4] and b[3] < a[4])

    def op(self, eng, method, *args, inc=None, drain=False, pre_drain=None, **kw):
        semname = inc
        tok = None
        if semname is not None:
            s = self.sems[semname]
            s.n += 1
            tok = (semname, s.n)
        if pre_drain is None and eng in ("vector", "scalar", "gpsimd"):
            dirty = self.dirty.setdefault(eng, [])
            rset = self.rdirty.setdefault(eng, [])
            outs = []
            if args and hasattr(args[0], "tensor"):
                bx = self._apbox(args[0])
                if bx is not None:
                    outs.append(bx)
            ao = kw.get("accum_out")
            if ao is not None and hasattr(ao, "tensor"):
                bx = self._apbox(ao)
                if bx is not None:
                    outs.append(bx)
            ins = []
            for a in args[1:]:
                if hasattr(a, "tensor"):
                    bx = self._apbox(a)
                    if bx is not None:
                        ins.append(bx)
            for k in ("bias", "scale", "scalar1", "scalar2", "in1", "data0",
                      "data1"):
                v = kw.get(k)
                if v is not None and hasattr(v, "tensor"):
                    bx = self._apbox(v)
                    if bx is not None:
                        ins.append(bx)
            pre_drain = (any(self._boxhit(b, d) for b in ins + outs for d in dirty)
                         or any(self._boxhit(o, r) for o in outs for r in rset))
            if pre_drain or (drain and not NO_MILESTONE_DRAIN):
                dirty.clear()
                rset.clear()
            if NO_MILESTONE_DRAIN or not drain:
                dirty.extend(outs)
                rset.extend(ins)
        elif pre_drain is None:
            pre_drain = False

        def fn(e, method=method, args=args, kw=kw, semname=semname, drain=drain,
               pre_drain=pre_drain):
            if pre_drain:
                e.drain()
            r = getattr(e, method)(*args, **kw)
            if semname is not None:
                s = self.sems[semname]
                if drain and not NO_MILESTONE_DRAIN:
                    e.drain().then_inc(s.h, 1)
                else:
                    r.then_inc(s.h, 1)

        self.emit(eng, fn)
        return tok

    def mm(self, out, lhsT, rhs, start, stop, inc=None):
        return self.op("tensor", "matmul", out, lhsT, rhs,
                       start=start, stop=stop, inc=inc)

    def barrier(self):
        toks = [self.now(s) for s in ("ts", "vs", "ss")]
        toks += self.dma_all_done("sync") + self.dma_all_done("gpsimd")
        for e in ENGS:
            self.waits(e, toks)
        return toks

    def build(self):
        block = self.stack.enter_context(self.nc.Block())
        plan = self

        def mk(eng):
            def body(e):
                for f in plan.ops[eng]:
                    f(e)
            return body

        block.sync(mk("sync"))
        block.gpsimd(mk("gpsimd"))
        block.tensor(mk("tensor"))
        block.vector(mk("vector"))
        block.scalar(mk("scalar"))


class PsumPool:
    def __init__(self, plan, tiles):
        self.plan = plan
        self.tiles = tiles
        self.rel = [None] * len(tiles)
        self.i = 0

    def acquire(self):
        i = self.i % len(self.tiles)
        self.i += 1
        return i, self.tiles[i], self.rel[i]

    def release(self, i, tok):
        self.rel[i] = tok


class Alloc:
    def __init__(self, nc, stack):
        self.nc = nc
        self.stack = stack
        self.default_stack = stack
        self.prefix = ""
        self.bytes_pp = 0

    def sb(self, name, shape, dtype, stack=None, side=None):
        name = self.prefix + name
        per = int(np.prod(shape[1:])) * mybir.dt.size(dtype)
        self.bytes_pp += ((per + 31) // 32) * 32
        st = stack or self.default_stack
        kw = {}
        if stack is not None:
            kw["side"] = "right"
        if side is not None:
            kw["side"] = side
        return st.enter_context(self.nc.sbuf_tensor(name, list(shape), dtype, **kw))

    def ps(self, name, shape, dtype=F32):
        return self.default_stack.enter_context(
            self.nc.psum_tensor(self.prefix + name, list(shape), dtype))


def chunk(t, c, w=T):
    return t[:, c * w:(c + 1) * w]


def tile(t, c, tt, w=T):
    return t[:, c * w + tt * TTW: c * w + tt * TTW + TTW]



def emit_ln(P, al, name, xn, consts, pp, x_dram=None, src_sbuf=None,
            src_toks=None, gain=None, bias=None, substack=None, narrow=False):
    ones128 = consts["ones128"]
    ones1 = consts["ones1"]
    st = substack or P.stack
    NB = 1 if narrow else 2
    arow = al.sb(name + "_arow", [1, 2 * T], BF16, stack=st)
    abc = al.sb(name + "_abc", [128, 2 * T], BF16, stack=st)
    scr = al.sb(name + "_scr", [1, 3 * TTW], F32, stack=st)
    if src_sbuf is None:
        xr = al.sb(name + "_xr", [128, NB * T], BF16, stack=st)
    sq = al.sb(name + "_sq", [128, NB * T], BF16, stack=st)

    banks = [pp.acquire() for _ in range(NTT)]
    P.waits("tensor", [g for (_, _, g) in banks])
    if narrow:
        srow = arow[:, T:2 * T]
    else:
        srow = al.sb(name + "_srow", [1, T], F32, stack=st)

    ld_rel = [None, None]
    mm_tok = None
    for c in range(NC):
        i = c % NB
        if src_sbuf is None:
            if ld_rel[i] is not None:
                P.wait("gpsimd", ld_rel[i])
            lt = P.dma("gpsimd", xr[:, i * T:(i + 1) * T],
                       x_dram[c * 128:(c + 1) * 128, :])
            src = xr[:, i * T:(i + 1) * T]
            P.wait("tensor", lt)
        else:
            src = src_sbuf(c)
            if src_toks is not None and c == 0:
                P.waits("tensor", src_toks)
        for tt in range(NTT):
            _, b, _ = banks[tt]
            P.mm(b[0:1, :], ones128, src[:, tt * TTW:(tt + 1) * TTW],
                 start=(c == 0), stop=(c == NC - 1), inc="ts")
        mm_tok = P.now("ts")
        ld_rel[i] = mm_tok
    P.wait("vector", mm_tok)
    for tt in range(NTT):
        _, b, _ = banks[tt]
        P.op("vector", "tensor_copy", srow[:, tt * TTW:(tt + 1) * TTW], b[0:1, :],
             inc="vs" if tt == NTT - 1 else None, drain=(tt == NTT - 1))
    s1ev = P.now("vs")

    sq_rel = [None, None]
    P.wait("tensor", s1ev)
    for c in range(NC):
        i = c % NB
        if src_sbuf is None:
            P.wait("gpsimd", mm_tok if c < NB else ld_rel[i])
            lt = P.dma("gpsimd", xr[:, i * T:(i + 1) * T],
                       x_dram[c * 128:(c + 1) * 128, :])
            src = xr[:, i * T:(i + 1) * T]
            P.wait("scalar", lt)
        else:
            src = src_sbuf(c)
            if src_toks is not None and c == 0:
                P.waits("scalar", src_toks)
        if sq_rel[i] is not None:
            P.wait("scalar", sq_rel[i])
        sqt = P.op("scalar", "activation", sq[:, i * T:(i + 1) * T], src, AF.Square,
                   inc="ss", drain=True)
        P.wait("tensor", sqt)
        for tt in range(NTT):
            _, b, _ = banks[tt]
            P.mm(b[0:1, :], ones128,
                 sq[:, i * T + tt * TTW: i * T + (tt + 1) * TTW],
                 start=(c == 0), stop=(c == NC - 1), inc="ts")
        mm_tok = P.now("ts")
        ld_rel[i] = mm_tok
        sq_rel[i] = mm_tok

    P.wait("vector", mm_tok)
    stok = None
    for tt in range(NTT):
        ib, b, _ = banks[tt]
        if stok is not None:
            P.wait("vector", stok)
        P.op("vector", "tensor_copy", scr[:, 0:TTW], srow[:, tt * TTW:(tt + 1) * TTW])
        P.op("vector", "tensor_copy", scr[:, TTW:2 * TTW], b[0:1, :],
             inc="vs", drain=True)
        pp.release(ib, P.now("vs"))
        P.op("vector", "tensor_scalar", scr[:, 0:TTW], scr[:, 0:TTW], 1.0 / D, None,
             op0=ALU.mult)
        P.op("vector", "tensor_tensor", scr[:, 2 * TTW:3 * TTW], scr[:, 0:TTW],
             scr[:, 0:TTW], op=ALU.mult)
        P.op("vector", "scalar_tensor_tensor", scr[:, TTW:2 * TTW],
             scr[:, TTW:2 * TTW], 1.0 / D, scr[:, 2 * TTW:3 * TTW],
             op0=ALU.mult, op1=ALU.subtract, inc="vs", drain=True)
        vtok = P.now("vs")
        P.wait("scalar", vtok)
        P.op("scalar", "activation", scr[:, 2 * TTW:3 * TTW], scr[:, TTW:2 * TTW],
             AF.Ln, bias=consts["eps"])
        P.op("scalar", "activation", arow[:, tt * TTW:(tt + 1) * TTW],
             scr[:, 2 * TTW:3 * TTW], AF.Exp, scale=-0.5, inc="ss", drain=True)
        sstok = P.now("ss")
        P.wait("vector", sstok)
        P.op("vector", "scalar_tensor_tensor",
             arow[:, T + tt * TTW: T + (tt + 1) * TTW], scr[:, 0:TTW], -1.0,
             arow[:, tt * TTW:(tt + 1) * TTW], op0=ALU.mult, op1=ALU.mult,
             inc="vs", drain=True)
        stok = P.now("vs")

    P.wait("tensor", stok)
    for half in range(2):
        for tt in range(NTT):
            ib, b, g = pp.acquire()
            P.wait("tensor", g)
            t = P.mm(b[0:128, :], ones1,
                     arow[:, half * T + tt * TTW: half * T + (tt + 1) * TTW],
                     start=True, stop=True, inc="ts")
            P.wait("vector", t)
            ev = P.op("vector", "tensor_copy",
                      abc[:, half * T + tt * TTW: half * T + (tt + 1) * TTW],
                      b[0:128, :], inc="vs", drain=True)
            pp.release(ib, ev)
    bc_tok = P.now("vs")

    out_toks = []
    ld_rel = [bc_tok, bc_tok]
    for c in range(NC):
        i = c % NB
        if src_sbuf is None:
            P.wait("gpsimd", ld_rel[i])
            lt = P.dma("gpsimd", xr[:, i * T:(i + 1) * T],
                       x_dram[c * 128:(c + 1) * 128, :])
            src = xr[:, i * T:(i + 1) * T]
            P.wait("vector", lt)
        else:
            src = src_sbuf(c)
            if c == 0:
                P.wait("vector", bc_tok)
                if src_toks is not None:
                    P.waits("vector", src_toks)
        dst = chunk(xn, c)
        P.op("vector", "tensor_tensor", dst, src, abc[:, 0:T], op=ALU.mult)
        tok = P.op("vector", "tensor_tensor", dst, dst, abc[:, T:2 * T],
                   op=ALU.add, inc="vs", drain=(gain is None))
        if gain is not None:
            tok = P.op("vector", "tensor_scalar", dst, dst, gain(c), bias(c),
                       op0=ALU.mult, op1=ALU.add, inc="vs", drain=True)
        ld_rel[i] = tok
        out_toks.append(tok)
    return out_toks



def load_weight(P, al, name, w_dram, in_f, out_f, dtype=BF16, stack=None):
    kcs = in_f // 128
    w = al.sb(name, [128, kcs * out_f], dtype, stack=stack)
    toks = []
    for kc in range(kcs):
        toks.append(P.dma("gpsimd", w[:, kc * out_f:(kc + 1) * out_f],
                          w_dram[kc * 128:(kc + 1) * 128, :]))
    return w, toks


def wslice(w, kc, oc, out_f, ow=128):
    return w[:, kc * out_f + oc * ow: kc * out_f + oc * ow + ow]


def load_bias(P, al, name, b_dram, n, stack=None):
    cs = n // 128
    b = al.sb(name, [128, cs], F32, stack=stack)
    tok = P.dma("gpsimd", b[:, :], b_dram)
    return b, [tok]


def emit_proj(P, al, pp, xn_src, w, w_toks, in_f, out_f, evac,
              extra_mm=None, dep_toks=None, n_tt=NTT, tts=None):
    kcs = in_f // 128
    ocs = out_f // 128
    if w_toks:
        P.waits("tensor", w_toks)
    if dep_toks:
        P.waits("tensor", dep_toks)
    for tt in (tts if tts is not None else range(n_tt)):
        for oc in range(ocs):
            ib, b, g = pp.acquire()
            P.wait("tensor", g)
            for kc in range(kcs):
                last = (kc == kcs - 1) and extra_mm is None
                P.mm(b[:, :], wslice(w, kc, oc, out_f), xn_src(kc, tt),
                     start=(kc == 0), stop=last,
                     inc="ts" if last else None)
            if extra_mm is not None:
                extra_mm(b, oc, tt)
            tok = P.now("ts")
            rel = evac(b, oc, tt, tok)
            pp.release(ib, rel)



def emit_A(P, al, consts, x_in, x_out, W):
    pp = PsumPool(P, [al.ps(f"ppA{i}", [128, 512]) for i in range(8)])
    xn = al.sb("A_xn", [128, NC * T], BF16)

    if True:
        st_ln = ExitStack()
        ln_toks = emit_ln(P, al, "A_ln", xn, consts, pp, x_dram=x_in,
                          substack=st_ln)
        w_s1, t_s1 = load_weight(P, al, "A_ws1", W["s_w1T"], 1024, 512)
        w_s2, t_s2 = load_weight(P, al, "A_ws2", W["s_w2T"], 512, 128)
        w_B, t_B = load_weight(P, al, "A_wB", W["B_wT"], 1024, 128)
        w_C, t_C = load_weight(P, al, "A_wC", W["C_wT"], 128, 1024)
        at = al.sb("A_at", [128, 128], BF16)
        t_at = P.dma("gpsimd", at[:, :], W["A_T"])
        an = al.sb("A_an", [128, 128], BF16)
        t_an = P.dma("gpsimd", an[:, :], W["A_nat"])
        b_s1, tb1 = load_bias(P, al, "A_bs1", W["s_b1"], 512)
        b_s2, tb2 = load_bias(P, al, "A_bs2", W["s_b2"], 128)
        b_B, tb3 = load_bias(P, al, "A_bB", W["B_b"], 128)
        b_C, tb4 = load_bias(P, al, "A_bC", W["C_b"], 1024)
        b_D, tb5 = load_bias(P, al, "A_bD", W["D_b"], 1024)
        b_g, tb6 = load_bias(P, al, "A_bg", W["g_b"], 1024)
        b_cd = al.sb("A_bcd", [128, 8], F32)
        P.waits("vector", tb4 + tb5)
        P.op("vector", "tensor_tensor", b_cd[:, :], b_C[:, :], b_D[:, :],
             op=ALU.add, inc="vs", drain=True)
        tb_cd = P.now("vs")

        if DEBUG_BARRIERS & 1:
            P.barrier()
        st_ln.close()
        st_s1 = ExitStack()
        s1 = al.sb("A_s1", [128, 4 * T], BF16, stack=st_s1)
        P.waits("scalar", tb1)

        def ev_s1(b, oc, tt, tok):
            P.wait("scalar", tok)
            return P.op("scalar", "activation", tile(s1, oc, tt), b[:, :], AF.Silu,
                        bias=b_s1[:, oc:oc + 1], inc="ss", drain=True)

        emit_proj(P, al, pp, lambda kc, tt: tile(xn, kc, tt), w_s1, t_s1,
                  1024, 512, ev_s1, dep_toks=ln_toks)

        s = al.sb("A_s", [128, T], BF16)
        P.waits("scalar", tb2)

        def ev_s2(b, oc, tt, tok):
            P.wait("scalar", tok)
            return P.op("scalar", "activation", s[:, tt * TTW:(tt + 1) * TTW],
                        b[:, :], AF.Sigmoid, bias=b_s2[:, 0:1], inc="ss", drain=True)

        emit_proj(P, al, pp, lambda kc, tt: tile(s1, kc, tt), w_s2, t_s2,
                  512, 128, ev_s2, dep_toks=[P.now("ss")])
        if DEBUG_TAPS is not None:
            P.wait("gpsimd", P.now("ss"))
            DEBUG_TAPS.setdefault("toks", []).append(
                P.dma("gpsimd", DEBUG_TAPS["s"], s[:, :]))
            for c in range(NC):
                DEBUG_TAPS["toks"].append(
                    P.dma("gpsimd", DEBUG_TAPS["xn"][c * 128:(c + 1) * 128, :],
                          chunk(xn, c)))

        if DEBUG_BARRIERS & 2:
            P.barrier()
        st_s1.close()
        st_u = ExitStack()
        u = al.sb("A_u", [128, T], F32, stack=st_u)
        btmp = al.sb("A_btmp", [128, 4 * TTW], F32, stack=st_u)
        P.waits("vector", tb3)
        btmp_rel = [None] * 4

        def ev_bt(b, oc, tt, tok):
            i = tt % 4
            P.wait("vector", tok)
            if btmp_rel[i] is not None:
                P.wait("vector", btmp_rel[i])
            t1 = P.op("vector", "tensor_scalar", btmp[:, i * TTW:(i + 1) * TTW],
                      b[:, :], b_B[:, 0:1], None, op0=ALU.add,
                      inc="vs", drain=True)
            vt = P.op("vector", "tensor_tensor", u[:, tt * TTW:(tt + 1) * TTW],
                      btmp[:, i * TTW:(i + 1) * TTW],
                      s[:, tt * TTW:(tt + 1) * TTW], op=ALU.mult,
                      inc="vs", drain=True)
            btmp_rel[i] = vt
            return t1

        emit_proj(P, al, pp, lambda kc, tt: tile(xn, kc, tt), w_B, t_B,
                  1024, 128, ev_bt, dep_toks=[P.now("ss")])
        u_tok = P.now("vs")
        if DEBUG_TAPS is not None:
            P.wait("gpsimd", u_tok)
            DEBUG_TAPS["toks"].append(P.dma("gpsimd", DEBUG_TAPS["u"], u[:, :]))

        if DEBUG_BARRIERS & 4:
            P.barrier()
        hs = al.sb("A_hs", [128, T], BF16)
        H = al.sb("A_H", [128, 64], BF16)
        Hend = al.sb("A_Hend", [128, 64], BF16)
        h2i = al.sb("A_h2i", [128, 64], BF16)
        pw = al.sb("A_pw", [128, 256], BF16)
        ib0, t0, g0 = pp.acquire()
        ib1, t1_, g1 = pp.acquire()
        ib2, t2_, g2 = pp.acquire()
        P.waits("tensor", [g0, g1, g2])
        psq = t0[:, 0:128]
        psqt = t0.bitcast(BF16)[:, 256:384]
        ps_h = [t1_[:, 0:64], t2_[:, 0:64]]
        ident = consts["ident"]

        P.waits("vector", [t_at, t_an])
        P.op("vector", "tensor_copy", pw[:, 0:128], an[:, :])
        P.op("vector", "tensor_copy", pw[:, 128:256], at[:, :], inc="vs", drain=True)
        P.wait("tensor", P.now("vs"))
        for k in range(6):
            t1 = P.mm(psq, pw[:, 128:256], pw[:, 0:128], start=True, stop=True,
                      inc="ts")
            P.wait("vector", t1)
            c1 = P.op("vector", "tensor_copy", pw[:, 0:128], psq,
                      inc="vs", drain=True)
            P.wait("tensor", c1)
            t2 = P.op("tensor", "transpose", psqt, pw[:, 0:128], ident,
                      inc="ts")
            P.wait("vector", t2)
            c2 = P.op("vector", "tensor_copy", pw[:, 128:256], psqt,
                      inc="vs", drain=True)
            P.wait("tensor", c2)

        P.op("vector", "memset", H[:, :], 0.0, inc="vs", drain=True)
        P.wait("tensor", u_tok)
        P.wait("tensor", P.now("vs"))
        for j in range(64):
            b = ps_h[j % 2]
            t1 = P.mm(b, at[:, :], H[:, :], start=True, stop=True, inc="ts")
            P.wait("vector", t1)
            c1 = P.op("vector", "tensor_tensor", H[:, :], b,
                      u[:, j::64], op=ALU.add, inc="vs", drain=True)
            P.wait("tensor", c1)
        P.op("vector", "tensor_copy", Hend[:, 0:1], H[:, 0:1], inc="vs", drain=True)
        P.wait("tensor", P.now("vs"))
        for c in range(1, 64):
            b = ps_h[c % 2]
            t1 = P.mm(b[:, 0:1], pw[:, 128:256], Hend[:, c - 1:c], start=True,
                      stop=True, inc="ts")
            P.wait("vector", t1)
            c1 = P.op("vector", "tensor_tensor", Hend[:, c:c + 1], b[:, 0:1],
                      H[:, c:c + 1], op=ALU.add, inc="vs", drain=True)
            P.wait("tensor", c1)
        P.op("vector", "memset", h2i[:, 0:1], 0.0)
        P.op("vector", "tensor_copy", h2i[:, 1:64], Hend[:, 0:63], inc="vs",
             drain=True)
        P.wait("tensor", P.now("vs"))
        for j in range(64):
            b = ps_h[j % 2]
            rhs = h2i[:, :] if j == 0 else hs[:, (j - 1)::64]
            t1 = P.mm(b, at[:, :], rhs, start=True, stop=True, inc="ts")
            P.wait("vector", t1)
            c1 = P.op("vector", "tensor_tensor", hs[:, j::64], b,
                      u[:, j::64], op=ALU.add, inc="vs", drain=True)
            P.wait("tensor", c1)
        hs_tok = P.now("vs")
        if DEBUG_TAPS is not None:
            P.wait("gpsimd", hs_tok)
            DEBUG_TAPS["toks"].append(P.dma("gpsimd", DEBUG_TAPS["hs"], hs[:, :]))
        pp.release(ib0, hs_tok)
        pp.release(ib1, hs_tok)
        pp.release(ib2, hs_tok)

        w_D, t_D = load_weight(P, al, "A_wD", W["D_wT"], 1024, 1024)
        w_g, t_g = load_weight(P, al, "A_wg", W["g_wT"], 1024, 1024)
        if DEBUG_BARRIERS & 8:
            P.barrier()
        st_u.close()

        gt = al.sb("A_gt", [128, 4 * TTW], BF16)
        xr2 = al.sb("A_xr2", [128, 4 * TTW], F32)
        yt = al.sb("A_yt", [128, 4 * TTW], F32)
        P.waits("tensor", t_D + t_g + t_C + [hs_tok])
        P.wait("sync", hs_tok)
        P.wait("scalar", hs_tok)
        P.waits("vector", [tb_cd])
        P.waits("scalar", tb6 + [tb_cd])
        out_toks = []
        rel = {"gt": [None] * 4, "xr": [None] * 4, "yt": [None] * 4}
        cnt = [0]
        for tt in range(NTT):
            for oc in range(NC):
                i = cnt[0] % 4
                cnt[0] += 1
                ibg, bg, gg = pp.acquire()
                P.wait("tensor", gg)
                for kc in range(NC):
                    P.mm(bg[:, :], wslice(w_g, kc, oc, 1024), tile(xn, kc, tt),
                         start=(kc == 0), stop=(kc == NC - 1),
                         inc="ts" if kc == NC - 1 else None)
                g_tok = P.now("ts")
                iby, by, gy = pp.acquire()
                P.wait("tensor", gy)
                for kc in range(NC):
                    P.mm(by[:, :], wslice(w_D, kc, oc, 1024), tile(xn, kc, tt),
                         start=(kc == 0), stop=False)
                P.mm(by[:, :], wslice(w_C, 0, oc, 1024),
                     hs[:, tt * TTW:(tt + 1) * TTW], start=False, stop=True,
                     inc="ts")
                y_tok = P.now("ts")
                P.wait("scalar", g_tok)
                if rel["gt"][i] is not None:
                    P.wait("scalar", rel["gt"][i])
                sg = P.op("scalar", "activation", gt[:, i * TTW:(i + 1) * TTW],
                          bg[:, :], AF.Sigmoid, bias=b_g[:, oc:oc + 1],
                          inc="ss", drain=True)
                pp.release(ibg, sg)
                P.wait("vector", y_tok)
                if rel["yt"][i] is not None:
                    P.wait("vector", rel["yt"][i])
                sy = P.op("vector", "tensor_scalar", yt[:, i * TTW:(i + 1) * TTW],
                          by[:, :], b_cd[:, oc:oc + 1], None, op0=ALU.add,
                          inc="vs", drain=True)
                pp.release(iby, sy)
                if rel["xr"][i] is not None:
                    P.wait("sync", rel["xr"][i])
                lt = P.dma("sync", xr2[:, i * TTW:(i + 1) * TTW],
                           x_in[oc * 128:(oc + 1) * 128, tt * TTW:(tt + 1) * TTW])
                P.waits("vector", [sg, sy, lt])
                P.op("vector", "tensor_tensor", yt[:, i * TTW:(i + 1) * TTW],
                     yt[:, i * TTW:(i + 1) * TTW], gt[:, i * TTW:(i + 1) * TTW],
                     op=ALU.mult)
                vt = P.op("vector", "tensor_tensor", yt[:, i * TTW:(i + 1) * TTW],
                          yt[:, i * TTW:(i + 1) * TTW],
                          xr2[:, i * TTW:(i + 1) * TTW], op=ALU.add,
                          inc="vs", drain=True)
                rel["gt"][i] = vt
                rel["xr"][i] = vt
                P.wait("sync", vt)
                ot = P.dma("sync",
                           x_out[oc * 128:(oc + 1) * 128, tt * TTW:(tt + 1) * TTW],
                           yt[:, i * TTW:(i + 1) * TTW])
                rel["yt"][i] = ot
                out_toks.append(ot)
    return out_toks



def _np(x):
    return np.ascontiguousarray(np.asarray(x, dtype=np.float32))


def _bvec(x):
    return np.ascontiguousarray(_np(x).reshape(-1, 128).T)


def make_consts():
    import ml_dtypes
    bf = ml_dtypes.bfloat16
    c = {}
    c["ident"] = np.eye(128, dtype=bf)
    c["ones128"] = np.ones((128, 1), dtype=bf)
    c["ones1"] = np.ones((1, 128), dtype=bf)
    c["eps"] = np.full((1, 1), EPS_LN, dtype=np.float32)
    sel8 = np.zeros((8, 8 * 128), dtype=bf)
    for e in range(8):
        sel8[e, e * 128:(e + 1) * 128] = 1
    c["sel8"] = sel8
    pair2 = np.zeros((128, 2), dtype=bf)
    pair2[0:64, 0] = 1
    pair2[64:128, 1] = 1
    c["pair2"] = pair2
    c["onesf1"] = np.ones((1, 128), dtype=np.float32)
    c["posc"] = (np.arange(128)[:, None] + 128 * np.arange(32)[None, :]).astype(np.float32)
    c["ones64r"] = np.ones((1, 64), dtype=bf)
    oh4 = np.zeros((1, 16), dtype=np.float32)
    for j in range(4):
        oh4[0, j * 4 + j] = 1.0
    c["oh4"] = oh4
    selb4 = np.zeros((4, 4 * 64), dtype=bf)
    for j in range(4):
        selb4[j, j * 64:(j + 1) * 64] = 1
    c["selb4"] = selb4
    R = np.zeros((128, 128), dtype=np.float32)
    for blk in range(2):
        for jj in range(32):
            a = blk * 64 + 2 * jj
            R[a, a + 1] = -1.0
            R[a + 1, a] = 1.0
    c["r2t"] = np.ascontiguousarray(R.T).astype(bf)
    c.update(_win_masks())
    return c


def declare_consts(nc, al, P, consts_np, need=None):
    out = {}
    toks = []
    for name, arr in consts_np.items():
        if need is not None and name not in need:
            continue
        dt = F32 if arr.dtype == np.float32 else BF16
        d = nc.dram_tensor("c_" + name, list(arr.shape), dt,
                           kind="ExternalInput").ap()
        s = al.sb("cs_" + name, list(arr.shape), dt)
        toks.append(P.dma("gpsimd", s[:, :], d))
        out[name] = s[:, :]
    return out, toks


def prep_A_weights(inp):
    W = {}
    W["A_T"] = _np(np.asarray(inp["L0_attn_a_A"]).T)
    W["A_nat"] = _np(inp["L0_attn_a_A"])
    W["B_wT"] = _np(np.asarray(inp["L0_attn_a_B_w"]).T)
    W["C_wT"] = _np(np.asarray(inp["L0_attn_a_C_w"]).T)
    W["D_wT"] = _np(np.asarray(inp["L0_attn_a_D_w"]).T)
    W["s_w1T"] = _np(np.asarray(inp["L0_attn_a_s_w1"]).T)
    W["s_w2T"] = _np(np.asarray(inp["L0_attn_a_s_w2"]).T)
    W["g_wT"] = _np(np.asarray(inp["L0_attn_a_g_w"]).T)
    W["B_b"] = _bvec(inp["L0_attn_a_B_b"])
    W["C_b"] = _bvec(inp["L0_attn_a_C_b"])
    W["D_b"] = _bvec(inp["L0_attn_a_D_b"])
    W["s_b1"] = _bvec(inp["L0_attn_a_s_b1"])
    W["s_b2"] = _bvec(inp["L0_attn_a_s_b2"])
    W["g_b"] = _bvec(inp["L0_attn_a_g_b"])
    return W


CONST_NEEDS = {
    "emit_A": ["ident", "ones128", "ones1", "eps"],
    "emit_B": ["ident", "ones128", "ones1", "eps", "sel8"],
    "emit_E": ["ident", "ones128", "ones1", "eps", "sel8"],
    "emit_C": ["ident", "ones128", "ones1", "eps", "pair2", "onesf1", "posc",
               "oh4", "selb4"],
    "emit_D": None,
    "emit_F": None,
}


def _const_need(emitter_name):
    base = ["ident", "ones128", "ones1", "eps"]
    if emitter_name in ("emit_A",):
        return base
    if emitter_name in ("emit_B", "emit_E"):
        return base + ["sel8"]
    if emitter_name == "emit_C":
        return base + ["pair2", "onesf1", "posc", "oh4", "selb4"]
    if emitter_name == "emit_D":
        return base + ["pair2", "oh4", "selb4", "r2t"]
    if emitter_name == "emit_F":
        return base + ["pair2", "oh4", "selb4", "onesf1"]
    return None


def build_launch(emitter, wshapes):
    global DEBUG_TAPS
    nc = bass.Bass("TRN2", target_bir_lowering=False)
    x_in = nc.dram_tensor("x_in", [D, T], F32, kind="ExternalInput").ap()
    x_out = nc.dram_tensor("x_out", [D, T], F32, kind="ExternalOutput").ap()
    Wd = {}
    for name, shp in wshapes.items():
        Wd[name] = nc.dram_tensor("w_" + name, list(shp), F32,
                                  kind="ExternalInput").ap()
    if DEBUG_TAPS is not None:
        DEBUG_TAPS = {
            "s": nc.dram_tensor("tap_s", [128, T], F32, kind="ExternalOutput").ap(),
            "u": nc.dram_tensor("tap_u", [128, T], F32, kind="ExternalOutput").ap(),
            "hs": nc.dram_tensor("tap_hs", [128, T], F32, kind="ExternalOutput").ap(),
            "xn": nc.dram_tensor("tap_xn", [D, T], F32, kind="ExternalOutput").ap(),
        }
    stack = ExitStack()
    with stack:
        al = Alloc(nc, stack)
        P = Plan(nc, stack)
        P.alloc = al
        consts, ctoks = declare_consts(nc, al, P, make_consts(),
                                       need=_const_need(emitter.__name__))
        P.waits("tensor", ctoks)
        P.waits("vector", ctoks)
        P.waits("scalar", ctoks)
        out_toks = emitter(P, al, consts, x_in, x_out, Wd)
        P.waits("sync", out_toks)
        if DEBUG_TAPS is not None:
            P.waits("gpsimd", DEBUG_TAPS["toks"])
        P.build()
    return nc


def run_layer(emitter, W, x_fm, trace=False):
    consts = make_consts()
    Ws = W if isinstance(W, list) else [W] * 8
    wshapes = {k: v.shape for k, v in Ws[0].items()}
    nc = build_launch(emitter, wshapes)
    need = _const_need(emitter.__name__)
    in_maps = []
    for c in range(8):
        m = {"x_in": x_fm[c]}
        for k, v in Ws[c].items():
            m["w_" + k] = v
        for k, v in consts.items():
            if need is None or k in need:
                m["c_" + k] = v
        in_maps.append(m)
    res = run_bass_kernel_spmd(nc, in_maps, core_ids=list(range(8)), trace=trace)
    return [res.results[c]["x_out"] for c in range(8)]


def kernel(**inputs):
    x = np.asarray(inputs["L0_attn_a_x"], dtype=np.float32)
    x_fm = [np.ascontiguousarray(x[c].T) for c in range(8)]
    if FUSED:
        x_fm = run_fused(inputs, x_fm)
        return np.stack([np.ascontiguousarray(np.asarray(xc).T)
                         for xc in x_fm], axis=0)
    WA = prep_A_weights(inputs)
    WB = prep_B_weights(inputs)
    WC = prep_C_weights(inputs)
    WD = prep_D_weights(inputs)
    WEs = [prep_E_weights(inputs, c) for c in range(8)]
    WF = prep_F_weights(inputs)
    x_fm = run_layer(emit_A, WA, x_fm)
    x_fm = run_layer(emit_B, WB, x_fm)
    x_fm = run_layer(emit_C, WC, x_fm)
    x_fm = run_layer(emit_B, WB, x_fm)
    x_fm = run_layer(emit_D, WD, x_fm)
    x_fm = run_layer(emit_E, WEs, x_fm)
    x_fm = run_layer(emit_F, WF, x_fm)
    x_fm = run_layer(emit_E, WEs, x_fm)
    out = np.stack([np.ascontiguousarray(np.asarray(xc).T) for xc in x_fm], axis=0)
    return out



def emit_router_logits(P, al, pp, xn, wr, wr_toks, name):
    lgall = al.sb(name + "_lg", [128, 32 * 8], F32)
    P.waits("tensor", wr_toks)
    for tc in range(32):
        ib, b, g = pp.acquire()
        P.wait("tensor", g)
        tt = tc // 4
        co = (tc % 4) * 128
        for kc in range(NC):
            P.mm(b[:, 0:8], tile(xn, kc, tt)[:, co:co + 128], wr[:, kc * 8:(kc + 1) * 8],
                 start=(kc == 0), stop=(kc == NC - 1),
                 inc="ts" if kc == NC - 1 else None)
        t = P.now("ts")
        P.wait("vector", t)
        ev = P.op("vector", "tensor_copy", lgall[:, tc * 8:(tc + 1) * 8], b[:, 0:8],
                  inc="vs", drain=True)
        pp.release(ib, ev)
    return lgall, P.now("vs")


def emit_B(P, al, consts, x_in, x_out, W):
    seed_toks = [P.dma("sync", x_out[oc * 128:(oc + 1) * 128, :],
                       x_in[oc * 128:(oc + 1) * 128, :]) for oc in range(NC)]
    """L0_ffn: x_out = x_in + (moe + shared)(LN(x_in))."""
    pp = PsumPool(P, [al.ps(f"ppB{i}", [128, 512]) for i in range(8)])
    xn = al.sb("B_xn", [128, NC * T], BF16)

    st_ln = ExitStack()
    ln_toks = emit_ln(P, al, "B_ln", xn, consts, pp, x_dram=x_in, substack=st_ln)
    wr, wr_toks = load_weight(P, al, "B_wr", W["w_router"], 1024, 8)
    st_ln.close()

    P.waits("tensor", ln_toks)
    lgall, lg_tok = emit_router_logits(P, al, pp, xn, wr, wr_toks, "B")
    w8all = al.sb("B_w8", [128, 32 * 8], BF16)
    v12 = al.sb("B_v12", [128, 4 * 32], F32)
    m1a = al.sb("B_m1a", [128, 256], BF16)
    m2a = al.sb("B_m2a", [128, 256], BF16)
    tmsk = al.sb("B_tmsk", [128, 256], F32)
    P.wait("vector", lg_tok)
    P.waits("scalar", [lg_tok])
    lg3 = lgall[:, :].rearrange("p (g e) -> p g e", e=8)
    v1 = v12[:, 0:32]
    v2 = v12[:, 32:64]
    dd = v12[:, 64:96]
    s1 = v12[:, 96:128]
    P.op("vector", "tensor_reduce", v1, lg3, axis=AX.X, op=ALU.max)
    P.op("vector", "tensor_tensor", m1a[:, :].rearrange("p (g e) -> p g e", e=8),
         lg3, v1.unsqueeze(2).broadcast_to([128, 32, 8]), op=ALU.is_ge)
    P.op("vector", "scalar_tensor_tensor", tmsk[:, :], m1a[:, :], -1e30,
         lgall[:, :], op0=ALU.mult, op1=ALU.add)
    P.op("vector", "tensor_reduce", v2, tmsk[:, :].rearrange("p (g e) -> p g e", e=8),
         axis=AX.X, op=ALU.max)
    P.op("vector", "tensor_tensor", m2a[:, :].rearrange("p (g e) -> p g e", e=8),
         tmsk[:, :].rearrange("p (g e) -> p g e", e=8),
         v2.unsqueeze(2).broadcast_to([128, 32, 8]), op=ALU.is_ge)
    P.op("vector", "tensor_tensor", dd, v1, v2, op=ALU.subtract, inc="vs",
         drain=True)
    P.wait("scalar", P.now("vs"))
    P.op("scalar", "activation", s1, dd, AF.Sigmoid, inc="ss", drain=True)
    P.wait("vector", P.now("ss"))
    P.op("vector", "tensor_tensor", tmsk[:, :].rearrange("p (g e) -> p g e", e=8),
         m1a[:, :].rearrange("p (g e) -> p g e", e=8),
         s1.unsqueeze(2).broadcast_to([128, 32, 8]), op=ALU.mult)
    P.op("vector", "tensor_scalar", s1, s1, -1.0, 1.0, op0=ALU.mult, op1=ALU.add)
    P.op("vector", "tensor_tensor", m2a[:, :].rearrange("p (g e) -> p g e", e=8),
         m2a[:, :].rearrange("p (g e) -> p g e", e=8),
         s1.unsqueeze(2).broadcast_to([128, 32, 8]), op=ALU.mult)
    P.op("vector", "tensor_tensor", w8all[:, :], tmsk[:, :], m2a[:, :],
         op=ALU.add, inc="vs", drain=True)
    w8_tok = P.now("vs")

    w8T = al.sb("B_w8T", [8, T], BF16)
    P.wait("tensor", w8_tok)
    for tc in range(32):
        ib, b, g = pp.acquire()
        P.wait("tensor", g)
        bb = b.bitcast(BF16)
        t = P.op("tensor", "transpose", bb[0:8, 0:128],
                 w8all[:, tc * 8:(tc + 1) * 8], consts["ident"], inc="ts")
        P.wait("vector", t)
        ev = P.op("vector", "tensor_copy", w8T[:, tc * 128:(tc + 1) * 128],
                  bb[0:8, 0:128], inc="vs", drain=True)
        pp.release(ib, ev)
    w8T_tok = P.now("vs")

    seeded = [False]
    st_sh = ExitStack()
    P.fence("gpsimd")
    w_gt, t_gt = load_weight(P, al, "B_wgt", W["gate_w"], 1024, 2048, stack=st_sh)
    w_up, t_up = load_weight(P, al, "B_wup", W["up_w"], 1024, 2048, stack=st_sh)
    w_dn, t_dn = load_weight(P, al, "B_wdn", W["down_w"], 2048, 1024, stack=st_sh)
    sil = al.sb("B_sil", [128, 4 * TTW], BF16, stack=st_sh)
    z = al.sb("B_z", [128, 16 * TTW], BF16, stack=st_sh)
    ost = al.sb("B_ost", [128, 4 * TTW], F32, stack=st_sh)
    P.waits("tensor", t_gt + t_up + t_dn + ln_toks)
    ost_rel = [None] * 4
    sil_rel = [None] * 4
    z_rel = [None] * 16
    for tt in range(NTT):
        for oc in range(16):
            i = oc % 4
            ib, b, g = pp.acquire()
            P.wait("tensor", g)
            for kc in range(NC):
                P.mm(b[:, :], wslice(w_gt, kc, oc, 2048), tile(xn, kc, tt),
                     start=(kc == 0), stop=(kc == NC - 1),
                     inc="ts" if kc == NC - 1 else None)
            t = P.now("ts")
            P.wait("scalar", t)
            if sil_rel[i] is not None:
                P.wait("scalar", sil_rel[i])
            sg = P.op("scalar", "activation", sil[:, i * TTW:(i + 1) * TTW],
                      b[:, :], AF.Silu, inc="ss", drain=True)
            pp.release(ib, sg)
            ib2, b2, g2 = pp.acquire()
            P.wait("tensor", g2)
            for kc in range(NC):
                P.mm(b2[:, :], wslice(w_up, kc, oc, 2048), tile(xn, kc, tt),
                     start=(kc == 0), stop=(kc == NC - 1),
                     inc="ts" if kc == NC - 1 else None)
            t2 = P.now("ts")
            P.waits("vector", [t2, sg])
            if z_rel[oc] is not None:
                P.wait("vector", z_rel[oc])
            zt = P.op("vector", "tensor_tensor", z[:, oc * TTW:(oc + 1) * TTW],
                      b2[:, :], sil[:, i * TTW:(i + 1) * TTW], op=ALU.mult,
                      inc="vs", drain=True)
            sil_rel[i] = zt
            pp.release(ib2, zt)
        z_tok = P.now("vs")
        for oc in range(NC):
            ib, b, g = pp.acquire()
            P.waits("tensor", [g, z_tok])
            for kc in range(16):
                P.mm(b[:, :], wslice(w_dn, kc, oc, 1024),
                     z[:, kc * TTW:(kc + 1) * TTW],
                     start=(kc == 0), stop=(kc == 15),
                     inc="ts" if kc == 15 else None)
            t = P.now("ts")
            i = (tt * NC + oc) % 4
            P.wait("vector", t)
            if ost_rel[i] is not None:
                P.wait("vector", ost_rel[i])
            ev = P.op("vector", "tensor_copy", ost[:, i * TTW:(i + 1) * TTW],
                      b[:, :], inc="vs", drain=True)
            pp.release(ib, ev)
            if oc == NC - 1:
                for kc2 in range(16):
                    z_rel[kc2] = P.now("ts")
            P.wait("gpsimd", ev)
            if not seeded[0]:
                P.waits("gpsimd", seed_toks)
                seeded[0] = True
            st = P.dma("gpsimd",
                       x_out[oc * 128:(oc + 1) * 128, tt * TTW:(tt + 1) * TTW],
                       ost[:, i * TTW:(i + 1) * TTW], accum_op=ALU.add)
            ost_rel[i] = st
    st_sh.close()

    st_e = ExitStack()
    P.fence("gpsimd")
    fc1 = al.sb("B_fc1", [128, 8 * 2048], BF16, stack=st_e)
    fc2 = al.sb("B_fc2", [128, 8 * 1024], BF16, stack=st_e)
    wbc = al.sb("B_wbc", [128, T], BF16, stack=st_e)
    sil2 = al.sb("B_sil2", [128, 4 * TTW], BF16, stack=st_e)
    z2 = al.sb("B_z2", [128, 8 * TTW], BF16, stack=st_e)
    est = al.sb("B_est", [128, 6 * TTW], BF16, stack=st_e)
    P.waits("tensor", [w8T_tok])
    fc_rel = None
    wbc_rel = None
    est_rel = [None] * 6
    sil2_rel = [None] * 4
    z2_rel = [None] * 8
    cnt = [0]
    for e in range(8):
        t_f1 = []
        t_f2 = []
        for kc in range(NC):
            if fc_rel is not None:
                P.wait("gpsimd", fc_rel)
            t_f1.append(P.dma("gpsimd", fc1[:, kc * 2048:(kc + 1) * 2048],
                              W["fc1"][e * 1024 + kc * 128: e * 1024 + (kc + 1) * 128, :]))
            t_f2.append(P.dma("gpsimd", fc2[:, kc * 1024:(kc + 1) * 1024],
                              W["fc2"][e * 1024 + kc * 128: e * 1024 + (kc + 1) * 128, :]))
        if wbc_rel is not None:
            P.wait("tensor", wbc_rel)
        for tt in range(NTT):
            ib, b, g = pp.acquire()
            P.wait("tensor", g)
            t = P.mm(b[:, :], consts["sel8"][:, e * 128:(e + 1) * 128],
                     w8T[:, tt * TTW:(tt + 1) * TTW], start=True, stop=True,
                     inc="ts")
            P.wait("vector", t)
            ev = P.op("vector", "tensor_copy", wbc[:, tt * TTW:(tt + 1) * TTW],
                      b[:, :], inc="vs", drain=True)
            pp.release(ib, ev)
        wbc_tok = P.now("vs")
        P.waits("tensor", t_f1 + t_f2 + [wbc_tok])
        for tt in range(NTT):
            for zc in range(NC):
                ib, b, g = pp.acquire()
                P.wait("tensor", g)
                for kc in range(NC):
                    P.mm(b[:, :], wslice(fc1, kc, zc, 2048), tile(xn, kc, tt),
                         start=(kc == 0), stop=(kc == NC - 1), inc="ts")
                t = P.now("ts")
                i2 = zc % 4
                P.wait("scalar", t)
                if sil2_rel[i2] is not None:
                    P.wait("scalar", sil2_rel[i2])
                sg = P.op("scalar", "activation", sil2[:, i2 * TTW:(i2 + 1) * TTW],
                          b[:, :], AF.Silu, inc="ss", drain=True)
                pp.release(ib, sg)
                ib2, b2, g2 = pp.acquire()
                P.wait("tensor", g2)
                for kc in range(NC):
                    P.mm(b2[:, :], wslice(fc1, kc, zc + 8, 2048), tile(xn, kc, tt),
                         start=(kc == 0), stop=(kc == NC - 1), inc="ts")
                t2 = P.now("ts")
                P.waits("vector", [t2, sg])
                if z2_rel[zc] is not None:
                    P.wait("vector", z2_rel[zc])
                zt = P.op("vector", "tensor_tensor", z2[:, zc * TTW:(zc + 1) * TTW],
                          b2[:, :], sil2[:, i2 * TTW:(i2 + 1) * TTW], op=ALU.mult,
                          inc="vs", drain=True)
                sil2_rel[i2] = zt
                pp.release(ib2, zt)
            z_tok = P.now("vs")
            for oc in range(NC):
                ib, b, g = pp.acquire()
                P.waits("tensor", [g, z_tok])
                for kc in range(NC):
                    P.mm(b[:, :], wslice(fc2, kc, oc, 1024),
                         z2[:, kc * TTW:(kc + 1) * TTW],
                         start=(kc == 0), stop=(kc == NC - 1), inc="ts")
                t = P.now("ts")
                i = cnt[0] % 6
                cnt[0] += 1
                P.wait("vector", t)
                if est_rel[i] is not None:
                    P.wait("vector", est_rel[i])
                ev = P.op("vector", "tensor_tensor", est[:, i * TTW:(i + 1) * TTW],
                          b[:, :], wbc[:, tt * TTW:(tt + 1) * TTW], op=ALU.mult,
                          inc="vs", drain=True)
                pp.release(ib, ev)
                if oc == NC - 1:
                    for kc2 in range(NC):
                        z2_rel[kc2] = P.now("ts")
                P.wait("gpsimd", ev)
                st = P.dma("gpsimd",
                           x_out[oc * 128:(oc + 1) * 128, tt * TTW:(tt + 1) * TTW],
                           est[:, i * TTW:(i + 1) * TTW], accum_op=ALU.add)
                est_rel[i] = st
        fc_rel = P.now("ts")
        wbc_rel = P.now("vs")
    st_e.close()

    return P.dma_all_done("gpsimd") + seed_toks


def prep_B_weights(inp):
    W = {}
    W["w_router"] = _np(inp["L0_ffn_w_router"])
    W["fc1"] = _np(inp["L0_ffn_fc1_w"]).reshape(8 * 1024, 2048)
    W["fc2"] = _np(inp["L0_ffn_fc2_w"]).reshape(8 * 1024, 1024)
    W["gate_w"] = _np(inp["L0_ffn_gate_w"])
    W["up_w"] = _np(inp["L0_ffn_up_w"])
    W["down_w"] = _np(inp["L0_ffn_down_w"])
    return W


DEBUG_NO_CC = False
CAPACITY = 5120


def emit_E(P, al, consts, x_in, x_out, W):
    seed_toks = [P.dma("sync", x_out[oc * 128:(oc + 1) * 128, :],
                       x_in[oc * 128:(oc + 1) * 128, :]) for oc in range(NC)]
    seeded = [False]
    """L1_ffn: capacity-constrained top-2 MoE (dense-masked compute)."""
    pp = PsumPool(P, [al.ps(f"ppE{i}", [128, 512]) for i in range(8)])
    xn = al.sb("E_xn", [128, NC * T], BF16)
    hid_dram = P.dram("E_hid", [2048, T], BF16)
    cc_in = P.dram("E_cci", [1, 512], F32)
    cc_out = P.dram("E_cco", [8, 512], F32)
    off_dram = P.dram("E_off", [8, 1], F32)

    st_ln = ExitStack()
    ln_toks = emit_ln(P, al, "E_ln", xn, consts, pp, x_dram=x_in, substack=st_ln)
    wr, wr_toks = load_weight(P, al, "E_wr", W["w_routerT"], 1024, 8)
    cmask = al.sb("E_cmask", [8, 1], F32)
    t_cm = P.dma("gpsimd", cmask[:, :], W["coremask"])
    st_ln.close()

    P.waits("tensor", ln_toks)
    lgall, lg_tok = emit_router_logits(P, al, pp, xn, wr, wr_toks, "E")
    st_rt = ExitStack()
    m1all = al.sb("E_m1", [128, 32 * 8], BF16, stack=st_rt)
    m2all = al.sb("E_m2", [128, 32 * 8], BF16, stack=st_rt)
    g12 = al.sb("E_g12", [128, 2 * 32], F32, stack=st_rt)
    v12 = al.sb("E_v12", [128, 4 * 32], F32, stack=st_rt)
    tmsk = al.sb("E_tmsk", [128, 256], F32, stack=st_rt)
    ez = al.sb("E_ez", [128, 256], F32, stack=st_rt)
    kepta = al.sb("E_kepta", [128, 256], BF16, stack=st_rt)
    maskT = al.sb("E_maskT", [8, T], F32, stack=st_rt)
    cum = al.sb("E_cum", [8, T], F32, stack=st_rt)
    keepT = al.sb("E_keepT", [8, T], BF16, stack=st_rt)
    w8all = al.sb("E_w8", [128, 32 * 8], BF16, stack=st_rt)
    P.wait("vector", lg_tok)
    P.waits("scalar", [lg_tok])
    lg3 = lgall[:, :].rearrange("p (g e) -> p g e", e=8)
    v1 = v12[:, 0:32]
    v2 = v12[:, 32:64]
    den = v12[:, 64:96]
    scx = v12[:, 96:128]
    P.op("vector", "tensor_reduce", v1, lg3, axis=AX.X, op=ALU.max)
    P.op("vector", "tensor_tensor", m1all[:, :].rearrange("p (g e) -> p g e", e=8),
         lg3, v1.unsqueeze(2).broadcast_to([128, 32, 8]), op=ALU.is_ge)
    P.op("vector", "scalar_tensor_tensor", tmsk[:, :], m1all[:, :], -1e30,
         lgall[:, :], op0=ALU.mult, op1=ALU.add)
    P.op("vector", "tensor_reduce", v2,
         tmsk[:, :].rearrange("p (g e) -> p g e", e=8), axis=AX.X, op=ALU.max)
    P.op("vector", "tensor_tensor", m2all[:, :].rearrange("p (g e) -> p g e", e=8),
         tmsk[:, :].rearrange("p (g e) -> p g e", e=8),
         v2.unsqueeze(2).broadcast_to([128, 32, 8]), op=ALU.is_ge)
    P.op("vector", "tensor_tensor", ez[:, :].rearrange("p (g e) -> p g e", e=8),
         lg3, v1.unsqueeze(2).broadcast_to([128, 32, 8]), op=ALU.subtract,
         inc="vs", drain=True)
    P.wait("scalar", P.now("vs"))
    P.op("scalar", "activation", ez[:, :], ez[:, :], AF.Exp, inc="ss", drain=True)
    P.wait("vector", P.now("ss"))
    P.op("vector", "tensor_reduce", den,
         ez[:, :].rearrange("p (g e) -> p g e", e=8), axis=AX.X, op=ALU.add)
    P.op("vector", "reciprocal", g12[:, 0:32], den)
    P.op("vector", "tensor_tensor", scx, v2, v1, op=ALU.subtract, inc="vs",
         drain=True)
    P.wait("scalar", P.now("vs"))
    P.op("scalar", "activation", scx, scx, AF.Exp, inc="ss", drain=True)
    P.wait("vector", P.now("ss"))
    P.op("vector", "tensor_tensor", g12[:, 32:64], scx, g12[:, 0:32],
         op=ALU.mult)
    P.op("vector", "tensor_tensor", w8all[:, :], m1all[:, :], m2all[:, :],
         op=ALU.add, inc="vs", drain=True)
    rt_tok = P.now("vs")

    P.wait("tensor", rt_tok)
    for tc in range(32):
        ib, b, g = pp.acquire()
        P.wait("tensor", g)
        bb = b.bitcast(BF16)
        t = P.op("tensor", "transpose", bb[0:8, 0:128],
                 w8all[:, tc * 8:(tc + 1) * 8], consts["ident"], inc="ts")
        P.wait("vector", t)
        ev = P.op("vector", "tensor_copy", maskT[:, tc * 128:(tc + 1) * 128],
                  bb[0:8, 0:128], inc="vs", drain=True)
        pp.release(ib, ev)
    mt_tok = P.now("vs")
    P.op("vector", "tensor_tensor_scan", cum[:, :], maskT[:, :], maskT[:, :],
         initial=0.0, op0=ALU.add, op1=ALU.bypass, inc="vs", drain=True)
    cum_tok = P.now("vs")
    P.wait("gpsimd", cum_tok)
    t1 = P.dma("gpsimd", cc_in[0:1, 0:8], cum[:, T - 1:T])
    P.wait("gpsimd", t1)
    if not DEBUG_NO_CC:
        P.op("gpsimd", "collective_compute", "AllGather", ALU.bypass,
             replica_groups=[list(range(8))], ins=[cc_in], outs=[cc_out],
             inc="cs", pre_drain=False)
    st_h = ExitStack()
    P.fence("gpsimd")
    w_gt, t_gt = load_weight(P, al, "E_wgt", W["w_gateT"], 1024, 2048, stack=st_h)
    hst = al.sb("E_hst", [128, 4 * TTW], BF16, stack=st_h)
    P.waits("tensor", t_gt)
    hst_rel = [None] * 4
    k = [0]
    for tt in range(NTT):
        for oc in range(16):
            ib, b, g = pp.acquire()
            P.wait("tensor", g)
            for kc in range(NC):
                P.mm(b[:, :], wslice(w_gt, kc, oc, 2048), tile(xn, kc, tt),
                     start=(kc == 0), stop=(kc == NC - 1),
                     inc="ts" if kc == NC - 1 else None)
            t = P.now("ts")
            i = k[0] % 4
            k[0] += 1
            P.wait("scalar", t)
            if hst_rel[i] is not None:
                P.wait("scalar", hst_rel[i])
            sg = P.op("scalar", "activation", hst[:, i * TTW:(i + 1) * TTW],
                      b[:, :], AF.Silu, inc="ss", drain=True)
            pp.release(ib, sg)
            P.wait("gpsimd", sg)
            st = P.dma("gpsimd",
                       hid_dram[oc * 128:(oc + 1) * 128, tt * TTW:(tt + 1) * TTW],
                       hst[:, i * TTW:(i + 1) * TTW])
            hst_rel[i] = st
    st_h.close()


    gath = al.sb("E_gath", [8, 8], F32, stack=st_rt)
    offs_p = al.sb("E_offsp", [1, 8], F32, stack=st_rt)
    offs8 = al.sb("E_offs8", [8, 1], F32, stack=st_rt)
    if DEBUG_NO_CC:
        t2 = P.op("gpsimd", "memset", gath[:, :], 0.0, inc="gs", drain=True,
                  pre_drain=False)
    else:
        P.wait("gpsimd", P.now("cs"))
        t2 = P.dma("gpsimd", gath[:, :], cc_out[:, 0:8])
    ib, b, g = pp.acquire()
    P.waits("tensor", [g, t2, t_cm])
    t3 = P.mm(b[0:1, 0:8], cmask[:, :], gath[:, :], start=True, stop=True, inc="ts")
    P.wait("vector", t3)
    ev = P.op("vector", "tensor_copy", offs_p[:, :], b[0:1, 0:8], inc="vs",
              drain=True)
    pp.release(ib, ev)
    P.wait("gpsimd", P.now("vs"))
    t4 = P.dma("gpsimd", off_dram[0:8, 0:1], offs_p[0:1, 0:8])
    P.wait("gpsimd", t4)
    t5 = P.dma("gpsimd", offs8[:, :], off_dram)
    P.waits("vector", [t5])
    P.op("vector", "tensor_scalar", keepT[:, :], cum[:, :], offs8[:, :],
         CAPACITY + 0.5, op0=ALU.add, op1=ALU.is_le)
    P.op("vector", "tensor_tensor", keepT[:, :], keepT[:, :], maskT[:, :],
         op=ALU.mult, inc="vs", drain=True)
    kt_tok = P.now("vs")

    P.wait("tensor", kt_tok)
    for tc in range(32):
        ib, b, g = pp.acquire()
        P.wait("tensor", g)
        bb = b.bitcast(BF16)
        t = P.op("tensor", "transpose", bb[0:128, 0:8],
                 keepT[:, tc * 128:(tc + 1) * 128], consts["ident"][0:8, 0:8],
                 inc="ts")
        P.wait("vector", t)
        ev = P.op("vector", "tensor_copy", kepta[:, tc * 8:(tc + 1) * 8],
                  bb[0:128, 0:8], inc="vs", drain=True)
        pp.release(ib, ev)
    P.op("vector", "tensor_tensor", tmsk[:, :], kepta[:, :], m1all[:, :],
         op=ALU.mult)
    P.op("vector", "tensor_reduce", v1,
         tmsk[:, :].rearrange("p (g e) -> p g e", e=8), axis=AX.X, op=ALU.add)
    P.op("vector", "tensor_tensor", tmsk[:, :], kepta[:, :], m2all[:, :],
         op=ALU.mult)
    P.op("vector", "tensor_reduce", v2,
         tmsk[:, :].rearrange("p (g e) -> p g e", e=8), axis=AX.X, op=ALU.add)
    P.op("vector", "tensor_tensor", v1, v1, g12[:, 0:32], op=ALU.mult)
    P.op("vector", "tensor_tensor", v2, v2, g12[:, 32:64], op=ALU.mult)
    P.op("vector", "scalar_tensor_tensor", den, v1, 1.0, v2, op0=ALU.mult,
         op1=ALU.add)
    P.op("vector", "tensor_scalar", den, den, 1e-6, None, op0=ALU.add)
    P.op("vector", "reciprocal", scx, den)
    P.op("vector", "tensor_tensor", v1, v1, scx, op=ALU.mult)
    P.op("vector", "tensor_tensor", v2, v2, scx, op=ALU.mult)
    P.op("vector", "tensor_tensor", tmsk[:, :].rearrange("p (g e) -> p g e", e=8),
         m1all[:, :].rearrange("p (g e) -> p g e", e=8),
         v1.unsqueeze(2).broadcast_to([128, 32, 8]), op=ALU.mult)
    P.op("vector", "tensor_tensor", ez[:, :].rearrange("p (g e) -> p g e", e=8),
         m2all[:, :].rearrange("p (g e) -> p g e", e=8),
         v2.unsqueeze(2).broadcast_to([128, 32, 8]), op=ALU.mult)
    P.op("vector", "tensor_tensor", w8all[:, :], tmsk[:, :], ez[:, :],
         op=ALU.add, inc="vs", drain=True)
    w8_tok = P.now("vs")

    w8T = al.sb("E_w8T", [8, T], BF16)
    P.wait("tensor", w8_tok)
    for tc in range(32):
        ib, b, g = pp.acquire()
        P.wait("tensor", g)
        bb = b.bitcast(BF16)
        t = P.op("tensor", "transpose", bb[0:8, 0:128],
                 w8all[:, tc * 8:(tc + 1) * 8], consts["ident"], inc="ts")
        P.wait("vector", t)
        ev = P.op("vector", "tensor_copy", w8T[:, tc * 128:(tc + 1) * 128],
                  bb[0:8, 0:128], inc="vs", drain=True)
        pp.release(ib, ev)
    w8T_tok = P.now("vs")
    st_rt.close()

    st_e = ExitStack()
    P.fence("gpsimd")
    wdn = al.sb("E_wdn", [128, 16 * 1024], BF16, stack=st_e)
    hid = al.sb("E_hidsb", [128, 2 * 16 * TTW], BF16, stack=st_e)
    wbc = al.sb("E_wbc", [128, T], BF16, stack=st_e)
    est = al.sb("E_est", [128, 6 * TTW], BF16, stack=st_e)
    wdn_rel = None
    wbc_rel = None
    hid_rel = [None, None]
    est_rel = [None] * 6
    cnt = [0]
    for e in range(8):
        t_w = []
        for kc in range(16):
            if wdn_rel is not None:
                P.wait("gpsimd", wdn_rel)
            t_w.append(P.dma("gpsimd", wdn[:, kc * 1024:(kc + 1) * 1024],
                             W["w_downT"][e * 2048 + kc * 128: e * 2048 + (kc + 1) * 128, :]))
        if wbc_rel is not None:
            P.wait("tensor", wbc_rel)
        for tt in range(NTT):
            ib, b, g = pp.acquire()
            P.wait("tensor", g)
            t = P.mm(b[:, :], consts["sel8"][:, e * 128:(e + 1) * 128],
                     w8T[:, tt * TTW:(tt + 1) * TTW], start=True, stop=True,
                     inc="ts")
            P.wait("vector", t)
            ev = P.op("vector", "tensor_copy", wbc[:, tt * TTW:(tt + 1) * TTW],
                      b[:, :], inc="vs", drain=True)
            pp.release(ib, ev)
        wbc_tok = P.now("vs")
        P.waits("tensor", t_w + [wbc_tok])
        for tt in range(NTT):
            ih = tt % 2
            if hid_rel[ih] is not None:
                P.wait("gpsimd", hid_rel[ih])
            src_ap = bass.AP(hid_dram.tensor, tt * TTW,
                             [[T, 128], [128 * T, 16], [1, TTW]])
            ht = P.dma("gpsimd", hid[:, ih * 16 * TTW:(ih + 1) * 16 * TTW],
                       src_ap)
            P.wait("tensor", ht)
            for oc in range(NC):
                ib, b, g = pp.acquire()
                P.wait("tensor", g)
                for kc in range(16):
                    P.mm(b[:, :], wslice(wdn, kc, oc, 1024),
                         hid[:, ih * 16 * TTW + kc * TTW: ih * 16 * TTW + (kc + 1) * TTW],
                         start=(kc == 0), stop=(kc == 15), inc="ts")
                t = P.now("ts")
                i = cnt[0] % 6
                cnt[0] += 1
                P.wait("vector", t)
                if est_rel[i] is not None:
                    P.wait("vector", est_rel[i])
                ev = P.op("vector", "tensor_tensor", est[:, i * TTW:(i + 1) * TTW],
                          b[:, :], wbc[:, tt * TTW:(tt + 1) * TTW], op=ALU.mult,
                          inc="vs", drain=True)
                pp.release(ib, ev)
                if oc == NC - 1:
                    hid_rel[ih] = P.now("ts")
                P.wait("gpsimd", ev)
                if not seeded[0]:
                    P.waits("gpsimd", seed_toks)
                    seeded[0] = True
                st = P.dma("gpsimd",
                           x_out[oc * 128:(oc + 1) * 128,
                                 tt * TTW:(tt + 1) * TTW],
                           est[:, i * TTW:(i + 1) * TTW], accum_op=ALU.add)
                est_rel[i] = st
        wdn_rel = P.now("ts")
        wbc_rel = P.now("vs")
    st_e.close()

    return P.dma_all_done("gpsimd") + seed_toks


def prep_E_weights(inp, core_id):
    W = {}
    W["w_routerT"] = _np(np.asarray(inp["L1_ffn_w_router"]).T)
    W["w_gateT"] = _np(np.asarray(inp["L1_ffn_w_gate"]).T)
    wd = np.asarray(inp["L1_ffn_w_down"])
    W["w_downT"] = _np(wd.transpose(0, 2, 1)).reshape(8 * 2048, 1024)
    W["coremask"] = (np.arange(8) < core_id).astype(np.float32).reshape(8, 1)
    return W


def emit_C(P, al, consts, x_in, x_out, W):
    pp = PsumPool(P, [al.ps(f"ppC{i}", [128, 512]) for i in range(8)])
    nc = P.nc
    kext_dram = P.dram("C_kext", [65, 32 * 2048], BF16)
    vext_dram = P.dram("C_vext", [T, 1040], BF16)
    mq_dram = P.dram("C_mq", [2, NC * T], BF16)
    attn_dram = P.dram("C_attn", [D, T], BF16)

    xn = al.sb("C_xn", [128, NC * T], BF16)
    st_ln = ExitStack()
    ln_toks = emit_ln(P, al, "C_ln", xn, consts, pp, x_dram=x_in, substack=st_ln)
    b_q, tbq = load_bias(P, al, "C_bq", W["bq"], 1024)
    b_k, tbk = load_bias(P, al, "C_bk", W["bk"], 1024)
    b_v, tbv = load_bias(P, al, "C_bv", W["bv"], 1024)
    slr = al.sb("C_slr", [1, 16], F32)
    t_sl = P.dma("gpsimd", slr[:, :], W["slopes"])
    k2m = al.sb("C_k2m", [2, 8], F32)
    nkx = al.sb("C_nkx", [2, 8], F32)
    st_ln.close()
    qf = al.sb("C_qf", [128, NC * T], BF16)

    slbc = al.sb("C_slbc", [128, 16], F32)
    offt = al.sb("C_offt", [128, 16], F32)
    ztab = al.sb("C_ztab", [128, 512], F32)
    ctab = al.sb("C_ctab", [128, 512], F32)
    ib, b, g = pp.acquire()
    P.waits("tensor", [g, t_sl])
    t = P.mm(b[0:128, 0:16], consts["onesf1"], slr[:, :], start=True, stop=True,
             inc="ts")
    P.wait("vector", t)
    ev = P.op("vector", "tensor_copy", slbc[:, :], b[0:128, 0:16], inc="vs",
              drain=True)
    pp.release(ib, ev)
    P.op("vector", "tensor_scalar", offt[:, :], slbc[:, :], 4095.0, 0.0,
         op0=ALU.mult, op1=ALU.max)
    for h in range(16):
        P.op("vector", "tensor_scalar", ztab[:, h * 32:(h + 1) * 32],
             consts["posc"], slbc[:, h:h + 1], offt[:, h:h + 1],
             op0=ALU.mult, op1=ALU.subtract)
    P.op("vector", "tensor_copy", ztab[0:1, 0:1], ztab[0:1, 0:1], inc="vs",
         drain=True)
    P.wait("scalar", P.now("vs"))
    ct_tok = P.op("scalar", "activation", ctab[:, :], ztab[:, :], AF.Exp,
                  inc="ss", drain=True)

    st_k = ExitStack()
    P.fence("gpsimd")
    w_k, t_k = load_weight(P, al, "C_wk", W["WkT"], 1024, 1024, stack=st_k)
    ktt = al.sb("C_ktt", [128, NC * TTW], BF16, stack=st_k)
    ksq = al.sb("C_ksq", [128, NC * TTW], BF16, stack=st_k)
    k2s8 = al.sb("C_k2s8", [2, NC * TTW], F32, stack=st_k)
    k2p8 = al.sb("C_k2p8", [2, 8], F32, stack=st_k)
    kst = al.sb("C_kst", [65, 2 * 2048], BF16, stack=st_k)
    k2p = al.sb("C_k2p", [2, 4], F32, stack=st_k)
    k2s = al.sb("C_k2s", [2, TTW], F32, stack=st_k)
    P.op("vector", "memset", k2m[:, :], 0.0)
    P.op("vector", "memset", kst[64:65, :], 1.0, inc="vs", drain=True)
    P.waits("tensor", t_k + ln_toks)
    P.waits("vector", tbk)
    kst_rel = [None, None]
    ksq_guard = None
    for tt in range(NTT):
        ev_last = None
        for oc in range(NC):
            ib, b, g = pp.acquire()
            P.wait("tensor", g)
            for kc in range(NC):
                P.mm(b[:, :], wslice(w_k, kc, oc, 1024), tile(xn, kc, tt),
                     start=(kc == 0), stop=(kc == NC - 1),
                     inc="ts" if kc == NC - 1 else None)
            t = P.now("ts")
            P.wait("vector", t)
            evk = P.op("vector", "tensor_scalar", ktt[:, oc * TTW:(oc + 1) * TTW],
                       b[:, :], b_k[:, oc:oc + 1], None, op0=ALU.add,
                       inc="vs", drain=(oc == NC - 1))
            pp.release(ib, evk)
            ev_last = evk
        P.wait("scalar", ev_last)
        if ksq_guard is not None:
            P.wait("scalar", ksq_guard)
        sqt = P.op("scalar", "activation", ksq[:, :], ktt[:, :], AF.Square,
                   inc="ss", drain=True)
        P.wait("tensor", sqt)
        pbs = []
        for oc in range(NC):
            ib2, b2, g2 = pp.acquire()
            P.wait("tensor", g2)
            t2 = P.mm(b2[0:2, :], consts["pair2"],
                      ksq[:, oc * TTW:(oc + 1) * TTW],
                      start=True, stop=True, inc="ts")
            pbs.append((ib2, b2, t2))
        ksq_guard = P.now("ts")
        P.wait("vector", ksq_guard)
        for oc, (ib2, b2, t2) in enumerate(pbs):
            cp = P.op("vector", "tensor_copy", k2s8[:, oc * TTW:(oc + 1) * TTW],
                      b2[0:2, :], inc="vs", drain=(oc == NC - 1))
            pp.release(ib2, cp)
        P.op("vector", "tensor_reduce", k2p8[:, :],
             k2s8[:, :].rearrange("p (g e) -> p g e", e=TTW), axis=AX.X,
             op=ALU.max)
        P.op("vector", "tensor_tensor", k2m[:, :], k2m[:, :], k2p8[:, :],
             op=ALU.max, inc="vs", drain=True)
        _stage_k(P, al, pp, consts, ktt, tt, kst, kst_rel, kext_dram)
    P.wait("scalar", P.now("vs"))
    P.op("scalar", "activation", nkx[:, :], k2m[:, :], AF.Sqrt, inc="ss",
         drain=True)
    P.wait("vector", P.now("ss"))
    P.op("vector", "tensor_scalar", nkx[:, :], nkx[:, :], -1.0, None,
         op0=ALU.mult, inc="vs", drain=True)
    nkx_tok = P.now("vs")
    st_k.close()

    st_q = ExitStack()
    P.fence("gpsimd")
    w_q, t_q = load_weight(P, al, "C_wq", W["WqT"], 1024, 1024, stack=st_q)
    P.waits("tensor", t_q)
    P.waits("vector", tbq + [nkx_tok])
    for tt in range(NTT):
        for oc in range(NC):
            ib, b, g = pp.acquire()
            P.wait("tensor", g)
            for kc in range(NC):
                P.mm(b[:, :], wslice(w_q, kc, oc, 1024), tile(xn, kc, tt),
                     start=(kc == 0), stop=(kc == NC - 1),
                     inc="ts" if kc == NC - 1 else None)
            t = P.now("ts")
            P.wait("vector", t)
            evq = P.op("vector", "tensor_scalar", tile(qf, oc, tt),
                       b[:, :], b_q[:, oc:oc + 1], None, op0=ALU.add,
                       inc="vs", drain=True)
            pp.release(ib, evq)
    _mq_pass(P, al, pp, consts, qf, nkx, mq_dram, st_q)
    st_q.close()

    st_v = ExitStack()
    P.fence("gpsimd")
    w_v, t_v = load_weight(P, al, "C_wv", W["WvT"], 1024, 1024, stack=st_v)
    vtt = al.sb("C_vtt", [128, NC * TTW], BF16, stack=st_v)
    vtmp = al.sb("C_vtmp", [64, 16 * 128], BF16, stack=st_v)
    vst = al.sb("C_vst", [128, 2 * 1040], BF16, stack=st_v)
    P.waits("tensor", t_v)
    P.waits("vector", tbv)
    P.wait("vector", ct_tok)
    vst_rel = [None, None]
    vtmp_rel = [None, None]
    for tt in range(NTT):
        for oc in range(NC):
            ib, b, g = pp.acquire()
            P.wait("tensor", g)
            for kc in range(NC):
                P.mm(b[:, :], wslice(w_v, kc, oc, 1024), tile(xn, kc, tt),
                     start=(kc == 0), stop=(kc == NC - 1),
                     inc="ts" if kc == NC - 1 else None)
            t = P.now("ts")
            P.wait("vector", t)
            evv = P.op("vector", "tensor_scalar", vtt[:, oc * TTW:(oc + 1) * TTW],
                       b[:, :], b_v[:, oc:oc + 1], None, op0=ALU.add,
                       inc="vs", drain=True)
            pp.release(ib, evv)
        for s4 in range(4):
            kc = tt * 4 + s4
            j = kc % 2
            if vst_rel[j] is not None:
                P.wait("vector", vst_rel[j])
            for h in range(16):
                c, p = h // 2, h % 2
                jj = h % 2
                if vtmp_rel[jj] is not None:
                    P.wait("vector", vtmp_rel[jj])
                cp = P.op("vector", "tensor_copy",
                          vtmp[0:64, jj * 128:(jj + 1) * 128],
                          vtt[p * 64:p * 64 + 64,
                              c * TTW + s4 * 128: c * TTW + (s4 + 1) * 128],
                          inc="vs", drain=True)
                ib2, b2, g2 = pp.acquire()
                P.waits("tensor", [g2, cp])
                bb2 = b2.bitcast(BF16)
                tr = P.op("tensor", "transpose", bb2[0:128, 0:64],
                          vtmp[0:64, jj * 128:(jj + 1) * 128],
                          consts["ident"][0:64, 0:64], inc="ts")
                vtmp_rel[jj] = tr
                P.wait("vector", tr)
                P.op("vector", "tensor_scalar",
                     vst[:, j * 1040 + h * 65: j * 1040 + h * 65 + 64],
                     bb2[0:128, 0:64], ctab[:, h * 32 + kc: h * 32 + kc + 1],
                     None, op0=ALU.mult)
                ev3 = P.op("vector", "tensor_copy",
                           vst[:, j * 1040 + h * 65 + 64: j * 1040 + h * 65 + 65],
                           ctab[:, h * 32 + kc: h * 32 + kc + 1],
                           inc="vs", drain=True)
                pp.release(ib2, ev3)
            cpt = P.now("vs")
            P.wait("gpsimd", cpt)
            st = P.dma("gpsimd", vext_dram[kc * 128:(kc + 1) * 128, :],
                       vst[:, j * 1040:(j + 1) * 1040])
            vst_rel[j] = st
    st_v.close()

    emit_attn_win(P, al, pp, consts, qf, kext_dram, vext_dram, mq_dram,
                  attn_dram, win_kcs_C)

    st_wo = ExitStack()
    P.fence("gpsimd")
    P.fence("sync")
    w_o, t_o = load_weight(P, al, "C_wo", W["WoT"], 1024, 1024, stack=st_wo)
    b_o, tbo = load_bias(P, al, "C_bo", W["bo"], 1024, stack=st_wo)
    att = al.sb("C_att", [128, 2 * NC * TTW], BF16, stack=st_wo)
    xr = al.sb("C_xr", [128, 4 * TTW], F32, stack=st_wo)
    yt = al.sb("C_yt", [128, 4 * TTW], F32, stack=st_wo)
    P.waits("tensor", t_o)
    P.waits("vector", tbo)
    att_rel = [None, None]
    rel = {"xr": [None] * 4, "yt": [None] * 4}
    out_toks = []
    gp_done = P.dma_all_done("gpsimd")
    P.waits("gpsimd", gp_done)
    k4 = [0]
    for tt in range(NTT):
        ia = tt % 2
        if att_rel[ia] is not None:
            P.wait("gpsimd", att_rel[ia])
        src_ap = bass.AP(attn_dram.tensor, tt * TTW,
                         [[T, 128], [128 * T, NC], [1, TTW]])
        la = P.dma("gpsimd", att[:, ia * NC * TTW:(ia + 1) * NC * TTW], src_ap)
        P.wait("tensor", la)
        for oc in range(NC):
            ib, b, g = pp.acquire()
            P.wait("tensor", g)
            for kc in range(NC):
                P.mm(b[:, :], wslice(w_o, kc, oc, 1024),
                     att[:, ia * NC * TTW + kc * TTW: ia * NC * TTW + (kc + 1) * TTW],
                     start=(kc == 0), stop=(kc == NC - 1),
                     inc="ts" if kc == NC - 1 else None)
            t = P.now("ts")
            i = k4[0] % 4
            k4[0] += 1
            if rel["xr"][i] is not None:
                P.wait("sync", rel["xr"][i])
            lt = P.dma("sync", xr[:, i * TTW:(i + 1) * TTW],
                       x_in[oc * 128:(oc + 1) * 128, tt * TTW:(tt + 1) * TTW])
            P.waits("vector", [t, lt])
            if rel["yt"][i] is not None:
                P.wait("vector", rel["yt"][i])
            P.op("vector", "tensor_scalar", yt[:, i * TTW:(i + 1) * TTW],
                 b[:, :], b_o[:, oc:oc + 1], None, op0=ALU.add)
            vt = P.op("vector", "tensor_tensor", yt[:, i * TTW:(i + 1) * TTW],
                      yt[:, i * TTW:(i + 1) * TTW], xr[:, i * TTW:(i + 1) * TTW],
                      op=ALU.add, inc="vs", drain=True)
            pp.release(ib, vt)
            rel["xr"][i] = vt
            if oc == NC - 1:
                att_rel[ia] = P.now("ts")
            P.wait("sync", vt)
            ot = P.dma("sync",
                       x_out[oc * 128:(oc + 1) * 128, tt * TTW:(tt + 1) * TTW],
                       yt[:, i * TTW:(i + 1) * TTW])
            rel["yt"][i] = ot
            out_toks.append(ot)
    st_wo.close()
    return out_toks


def prep_C_weights(inp):
    W = {}
    W["WqT"] = _np(np.asarray(inp["L0_attn_b_Wq"]).T)
    W["WkT"] = _np(np.asarray(inp["L0_attn_b_Wk"]).T)
    W["WvT"] = _np(np.asarray(inp["L0_attn_b_Wv"]).T)
    W["WoT"] = _np(np.asarray(inp["L0_attn_b_Wo"]).T)
    W["bq"] = _bvec(inp["L0_attn_b_bq"])
    W["bk"] = _bvec(inp["L0_attn_b_bk"])
    W["bv"] = _bvec(inp["L0_attn_b_bv"])
    W["bo"] = _bvec(inp["L0_attn_b_bo"])
    W["slopes"] = _np(inp["L0_attn_b_slopes"]).reshape(1, 16)
    return W



_MASKS = {}
_MASK_BY_KEY = {}
_REL = (np.arange(512)[None, :] - np.arange(128)[:, None])


def _mask_name(kind, w, d):
    imj = 128 * d + _REL
    if kind == "D":
        m = (imj >= 0) & (imj <= 128)
    else:
        m = (imj >= 0) & (imj < w)
    if m.all():
        return None
    if not m.any():
        return "SKIP"
    key = m.tobytes()
    if key not in _MASK_BY_KEY:
        import ml_dtypes
        nm = f"mk{len(_MASK_BY_KEY)}"
        _MASK_BY_KEY[key] = nm
        _MASKS[nm] = m.astype(ml_dtypes.bfloat16)
    return _MASK_BY_KEY[key]


def win_kcs_D(h, qg):
    lo = max(0, 4 * qg - 1)
    out = []
    for kc in range(lo, 4 * qg + 4):
        nm = _mask_name("D", 128, 4 * qg - kc)
        if nm != "SKIP":
            out.append((kc, nm))
    return out


def win_kcs_F(h, qg):
    w = max(4, 1024 >> h)
    lo = max(0, (qg * 512 - w + 1) // 128)
    out = []
    for kc in range(lo, 4 * qg + 4):
        nm = _mask_name("F", w, 4 * qg - kc)
        if nm != "SKIP":
            out.append((kc, nm))
    return out


def _win_masks():
    for qg in range(8):
        for h in range(16):
            win_kcs_D(h, qg)
            win_kcs_F(h, qg)
    return dict(_MASKS)


def _mask_list(kcs_fn):
    names = set()
    for qg in range(8):
        for h in range(16):
            names |= {nm for (_, nm) in kcs_fn(h, qg) if nm}
    return sorted(names)


def prep_masks(kcs_fn):
    names = _mask_list(kcs_fn)
    arrs = [np.asarray(_MASKS[nm], dtype=np.float32) for nm in names]
    return np.ascontiguousarray(np.concatenate(arrs, axis=1))


def emit_attn_win(P, al, pp, consts, qf, kext_dram, vext_dram, mq_dram,
                  attn_dram, kcs_fn, gate_fn=None, masks_dram=None):
    st = ExitStack()
    P.fence()
    mnames = _mask_list(kcs_fn)
    midx = {nm: i for i, nm in enumerate(mnames)}
    if mnames:
        marena = al.sb("aw_masks", [128, len(mnames) * TTW], BF16, stack=st)
        mt = P.dma("gpsimd", marena[:, :], masks_dram)
        P.waits("vector", [mt])

    def mslice(nm):
        i = midx[nm]
        return marena[:, i * TTW:(i + 1) * TTW]

    NSL = 6
    NPT = 8
    KW = 512
    VW = 260
    kslab = al.sb("aw_kslab", [65, NSL * KW], BF16, stack=st)
    vslab = al.sb("aw_vslab", [128, NSL * VW], BF16, stack=st)
    qstg = al.sb("aw_qstg", [65, 2 * 4 * TTW], BF16, stack=st)
    ptile = al.sb("aw_pt", [128, NPT * TTW], BF16, stack=st)
    rtmp4 = al.sb("aw_rt", [4, TTW], F32, stack=st)
    rb4 = al.sb("aw_rb", [4, TTW], BF16, stack=st)
    dtmp = al.sb("aw_dt", [1, 4 * TTW], F32, stack=st)
    bcs = al.sb("aw_bcs", [64, 2 * TTW], BF16, stack=st)
    ostg = al.sb("aw_ostg", [64, 4 * TTW], BF16, stack=st)
    P.waits("tensor", [pp.rel[i] for i in range(8)] + [P.now("vs"), P.now("ss")])
    av_banks = pp.tiles[0:4]
    s_banks = pp.tiles[4:7]
    bc_bank = pp.tiles[7]
    kv_rel = [None] * NSL
    slab_key = [None] * NSL
    slab_tok = [None] * NSL
    q_rel = None
    pt_rel = [None] * NPT
    o_rel = [None] * 4
    bc_rel = None
    bcs_rel = [None, None]
    s_rel = [None] * 3
    sidx = [0]
    ptidx = [0]
    ocnt = [0]

    pend_by_kc = {}

    def emit_av(it):
        (j, h, jj, pt, rdy, first, last, kc0) = it
        P.wait("tensor", rdy)
        t2 = P.mm(av_banks[j][0:65, :],
                  vslab[:, jj * VW + (h % 4) * 65: jj * VW + (h % 4 + 1) * 65],
                  ptile[:, pt * TTW:(pt + 1) * TTW],
                  start=first, stop=last, inc="ts")
        pt_rel[pt] = t2
        pend_by_kc[kc0] -= 1
        if pend_by_kc[kc0] == 0:
            kv_rel[kc0 % NSL] = t2

    def ensure_slab(shg, kc):
        jj = kc % NSL
        if slab_key[jj] != (shg, kc):
            if slab_key[jj] is not None:
                P.wait("gpsimd", kv_rel[jj])
            lk = P.dma("gpsimd", kslab[:, jj * KW:(jj + 1) * KW],
                       kext_dram[0:65,
                                 kc * 2048 + shg * 512: kc * 2048 + (shg + 1) * 512])
            lv = P.dma("gpsimd", vslab[:, jj * VW:(jj + 1) * VW],
                       vext_dram[kc * 128:(kc + 1) * 128,
                                 shg * 260:(shg + 1) * 260])
            slab_key[jj] = (shg, kc)
            slab_tok[jj] = [lk, lv]

    passes = [(hg, qg) for hg in range(4) for qg in range(NTT)]
    qrel_hist = [None, None]
    qtoks = [None, None]

    def build_qstg(p2):
        slot = p2 % 2
        off = slot * 4 * TTW
        hg2, qg2 = passes[p2]
        if qrel_hist[slot] is not None:
            P.waits("vector", [qrel_hist[slot]])
            P.waits("gpsimd", [qrel_hist[slot]])
        for j2, h2 in enumerate([4 * hg2 + j for j in range(4)]):
            c, p = h2 // 2, h2 % 2
            P.op("vector", "tensor_copy",
                 qstg[0:64, off + j2 * TTW: off + (j2 + 1) * TTW],
                 qf[p * 64:p * 64 + 64,
                    c * T + qg2 * TTW: c * T + (qg2 + 1) * TTW],
                 inc="vs" if j2 == 3 else None)
        toks = [P.now("vs")]
        for j2, h2 in enumerate([4 * hg2 + j for j in range(4)]):
            c, p = h2 // 2, h2 % 2
            toks.append(P.dma("gpsimd",
                              qstg[64:65, off + j2 * TTW: off + (j2 + 1) * TTW],
                              mq_dram[p:p + 1,
                                      c * T + qg2 * TTW: c * T + (qg2 + 1) * TTW]))
        qtoks[slot] = toks

    for pidx, (hg, qg) in enumerate(passes):
        if True:
            heads = [4 * hg + j for j in range(4)]
            plans = {h: kcs_fn(h, qg) for h in heads}
            union = sorted({kc for h in heads for (kc, _) in plans[h]})
            qoff = (pidx % 2) * 4 * TTW
            if pidx == 0:
                build_qstg(0)
            P.waits("tensor", qtoks[pidx % 2])
            started = {h: False for h in heads}
            nkc = {h: len(plans[h]) for h in heads}
            done_cnt = {h: 0 for h in heads}
            pend = []
            prev_kc = None
            for kc in union:
                jj = kc % NSL
                ensure_slab(hg, kc)
                if slab_tok[jj] is not None:
                    P.waits("tensor", slab_tok[jj])
                    slab_tok[jj] = None
                kc_items = []
                for j, h in enumerate(heads):
                    ms = [m for (kk, m) in plans[h] if kk == kc]
                    if not ms:
                        continue
                    mname = ms[0]
                    sj = sidx[0] % 3
                    sidx[0] += 1
                    sb = s_banks[sj]
                    if s_rel[sj] is not None:
                        P.wait("tensor", s_rel[sj])
                    t1 = P.mm(sb[:, :],
                              kslab[:, jj * KW + (h % 4) * 128:
                                       jj * KW + (h % 4 + 1) * 128],
                              qstg[0:65, qoff + j * TTW: qoff + (j + 1) * TTW],
                              start=True, stop=True, inc="ts")
                    pt = ptidx[0] % NPT
                    ptidx[0] += 1
                    P.wait("scalar", t1)
                    if pt_rel[pt] is not None:
                        P.wait("scalar", pt_rel[pt])
                    ex = P.op("scalar", "activation",
                              ptile[:, pt * TTW:(pt + 1) * TTW], sb[:, :],
                              AF.Exp, scale=0.125, inc="ss", pre_drain=False)
                    s_rel[sj] = ex
                    kc_items.append((j, h, jj, pt, mname))
                if not kc_items:
                    continue
                bt = P.op("scalar", "drain", inc="ss", pre_drain=False)
                any_mask = any(m is not None for (_, _, _, _, m) in kc_items)
                if any_mask:
                    P.wait("vector", bt)
                    for (j, h, jj2, pt, mname) in kc_items:
                        if mname is None:
                            continue
                        P.op("vector", "tensor_tensor",
                             ptile[:, pt * TTW:(pt + 1) * TTW],
                             ptile[:, pt * TTW:(pt + 1) * TTW],
                             mslice(mname), op=ALU.mult, pre_drain=False)
                    bt = P.op("vector", "drain", inc="vs", pre_drain=False)
                pend_by_kc[kc] = len(kc_items)
                for (j, h, jj2, pt, mname) in kc_items:
                    done_cnt[h] += 1
                    pend.append((j, h, jj2, pt, bt, not started[h],
                                 done_cnt[h] == nkc[h], kc))
                    started[h] = True
                while pend and (len(pend) > 4 or pend[0][7] <= kc - 2):
                    emit_av(pend.pop(0))
            while pend:
                emit_av(pend.pop(0))
            q_rel = P.now("ts")
            qrel_hist[pidx % 2] = q_rel
            for jj2 in range(NSL):
                kv_rel[jj2] = q_rel
            if pidx + 1 < len(passes):
                build_qstg(pidx + 1)
            if pidx + 1 < len(passes):
                nhg, nqg = passes[pidx + 1]
                nheads = [4 * nhg + j for j in range(4)]
                nunion = sorted({kc for h2 in nheads
                                 for (kc, _) in kcs_fn(h2, nqg)})
                for kc2 in nunion[:4]:
                    ensure_slab(nhg, kc2)
            P.waits("scalar", [q_rel])
            for j in range(4):
                P.op("scalar", "activation", dtmp[0:1, j * TTW:(j + 1) * TTW],
                     av_banks[j][64:65, :], AF.Copy, bias=1e-30,
                     inc="ss" if j == 3 else None)
            dt_tok = P.now("ss")
            sb0 = s_banks[0]
            P.waits("tensor", [dt_tok] + ([s_rel[0]] if s_rel[0] else []))
            for j in range(4):
                P.mm(sb0[0:4, :], consts["oh4"][0:1, j * 4:(j + 1) * 4],
                     dtmp[0:1, j * TTW:(j + 1) * TTW],
                     start=(j == 0), stop=(j == 3),
                     inc="ts" if j == 3 else None)
            gat_tok = P.now("ts")
            P.wait("vector", gat_tok)
            P.op("vector", "reciprocal", rtmp4[:, :], sb0[0:4, :], inc="vs")
            s_rel[0] = P.now("vs")
            if gate_fn is not None:
                gate_fn(P, pp, hg, qg, rtmp4, bc_bank, bc_rel)
            P.op("vector", "tensor_copy", rb4[:, :], rtmp4[:, :], inc="vs")
            rb_tok = P.now("vs")
            for j, h in enumerate(heads):
                av = av_banks[j]
                if bc_rel is not None:
                    P.wait("tensor", bc_rel)
                P.wait("tensor", rb_tok)
                t3 = P.mm(bc_bank[0:64, :],
                          consts["selb4"][:, j * 64:(j + 1) * 64], rb4[:, :],
                          start=True, stop=True, inc="ts")
                P.wait("scalar", t3)
                ic = j % 2
                if bcs_rel[ic] is not None:
                    P.wait("scalar", bcs_rel[ic])
                P.op("scalar", "activation", bcs[:, ic * TTW:(ic + 1) * TTW],
                     bc_bank[0:64, :], AF.Copy, inc="ss")
                bc_rel = P.now("ss")
                io = ocnt[0] % 4
                ocnt[0] += 1
                if o_rel[io] is not None:
                    P.wait("vector", o_rel[io])
                P.wait("vector", bc_rel)
                ov = P.op("vector", "tensor_tensor",
                          ostg[:, io * TTW:(io + 1) * TTW], av[0:64, :],
                          bcs[:, ic * TTW:(ic + 1) * TTW], op=ALU.mult,
                          inc="vs")
                bcs_rel[ic] = ov
                P.wait("gpsimd", ov)
                stt = P.dma("gpsimd",
                            attn_dram[h * 64:(h + 1) * 64,
                                      qg * TTW:(qg + 1) * TTW],
                            ostg[:, io * TTW:(io + 1) * TTW])
                o_rel[io] = stt
            P.waits("tensor", [P.now("vs")])
    st.close()
    for i in range(8):
        pp.rel[i] = P.now("vs")


def win_kcs_C(h, qg):
    return [(kc, None) for kc in range(32)]


def _stage_k(P, al, pp, consts, ktt, tt, kst, kst_rel, kext_dram):
    for s4 in range(4):
        kc = tt * 4 + s4
        j = kc % 2
        if kst_rel[j] is not None:
            P.wait("vector", kst_rel[j])
        for h in range(16):
            c, p = h // 2, h % 2
            P.op("vector", "tensor_copy",
                 kst[0:64, j * 2048 + h * 128: j * 2048 + (h + 1) * 128],
                 ktt[p * 64:p * 64 + 64,
                     c * TTW + s4 * 128: c * TTW + (s4 + 1) * 128],
                 inc="vs" if h == 15 else None, drain=(h == 15))
        cpt = P.now("vs")
        P.wait("gpsimd", cpt)
        stt = P.dma("gpsimd", kext_dram[0:65, kc * 2048:(kc + 1) * 2048],
                    kst[0:65, j * 2048:(j + 1) * 2048])
        kst_rel[j] = stt


def _stage_v(P, al, pp, consts, vtt, tt, vst, vst_rel, vtmp, vtmp_rel,
             vext_dram, ctab=None):
    for s4 in range(4):
        kc = tt * 4 + s4
        j = kc % 2
        if vtmp_rel[0] is not None:
            P.wait("vector", vtmp_rel[0])
        for h in range(16):
            c, p = h // 2, h % 2
            P.op("vector", "tensor_copy",
                 vtmp[0:64, h * 128:(h + 1) * 128],
                 vtt[p * 64:p * 64 + 64,
                     c * TTW + s4 * 128: c * TTW + (s4 + 1) * 128],
                 inc="vs" if h == 15 else None, drain=(h == 15))
        cp_tok = P.now("vs")
        P.wait("tensor", cp_tok)
        if vst_rel[j] is not None:
            P.wait("vector", vst_rel[j])
        trs = []
        for h in range(16):
            ib2, b2, g2 = pp.acquire()
            P.wait("tensor", g2)
            bb2 = b2.bitcast(BF16)
            tr = P.op("tensor", "transpose", bb2[0:128, 0:64],
                      vtmp[0:64, h * 128:(h + 1) * 128],
                      consts["ident"][0:64, 0:64], inc="ts")
            P.wait("vector", tr)
            if ctab is not None:
                P.op("vector", "tensor_scalar",
                     vst[:, j * 1040 + h * 65: j * 1040 + h * 65 + 64],
                     bb2[0:128, 0:64], ctab[:, h * 32 + kc: h * 32 + kc + 1],
                     None, op0=ALU.mult)
                ev3 = P.op("vector", "tensor_copy",
                           vst[:, j * 1040 + h * 65 + 64: j * 1040 + h * 65 + 65],
                           ctab[:, h * 32 + kc: h * 32 + kc + 1],
                           inc="vs", drain=(h == 15))
            else:
                ev3 = P.op("vector", "tensor_copy",
                           vst[:, j * 1040 + h * 65: j * 1040 + h * 65 + 64],
                           bb2[0:128, 0:64],
                           inc="vs", drain=(h == 15))
            pp.release(ib2, ev3)
        vtmp_rel[0] = P.now("vs")
        cpt = P.now("vs")
        P.wait("gpsimd", cpt)
        stt = P.dma("gpsimd", vext_dram[kc * 128:(kc + 1) * 128, :],
                    vst[:, j * 1040:(j + 1) * 1040])
        vst_rel[j] = stt


def _knorm_update(P, al, pp, consts, src_tile, k2m, oc, ksq, ksq_rel, k2p, k2s,
                  evk):
    i = oc % len(ksq_rel)
    P.wait("scalar", evk)
    if ksq_rel[i] is not None:
        P.wait("scalar", ksq_rel[i])
    sqt = P.op("scalar", "activation", ksq[:, i * TTW:(i + 1) * TTW],
               src_tile, AF.Square, inc="ss", drain=True)
    ib2, b2, g2 = pp.acquire()
    P.waits("tensor", [g2, sqt])
    t2 = P.mm(b2[0:2, :], consts["pair2"], ksq[:, i * TTW:(i + 1) * TTW],
              start=True, stop=True, inc="ts")
    ksq_rel[i] = t2
    P.wait("vector", t2)
    P.op("vector", "tensor_scalar", k2s[:, :], b2[0:2, :], 0.0, -1e30,
         op0=ALU.add, op1=ALU.max, accum_out=k2p[:, 1:2])
    ev2 = P.op("vector", "tensor_tensor", k2m[:, oc:oc + 1],
               k2m[:, oc:oc + 1], k2p[:, 1:2], op=ALU.max, inc="vs", drain=True)
    pp.release(ib2, ev2)
    return ev2


def _mq_pass(P, al, pp, consts, qf, nkx, mq_dram, st):
    qsq = al.sb("mq_qsq", [128, T], BF16, stack=st)
    qn8 = al.sb("mq_qn8", [2, T], F32, stack=st)
    mqt8 = al.sb("mq_mqt8", [2, T], BF16, stack=st)
    P.waits("scalar", [P.now("vs")])
    P.waits("tensor", [P.now("vs")])
    P.waits("vector", [P.now("vs"), P.now("ss")])
    sq_guard = None
    mq_rel = None
    for oc in range(NC):
        if sq_guard is not None:
            P.wait("scalar", sq_guard)
        sqt = P.op("scalar", "activation", qsq[:, :], chunk(qf, oc), AF.Square,
                   inc="ss", drain=True)
        P.wait("tensor", sqt)
        pbs = []
        for tt in range(NTT):
            ib2, b2, g2 = pp.acquire()
            P.wait("tensor", g2)
            t2 = P.mm(b2[0:2, :], consts["pair2"],
                      qsq[:, tt * TTW:(tt + 1) * TTW],
                      start=True, stop=True, inc="ts")
            pbs.append((ib2, b2))
        sq_guard = P.now("ts")
        P.wait("vector", sq_guard)
        if mq_rel is not None:
            P.wait("vector", mq_rel)
        for tt, (ib2, b2) in enumerate(pbs):
            cp = P.op("vector", "tensor_copy", qn8[:, tt * TTW:(tt + 1) * TTW],
                      b2[0:2, :], inc="vs", drain=(tt == NTT - 1))
            pp.release(ib2, cp)
        P.wait("scalar", P.now("vs"))
        st2 = P.op("scalar", "activation", qn8[:, :], qn8[:, :], AF.Sqrt,
                   inc="ss", drain=True)
        P.wait("vector", st2)
        mv = P.op("vector", "tensor_scalar", mqt8[:, :], qn8[:, :],
                  nkx[:, oc:oc + 1], None, op0=ALU.mult, inc="vs", drain=True)
        P.wait("gpsimd", mv)
        stt = P.dma("gpsimd", mq_dram[0:2, oc * T:(oc + 1) * T], mqt8[:, :])
        mq_rel = stt


def emit_D(P, al, consts, x_in, x_out, W):
    pp = PsumPool(P, [al.ps(f"ppD{i}", [128, 512]) for i in range(8)])
    nc = P.nc
    kext_dram = P.dram("D_kext", [65, 32 * 2048], BF16)
    vext_dram = P.dram("D_vext", [T, 1040], BF16)
    mq_dram = P.dram("D_mq", [2, NC * T], BF16)
    attn_dram = P.dram("D_attn", [D, T], BF16)
    xn_dram = P.dram("D_xn", [D, T], BF16)
    h_dram = P.dram("D_h", [D, T], BF16)

    k2m = al.sb("D_k2m", [2, 8], F32)
    nkx = al.sb("D_nkx", [2, 8], F32)
    g1b, tg1 = load_bias(P, al, "D_g1", W["ln1_g"], 1024)
    b1b, tb1 = load_bias(P, al, "D_b1", W["ln1_b"], 1024)
    g2b, tg2 = load_bias(P, al, "D_g2", W["ln2_g"], 1024)
    b2b, tb2 = load_bias(P, al, "D_b2", W["ln2_b"], 1024)

    st_qf = ExitStack()
    qf = al.sb("D_qf", [128, NC * T], BF16, stack=st_qf)
    st_x = ExitStack()
    X = al.sb("D_X", [128, NC * T], BF16, stack=st_x)

    st_ln = ExitStack()
    ln_toks = emit_ln(P, al, "D_ln", X, consts, pp, x_dram=x_in, substack=st_ln,
                      narrow=True)
    st_ln.close()
    sp_toks = []
    for c in range(NC):
        P.wait("gpsimd", ln_toks[c])
        sp_toks.append(P.dma("gpsimd", xn_dram[c * 128:(c + 1) * 128, :],
                             chunk(X, c)))
    st_ln1 = ExitStack()
    P.waits("vector", tg1 + tb1 + sp_toks)
    ln1_toks = emit_ln(P, al, "D_ln1", X, consts, pp, src_sbuf=lambda c: chunk(X, c),
                       src_toks=ln_toks + sp_toks,
                       gain=lambda c: g1b[:, c:c + 1], bias=lambda c: b1b[:, c:c + 1],
                       substack=st_ln1, narrow=True)
    st_ln1.close()

    st_cs = ExitStack()
    P.fence()
    cosf = al.sb("D_cosf", [128, T], BF16, stack=st_cs)
    sinf = al.sb("D_sinf", [128, T], BF16, stack=st_cs)
    t_cs = [P.dma("gpsimd", cosf[:, :], W["cosf"]),
            P.dma("gpsimd", sinf[:, :], W["sinf"])]
    st_k = ExitStack()
    P.fence()
    w_k, t_k = load_weight(P, al, "D_wk", W["wk"], 1024, 1024, stack=st_k)
    ktt = al.sb("D_ktt", [128, NC * TTW], BF16, stack=st_k)
    ksq = al.sb("D_ksq", [128, 4 * TTW], BF16, stack=st_k)
    kst = al.sb("D_kst", [65, 2 * 2048], BF16, stack=st_k)
    k2p = al.sb("D_k2p", [2, 4], F32, stack=st_k)
    k2s = al.sb("D_k2s", [2, TTW], F32, stack=st_k)
    rtile = al.sb("D_rtile", [128, 4 * TTW], BF16, stack=st_k)
    P.op("vector", "memset", k2m[:, :], 0.0)
    P.op("vector", "memset", kst[64:65, :], 1.0, inc="vs", drain=True)
    P.waits("tensor", t_k + ln1_toks + t_cs)
    P.waits("vector", t_cs)
    kst_rel = [None, None]
    ksq_rel = [None] * 4
    rt_rel = [None] * 4

    def rope_tile(dst, b_proj, oc, tt, ib):
        i = (tt * NC + oc) % 4
        ev0 = P.op("vector", "tensor_copy", dst, b_proj[:, :], inc="vs",
                   drain=True)
        ib3, b3, g3 = pp.acquire()
        P.waits("tensor", [g3, ev0])
        tr = P.mm(b3[:, :], consts["r2t"], dst, start=True, stop=True, inc="ts")
        P.wait("vector", tr)
        if rt_rel[i] is not None:
            P.wait("vector", rt_rel[i])
        P.op("vector", "tensor_tensor", rtile[:, i * TTW:(i + 1) * TTW],
             b3[:, :], sinf[:, tt * TTW:(tt + 1) * TTW], op=ALU.mult)
        P.op("vector", "tensor_tensor", dst, dst,
             cosf[:, tt * TTW:(tt + 1) * TTW], op=ALU.mult)
        rv = P.op("vector", "tensor_tensor", dst, dst,
                  rtile[:, i * TTW:(i + 1) * TTW], op=ALU.add, inc="vs",
                  drain=True)
        rt_rel[i] = rv
        pp.release(ib3, rv)
        pp.release(ib, ev0)
        return rv

    for tt in range(NTT):
        for oc in range(NC):
            ib, b, g = pp.acquire()
            P.wait("tensor", g)
            for kc in range(NC):
                P.mm(b[:, :], wslice(w_k, kc, oc, 1024), tile(X, kc, tt),
                     start=(kc == 0), stop=(kc == NC - 1),
                     inc="ts" if kc == NC - 1 else None)
            t = P.now("ts")
            P.wait("vector", t)
            rv = rope_tile(ktt[:, oc * TTW:(oc + 1) * TTW], b, oc, tt, ib)
            _knorm_update(P, al, pp, consts, ktt[:, oc * TTW:(oc + 1) * TTW],
                          k2m, oc, ksq, ksq_rel, k2p, k2s, rv)
        _stage_k(P, al, pp, consts, ktt, tt, kst, kst_rel, kext_dram)
    P.wait("scalar", P.now("vs"))
    P.op("scalar", "activation", nkx[:, :], k2m[:, :], AF.Sqrt, inc="ss",
         drain=True)
    P.wait("vector", P.now("ss"))
    P.op("vector", "tensor_scalar", nkx[:, :], nkx[:, :], -1.0, None,
         op0=ALU.mult, inc="vs", drain=True)
    st_k.close()

    st_q = ExitStack()
    P.fence()
    w_q, t_q = load_weight(P, al, "D_wq", W["wq"], 1024, 1024, stack=st_q)
    rtile2 = al.sb("D_rtile2", [128, 4 * TTW], BF16, stack=st_q)
    P.waits("tensor", t_q)
    rt2_rel = [None] * 4
    for tt in range(NTT):
        for oc in range(NC):
            ib, b, g = pp.acquire()
            P.wait("tensor", g)
            for kc in range(NC):
                P.mm(b[:, :], wslice(w_q, kc, oc, 1024), tile(X, kc, tt),
                     start=(kc == 0), stop=(kc == NC - 1),
                     inc="ts" if kc == NC - 1 else None)
            t = P.now("ts")
            P.wait("vector", t)
            dst = tile(qf, oc, tt)
            i = (tt * NC + oc) % 4
            ev0 = P.op("vector", "tensor_copy", dst, b[:, :], inc="vs",
                       drain=True)
            ib3, b3, g3 = pp.acquire()
            P.waits("tensor", [g3, ev0])
            tr = P.mm(b3[:, :], consts["r2t"], dst, start=True, stop=True,
                      inc="ts")
            P.wait("vector", tr)
            if rt2_rel[i] is not None:
                P.wait("vector", rt2_rel[i])
            P.op("vector", "tensor_tensor", rtile2[:, i * TTW:(i + 1) * TTW],
                 b3[:, :], sinf[:, tt * TTW:(tt + 1) * TTW],
                 op=ALU.mult)
            P.op("vector", "tensor_tensor", dst, dst,
                 cosf[:, tt * TTW:(tt + 1) * TTW], op=ALU.mult)
            rv = P.op("vector", "tensor_tensor", dst, dst,
                      rtile2[:, i * TTW:(i + 1) * TTW], op=ALU.add, inc="vs",
                      drain=True)
            rt2_rel[i] = rv
            pp.release(ib3, rv)
            pp.release(ib, ev0)
    _mq_pass(P, al, pp, consts, qf, nkx, mq_dram, st_q)
    st_q.close()
    st_cs.close()

    st_v = ExitStack()
    P.fence()
    w_v, t_v = load_weight(P, al, "D_wv", W["wv"], 1024, 1024, stack=st_v)
    vtt = al.sb("D_vtt", [128, NC * TTW], BF16, stack=st_v)
    vtmp = al.sb("D_vtmp", [64, 16 * 128], BF16, stack=st_v)
    vst = al.sb("D_vst", [128, 2 * 1040], BF16, stack=st_v)
    P.op("vector", "memset", vst[:, :], 1.0, inc="vs", drain=True)
    P.waits("tensor", t_v + [P.now("vs")])
    vst_rel = [None, None]
    vtmp_rel = [None, None]
    for tt in range(NTT):
        for oc in range(NC):
            ib, b, g = pp.acquire()
            P.wait("tensor", g)
            for kc in range(NC):
                P.mm(b[:, :], wslice(w_v, kc, oc, 1024), tile(X, kc, tt),
                     start=(kc == 0), stop=(kc == NC - 1),
                     inc="ts" if kc == NC - 1 else None)
            t = P.now("ts")
            P.wait("vector", t)
            evv = P.op("vector", "tensor_copy", vtt[:, oc * TTW:(oc + 1) * TTW],
                       b[:, :], inc="vs", drain=True)
            pp.release(ib, evv)
        _stage_v(P, al, pp, consts, vtt, tt, vst, vst_rel, vtmp, vtmp_rel,
                 vext_dram, ctab=None)
    st_v.close()
    st_x.close()

    emit_attn_win(P, al, pp, consts, qf, kext_dram, vext_dram, mq_dram,
                  attn_dram, win_kcs_D, masks_dram=W["masks"])
    st_qf.close()

    st_h = ExitStack()
    P.fence()
    hbuf = al.sb("D_hbuf", [128, NC * T], BF16, stack=st_h)
    st_wo = ExitStack()
    w_o, t_o = load_weight(P, al, "D_wo", W["wo"], 1024, 1024, stack=st_wo)
    att = al.sb("D_att", [128, 2 * NC * TTW], BF16, stack=st_wo)
    xnr = al.sb("D_xnr", [128, 4 * TTW], BF16, stack=st_wo)
    P.waits("tensor", t_o)
    att_rel = [None, None]
    xnr_rel = [None] * 4
    h_toks = []
    k4 = [0]
    for tt in range(NTT):
        ia = tt % 2
        if att_rel[ia] is not None:
            P.wait("gpsimd", att_rel[ia])
        src_ap = bass.AP(attn_dram.tensor, tt * TTW,
                         [[T, 128], [128 * T, NC], [1, TTW]])
        la = P.dma("gpsimd", att[:, ia * NC * TTW:(ia + 1) * NC * TTW], src_ap)
        P.wait("tensor", la)
        for oc in range(NC):
            ib, b, g = pp.acquire()
            P.wait("tensor", g)
            for kc in range(NC):
                P.mm(b[:, :], wslice(w_o, kc, oc, 1024),
                     att[:, ia * NC * TTW + kc * TTW: ia * NC * TTW + (kc + 1) * TTW],
                     start=(kc == 0), stop=(kc == NC - 1),
                     inc="ts" if kc == NC - 1 else None)
            t = P.now("ts")
            i = k4[0] % 4
            k4[0] += 1
            if xnr_rel[i] is not None:
                P.wait("sync", xnr_rel[i])
            lt = P.dma("sync", xnr[:, i * TTW:(i + 1) * TTW],
                       xn_dram[oc * 128:(oc + 1) * 128, tt * TTW:(tt + 1) * TTW])
            P.waits("vector", [t, lt])
            vt = P.op("vector", "tensor_tensor", tile(hbuf, oc, tt),
                      b[:, :], xnr[:, i * TTW:(i + 1) * TTW],
                      op=ALU.add, inc="vs", drain=True)
            pp.release(ib, vt)
            xnr_rel[i] = vt
            if oc == NC - 1:
                att_rel[ia] = P.now("ts")
            P.wait("gpsimd", vt)
            ht = P.dma("gpsimd",
                       h_dram[oc * 128:(oc + 1) * 128, tt * TTW:(tt + 1) * TTW],
                       tile(hbuf, oc, tt))
            h_toks.append(ht)
    st_wo.close()

    st_ln2 = ExitStack()
    P.fence()
    P.waits("vector", tg2 + tb2)
    ln2_toks = emit_ln(P, al, "D_ln2", hbuf, consts, pp,
                       src_sbuf=lambda c: chunk(hbuf, c),
                       src_toks=[P.now("vs")] + h_toks,
                       gain=lambda c: g2b[:, c:c + 1], bias=lambda c: b2b[:, c:c + 1],
                       substack=st_ln2, narrow=True)
    st_ln2.close()

    st_f = ExitStack()
    P.fence()
    w1a, t_w1 = load_weight(P, al, "D_w1f", W["w1"], 1024, 2816, stack=st_f)
    w3a, t_w3 = load_weight(P, al, "D_w3f", W["w3"], 1024, 2816, stack=st_f)
    w2a = al.sb("D_w2a", [128, 2 * 22 * 128], BF16, stack=st_f)
    z = al.sb("D_z", [128, 22 * TTW], BF16, stack=st_f)
    sil = al.sb("D_sil", [128, 2 * TTW], BF16, stack=st_f)
    xr = al.sb("D_xr", [128, 2 * TTW], F32, stack=st_f)
    hr = al.sb("D_hr", [128, 2 * TTW], BF16, stack=st_f)
    yt = al.sb("D_yt", [128, 2 * TTW], F32, stack=st_f)
    P.waits("tensor", ln2_toks + t_w1 + t_w3)
    out_toks = []
    w2_rel = [None, None]
    sil_rel = [None, None]
    z_rel = [None] * 22
    rel = {"xr": [None, None], "hr": [None, None], "yt": [None, None]}
    k5 = [0]
    for tt in range(NTT):
        for oc in range(22):
            ib, b, g = pp.acquire()
            P.wait("tensor", g)
            for kc in range(NC):
                P.mm(b[:, :], wslice(w1a, kc, oc, 2816),
                     tile(hbuf, kc, tt), start=(kc == 0), stop=(kc == NC - 1),
                     inc="ts" if kc == NC - 1 else None)
            t = P.now("ts")
            isl = oc % 2
            P.wait("scalar", t)
            if sil_rel[isl] is not None:
                P.wait("scalar", sil_rel[isl])
            sg = P.op("scalar", "activation", sil[:, isl * TTW:(isl + 1) * TTW],
                      b[:, :], AF.Silu, inc="ss", drain=True)
            pp.release(ib, sg)
            ib2, b2, g2 = pp.acquire()
            P.wait("tensor", g2)
            for kc in range(NC):
                P.mm(b2[:, :], wslice(w3a, kc, oc, 2816),
                     tile(hbuf, kc, tt), start=(kc == 0), stop=(kc == NC - 1),
                     inc="ts" if kc == NC - 1 else None)
            t2 = P.now("ts")
            P.waits("vector", [t2, sg])
            if z_rel[oc] is not None:
                P.wait("vector", z_rel[oc])
            zt = P.op("vector", "tensor_tensor", z[:, oc * TTW:(oc + 1) * TTW],
                      b2[:, :], sil[:, isl * TTW:(isl + 1) * TTW], op=ALU.mult,
                      inc="vs", drain=True)
            sil_rel[isl] = zt
            pp.release(ib2, zt)
        z_tok = P.now("vs")
        for oc in range(NC):
            iw2 = oc % 2
            if w2_rel[iw2] is not None:
                P.wait("gpsimd", w2_rel[iw2])
            tw2 = P.dma("gpsimd", w2a[:, iw2 * 22 * 128:(iw2 + 1) * 22 * 128],
                        bass.AP(W["w2"].tensor, oc * 128,
                                [[1024, 128], [128 * 1024, 22], [1, 128]]))
            ib, b, g = pp.acquire()
            P.waits("tensor", [g, z_tok, tw2])
            for kc in range(22):
                P.mm(b[:, :],
                     w2a[:, iw2 * 22 * 128 + kc * 128: iw2 * 22 * 128 + (kc + 1) * 128],
                     z[:, kc * TTW:(kc + 1) * TTW],
                     start=(kc == 0), stop=(kc == 21), inc="ts")
            t = P.now("ts")
            w2_rel[iw2] = t
            i = k5[0] % 2
            i2 = k5[0] % 2
            k5[0] += 1
            if rel["xr"][i2] is not None:
                P.wait("sync", rel["xr"][i2])
            lt = P.dma("sync", xr[:, i2 * TTW:(i2 + 1) * TTW],
                       x_in[oc * 128:(oc + 1) * 128, tt * TTW:(tt + 1) * TTW])
            if rel["hr"][i2] is not None:
                P.wait("sync", rel["hr"][i2])
            lh = P.dma("sync", hr[:, i2 * TTW:(i2 + 1) * TTW],
                       h_dram[oc * 128:(oc + 1) * 128, tt * TTW:(tt + 1) * TTW])
            P.waits("vector", [t, lt, lh])
            if rel["yt"][i] is not None:
                P.wait("vector", rel["yt"][i])
            P.op("vector", "tensor_tensor", yt[:, i * TTW:(i + 1) * TTW],
                 b[:, :], hr[:, i2 * TTW:(i2 + 1) * TTW], op=ALU.add)
            vt = P.op("vector", "tensor_tensor", yt[:, i * TTW:(i + 1) * TTW],
                      yt[:, i * TTW:(i + 1) * TTW], xr[:, i2 * TTW:(i2 + 1) * TTW],
                      op=ALU.add, inc="vs", drain=True)
            pp.release(ib, vt)
            rel["xr"][i2] = vt
            rel["hr"][i2] = vt
            if oc == NC - 1:
                for kk in range(22):
                    z_rel[kk] = P.now("ts")
            P.wait("sync", vt)
            ot = P.dma("sync",
                       x_out[oc * 128:(oc + 1) * 128, tt * TTW:(tt + 1) * TTW],
                       yt[:, i * TTW:(i + 1) * TTW])
            rel["yt"][i] = ot
            out_toks.append(ot)
    st_f.close()
    st_h.close()
    return out_toks


def prep_D_weights(inp):
    W = {}
    for k in ("wq", "wk", "wv", "wo", "w1", "w2", "w3"):
        W[k] = _np(inp["L1_attn_a_" + k])
    freqs = 1.0 / (10000.0 ** (np.arange(0, 64, 2, dtype=np.float64) / 64))
    t = np.arange(T, dtype=np.float64)
    ang = np.outer(freqs, t)
    j = (np.arange(128) % 64) // 2
    W["cosf"] = np.cos(ang)[j].astype(np.float32)
    W["sinf"] = np.sin(ang)[j].astype(np.float32)
    W["masks"] = prep_masks(win_kcs_D)
    W["ln1_g"] = _bvec(inp["L1_attn_a_ln1_g"])
    W["ln1_b"] = _bvec(inp["L1_attn_a_ln1_b"])
    W["ln2_g"] = _bvec(inp["L1_attn_a_ln2_g"])
    W["ln2_b"] = _bvec(inp["L1_attn_a_ln2_b"])
    return W


def emit_F(P, al, consts, x_in, x_out, W):
    pp = PsumPool(P, [al.ps(f"ppF{i}", [128, 512]) for i in range(8)])
    nc = P.nc
    kext_dram = P.dram("F_kext", [65, 32 * 2048], BF16)
    vext_dram = P.dram("F_vext", [T, 1040], BF16)
    mq_dram = P.dram("F_mq", [2, NC * T], BF16)
    attn_dram = P.dram("F_attn", [D, T], BF16)

    k2m = al.sb("F_k2m", [2, 8], F32)
    nkx = al.sb("F_nkx", [2, 8], F32)
    gT = al.sb("F_gT", [16, T], BF16)

    st_qf = ExitStack()
    qf = al.sb("F_qf", [128, NC * T], BF16, stack=st_qf)
    st_x = ExitStack()
    xn = al.sb("F_xn", [128, NC * T], BF16, stack=st_x)
    st_ln = ExitStack()
    ln_toks = emit_ln(P, al, "F_ln", xn, consts, pp, x_dram=x_in, substack=st_ln,
                      narrow=True)
    wg, t_wg = load_weight(P, al, "F_wg", W["Wgate"], 1024, 16)
    bgr = al.sb("F_bgr", [1, 16], F32)
    t_bg = P.dma("gpsimd", bgr[:, :], W["bgate"])
    b_qk, t_bqk = load_bias(P, al, "F_bqk", W["bqk"], 2048)
    st_ln.close()

    st_g = ExitStack()
    P.fence()
    bgbc = al.sb("F_bgbc", [128, 16], F32, stack=st_g)
    glg = al.sb("F_glg", [128, 16], F32, stack=st_g)
    gsc = al.sb("F_gsc", [128, 8], F32, stack=st_g)
    get = al.sb("F_get", [128, 16], F32, stack=st_g)
    gbf = al.sb("F_gbf", [128, 16], BF16, stack=st_g)
    ib, b, g = pp.acquire()
    P.waits("tensor", [g, t_bg])
    t = P.mm(b[0:128, 0:16], consts["onesf1"], bgr[:, :], start=True, stop=True,
             inc="ts")
    P.wait("vector", t)
    ev = P.op("vector", "tensor_copy", bgbc[:, :], b[0:128, 0:16], inc="vs",
              drain=True)
    pp.release(ib, ev)
    P.waits("tensor", t_wg + ln_toks)
    for tc in range(32):
        ib, b, g = pp.acquire()
        P.wait("tensor", g)
        tt = tc // 4
        co = (tc % 4) * 128
        for kc in range(NC):
            P.mm(b[:, 0:16], tile(xn, kc, tt)[:, co:co + 128],
                 wg[:, kc * 16:(kc + 1) * 16],
                 start=(kc == 0), stop=(kc == NC - 1),
                 inc="ts" if kc == NC - 1 else None)
        t = P.now("ts")
        P.wait("vector", t)
        P.op("vector", "tensor_tensor", glg[:, :], b[:, 0:16], bgbc[:, :],
             op=ALU.add, inc="vs", drain=True)
        pp.release(ib, P.now("vs"))
        P.op("vector", "max", gsc[:, :], glg[:, :])
        P.op("vector", "tensor_scalar", gsc[:, 1:2], gsc[:, 0:1], -1.0, None,
             op0=ALU.mult, inc="vs", drain=True)
        P.wait("scalar", P.now("vs"))
        P.op("scalar", "activation", get[:, :], glg[:, :], AF.Exp,
             bias=gsc[:, 1:2], accum_out=gsc[:, 2:3], inc="ss", drain=True)
        P.wait("vector", P.now("ss"))
        P.op("vector", "reciprocal", gsc[:, 3:4], gsc[:, 2:3])
        P.op("vector", "tensor_scalar", gbf[:, :], get[:, :], gsc[:, 3:4], None,
             op0=ALU.mult, inc="vs", drain=True)
        ib2, b2, g2 = pp.acquire()
        P.waits("tensor", [g2, P.now("vs")])
        bb2 = b2.bitcast(BF16)
        tr = P.op("tensor", "transpose", bb2[0:16, 0:128], gbf[:, :],
                  consts["ident"], inc="ts")
        P.wait("vector", tr)
        ev2 = P.op("vector", "tensor_copy", gT[:, tc * 128:(tc + 1) * 128],
                   bb2[0:16, 0:128], inc="vs", drain=True)
        pp.release(ib2, ev2)
    st_g.close()

    st_p = ExitStack()
    P.fence()
    wqk, t_wqk = load_weight(P, al, "F_wqk", W["Wqk"], 1024, 2048, stack=st_p)
    ktt = al.sb("F_ktt", [128, NC * TTW], BF16, stack=st_p)
    ksq = al.sb("F_ksq", [128, 2 * TTW], BF16, stack=st_p)
    kst = al.sb("F_kst", [65, 2 * 2048], BF16, stack=st_p)
    k2p = al.sb("F_k2p", [2, 4], F32, stack=st_p)
    k2s = al.sb("F_k2s", [2, TTW], F32, stack=st_p)
    vtmp = al.sb("F_vtmp", [64, 16 * 128], BF16, stack=st_p)
    vst = al.sb("F_vst", [128, 2 * 1040], BF16, stack=st_p)
    P.op("vector", "memset", k2m[:, :], 0.0)
    P.op("vector", "memset", kst[64:65, :], 1.0)
    P.op("vector", "memset", vst[:, :], 1.0, inc="vs", drain=True)
    P.waits("tensor", t_wqk + [P.now("vs")])
    P.waits("vector", t_bqk)
    kst_rel = [None, None]
    ksq_rel = [None, None]
    vst_rel = [None, None]
    vtmp_rel = [None, None]
    for tt in range(NTT):
        for oc in range(16):
            ib, b, g = pp.acquire()
            P.wait("tensor", g)
            for kc in range(NC):
                P.mm(b[:, :], wslice(wqk, kc, oc, 2048), tile(xn, kc, tt),
                     start=(kc == 0), stop=(kc == NC - 1),
                     inc="ts" if kc == NC - 1 else None)
            t = P.now("ts")
            P.wait("vector", t)
            if oc < 8:
                ev = P.op("vector", "tensor_scalar", tile(qf, oc, tt),
                          b[:, :], b_qk[:, oc:oc + 1], None, op0=ALU.add,
                          inc="vs", drain=True)
                pp.release(ib, ev)
            else:
                ko = oc - 8
                ev = P.op("vector", "tensor_scalar",
                          ktt[:, ko * TTW:(ko + 1) * TTW],
                          b[:, :], b_qk[:, oc:oc + 1], None, op0=ALU.add,
                          inc="vs", drain=True)
                pp.release(ib, ev)
                _knorm_update(P, al, pp, consts, ktt[:, ko * TTW:(ko + 1) * TTW],
                              k2m, ko, ksq, ksq_rel, k2p, k2s, ev)
        _stage_k(P, al, pp, consts, ktt, tt, kst, kst_rel, kext_dram)
        _stage_v(P, al, pp, consts, ktt, tt, vst, vst_rel, vtmp, vtmp_rel,
                 vext_dram, ctab=None)
    P.wait("scalar", P.now("vs"))
    P.op("scalar", "activation", nkx[:, :], k2m[:, :], AF.Sqrt, inc="ss",
         drain=True)
    P.wait("vector", P.now("ss"))
    P.op("vector", "tensor_scalar", nkx[:, :], nkx[:, :], -1.0, None,
         op0=ALU.mult, inc="vs", drain=True)
    st_p.close()
    st_x.close()
    st_mq = ExitStack()
    P.fence()
    _mq_pass(P, al, pp, consts, qf, nkx, mq_dram, st_mq)
    st_mq.close()

    g7_rel = [None]

    def gate_fn(P2, pp2, hg, qg, rtmp4, bank7, bc_rel):
        if bc_rel is not None:
            P2.wait("tensor", bc_rel)
        tg = P2.mm(bank7[0:4, :], consts["ident"][0:16, hg * 4:hg * 4 + 4],
                   gT[0:16, qg * TTW:(qg + 1) * TTW], start=True, stop=True,
                   inc="ts")
        P2.wait("vector", tg)
        P2.op("vector", "tensor_tensor", rtmp4[:, :], rtmp4[:, :],
              bank7[0:4, :], op=ALU.mult, inc="vs")

    emit_attn_win(P, al, pp, consts, qf, kext_dram, vext_dram, mq_dram,
                  attn_dram, win_kcs_F, gate_fn=gate_fn, masks_dram=W["masks"])
    st_qf.close()

    st_wo = ExitStack()
    P.fence()
    w_o, t_o = load_weight(P, al, "F_wo", W["Wproj"], 1024, 1024, stack=st_wo)
    b_o, tbo = load_bias(P, al, "F_bo", W["bproj"], 1024, stack=st_wo)
    att = al.sb("F_att", [128, 2 * NC * TTW], BF16, stack=st_wo)
    xr = al.sb("F_xr", [128, 4 * TTW], F32, stack=st_wo)
    yt = al.sb("F_yt", [128, 4 * TTW], F32, stack=st_wo)
    P.waits("tensor", t_o)
    P.waits("vector", tbo)
    att_rel = [None, None]
    rel = {"xr": [None] * 4, "yt": [None] * 4}
    out_toks = []
    P.waits("gpsimd", P.dma_all_done("gpsimd"))
    k4 = [0]
    for tt in range(NTT):
        ia = tt % 2
        if att_rel[ia] is not None:
            P.wait("gpsimd", att_rel[ia])
        src_ap = bass.AP(attn_dram.tensor, tt * TTW,
                         [[T, 128], [128 * T, NC], [1, TTW]])
        la = P.dma("gpsimd", att[:, ia * NC * TTW:(ia + 1) * NC * TTW], src_ap)
        P.wait("tensor", la)
        for oc in range(NC):
            ib, b, g = pp.acquire()
            P.wait("tensor", g)
            for kc in range(NC):
                P.mm(b[:, :], wslice(w_o, kc, oc, 1024),
                     att[:, ia * NC * TTW + kc * TTW: ia * NC * TTW + (kc + 1) * TTW],
                     start=(kc == 0), stop=(kc == NC - 1),
                     inc="ts" if kc == NC - 1 else None)
            t = P.now("ts")
            i = k4[0] % 4
            k4[0] += 1
            if rel["xr"][i] is not None:
                P.wait("sync", rel["xr"][i])
            lt = P.dma("sync", xr[:, i * TTW:(i + 1) * TTW],
                       x_in[oc * 128:(oc + 1) * 128, tt * TTW:(tt + 1) * TTW])
            P.waits("vector", [t, lt])
            if rel["yt"][i] is not None:
                P.wait("vector", rel["yt"][i])
            P.op("vector", "tensor_scalar", yt[:, i * TTW:(i + 1) * TTW],
                 b[:, :], b_o[:, oc:oc + 1], None, op0=ALU.add)
            vt = P.op("vector", "tensor_tensor", yt[:, i * TTW:(i + 1) * TTW],
                      yt[:, i * TTW:(i + 1) * TTW], xr[:, i * TTW:(i + 1) * TTW],
                      op=ALU.add, inc="vs", drain=True)
            pp.release(ib, vt)
            rel["xr"][i] = vt
            if oc == NC - 1:
                att_rel[ia] = P.now("ts")
            P.wait("sync", vt)
            ot = P.dma("sync",
                       x_out[oc * 128:(oc + 1) * 128, tt * TTW:(tt + 1) * TTW],
                       yt[:, i * TTW:(i + 1) * TTW])
            rel["yt"][i] = ot
            out_toks.append(ot)
    st_wo.close()
    return out_toks


def prep_F_weights(inp):
    W = {}
    W["Wqk"] = _np(inp["L1_attn_b_Wqk"])
    W["bqk"] = _bvec(inp["L1_attn_b_bqk"])
    W["Wgate"] = _np(inp["L1_attn_b_Wgate"])
    W["bgate"] = _np(inp["L1_attn_b_bgate"]).reshape(1, 16)
    W["Wproj"] = _np(inp["L1_attn_b_Wproj"])
    W["bproj"] = _bvec(inp["L1_attn_b_bproj"])
    W["masks"] = prep_masks(win_kcs_F)
    return W


FUSED = True


def build_fused(seq, wshapes_by_pfx):
    nc = bass.Bass("TRN2", target_bir_lowering=False)
    x_in = nc.dram_tensor("x_in", [D, T], F32, kind="ExternalInput").ap()
    x_out = nc.dram_tensor("x_out", [D, T], F32, kind="ExternalOutput").ap()
    streams = [x_in]
    for i in range(len(seq) - 1):
        streams.append(nc.dram_tensor(f"xs{i}", [D, T], F32).ap())
    streams.append(x_out)
    Wd_by_pfx = {}
    for pfx, shapes in wshapes_by_pfx.items():
        Wd_by_pfx[pfx] = {k: nc.dram_tensor(f"w_{pfx}{k}", list(s), F32,
                                            kind="ExternalInput").ap()
                          for k, s in shapes.items()}
    need = sorted({n for e, _ in seq for n in _const_need(e.__name__)})
    stack = ExitStack()
    with stack:
        al = Alloc(nc, stack)
        P = Plan(nc, stack)
        P.alloc = al
        consts, ctoks = declare_consts(nc, al, P, make_consts(), need=need)
        P.waits("tensor", ctoks)
        P.waits("vector", ctoks)
        P.waits("scalar", ctoks)
        out_toks = None
        for i, (emitter, pfx) in enumerate(seq):
            sub = ExitStack()
            al.default_stack = sub
            al.prefix = f"c{i}_"
            P.fence()
            out_toks = emitter(P, al, consts, streams[i], streams[i + 1],
                               Wd_by_pfx[pfx])
            P.fence()
            sub.close()
            al.default_stack = stack
            al.prefix = ""
        P.waits("sync", out_toks)
        P.waits("sync", P.dma_all_done("sync") + P.dma_all_done("gpsimd"))
        P.build()
    return nc


def run_fused(inputs, x_fm):
    WA = prep_A_weights(inputs)
    WB = prep_B_weights(inputs)
    WC = prep_C_weights(inputs)
    WD = prep_D_weights(inputs)
    WEs = [prep_E_weights(inputs, c) for c in range(8)]
    WF = prep_F_weights(inputs)
    seq = [(emit_A, "A"), (emit_B, "B"), (emit_C, "C"), (emit_B, "B"),
           (emit_D, "Dd"), (emit_E, "E"), (emit_F, "F"), (emit_E, "E")]
    wshapes = {"A": {k: v.shape for k, v in WA.items()},
               "B": {k: v.shape for k, v in WB.items()},
               "C": {k: v.shape for k, v in WC.items()},
               "Dd": {k: v.shape for k, v in WD.items()},
               "E": {k: v.shape for k, v in WEs[0].items()},
               "F": {k: v.shape for k, v in WF.items()}}
    nc = build_fused(seq, wshapes)
    consts = make_consts()
    need = sorted({n for e, _ in seq for n in _const_need(e.__name__)})
    in_maps = []
    for c in range(8):
        m = {"x_in": x_fm[c]}
        for pfx, W in [("A", WA), ("B", WB), ("C", WC), ("Dd", WD),
                       ("E", WEs[c]), ("F", WF)]:
            for k, v in W.items():
                m[f"w_{pfx}{k}"] = v
        for k, v in consts.items():
            if k in need:
                m["c_" + k] = v
        in_maps.append(m)
    res = run_bass_kernel_spmd(nc, in_maps, core_ids=list(range(8)))
    return [res.results[c]["x_out"] for c in range(8)]
